# version 03
# baseline (speedup 1.0000x reference)
import numpy as np
import contextlib
import concourse.bass as bass
import concourse.mybir as mybir
from concourse.bass_utils import run_bass_kernel_spmd

F32 = mybir.dt.float32
BF16 = mybir.dt.bfloat16
I16 = mybir.dt.int16
U16 = mybir.dt.uint16
I8 = mybir.dt.int8
AF = mybir.ActivationFunctionType
AX = mybir.AxisListType
ALU = mybir.AluOpType

DM = 1024
NCH = DM // 128
SSM = 64
RNN = 128
EPS = 1e-5
ENGS = ["sync", "tensor", "scalar", "vector", "gpsimd"]


def build_program(SEQ, DI, n_cores=8):
    HALF = 256
    NHALF = SEQ // HALF
    NBODY = NHALF // 2
    OWN = SEQ // 4
    NTI = OWN // 128
    NSL_ALL = SEQ // 512
    DIC = DI // 128
    DISH = DI // 8
    NG = DI // 512

    nc = bass.Bass("TRN2", target_bir_lowering=False)

    ap_in = lambda n, s, d=F32: nc.dram_tensor(n, s, d, kind="ExternalInput").ap()
    xb = ap_in("xb", [SEQ, DM])
    xq = ap_in("xq", [OWN, DM])
    role = ap_in("role", [1, 8], mybir.dt.int32)
    whhT = ap_in("whhT", [RNN, 384])
    aT = ap_in("aT", [SSM, SSM])
    wall = ap_in("wall", [DM, 512])
    bias_g = ap_in("bias_g", [128, 4])
    bhh_g = ap_in("bhh_g", [128, 3])
    ident = ap_in("ident", [128, 128])
    cwT = ap_in("cwT", [SSM, DM])
    projT = ap_in("projT", [RNN, DM])
    cb_row = ap_in("cb_row", [1, DM])
    db_row = ap_in("db_row", [1, DM])
    pb_row = ap_in("pb_row", [1, DM])
    dwT = ap_in("dwT", [DM, DM])
    wg_s = ap_in("wg_s", [DISH, DM])
    wu_s = ap_in("wu_s", [DISH, DM])
    wd_s = ap_in("wd_s", [DM // 8, DI])
    out = nc.dram_tensor("out", [OWN, DM], F32, kind="ExternalOutput").ap()

    dr = lambda n, s, d: nc.dram_tensor(n, s, d).ap()
    s_x1tm = dr("s_x1tm", [SEQ, DM], BF16)
    s_x1fm = dr("s_x1fm", [DM, SEQ], BF16)
    s_ir = dr("s_ir", [RNN, SEQ + 128], BF16)
    s_iz = dr("s_iz", [RNN, SEQ + 128], BF16)
    s_bx = dr("s_bx", [SSM, SEQ + 128], BF16)
    s_in = dr("s_in", [RNN, SEQ + 256], F32)
    s_gx = dr("s_gx", [2, SEQ], F32)
    s_hg = dr("s_hg", [RNN, SEQ], BF16)
    s_hs = dr("s_hs", [SSM, SEQ], BF16)
    s_x2 = dr("s_x2", [OWN, DM], F32)
    s_l3 = dr("s_l3", [OWN, 2], F32)
    s_up = dr("s_up", [OWN, DI], BF16)
    s_gt = dr("s_gt", [OWN, DI], BF16)
    s_gs = dr("s_gs", [DI, OWN], BF16)
    s_op = dr("s_op", [DI, OWN], BF16)
    s_dum = dr("s_dum", [RNN, 4 * 256], BF16)
    cc_i = dr("cc_i", [1, 8], F32)
    cc_o = dr("cc_o", [1, 8], F32)
    ag_g_i = dr("ag_g_i", [DISH, DM], BF16)
    ag_g_o = dr("ag_g_o", [DI, DM], BF16)
    ag_u_i = dr("ag_u_i", [DISH, DM], BF16)
    ag_u_o = dr("ag_u_o", [DI, DM], BF16)
    ag_d_i = dr("ag_d_i", [DM // 8, DI], BF16)
    ag_d_o = dr("ag_d_o", [DM, DI], BF16)
    RG = [list(range(n_cores))]

    es = contextlib.ExitStack()
    with es:
        _all_sb = []

        def sb(n, s, d):
            t = es.enter_context(nc.sbuf_tensor(n, s, d))
            _all_sb.append(t)
            return t
        sem = lambda n: es.enter_context(nc.semaphore(n))

        wscan = sb("wscan", [128, 448], BF16)
        i128 = sb("i128", [128, 128], BF16)
        tp_id = sb("tp_id", [128, 128], BF16)
        wallb = sb("wallb", [128, NCH * 512], BF16)
        cwTb = sb("cwTb", [SSM, DM], BF16)
        projTb = sb("projTb", [RNN, DM], BF16)
        biasg = sb("biasg", [128, 4], F32)
        bhhn_blk = sb("bhhn_blk", [128, 128], BF16)
        cbdb_b = sb("cbdb_b", [128, DM], F32)
        projb_b = sb("projb_b", [128, DM], F32)
        ones1 = sb("ones1", [1, DM], BF16)
        scal = sb("scal", [128, 16], F32)
        rolev = sb("rolev", [1, 8], mybir.dt.int32)
        pre_t = [[sb(f"pre{p}{s}", [128, 512], BF16) for s in range(2)] for p in range(2)]
        in_t = [sb(f"int{s}", [RNN, HALF], F32) for s in range(2)]
        hgw = sb("hgw", [RNN, 2 * HALF], BF16)
        hsw = sb("hsw", [SSM, 2 * HALF], BF16)
        st3 = sb("st3", [128, 6], F32)
        ncol = sb("ncol", [128, 2], F32)
        csc = sb("csc", [128, 2], F32)
        xtile = sb("xtile", [128, DM], F32)
        xtile2 = sb("xtile2", [128, DM], F32)
        stat6 = sb("stat6", [128, 64], F32)
        stat2 = sb("stat2", [128, 2], F32)
        tok4 = sb("tok4", [128, 8], F32)
        x1t = sb("x1t", [128, DM], BF16)
        xfm = [sb(f"xfm{s}", [128, NCH * 512], BF16) for s in range(2)]
        prodo = sb("prodo", [128, 512], BF16)
        prodf = sb("prodf", [128, 512], F32)
        hgt = sb("hgt", [RNN, 128], BF16)
        hst = sb("hst", [SSM, 128], BF16)
        hgO = sb("hgO", [RNN, SEQ // 4], BF16)
        hsO = sb("hsO", [SSM, SEQ // 4], BF16)
        gxO = sb("gxO", [2, SEQ // 4], F32)
        g2t = sb("g2t", [128, 2], F32)
        ytile = sb("ytile", [128, DM], F32)
        x2t = sb("x2t", [128, DM], F32)
        maxa = sb("maxa", [128, 4], F32)
        BIGW = max(NCH * (SEQ // 4), DIC * DM + NCH * DM)
        bigT = sb("bigT", [128, BIGW], BF16)
        wff = sb("wff", [128, NCH * 512], BF16)
        identf = sb("identf", [128, 128], F32)
        onesf1 = sb("onesf1", [1, 128], F32)
        ccrow = sb("ccrow", [1, 8], F32)
        ccrow2 = sb("ccrow2", [1, 8], F32)
        projrow = sb("projrow", [1, DM], BF16)
        oslc = sb("oslc", [128, max(DIC * 512, NCH * DM)], BF16)
        dwTb = oslc
        fft = sb("fft", [128, 512], BF16)
        ffta = sb("ffta", [128, 512], F32)
        fftb = sb("fftb", [128, 512], F32)
        fftc = sb("fftc", [128, 512], F32)
        i16t = prodf.bitcast(I16)
        u16t = prodo.bitcast(U16)
        pb = [es.enter_context(nc.psum_tensor(f"pb{p}", [128, 512], F32)) for p in range(2)]
        pp = [es.enter_context(nc.psum_tensor(f"pp{p}", [128, 512], F32)) for p in range(3)]
        pq = es.enter_context(nc.psum_tensor("pq", [128, 512], F32))
        ptr = es.enter_context(nc.psum_tensor("ptr", [128, 128], BF16))

        ld = sem("ld"); sd_pre = sem("sd_pre"); sd_h = sem("sd_h"); sd_o = sem("sd_o")
        s_mm = sem("s_mm"); s_act = sem("s_act"); s_dve = sem("s_dve")
        s_mm2 = sem("s_mm2"); s_act2 = sem("s_act2"); s_premm = sem("s_premm")
        s_sg = sem("s_sg"); s_c1 = sem("s_c1"); s_cc = sem("s_cc"); s_gen = sem("s_gen")
        bar = [sem(f"bar{p}") for p in range(2)]
        go = [sem(f"go{p}") for p in range(2)]
        block = es.enter_context(nc.Block())

        T, V, S, G, SY = nc.tensor, nc.vector, nc.scalar, nc.gpsimd, nc.sync
        cnt = {"ld": 0, "sd_o": 0, "gen": 0, "mm": 0, "act": 0, "cc": 0}

        def ldma(out_ap, in_ap, eng=SY, transpose=False, guard=True):
            if guard:
                eng.wait_ge(s_gen, cnt["gen"])
                eng.wait_ge(ld, cnt["ld"])
                eng.wait_ge(sd_o, cnt["sd_o"])
            cnt["ld"] += 16
            if transpose:
                eng.dma_start_transpose(out_ap, in_ap).then_inc(ld, 16)
            else:
                eng.dma_start(out=out_ap, in_=in_ap).then_inc(ld, 16)
            return cnt["ld"]

        def odma(out_ap, in_ap, eng=G):
            eng.wait_ge(sd_o, cnt["sd_o"])
            cnt["sd_o"] += 16
            eng.dma_start(out=out_ap, in_=in_ap).then_inc(sd_o, 16)
            return cnt["sd_o"]

        def gen(instr):
            cnt["gen"] += 1
            instr.then_inc(s_gen, 1)
            return cnt["gen"]

        def mm(instr):
            cnt["mm"] += 1
            instr.then_inc(s_mm, 1)
            return cnt["mm"]

        def act(instr):
            cnt["act"] += 1
            instr.then_inc(s_act, 1)
            return cnt["act"]

        def vwait(c=None):
            V.wait_ge(s_gen, cnt["gen"] if c is None else c)

        for _t in _all_sb:
            gen(V.memset(_t.ap(), 0)); vwait()
        v = ldma(rolev[:, :], role[:, :])
        v = ldma(xtile[0:RNN, 0:384], whhT[:, :])
        v = ldma(xtile[0:SSM, 384:448], aT[:, :])
        v = ldma(xtile2[:, 0:128], ident[:, :])
        v = ldma(biasg[:, 0:4], bias_g[:, :])
        v = ldma(tok4[:, 0:3], bhh_g[:, :])
        V.wait_ge(ld, v)
        gen(V.memset(wscan[:, :], 0.0)); vwait()
        gen(V.tensor_copy(wscan[0:RNN, 0:384], xtile[0:RNN, 0:384])); vwait()
        gen(V.tensor_copy(wscan[0:SSM, 384:448], xtile[0:SSM, 384:448])); vwait()
        gen(V.tensor_copy(i128[:, :], xtile2[:, 0:128])); vwait()
        gen(V.tensor_copy(tp_id[:, :], xtile2[:, 0:128])); vwait()
        gen(V.tensor_add(biasg[:, 0:2], biasg[:, 0:2], tok4[:, 0:2])); vwait()
        gen(V.memset(bhhn_blk[:, :], 0.0)); vwait()
        gen(V.tensor_scalar(bhhn_blk[:, :], bhhn_blk[:, :], tok4[:, 2:3], None, ALU.add)); vwait()
        gen(V.memset(ones1[:, :], 1.0)); vwait()
        gen(V.memset(hgw[:, :], 0.0)); vwait()
        gen(V.memset(hsw[:, :], 0.0)); vwait()
        gen(V.memset(maxa[:, :], 0.0)); vwait()
        gen(V.memset(scal[:, :], 0.0)); vwait()
        for ch in range(NCH):
            v = ldma(xtile[:, 0:512], wall.rearrange("(c p) s -> c p s", p=128)[ch])
            V.wait_ge(ld, v); vwait()
            gen(V.tensor_copy(wallb[:, 512 * ch:512 * (ch + 1)], xtile[:, 0:512]))
        for ch in range(NCH):
            v = ldma(xtile[:, 0:DM], dwT.rearrange("(c p) s -> c p s", p=128)[ch])
            V.wait_ge(ld, v); vwait()
            gen(V.tensor_copy(dwTb[:, DM * ch:DM * (ch + 1)], xtile[:, 0:DM]))
        v = ldma(xtile[0:SSM, 0:DM], cwT[:, :])
        V.wait_ge(ld, v); vwait()
        gen(V.tensor_copy(cwTb[:, :], xtile[0:SSM, 0:DM]))
        v = ldma(xtile[0:RNN, 0:DM], projT[:, :])
        V.wait_ge(ld, v); vwait()
        gen(V.tensor_copy(projTb[:, :], xtile[0:RNN, 0:DM]))
        v = ldma(xtile[0:1, 0:DM], cb_row[:, :])
        v = ldma(xtile2[0:1, 0:DM], db_row[:, :])
        v = ldma(ytile[0:1, 0:DM], pb_row[:, :])
        V.wait_ge(ld, v); vwait()
        gen(V.tensor_add(xtile[0:1, 0:DM], xtile[0:1, 0:DM], xtile2[0:1, 0:DM])); vwait()
        gen(V.tensor_copy(x1t[0:1, 0:DM], xtile[0:1, 0:DM])); vwait()
        gen(V.tensor_copy(projrow[0:1, 0:DM], ytile[0:1, 0:DM]))
        T.wait_ge(s_gen, cnt["gen"])
        for hh in range(DM // 512):
            m = mm(T.matmul(pp[2][:, 0:512], ones1[0:1, 0:128], x1t[0:1, 512 * hh:512 * (hh + 1)],
                            start=True, stop=True))
            V.wait_ge(s_mm, m)
            gen(V.tensor_copy(cbdb_b[:, 512 * hh:512 * (hh + 1)], pp[2][:, 0:512]))
            T.wait_ge(s_gen, cnt["gen"])
            m = mm(T.matmul(pp[2][:, 0:512], ones1[0:1, 0:128], projrow[0:1, 512 * hh:512 * (hh + 1)],
                            start=True, stop=True))
            V.wait_ge(s_mm, m)
            gen(V.tensor_copy(projb_b[:, 512 * hh:512 * (hh + 1)], pp[2][:, 0:512]))
            T.wait_ge(s_gen, cnt["gen"])

        def dq_panel(rr, cols):
            V.wait_ge(sd_o, cnt["sd_o"])
            nb = cols // 32
            gen(V.tensor_reduce(stat6[0:rr, 0:nb], xtile[0:rr, 0:cols].rearrange("p (b k) -> p b k", k=32),
                                AX.X, ALU.max, apply_absolute_value=True)); vwait()
            gen(V.tensor_scalar(stat6[0:rr, 0:nb], stat6[0:rr, 0:nb], 1.0 / 7.0, 1e-12, ALU.mult, ALU.max)); vwait()
            gen(V.reciprocal(stat6[0:rr, nb:2 * nb], stat6[0:rr, 0:nb]))
            for b in range(nb):
                vwait()
                gen(V.tensor_scalar(xtile2[0:rr, 32 * b:32 * (b + 1)], xtile[0:rr, 32 * b:32 * (b + 1)],
                                    stat6[0:rr, nb + b:nb + b + 1], None, ALU.mult))
            vwait()
            gen(V.tensor_copy(i16t[0:rr, 0:cols], xtile2[0:rr, 0:cols])); vwait()
            gen(V.tensor_copy(xtile2[0:rr, 0:cols], i16t[0:rr, 0:cols]))
            for b in range(nb):
                vwait()
                gen(V.tensor_scalar(x1t[0:rr, 32 * b:32 * (b + 1)], xtile2[0:rr, 32 * b:32 * (b + 1)],
                                    stat6[0:rr, b:b + 1], None, ALU.mult))

        def dequant(src_ap, rows, cols, dst_dram):
            ngr = (rows + 127) // 128
            for g in range(ngr):
                r0 = g * 128
                rr = min(128, rows - r0)
                for p0 in range(0, cols, DM):
                    pw = min(DM, cols - p0)
                    v = ldma(xtile[0:rr, 0:pw], src_ap[r0:r0 + rr, p0:p0 + pw])
                    V.wait_ge(ld, v)
                    dq_panel(rr, pw)
                    G.wait_ge(s_gen, cnt["gen"])
                    odma(dst_dram[r0:r0 + rr, p0:p0 + pw], x1t[0:rr, 0:pw])

        dequant(wg_s, DISH, DM, ag_g_i)
        dequant(wu_s, DISH, DM, ag_u_i)
        dequant(wd_s, DM // 8, DI, ag_d_i)
        G.wait_ge(sd_o, cnt["sd_o"])
        for (ai, ao) in ((ag_g_i, ag_g_o), (ag_u_i, ag_u_o), (ag_d_i, ag_d_o)):
            cnt["cc"] += 1
            G.collective_compute("AllGather", ALU.bypass, replica_groups=RG,
                                 ins=[ai[:, :]], outs=[ao[:, :]]).then_inc(s_cc, 1)

        NT_ALL = SEQ // 128
        for ti in range(NT_ALL):
            xt = xtile if ti % 2 == 0 else xtile2
            v = ldma(xt[:, :], xb[128 * ti:128 * (ti + 1), :])
            V.wait_ge(ld, v)
            vwait()
            gen(V.bn_stats(stat6[:, 0:6], xt[:, 0:512])); vwait()
            gen(V.bn_stats(stat6[:, 6:12], xt[:, 512:1024])); vwait()
            gen(V.bn_aggr(stat2[:, 0:2], stat6[:, 0:12])); vwait()
            gen(V.tensor_scalar(tok4[:, 2:3], stat2[:, 1:2], EPS, None, ALU.add))
            S.wait_ge(s_gen, cnt["gen"])
            a = act(S.activation(tok4[:, 3:4], tok4[:, 2:3], AF.Sqrt))
            V.wait_ge(s_act, a)
            gen(V.reciprocal(tok4[:, 1:2], tok4[:, 3:4])); vwait()
            V.wait_ge(sd_o, cnt["sd_o"])
            gen(V.tensor_scalar(x1t[:, :], xt[:, :], stat2[:, 0:1], tok4[:, 1:2], ALU.subtract, ALU.mult))
            G.wait_ge(s_gen, cnt["gen"])
            odma(s_x1tm[128 * ti:128 * (ti + 1), :], x1t[:, :])
        G.wait_ge(sd_o, cnt["sd_o"])
        SY.wait_ge(sd_o, cnt["sd_o"])

        evac_done = {0: 0, 1: 0}
        buf_guard = {0: 0, 1: 0}
        dump_guard = {0: 0, 1: 0}
        SHF = SEQ // 2
        for sl in range(NSL_ALL):
            buf = xfm[sl % 2]
            SY.wait_ge(s_mm, buf_guard[sl % 2])
            SY.wait_ge(sd_o, dump_guard[sl % 2])
            SY.wait_ge(ld, cnt["ld"])
            for ch in range(NCH):
                v = ldma(buf[:, 512 * ch:512 * (ch + 1)],
                         s_x1tm[512 * sl:512 * (sl + 1), 128 * ch:128 * (ch + 1)], transpose=True, guard=False)
            G.wait_ge(ld, v)
            for ch in range(NCH):
                odma(s_x1fm.rearrange("(c p) t -> c p t", p=128)[ch][:, 512 * sl:512 * (sl + 1)],
                     buf[:, 512 * ch:512 * (ch + 1)])
            T.wait_ge(ld, v)
            for g in range(4):
                bank = pp[g % 2]
                T.wait_ge(s_gen, evac_done[g % 2])
                for ch in range(NCH):
                    mi = T.matmul(bank[:, 0:512], wallb[:, 512 * ch + 128 * g:512 * ch + 128 * (g + 1)],
                                  buf[:, 512 * ch:512 * (ch + 1)],
                                  start=(ch == 0), stop=(ch == NCH - 1), skip_group_check=True)
                    if ch == NCH - 1:
                        m = mm(mi)
                V.wait_ge(s_mm, m)
                V.wait_ge(sd_o, cnt["sd_o"])
                if g < 2:
                    dst = s_ir if g == 0 else s_iz
                    gen(V.tensor_scalar(prodo[:, 0:256],
                                        bank[:, 0:512].rearrange("p (t two) -> p two t", two=2)[:, 0, :],
                                        biasg[:, g:g + 1], None, ALU.add)); vwait()
                    gen(V.tensor_scalar(prodo[:, 256:512],
                                        bank[:, 0:512].rearrange("p (t two) -> p two t", two=2)[:, 1, :],
                                        biasg[:, g:g + 1], None, ALU.add))
                    evac_done[g % 2] = cnt["gen"]
                    G.wait_ge(s_gen, cnt["gen"])
                    odma(dst[:, 256 * sl:256 * (sl + 1)], prodo[:, 0:256])
                    odma(dst[:, SHF + 256 * sl:SHF + 256 * (sl + 1)], prodo[:, 256:512])
                elif g == 2:
                    gen(V.tensor_scalar(prodf[:, 0:512], bank[:, 0:512], biasg[:, 2:3], None, ALU.add))
                    evac_done[g % 2] = cnt["gen"]
                    G.wait_ge(s_gen, cnt["gen"])
                    odma(s_in[:, 512 * sl:512 * (sl + 1)], prodf[:, 0:512])
                else:
                    gen(V.tensor_scalar(prodf[:, 0:512], bank[:, 0:512], biasg[:, 3:4], None, ALU.add))
                    ec = cnt["gen"]
                    vwait()
                    gen(V.tensor_copy(prodo[0:SSM, 0:256],
                                      prodf[0:SSM, 0:512].rearrange("p (t two) -> p two t", two=2)[:, 0, :])); vwait()
                    gen(V.tensor_copy(prodo[0:SSM, 256:512],
                                      prodf[0:SSM, 0:512].rearrange("p (t two) -> p two t", two=2)[:, 1, :]))
                    S.wait_ge(s_gen, ec)
                    S.wait_ge(sd_o, cnt["sd_o"])
                    a = act(S.activation(fftb[64:66, 0:512], prodf[64:66, 0:512], AF.Sigmoid))
                    evac_done[g % 2] = cnt["gen"]
                    G.wait_ge(s_gen, cnt["gen"])
                    G.wait_ge(s_act, a)
                    odma(s_bx[:, 256 * sl:256 * (sl + 1)], prodo[0:SSM, 0:256])
                    odma(s_bx[:, SHF + 256 * sl:SHF + 256 * (sl + 1)], prodo[0:SSM, 256:512])
                    odma(s_gx[:, 512 * sl:512 * (sl + 1)], fftb[64:66, 0:512])
            buf_guard[sl % 2] = cnt["mm"]
            dump_guard[sl % 2] = cnt["sd_o"]
        G.wait_ge(sd_o, cnt["sd_o"])
        G.wait_ge(s_cc, cnt["cc"])

        PREDMA = 7 * 16

        def issue_half_streams(eng, g_expr, slot, prior=None):
            if prior is not None:
                eng.wait_ge(sd_pre, prior)
            e0 = g_expr * HALF // 2 if False else None
            base_e = g_expr * 128
            eng.dma_start(out=pre_t[0][slot][0:RNN, 0:128], in_=s_ir[:, bass.ts(g_expr, 128)]).then_inc(sd_pre, 16)
            eng.dma_start(out=pre_t[0][slot][0:RNN, 128:256], in_=s_iz[:, bass.ts(g_expr, 128)]).then_inc(sd_pre, 16)
            eng.dma_start(out=pre_t[0][slot][0:SSM, 256:384], in_=s_bx[:, bass.ts(g_expr, 128)]).then_inc(sd_pre, 16)
            eng.dma_start(out=pre_t[1][slot][0:RNN, 0:128],
                          in_=s_ir[:, SHF:SEQ + 128][:, bass.ts(g_expr, 128)]).then_inc(sd_pre, 16)
            eng.dma_start(out=pre_t[1][slot][0:RNN, 128:256],
                          in_=s_iz[:, SHF:SEQ + 128][:, bass.ts(g_expr, 128)]).then_inc(sd_pre, 16)
            eng.dma_start(out=pre_t[1][slot][0:SSM, 256:384],
                          in_=s_bx[:, SHF:SEQ + 128][:, bass.ts(g_expr, 128)]).then_inc(sd_pre, 16)
            eng.dma_start(out=in_t[slot][:, :], in_=s_in[:, bass.ts(g_expr, HALF)]).then_inc(sd_pre, 16)

        V.wait_ge(s_act, cnt["act"])
        V.wait_ge(sd_o, cnt["sd_o"])
        gen(V.memset(pre_t[0][0][SSM:128, 256:384], 0.0)); vwait()
        gen(V.memset(pre_t[0][1][SSM:128, 256:384], 0.0)); vwait()
        gen(V.memset(pre_t[1][0][SSM:128, 256:384], 0.0)); vwait()
        gen(V.memset(pre_t[1][1][SSM:128, 256:384], 0.0)); vwait()
        gen(V.memset(prodo[:, :], 0.0)); vwait()
        gen(V.memset(prodf[:, :], 0.0)); vwait()
        G.wait_ge(s_gen, cnt["gen"])
        odma(s_ir[:, SEQ:SEQ + 128], prodo[0:RNN, 0:128])
        odma(s_iz[:, SEQ:SEQ + 128], prodo[0:RNN, 0:128])
        odma(s_bx[:, SEQ:SEQ + 128], prodo[0:SSM, 0:128])
        odma(s_in[:, SEQ:SEQ + 256], prodf[0:RNN, 0:256])
        for dk in range(2):
            G.wait_ge(sd_h, 16 * dk)
            G.dma_start(out=s_dum[:, 256 * dk:256 * dk + HALF], in_=hgw[:, 0:HALF]).then_inc(sd_h, 16)
        SY.wait_ge(sd_o, cnt["sd_o"])
        issue_half_streams(SY, 0, 0)
        G.wait_ge(sd_o, cnt["sd_o"])
        for e in ENGS:
            eng = getattr(nc, e)
            eng.drain().then_inc(bar[1], 1)
            eng.wait_ge(bar[1], 5)

        mm0 = cnt["mm"]
        act0 = cnt["act"]
        rT_dve = T.alloc_register("rT_dve")
        rT_pre = T.alloc_register("rT_pre")
        rT_h = T.alloc_register("rT_h")
        rV_act = V.alloc_register("rV_act")
        rV_c1 = V.alloc_register("rV_c1")
        rV_h = V.alloc_register("rV_h")
        rS_mm = S.alloc_register("rS_mm")
        rS_sg = S.alloc_register("rS_sg")
        rS_pre = S.alloc_register("rS_pre")
        rY_premm = SY.alloc_register("rY_premm")
        rY_act = SY.alloc_register("rY_act")
        rY_pre = SY.alloc_register("rY_pre")
        rG_dve = G.alloc_register("rG_dve")
        rG_h = G.alloc_register("rG_h")
        T.reg_mov(rT_dve, 0)
        T.reg_mov(rT_pre, PREDMA)
        T.reg_mov(rT_h, 32)
        V.reg_mov(rV_act, 1)
        V.reg_mov(rV_c1, 1)
        V.reg_mov(rV_h, 32)
        S.reg_mov(rS_mm, 1)
        S.reg_mov(rS_sg, 1)
        S.reg_mov(rS_pre, PREDMA)
        SY.reg_mov(rY_premm, 0)
        SY.reg_mov(rY_act, 0)
        SY.reg_mov(rY_pre, PREDMA)
        G.reg_mov(rG_dve, HALF)
        G.reg_mov(rG_h, 32)
        with nc.Fori(0, NBODY, 1) as ib:
            for X in range(2):
                g_idx = ib * 2 + X
                SY.wait_ge(sd_pre, rY_pre)
                SY.reg_add(rY_pre, rY_pre, PREDMA)
                SY.wait_ge(s_premm, rY_premm)
                SY.reg_add(rY_premm, rY_premm, 2)
                SY.wait_ge(s_act2, rY_act)
                SY.reg_add(rY_act, rY_act, HALF)
                issue_half_streams(SY, g_idx + 1, 1 - X)
                T.wait_ge(sd_h, rT_h)
                T.reg_add(rT_h, rT_h, 32)
                V.wait_ge(sd_h, rV_h)
                V.reg_add(rV_h, rV_h, 32)
                T.wait_ge(sd_pre, rT_pre)
                T.reg_add(rT_pre, rT_pre, PREDMA)
                S.wait_ge(sd_pre, rS_pre)
                S.reg_add(rS_pre, rS_pre, PREDMA)
                T.matmul(pb[0][:, 0:384], i128[:, :], pre_t[0][X][:, 0:384], start=True, stop=False,
                         skip_group_check=True)
                T.matmul(pb[0][:, 384:512], i128[:, :], bhhn_blk[:, :], start=False, stop=True,
                         skip_group_check=True).then_inc(s_premm, 1)
                T.matmul(pb[1][:, 0:384], i128[:, :], pre_t[1][X][:, 0:384], start=True, stop=False,
                         skip_group_check=True)
                T.matmul(pb[1][:, 384:512], i128[:, :], bhhn_blk[:, :], start=False, stop=True,
                         skip_group_check=True).then_inc(s_premm, 1)
                for t in range(HALF):
                    par = t % 2
                    j = t // 2
                    sslot = t % 2
                    hprev = (X * HALF + t - 1) % (2 * HALF)
                    hcur = X * HALF + t
                    bank = pb[par]
                    T.wait_ge(s_dve, rT_dve)
                    T.reg_add(rT_dve, rT_dve, 1)
                    T.matmul(bank[:, j:j + 1], wscan[:, 0:128], hgw[:, hprev:hprev + 1],
                             start=False, stop=False, skip_group_check=True)
                    T.matmul(bank[:, 128 + j:129 + j], wscan[:, 128:256], hgw[:, hprev:hprev + 1],
                             start=False, stop=False, skip_group_check=True)
                    T.matmul(bank[0:SSM, 256 + j:257 + j], wscan[0:SSM, 384:448], hsw[:, hprev:hprev + 1],
                             start=False, stop=False, skip_group_check=True).then_inc(s_mm2, 1)
                    T.matmul(bank[:, 384 + j:385 + j], wscan[:, 256:384], hgw[:, hprev:hprev + 1],
                             start=False, stop=False, skip_group_check=True).then_inc(s_mm2, 1)
                    S.wait_ge(s_mm2, rS_mm)
                    S.reg_add(rS_mm, rS_mm, 1)
                    S.activation(st3[:, 3 * sslot:3 * sslot + 3],
                                 bank[:, 0:512].rearrange("p (b j) -> p j b", b=4)[:, j, 0:3],
                                 AF.Sigmoid).then_inc(s_sg, 1)
                    S.wait_ge(s_mm2, rS_mm)
                    S.reg_add(rS_mm, rS_mm, 1)
                    S.wait_ge(s_sg, rS_sg)
                    S.reg_add(rS_sg, rS_sg, 1)
                    S.activation(ncol[:, sslot:sslot + 1], bank[:, 384 + j:385 + j], AF.Tanh,
                                 scale=st3[:, 3 * sslot:3 * sslot + 1],
                                 bias=in_t[X][:, t:t + 1]).then_inc(s_act2, 1)
                    V.wait_ge(s_act2, rV_act)
                    V.reg_add(rV_act, rV_act, 1)
                    V.tensor_scalar(csc[:, sslot:sslot + 1], hgw[:, hprev:hprev + 1],
                                    ncol[:, sslot:sslot + 1], st3[:, 3 * sslot + 1:3 * sslot + 2],
                                    ALU.subtract, ALU.mult).then_inc(s_c1, 1)
                    V.wait_ge(s_c1, rV_c1)
                    V.reg_add(rV_c1, rV_c1, 1)
                    V.tensor_add(hgw[:, hcur:hcur + 1], csc[:, sslot:sslot + 1], ncol[:, sslot:sslot + 1])
                    V.tensor_mul(hsw[:, hcur:hcur + 1], bank[0:SSM, 256 + j:257 + j],
                                 st3[0:SSM, 3 * sslot + 2:3 * sslot + 3]).then_inc(s_dve, 1)
                G.wait_ge(s_dve, rG_dve)
                G.reg_add(rG_dve, rG_dve, HALF)
                G.wait_ge(sd_h, rG_h)
                G.reg_add(rG_h, rG_h, 32)
                G.dma_start(out=s_hg[:, bass.ts(g_idx, HALF)],
                            in_=hgw[:, X * HALF:(X + 1) * HALF]).then_inc(sd_h, 16)
                G.dma_start(out=s_hs[:, bass.ts(g_idx, HALF)],
                            in_=hsw[:, X * HALF:(X + 1) * HALF]).then_inc(sd_h, 16)

        for e in ENGS:
            eng = getattr(nc, e)
            eng.drain().then_inc(bar[0], 1)
            eng.wait_ge(bar[0], 5)
        for e in ENGS:
            getattr(nc, e).wait_ge(sd_h, 32 + 32 * NHALF)

        NSLO = OWN // 512

        v = ldma(identf[:, :], ident[:, :])
        V.wait_ge(ld, v)
        gen(V.memset(onesf1[:, :], 1.0)); vwait()
        gen(V.memset(maxa[:, :], 0.0)); vwait()
        gen(V.memset(ccrow[:, :], 0.0)); vwait()

        def bcast_scal(col, src_ap):
            T.wait_ge(s_gen, cnt["gen"])
            m = mm(T.matmul(pq[:, 0:1], onesf1[0:1, 0:128], src_ap, start=True, stop=True))
            V.wait_ge(s_mm, m)
            gen(V.tensor_copy(scal[:, col:col + 1], pq[:, 0:1]))

        def maxcol_to_row(col, dst_col):
            vwait()
            V.wait_ge(sd_o, cnt["sd_o"])
            V.wait_ge(s_act, cnt["act"])
            gen(V.tensor_copy(fft[:, 0:1], maxa[:, col:col + 1]))
            T.wait_ge(s_gen, cnt["gen"])
            m = mm(T.transpose(ptr[0:1, 0:128], fft[:, 0:1], tp_id[:, :]))
            V.wait_ge(s_mm, m)
            gen(V.reduce_max(ccrow[0:1, dst_col:dst_col + 1], ptr[0:1, 0:128], AX.X)); vwait()

        qv = nc.sync.cc_rank(RG) % 4

        for ch in range(NCH):
            v = ldma(bigT[:, ch * OWN:(ch + 1) * OWN],
                     s_x1fm.rearrange("(c p) t -> c p t", p=128)[ch][:, :][:, bass.ts(qv, OWN)])
        v = ldma(hgO[:, :], s_hg[:, bass.ts(qv, OWN)])
        v = ldma(hsO[:, :], s_hs[:, bass.ts(qv, OWN)])
        v = ldma(gxO[:, :], s_gx[:, bass.ts(qv, OWN)])
        T.wait_ge(ld, v)

        for k in range(NTI):
            V.wait_ge(sd_o, cnt["sd_o"])
            v1 = ldma(xtile[:, :], xq[128 * k:128 * (k + 1), :])
            for half in range(2):
                T.wait_ge(ld, v1)
                for ch in range(NCH):
                    mi = T.matmul(pp[half][:, 0:512], bigT[:, ch * OWN + 128 * k:ch * OWN + 128 * (k + 1)],
                                  dwTb[:, DM * ch + 512 * half:DM * ch + 512 * (half + 1)],
                                  start=(ch == 0), stop=(ch == NCH - 1), skip_group_check=True)
                    if ch == NCH - 1:
                        m = mm(mi)
            for half in range(2):
                m = mm(T.matmul(pb[half][:, 0:512], hsO[0:SSM, 128 * k:128 * (k + 1)],
                                cwTb[0:SSM, 512 * half:512 * (half + 1)],
                                start=True, stop=True, skip_group_check=True))
            m = mm(T.matmul(pq[:, 0:512], hgO[:, 128 * k:128 * (k + 1)], projTb[:, 0:512], start=True, stop=True,
                            skip_group_check=True))
            m = mm(T.matmul(pp[2][:, 0:512], hgO[:, 128 * k:128 * (k + 1)], projTb[:, 512:1024], start=True, stop=True,
                            skip_group_check=True))
            V.wait_ge(ld, v1)
            gen(V.tensor_copy(fft[0:2, 0:128], gxO[0:2, 128 * k:128 * (k + 1)]))
            T.wait_ge(s_gen, cnt["gen"])
            m2 = mm(T.transpose(ptr[:, 0:2], fft[0:2, 0:128], tp_id[0:2, 0:2]))
            V.wait_ge(s_mm, m)
            V.wait_ge(s_mm, m2)
            gen(V.tensor_copy(g2t[:, 0:2], ptr[:, 0:2])); vwait()
            for half in range(2):
                H = slice(512 * half, 512 * (half + 1))
                gen(V.tensor_add(ytile[:, H], pp[half][:, 0:512], cbdb_b[:, H])); vwait()
                gen(V.tensor_add(ytile[:, H], ytile[:, H], pb[half][:, 0:512])); vwait()
                gen(V.tensor_scalar(ytile[:, H], ytile[:, H], g2t[:, 0:1], None, ALU.mult)); vwait()
                rb = pq if half == 0 else pp[2]
                gen(V.tensor_add(x2t[:, H], rb[:, 0:512], projb_b[:, H])); vwait()
                gen(V.tensor_scalar(x2t[:, H], x2t[:, H], g2t[:, 1:2], None, ALU.mult)); vwait()
                gen(V.tensor_add(ytile[:, H], ytile[:, H], x2t[:, H])); vwait()
            gen(V.bn_stats(stat6[:, 0:6], ytile[:, 0:512])); vwait()
            gen(V.bn_stats(stat6[:, 6:12], ytile[:, 512:1024])); vwait()
            gen(V.bn_aggr(stat2[:, 0:2], stat6[:, 0:12])); vwait()
            gen(V.tensor_scalar(tok4[:, 2:3], stat2[:, 1:2], EPS, None, ALU.add))
            S.wait_ge(s_gen, cnt["gen"])
            a = act(S.activation(tok4[:, 3:4], tok4[:, 2:3], AF.Sqrt))
            V.wait_ge(s_act, a)
            gen(V.reciprocal(tok4[:, 1:2], tok4[:, 3:4])); vwait()
            gen(V.tensor_scalar(ytile[:, :], ytile[:, :], stat2[:, 0:1], tok4[:, 1:2],
                                ALU.subtract, ALU.mult)); vwait()
            gen(V.tensor_add(x2t[:, :], ytile[:, :], xtile[:, :])); vwait()
            G.wait_ge(s_gen, cnt["gen"])
            odma(s_x2[128 * k:128 * (k + 1), :], x2t[:, :])
            gen(V.bn_stats(stat6[:, 0:6], x2t[:, 0:512])); vwait()
            gen(V.bn_stats(stat6[:, 6:12], x2t[:, 512:1024])); vwait()
            gen(V.bn_aggr(stat2[:, 0:2], stat6[:, 0:12])); vwait()
            gen(V.tensor_scalar(tok4[:, 2:3], stat2[:, 1:2], EPS, None, ALU.add))
            S.wait_ge(s_gen, cnt["gen"])
            a = act(S.activation(tok4[:, 3:4], tok4[:, 2:3], AF.Sqrt))
            V.wait_ge(s_act, a)
            gen(V.reciprocal(tok4[:, 1:2], tok4[:, 3:4])); vwait()
            gen(V.tensor_copy(tok4[:, 0:1], stat2[:, 0:1])); vwait()
            G.wait_ge(s_gen, cnt["gen"])
            odma(s_l3[128 * k:128 * (k + 1), 0:2], tok4[:, 0:2])
            gen(V.tensor_scalar(ytile[:, :], x2t[:, :], tok4[:, 0:1], None, ALU.subtract)); vwait()
            gen(V.reduce_max(tok4[:, 4:5], ytile[:, :], AX.X, apply_absolute_value=True)); vwait()
            gen(V.tensor_scalar(tok4[:, 5:6], tok4[:, 4:5], tok4[:, 1:2], None, ALU.mult)); vwait()
            gen(V.tensor_max(maxa[:, 0:1], maxa[:, 0:1], tok4[:, 5:6])); vwait()
        maxcol_to_row(0, 0)
        G.wait_ge(s_gen, cnt["gen"])
        odma(cc_i[0:1, 0:8], ccrow[0:1, 0:8])
        G.wait_ge(sd_o, cnt["sd_o"])
        cnt["cc"] += 1
        G.collective_compute("AllReduce", ALU.max, replica_groups=RG,
                             ins=[cc_i[:, :]], outs=[cc_o[:, :]]).then_inc(s_cc, 1)
        G.wait_ge(s_cc, cnt["cc"])
        SY.wait_ge(s_cc, cnt["cc"])
        v = ldma(ccrow2[0:1, 0:8], cc_o[0:1, 0:8])
        V.wait_ge(ld, v)
        gen(V.tensor_scalar(ccrow2[0:1, 1:2], ccrow2[0:1, 0:1], 1.0 / 32767.0, 1e-12, ALU.mult, ALU.max)); vwait()
        gen(V.reciprocal(ccrow2[0:1, 2:3], ccrow2[0:1, 1:2])); vwait()
        bcast_scal(0, ccrow2[0:1, 1:2])
        bcast_scal(1, ccrow2[0:1, 2:3])

        for k in range(NTI):
            V.wait_ge(sd_o, cnt["sd_o"])
            v1 = ldma(x2t[:, :], s_x2[128 * k:128 * (k + 1), :])
            v1 = ldma(tok4[:, 0:2], s_l3[128 * k:128 * (k + 1), 0:2])
            V.wait_ge(ld, v1)
            gen(V.tensor_scalar(tok4[:, 2:3], tok4[:, 1:2], scal[:, 1:2], None, ALU.mult)); vwait()
            gen(V.tensor_scalar(ytile[:, :], x2t[:, :], tok4[:, 0:1], tok4[:, 2:3],
                                ALU.subtract, ALU.mult)); vwait()
            gen(V.tensor_copy(i16t[:, :], ytile[:, :])); vwait()
            gen(V.tensor_copy(x1t[:, :], i16t[:, :])); vwait()
            for ch in range(NCH):
                T.wait_ge(s_gen, cnt["gen"])
                m = mm(T.transpose(ptr[:, :], x1t[:, 128 * ch:128 * (ch + 1)], tp_id[:, :]))
                V.wait_ge(s_mm, m)
                gen(V.tensor_copy(bigT[:, ch * OWN + 128 * k:ch * OWN + 128 * (k + 1)], ptr[:, :]))
        for (wsrc, dst, mcol) in ((ag_u_o, s_up, 1), (ag_g_o, s_gt, 2)):
            for g in range(NG):
                SY.wait_ge(s_mm, cnt["mm"])
                SY.wait_ge(ld, cnt["ld"])
                for ch in range(NCH):
                    v = ldma(wff[:, 512 * ch:512 * (ch + 1)],
                             wsrc[512 * g:512 * (g + 1), 128 * ch:128 * (ch + 1)], transpose=True, guard=False)
                T.wait_ge(ld, v)
                for k in range(NTI):
                    bank = pp[k % 2]
                    T.wait_ge(s_gen, cnt["gen"])
                    for ch in range(NCH):
                        mi = T.matmul(bank[:, 0:512], bigT[:, ch * OWN + 128 * k:ch * OWN + 128 * (k + 1)],
                                      wff[:, 512 * ch:512 * (ch + 1)],
                                      start=(ch == 0), stop=(ch == NCH - 1), skip_group_check=True)
                        if ch == NCH - 1:
                            m = mm(mi)
                    V.wait_ge(s_mm, m)
                    V.wait_ge(sd_o, cnt["sd_o"])
                    gen(V.tensor_scalar(ffta[:, 0:512], bank[:, 0:512], scal[:, 0:1], None, ALU.mult)); vwait()
                    gen(V.tensor_copy(fft[:, 0:512], ffta[:, 0:512])); vwait()
                    gen(V.reduce_max(tok4[:, 6:7], ffta[:, 0:512], AX.X, apply_absolute_value=True)); vwait()
                    gen(V.tensor_max(maxa[:, mcol:mcol + 1], maxa[:, mcol:mcol + 1], tok4[:, 6:7])); vwait()
                    G.wait_ge(s_gen, cnt["gen"])
                    odma(dst[128 * k:128 * (k + 1), 512 * g:512 * (g + 1)], fft[:, 0:512])
        maxcol_to_row(1, 1)
        maxcol_to_row(2, 2)
        G.wait_ge(s_gen, cnt["gen"])
        odma(cc_i[0:1, 0:8], ccrow[0:1, 0:8])
        G.wait_ge(sd_o, cnt["sd_o"])
        cnt["cc"] += 1
        G.collective_compute("AllReduce", ALU.max, replica_groups=RG,
                             ins=[cc_i[:, :]], outs=[cc_o[:, :]]).then_inc(s_cc, 1)
        G.wait_ge(s_cc, cnt["cc"])
        SY.wait_ge(s_cc, cnt["cc"])
        v = ldma(ccrow2[0:1, 0:8], cc_o[0:1, 0:8])
        V.wait_ge(ld, v)
        gen(V.tensor_scalar(ccrow2[0:1, 3:4], ccrow2[0:1, 1:2], 1.0 / 32767.0, 1e-12, ALU.mult, ALU.max)); vwait()
        gen(V.tensor_scalar(ccrow2[0:1, 4:5], ccrow2[0:1, 2:3], 1.0 / 32767.0, 1e-12, ALU.mult, ALU.max)); vwait()
        gen(V.reciprocal(ccrow2[0:1, 5:6], ccrow2[0:1, 3:4])); vwait()
        gen(V.reciprocal(ccrow2[0:1, 6:7], ccrow2[0:1, 4:5])); vwait()
        bcast_scal(2, ccrow2[0:1, 3:4])
        bcast_scal(3, ccrow2[0:1, 5:6])
        bcast_scal(4, ccrow2[0:1, 4:5])
        bcast_scal(5, ccrow2[0:1, 6:7])
        gen(V.memset(ccrow[:, :], 0.0)); vwait()

        for sl in range(NSLO):
            for ic in range(DIC):
                v = ldma(fft[:, 0:512], s_gt[512 * sl:512 * (sl + 1), 128 * ic:128 * (ic + 1)], transpose=True)
                V.wait_ge(ld, v)
                V.wait_ge(sd_o, cnt["sd_o"])
                gen(V.tensor_scalar(ffta[:, 0:512], fft[:, 0:512], scal[:, 5:6], None, ALU.mult)); vwait()
                gen(V.tensor_copy(i16t[:, 0:512], ffta[:, 0:512])); vwait()
                gen(V.tensor_copy(ffta[:, 0:512], i16t[:, 0:512]))
                S.wait_ge(s_gen, cnt["gen"])
                a = act(S.activation(fftb[:, 0:512], ffta[:, 0:512], AF.Sigmoid, scale=scal[:, 4:5]))
                V.wait_ge(s_act, a)
                gen(V.tensor_scalar(fftb[:, 0:512], fftb[:, 0:512], 65536.0, None, ALU.mult)); vwait()
                gen(V.tensor_copy(u16t[:, 0:512], fftb[:, 0:512])); vwait()
                gen(V.tensor_copy(fftb[:, 0:512], u16t[:, 0:512])); vwait()
                gen(V.tensor_mul(fftc[:, 0:512], ffta[:, 0:512], fftb[:, 0:512])); vwait()
                gen(V.tensor_copy(fft[:, 0:512], fftc[:, 0:512])); vwait()
                gen(V.reduce_max(tok4[:, 6:7], fftc[:, 0:512], AX.X, apply_absolute_value=True)); vwait()
                gen(V.tensor_max(maxa[:, 3:4], maxa[:, 3:4], tok4[:, 6:7])); vwait()
                G.wait_ge(s_gen, cnt["gen"])
                odma(s_gs[128 * ic:128 * (ic + 1), 512 * sl + qv:512 * sl + qv + 512] if False else
                     s_gs[128 * ic:128 * (ic + 1), 512 * sl:512 * (sl + 1)], fft[:, 0:512])
        vwait()
        gen(V.tensor_scalar(maxa[:, 3:4], maxa[:, 3:4], scal[:, 4:5], 1.0 / 65536.0, ALU.mult, ALU.mult)); vwait()
        maxcol_to_row(3, 3)
        G.wait_ge(s_gen, cnt["gen"])
        odma(cc_i[0:1, 0:8], ccrow[0:1, 0:8])
        G.wait_ge(sd_o, cnt["sd_o"])
        cnt["cc"] += 1
        G.collective_compute("AllReduce", ALU.max, replica_groups=RG,
                             ins=[cc_i[:, :]], outs=[cc_o[:, :]]).then_inc(s_cc, 1)
        G.wait_ge(s_cc, cnt["cc"])
        SY.wait_ge(s_cc, cnt["cc"])
        v = ldma(ccrow2[0:1, 0:8], cc_o[0:1, 0:8])
        V.wait_ge(ld, v)
        gen(V.tensor_scalar(ccrow2[0:1, 3:4], ccrow2[0:1, 3:4], 1.0 / 32767.0, 1e-12, ALU.mult, ALU.max)); vwait()
        gen(V.reciprocal(ccrow2[0:1, 5:6], ccrow2[0:1, 3:4])); vwait()
        gen(V.tensor_scalar(ccrow2[0:1, 6:7], ccrow2[0:1, 5:6], 1.0 / 65536.0, None, ALU.mult)); vwait()
        bcast_scal(6, ccrow2[0:1, 3:4])
        bcast_scal(7, ccrow2[0:1, 6:7])
        vwait()
        gen(V.tensor_mul(scal[:, 8:9], scal[:, 7:8], scal[:, 4:5])); vwait()
        gen(V.memset(ccrow[:, :], 0.0)); vwait()
        gen(V.memset(maxa[:, 0:1], 0.0)); vwait()

        for sl in range(NSLO):
            for ic in range(DIC):
                v = ldma(fft[:, 0:512], s_gs[128 * ic:128 * (ic + 1), 512 * sl:512 * (sl + 1)])
                v = ldma(fftc.bitcast(BF16)[:, 0:512], s_up[512 * sl:512 * (sl + 1), 128 * ic:128 * (ic + 1)],
                         transpose=True)
                V.wait_ge(ld, v)
                V.wait_ge(sd_o, cnt["sd_o"])
                gen(V.tensor_scalar(ffta[:, 0:512], fft[:, 0:512], scal[:, 8:9], None, ALU.mult)); vwait()
                gen(V.tensor_copy(i16t[:, 0:512], ffta[:, 0:512])); vwait()
                gen(V.tensor_copy(ffta[:, 0:512], i16t[:, 0:512])); vwait()
                gen(V.tensor_scalar(fftb[:, 0:512], fftc.bitcast(BF16)[:, 0:512], scal[:, 3:4], None, ALU.mult)); vwait()
                gen(V.tensor_copy(i16t[:, 0:512], fftb[:, 0:512])); vwait()
                gen(V.tensor_copy(fftb[:, 0:512], i16t[:, 0:512])); vwait()
                gen(V.tensor_mul(fftc[:, 0:512], ffta[:, 0:512], fftb[:, 0:512])); vwait()
                gen(V.tensor_copy(fft[:, 0:512], fftc[:, 0:512])); vwait()
                gen(V.reduce_max(tok4[:, 6:7], fftc[:, 0:512], AX.X, apply_absolute_value=True)); vwait()
                gen(V.tensor_max(maxa[:, 0:1], maxa[:, 0:1], tok4[:, 6:7])); vwait()
                G.wait_ge(s_gen, cnt["gen"])
                odma(s_op[128 * ic:128 * (ic + 1), 512 * sl:512 * (sl + 1)], fft[:, 0:512])
        vwait()
        gen(V.tensor_mul(scal[:, 9:10], scal[:, 6:7], scal[:, 2:3])); vwait()
        gen(V.tensor_scalar(maxa[:, 0:1], maxa[:, 0:1], scal[:, 9:10], None, ALU.mult)); vwait()
        maxcol_to_row(0, 4)
        G.wait_ge(s_gen, cnt["gen"])
        odma(cc_i[0:1, 0:8], ccrow[0:1, 0:8])
        G.wait_ge(sd_o, cnt["sd_o"])
        cnt["cc"] += 1
        G.collective_compute("AllReduce", ALU.max, replica_groups=RG,
                             ins=[cc_i[:, :]], outs=[cc_o[:, :]]).then_inc(s_cc, 1)
        G.wait_ge(s_cc, cnt["cc"])
        SY.wait_ge(s_cc, cnt["cc"])
        v = ldma(ccrow2[0:1, 0:8], cc_o[0:1, 0:8])
        V.wait_ge(ld, v)
        gen(V.tensor_scalar(ccrow2[0:1, 4:5], ccrow2[0:1, 4:5], 1.0 / 32767.0, 1e-12, ALU.mult, ALU.max)); vwait()
        gen(V.reciprocal(ccrow2[0:1, 5:6], ccrow2[0:1, 4:5])); vwait()
        bcast_scal(10, ccrow2[0:1, 4:5])
        bcast_scal(11, ccrow2[0:1, 5:6])
        vwait()
        gen(V.tensor_mul(scal[:, 12:13], scal[:, 9:10], scal[:, 11:12])); vwait()

        wdTb = bigT
        dwn = bigT.bitcast(F32)
        outt = ytile
        SY.wait_ge(s_mm, cnt["mm"])
        for ic in range(DIC):
            for dch in range(NCH):
                v = ldma(wdTb[:, (ic * NCH + dch) * 128:(ic * NCH + dch) * 128 + 128],
                         ag_d_o[128 * dch:128 * (dch + 1), 128 * ic:128 * (ic + 1)], transpose=True)
        T.wait_ge(ld, v)

        for sl in range(NSLO):
            for ic in range(DIC):
                v = ldma(fft[:, 0:512], s_op[128 * ic:128 * (ic + 1), 512 * sl:512 * (sl + 1)])
                V.wait_ge(ld, v)
                V.wait_ge(sd_o, cnt["sd_o"])
                gen(V.tensor_scalar(ffta[:, 0:512], fft[:, 0:512], scal[:, 12:13], None, ALU.mult)); vwait()
                gen(V.tensor_copy(i16t[:, 0:512], ffta[:, 0:512])); vwait()
                gen(V.tensor_copy(oslc[:, 512 * ic:512 * (ic + 1)], i16t[:, 0:512])); vwait()
            T.wait_ge(s_gen, cnt["gen"])
            for dch in range(NCH):
                bank = pp[dch % 2]
                T.wait_ge(s_gen, cnt["gen"])
                for ic in range(DIC):
                    mi = T.matmul(bank[:, 0:512], wdTb[:, (ic * NCH + dch) * 128:(ic * NCH + dch) * 128 + 128],
                                  oslc[:, 512 * ic:512 * (ic + 1)],
                                  start=(ic == 0), stop=(ic == DIC - 1), skip_group_check=True)
                    if ic == DIC - 1:
                        m = mm(mi)
                V.wait_ge(s_mm, m)
                gen(V.tensor_scalar(dwn[:, DIC * DM // 2 + 512 * dch:DIC * DM // 2 + 512 * (dch + 1)], bank[:, 0:512],
                                    scal[:, 10:11], None, ALU.mult)); vwait()
            for sub in range(4):
                V.wait_ge(sd_o, cnt["sd_o"])
                for dch in range(NCH):
                    T.wait_ge(s_gen, cnt["gen"])
                    m = mm(T.matmul(pq[:, 0:128], dwn[:, DIC * DM // 2 + 512 * dch + 128 * sub:DIC * DM // 2 + 512 * dch + 128 * (sub + 1)],
                                    identf[:, :], start=True, stop=True, is_transpose=True))
                    V.wait_ge(s_mm, m)
                    gen(V.tensor_copy(outt[:, 128 * dch:128 * (dch + 1)], pq[:, 0:128]))
                trow = 512 * sl + 128 * sub
                v1 = ldma(x2t[:, :], s_x2[trow:trow + 128, :])
                V.wait_ge(ld, v1); vwait()
                V.wait_ge(sd_o, cnt["sd_o"])
                gen(V.bn_stats(stat6[:, 0:6], outt[:, 0:512])); vwait()
                gen(V.bn_stats(stat6[:, 6:12], outt[:, 512:1024])); vwait()
                gen(V.bn_aggr(stat2[:, 0:2], stat6[:, 0:12])); vwait()
                gen(V.tensor_scalar(tok4[:, 2:3], stat2[:, 1:2], EPS, None, ALU.add))
                S.wait_ge(s_gen, cnt["gen"])
                a = act(S.activation(tok4[:, 3:4], tok4[:, 2:3], AF.Sqrt))
                V.wait_ge(s_act, a)
                gen(V.reciprocal(tok4[:, 1:2], tok4[:, 3:4])); vwait()
                gen(V.tensor_scalar(outt[:, :], outt[:, :], stat2[:, 0:1], tok4[:, 1:2],
                                    ALU.subtract, ALU.mult)); vwait()
                gen(V.tensor_add(outt[:, :], outt[:, :], x2t[:, :])); vwait()
                G.wait_ge(s_gen, cnt["gen"])
                odma(out[trow:trow + 128, :], outt[:, :])
        G.wait_ge(sd_o, cnt["sd_o"])
    return nc


def _stage(inputs, q, b, SEQ, DI):
    x = np.ascontiguousarray(inputs["L0_attn_x"][b]).astype(np.float32)
    OWN = SEQ // 4
    whh = inputs["L0_attn_W_hh"]
    wih = inputs["L0_attn_W_ih"]
    wall = np.zeros((DM, 512), np.float32)
    wall[:, 0:128] = wih[256:384].T * 0 if False else wih[0:128].T
    wall[:, 128:256] = wih[128:256].T
    wall[:, 256:384] = wih[256:384].T
    wall[:, 384:448] = inputs["L0_attn_Bw"].T
    wall[:, 448:450] = inputs["L0_attn_gate_w"].T
    bias_g = np.zeros((128, 4), np.float32)
    bih = inputs["L0_attn_b_ih"]
    bias_g[:, 0] = bih[0:128]
    bias_g[:, 1] = bih[128:256]
    bias_g[:, 2] = bih[256:384]
    bias_g[0:SSM, 3] = inputs["L0_attn_Bb"]
    bias_g[SSM:SSM + 2, 3] = inputs["L0_attn_gate_b"]
    bhh_g = np.ascontiguousarray(inputs["L0_attn_b_hh"].reshape(3, 128).T)
    d = {
        "xb": x,
        "xq": np.ascontiguousarray(x[q * OWN:(q + 1) * OWN]),
        "role": np.array([[q, b, 0, 0, 0, 0, 0, 0]], np.int32),
        "whhT": np.ascontiguousarray(inputs["L0_attn_W_hh"].T),
        "aT": np.ascontiguousarray(inputs["L0_attn_A"].T),
        "wall": wall,
        "bias_g": bias_g,
        "bhh_g": bhh_g,
        "ident": np.eye(128, dtype=np.float32),
        "cwT": np.ascontiguousarray(inputs["L0_attn_Cw"].T),
        "projT": np.ascontiguousarray(inputs["L0_attn_proj_w"].T),
        "cb_row": inputs["L0_attn_Cb"].reshape(1, DM).astype(np.float32),
        "db_row": inputs["L0_attn_Db"].reshape(1, DM).astype(np.float32),
        "pb_row": inputs["L0_attn_proj_b"].reshape(1, DM).astype(np.float32),
        "dwT": np.ascontiguousarray(inputs["L0_attn_Dw"].T),
        "wg_s": np.ascontiguousarray(inputs["L0_ffn_w_gate"][(q + 4 * b) * (DI // 8):(q + 4 * b + 1) * (DI // 8)]),
        "wu_s": np.ascontiguousarray(inputs["L0_ffn_w_up"][(q + 4 * b) * (DI // 8):(q + 4 * b + 1) * (DI // 8)]),
        "wd_s": np.ascontiguousarray(inputs["L0_ffn_w_down"][(q + 4 * b) * (DM // 8):(q + 4 * b + 1) * (DM // 8)]),
    }
    return {k: np.asarray(vv) for k, vv in d.items()}


def kernel(**inputs):
    SEQ = int(inputs["L0_attn_x"].shape[1])
    DI = int(inputs["L0_ffn_w_up"].shape[0])
    nc = build_program(SEQ, DI)
    in_maps = []
    for core in range(8):
        b, q = core // 4, core % 4
        in_maps.append(_stage(inputs, q, b, SEQ, DI))
    res = run_bass_kernel_spmd(nc, in_maps, core_ids=list(range(8)))
    OWN = SEQ // 4
    outp = np.zeros((2, SEQ, DM), np.float32)
    for core in range(8):
        b, q = core // 4, core % 4
        outp[b, q * OWN:(q + 1) * OWN] = res.results[core]["out"]
    return outp


# version 04
# speedup vs baseline: 1.2161x; 1.2161x over previous
import numpy as np
import contextlib
import concourse.bass as bass
import concourse.mybir as mybir
from concourse.bass_utils import run_bass_kernel_spmd

F32 = mybir.dt.float32
BF16 = mybir.dt.bfloat16
I16 = mybir.dt.int16
U16 = mybir.dt.uint16
I8 = mybir.dt.int8
AF = mybir.ActivationFunctionType
AX = mybir.AxisListType
ALU = mybir.AluOpType

DM = 1024
NCH = DM // 128
SSM = 64
RNN = 128
EPS = 1e-5
ENGS = ["sync", "tensor", "scalar", "vector", "gpsimd"]


def build_program(SEQ, DI, n_cores=8):
    HALF = 256
    NHALF = SEQ // HALF
    NBODY = NHALF // 2
    OWN = SEQ // 4
    NTI = OWN // 128
    NSL_ALL = SEQ // 512
    DIC = DI // 128
    DISH = DI // 8
    NG = DI // 512

    nc = bass.Bass("TRN2", target_bir_lowering=False)

    ap_in = lambda n, s, d=F32: nc.dram_tensor(n, s, d, kind="ExternalInput").ap()
    xb = ap_in("xb", [SEQ, DM])
    xq = ap_in("xq", [OWN, DM])
    role = ap_in("role", [1, 8], mybir.dt.int32)
    whhT = ap_in("whhT", [RNN, 384])
    aT = ap_in("aT", [SSM, SSM])
    wall = ap_in("wall", [DM, 512])
    bias_g = ap_in("bias_g", [128, 4])
    bhh_g = ap_in("bhh_g", [128, 3])
    ident = ap_in("ident", [128, 128])
    cwT = ap_in("cwT", [SSM, DM])
    projT = ap_in("projT", [RNN, DM])
    cb_row = ap_in("cb_row", [1, DM])
    db_row = ap_in("db_row", [1, DM])
    pb_row = ap_in("pb_row", [1, DM])
    dwT = ap_in("dwT", [DM, DM])
    wg_s = ap_in("wg_s", [DISH, DM])
    wu_s = ap_in("wu_s", [DISH, DM])
    wd_s = ap_in("wd_s", [DM // 8, DI])
    out = nc.dram_tensor("out", [OWN, DM], F32, kind="ExternalOutput").ap()

    dr = lambda n, s, d: nc.dram_tensor(n, s, d).ap()
    s_x1tm = dr("s_x1tm", [SEQ, DM], BF16)
    s_x1fm = dr("s_x1fm", [DM, SEQ], BF16)
    s_ir = dr("s_ir", [RNN, SEQ + 128], BF16)
    s_iz = dr("s_iz", [RNN, SEQ + 128], BF16)
    s_bx = dr("s_bx", [SSM, SEQ + 128], BF16)
    s_in = dr("s_in", [RNN, SEQ + 256], F32)
    s_gx = dr("s_gx", [2, SEQ], F32)
    s_hg = dr("s_hg", [RNN, SEQ], BF16)
    s_hs = dr("s_hs", [SSM, SEQ], BF16)
    s_x2 = dr("s_x2", [OWN, DM], F32)
    s_l3 = dr("s_l3", [OWN, 2], F32)
    s_up = dr("s_up", [OWN, DI], BF16)
    s_gt = dr("s_gt", [OWN, DI], BF16)
    s_gs = dr("s_gs", [DI, OWN], BF16)
    s_op = dr("s_op", [DI, OWN], BF16)
    s_dum = dr("s_dum", [RNN, 4 * 256], BF16)
    cc_i = dr("cc_i", [1, 8], F32)
    cc_o = dr("cc_o", [1, 8], F32)
    ag_g_i = dr("ag_g_i", [DISH, DM], BF16)
    ag_g_o = dr("ag_g_o", [DI, DM], BF16)
    ag_u_i = dr("ag_u_i", [DISH, DM], BF16)
    ag_u_o = dr("ag_u_o", [DI, DM], BF16)
    ag_d_i = dr("ag_d_i", [DM // 8, DI], BF16)
    ag_d_o = dr("ag_d_o", [DM, DI], BF16)
    RG = [list(range(n_cores))]

    es = contextlib.ExitStack()
    with es:
        _all_sb = []

        def sb(n, s, d):
            t = es.enter_context(nc.sbuf_tensor(n, s, d))
            _all_sb.append(t)
            return t
        sem = lambda n: es.enter_context(nc.semaphore(n))

        wscan = sb("wscan", [128, 448], BF16)
        i128 = sb("i128", [128, 128], BF16)
        tp_id = sb("tp_id", [128, 128], BF16)
        wallb = sb("wallb", [128, NCH * 512], BF16)
        cwTb = sb("cwTb", [SSM, DM], BF16)
        projTb = sb("projTb", [RNN, DM], BF16)
        biasg = sb("biasg", [128, 4], F32)
        bhhn_blk = sb("bhhn_blk", [128, 128], BF16)
        cbdb_b = sb("cbdb_b", [128, DM], F32)
        projb_b = sb("projb_b", [128, DM], F32)
        ones1 = sb("ones1", [1, DM], BF16)
        scal = sb("scal", [128, 16], F32)
        rolev = sb("rolev", [1, 8], mybir.dt.int32)
        pre_t = [[sb(f"pre{p}{s}", [128, 512], BF16) for s in range(2)] for p in range(2)]
        in_t = [sb(f"int{s}", [RNN, HALF], F32) for s in range(2)]
        hgw = sb("hgw", [RNN, 2 * HALF], BF16)
        hsw = sb("hsw", [SSM, 2 * HALF], BF16)
        st3 = sb("st3", [128, 6], F32)
        ncol = sb("ncol", [128, 2], F32)
        csc = sb("csc", [128, 2], F32)
        xtile = sb("xtile", [128, DM], F32)
        xtile2 = sb("xtile2", [128, DM], F32)
        stat6 = sb("stat6", [128, 64], F32)
        stat2 = sb("stat2", [128, 2], F32)
        tok4 = sb("tok4", [128, 8], F32)
        x1t = sb("x1t", [128, DM], BF16)
        xfm = [sb(f"xfm{s}", [128, NCH * 512], BF16) for s in range(2)]
        prodo = sb("prodo", [128, 512], BF16)
        prodf = sb("prodf", [128, 512], F32)
        hgt = sb("hgt", [RNN, 128], BF16)
        hst = sb("hst", [SSM, 128], BF16)
        hgO = sb("hgO", [RNN, SEQ // 4], BF16)
        hsO = sb("hsO", [SSM, SEQ // 4], BF16)
        gxO = sb("gxO", [2, SEQ // 4], F32)
        g2t = sb("g2t", [128, 2], F32)
        ytile = sb("ytile", [128, DM], F32)
        x2t = sb("x2t", [128, DM], F32)
        maxa = sb("maxa", [128, 4], F32)
        BIGW = max(NCH * (SEQ // 4), DIC * DM + NCH * DM)
        bigT = sb("bigT", [128, BIGW], BF16)
        wff = sb("wff", [128, NCH * 512], BF16)
        identf = sb("identf", [128, 128], F32)
        onesf1 = sb("onesf1", [1, 128], F32)
        onesc = sb("onesc", [128, 1], F32)
        ccrow = sb("ccrow", [1, 8], F32)
        ccrow2 = sb("ccrow2", [1, 8], F32)
        projrow = sb("projrow", [1, DM], BF16)
        oslc = sb("oslc", [128, max(DIC * 512, NCH * DM)], BF16)
        dwTb = oslc
        fft = sb("fft", [128, 512], BF16)
        ffta = sb("ffta", [128, 512], F32)
        fftb = sb("fftb", [128, 512], F32)
        fftc = sb("fftc", [128, 512], F32)
        i16t = prodf.bitcast(I16)
        u16t = prodo.bitcast(U16)
        pb = [es.enter_context(nc.psum_tensor(f"pb{p}", [128, 512], F32)) for p in range(2)]
        pp = [es.enter_context(nc.psum_tensor(f"pp{p}", [128, 512], F32)) for p in range(3)]
        pq = es.enter_context(nc.psum_tensor("pq", [128, 512], F32))
        ptr = es.enter_context(nc.psum_tensor("ptr", [128, 128], BF16))

        ld = sem("ld"); sd_pre = sem("sd_pre"); sd_h = sem("sd_h"); sd_o = sem("sd_o")
        s_mm = sem("s_mm"); s_act = sem("s_act"); s_dve = sem("s_dve")
        s_mm2 = sem("s_mm2"); s_act2 = sem("s_act2"); s_premm = sem("s_premm")
        s_sg = sem("s_sg"); s_c1 = sem("s_c1"); s_cc = sem("s_cc"); s_gen = sem("s_gen")
        bar = [sem(f"bar{p}") for p in range(2)]
        go = [sem(f"go{p}") for p in range(2)]
        block = es.enter_context(nc.Block())

        T, V, S, G, SY = nc.tensor, nc.vector, nc.scalar, nc.gpsimd, nc.sync
        cnt = {"ld": 0, "sd_o": 0, "gen": 0, "mm": 0, "act": 0, "cc": 0}

        def ldma(out_ap, in_ap, eng=SY, transpose=False, guard=True):
            if guard:
                eng.wait_ge(s_gen, cnt["gen"])
                eng.wait_ge(ld, cnt["ld"])
                eng.wait_ge(sd_o, cnt["sd_o"])
            cnt["ld"] += 16
            if transpose:
                eng.dma_start_transpose(out_ap, in_ap).then_inc(ld, 16)
            else:
                eng.dma_start(out=out_ap, in_=in_ap).then_inc(ld, 16)
            return cnt["ld"]

        def odma(out_ap, in_ap, eng=G):
            eng.wait_ge(sd_o, cnt["sd_o"])
            cnt["sd_o"] += 16
            eng.dma_start(out=out_ap, in_=in_ap).then_inc(sd_o, 16)
            return cnt["sd_o"]

        def gen(instr):
            cnt["gen"] += 1
            instr.then_inc(s_gen, 1)
            return cnt["gen"]

        def mm(instr):
            cnt["mm"] += 1
            instr.then_inc(s_mm, 1)
            return cnt["mm"]

        def act(instr):
            cnt["act"] += 1
            instr.then_inc(s_act, 1)
            return cnt["act"]

        def vwait(c=None):
            V.wait_ge(s_gen, cnt["gen"] if c is None else c)

        for _t in _all_sb:
            gen(V.memset(_t.ap(), 0)); vwait()
        v = ldma(rolev[:, :], role[:, :])
        v = ldma(xtile[0:RNN, 0:384], whhT[:, :])
        v = ldma(xtile[0:SSM, 384:448], aT[:, :])
        v = ldma(xtile2[:, 0:128], ident[:, :])
        v = ldma(biasg[:, 0:4], bias_g[:, :])
        v = ldma(tok4[:, 0:3], bhh_g[:, :])
        V.wait_ge(ld, v)
        gen(V.memset(wscan[:, :], 0.0)); vwait()
        gen(V.tensor_copy(wscan[0:RNN, 0:384], xtile[0:RNN, 0:384])); vwait()
        gen(V.tensor_copy(wscan[0:SSM, 384:448], xtile[0:SSM, 384:448])); vwait()
        gen(V.tensor_copy(i128[:, :], xtile2[:, 0:128])); vwait()
        gen(V.tensor_copy(tp_id[:, :], xtile2[:, 0:128])); vwait()
        gen(V.tensor_add(biasg[:, 0:2], biasg[:, 0:2], tok4[:, 0:2])); vwait()
        gen(V.memset(bhhn_blk[:, :], 0.0)); vwait()
        gen(V.tensor_scalar(bhhn_blk[:, :], bhhn_blk[:, :], tok4[:, 2:3], None, ALU.add)); vwait()
        gen(V.memset(ones1[:, :], 1.0)); vwait()
        gen(V.memset(hgw[:, :], 0.0)); vwait()
        gen(V.memset(hsw[:, :], 0.0)); vwait()
        gen(V.memset(onesc[:, :], 1.0)); vwait()
        gen(V.memset(maxa[:, :], 0.0)); vwait()
        gen(V.memset(scal[:, :], 0.0)); vwait()
        for ch in range(NCH):
            v = ldma(xtile[:, 0:512], wall.rearrange("(c p) s -> c p s", p=128)[ch])
            V.wait_ge(ld, v); vwait()
            gen(V.tensor_copy(wallb[:, 512 * ch:512 * (ch + 1)], xtile[:, 0:512]))
        for ch in range(NCH):
            v = ldma(xtile[:, 0:DM], dwT.rearrange("(c p) s -> c p s", p=128)[ch])
            V.wait_ge(ld, v); vwait()
            gen(V.tensor_copy(dwTb[:, DM * ch:DM * (ch + 1)], xtile[:, 0:DM]))
        v = ldma(xtile[0:SSM, 0:DM], cwT[:, :])
        V.wait_ge(ld, v); vwait()
        gen(V.tensor_copy(cwTb[:, :], xtile[0:SSM, 0:DM]))
        v = ldma(xtile[0:RNN, 0:DM], projT[:, :])
        V.wait_ge(ld, v); vwait()
        gen(V.tensor_copy(projTb[:, :], xtile[0:RNN, 0:DM]))
        v = ldma(xtile[0:1, 0:DM], cb_row[:, :])
        v = ldma(xtile2[0:1, 0:DM], db_row[:, :])
        v = ldma(ytile[0:1, 0:DM], pb_row[:, :])
        V.wait_ge(ld, v); vwait()
        gen(V.tensor_add(xtile[0:1, 0:DM], xtile[0:1, 0:DM], xtile2[0:1, 0:DM])); vwait()
        gen(V.tensor_copy(x1t[0:1, 0:DM], xtile[0:1, 0:DM])); vwait()
        gen(V.tensor_copy(projrow[0:1, 0:DM], ytile[0:1, 0:DM]))
        T.wait_ge(s_gen, cnt["gen"])
        for hh in range(DM // 512):
            m = mm(T.matmul(pp[2][:, 0:512], ones1[0:1, 0:128], x1t[0:1, 512 * hh:512 * (hh + 1)],
                            start=True, stop=True))
            V.wait_ge(s_mm, m)
            gen(V.tensor_copy(cbdb_b[:, 512 * hh:512 * (hh + 1)], pp[2][:, 0:512]))
            T.wait_ge(s_gen, cnt["gen"])
            m = mm(T.matmul(pp[2][:, 0:512], ones1[0:1, 0:128], projrow[0:1, 512 * hh:512 * (hh + 1)],
                            start=True, stop=True))
            V.wait_ge(s_mm, m)
            gen(V.tensor_copy(projb_b[:, 512 * hh:512 * (hh + 1)], pp[2][:, 0:512]))
            T.wait_ge(s_gen, cnt["gen"])

        def dq_panel(rr, cols):
            V.wait_ge(sd_o, cnt["sd_o"])
            nb = cols // 32
            gen(V.tensor_reduce(stat6[0:rr, 0:nb], xtile[0:rr, 0:cols].rearrange("p (b k) -> p b k", k=32),
                                AX.X, ALU.max, apply_absolute_value=True)); vwait()
            gen(V.tensor_scalar(stat6[0:rr, 0:nb], stat6[0:rr, 0:nb], 1.0 / 7.0, 1e-12, ALU.mult, ALU.max)); vwait()
            gen(V.reciprocal(stat6[0:rr, nb:2 * nb], stat6[0:rr, 0:nb]))
            for b in range(nb):
                vwait()
                gen(V.tensor_scalar(xtile2[0:rr, 32 * b:32 * (b + 1)], xtile[0:rr, 32 * b:32 * (b + 1)],
                                    stat6[0:rr, nb + b:nb + b + 1], None, ALU.mult))
            vwait()
            gen(V.tensor_copy(i16t[0:rr, 0:cols], xtile2[0:rr, 0:cols])); vwait()
            gen(V.tensor_copy(xtile2[0:rr, 0:cols], i16t[0:rr, 0:cols]))
            for b in range(nb):
                vwait()
                gen(V.tensor_scalar(x1t[0:rr, 32 * b:32 * (b + 1)], xtile2[0:rr, 32 * b:32 * (b + 1)],
                                    stat6[0:rr, b:b + 1], None, ALU.mult))

        def dequant(src_ap, rows, cols, dst_dram):
            ngr = (rows + 127) // 128
            for g in range(ngr):
                r0 = g * 128
                rr = min(128, rows - r0)
                for p0 in range(0, cols, DM):
                    pw = min(DM, cols - p0)
                    v = ldma(xtile[0:rr, 0:pw], src_ap[r0:r0 + rr, p0:p0 + pw])
                    V.wait_ge(ld, v)
                    dq_panel(rr, pw)
                    G.wait_ge(s_gen, cnt["gen"])
                    odma(dst_dram[r0:r0 + rr, p0:p0 + pw], x1t[0:rr, 0:pw])

        dequant(wg_s, DISH, DM, ag_g_i)
        dequant(wu_s, DISH, DM, ag_u_i)
        dequant(wd_s, DM // 8, DI, ag_d_i)
        G.wait_ge(sd_o, cnt["sd_o"])
        for (ai, ao) in ((ag_g_i, ag_g_o), (ag_u_i, ag_u_o), (ag_d_i, ag_d_o)):
            cnt["cc"] += 1
            G.collective_compute("AllGather", ALU.bypass, replica_groups=RG,
                                 ins=[ai[:, :]], outs=[ao[:, :]]).then_inc(s_cc, 1)

        NT_ALL = SEQ // 128
        for ti in range(NT_ALL):
            xt = xtile if ti % 2 == 0 else xtile2
            v = ldma(xt[:, :], xb[128 * ti:128 * (ti + 1), :])
            V.wait_ge(ld, v)
            vwait()
            gen(V.bn_stats(stat6[:, 0:6], xt[:, 0:512])); vwait()
            gen(V.bn_stats(stat6[:, 6:12], xt[:, 512:1024])); vwait()
            gen(V.bn_aggr(stat2[:, 0:2], stat6[:, 0:12])); vwait()
            gen(V.tensor_scalar(tok4[:, 2:3], stat2[:, 1:2], EPS, None, ALU.add))
            S.wait_ge(s_gen, cnt["gen"])
            a = act(S.activation(tok4[:, 3:4], tok4[:, 2:3], AF.Sqrt))
            V.wait_ge(s_act, a)
            gen(V.reciprocal(tok4[:, 1:2], tok4[:, 3:4])); vwait()
            V.wait_ge(sd_o, cnt["sd_o"])
            gen(V.tensor_scalar(x1t[:, :], xt[:, :], stat2[:, 0:1], tok4[:, 1:2], ALU.subtract, ALU.mult))
            G.wait_ge(s_gen, cnt["gen"])
            odma(s_x1tm[128 * ti:128 * (ti + 1), :], x1t[:, :])
        G.wait_ge(sd_o, cnt["sd_o"])
        SY.wait_ge(sd_o, cnt["sd_o"])

        evac_done = {0: 0, 1: 0}
        buf_guard = {0: 0, 1: 0}
        dump_guard = {0: 0, 1: 0}
        SHF = SEQ // 2
        for sl in range(NSL_ALL):
            buf = xfm[sl % 2]
            SY.wait_ge(s_mm, buf_guard[sl % 2])
            SY.wait_ge(sd_o, dump_guard[sl % 2])
            SY.wait_ge(ld, cnt["ld"])
            for ch in range(NCH):
                v = ldma(buf[:, 512 * ch:512 * (ch + 1)],
                         s_x1tm[512 * sl:512 * (sl + 1), 128 * ch:128 * (ch + 1)], transpose=True, guard=False)
            G.wait_ge(ld, v)
            for ch in range(NCH):
                odma(s_x1fm.rearrange("(c p) t -> c p t", p=128)[ch][:, 512 * sl:512 * (sl + 1)],
                     buf[:, 512 * ch:512 * (ch + 1)])
            T.wait_ge(ld, v)
            for g in range(4):
                bank = pp[g % 2]
                T.wait_ge(s_gen, evac_done[g % 2])
                for ch in range(NCH):
                    mi = T.matmul(bank[:, 0:512], wallb[:, 512 * ch + 128 * g:512 * ch + 128 * (g + 1)],
                                  buf[:, 512 * ch:512 * (ch + 1)],
                                  start=(ch == 0), stop=(ch == NCH - 1), skip_group_check=True)
                    if ch == NCH - 1:
                        m = mm(mi)
                V.wait_ge(s_mm, m)
                V.wait_ge(sd_o, cnt["sd_o"])
                if g < 2:
                    dst = s_ir if g == 0 else s_iz
                    gen(V.tensor_scalar(prodo[:, 0:256],
                                        bank[:, 0:512].rearrange("p (t two) -> p two t", two=2)[:, 0, :],
                                        biasg[:, g:g + 1], None, ALU.add)); vwait()
                    gen(V.tensor_scalar(prodo[:, 256:512],
                                        bank[:, 0:512].rearrange("p (t two) -> p two t", two=2)[:, 1, :],
                                        biasg[:, g:g + 1], None, ALU.add))
                    evac_done[g % 2] = cnt["gen"]
                    G.wait_ge(s_gen, cnt["gen"])
                    odma(dst[:, 256 * sl:256 * (sl + 1)], prodo[:, 0:256])
                    odma(dst[:, SHF + 256 * sl:SHF + 256 * (sl + 1)], prodo[:, 256:512])
                elif g == 2:
                    gen(V.tensor_scalar(prodf[:, 0:512], bank[:, 0:512], biasg[:, 2:3], None, ALU.add))
                    evac_done[g % 2] = cnt["gen"]
                    G.wait_ge(s_gen, cnt["gen"])
                    odma(s_in[:, 512 * sl:512 * (sl + 1)], prodf[:, 0:512])
                else:
                    gen(V.tensor_scalar(prodf[:, 0:512], bank[:, 0:512], biasg[:, 3:4], None, ALU.add))
                    ec = cnt["gen"]
                    vwait()
                    gen(V.tensor_copy(prodo[0:SSM, 0:256],
                                      prodf[0:SSM, 0:512].rearrange("p (t two) -> p two t", two=2)[:, 0, :])); vwait()
                    gen(V.tensor_copy(prodo[0:SSM, 256:512],
                                      prodf[0:SSM, 0:512].rearrange("p (t two) -> p two t", two=2)[:, 1, :]))
                    S.wait_ge(s_gen, ec)
                    S.wait_ge(sd_o, cnt["sd_o"])
                    a = act(S.activation(fftb[64:66, 0:512], prodf[64:66, 0:512], AF.Sigmoid))
                    evac_done[g % 2] = cnt["gen"]
                    G.wait_ge(s_gen, cnt["gen"])
                    G.wait_ge(s_act, a)
                    odma(s_bx[:, 256 * sl:256 * (sl + 1)], prodo[0:SSM, 0:256])
                    odma(s_bx[:, SHF + 256 * sl:SHF + 256 * (sl + 1)], prodo[0:SSM, 256:512])
                    odma(s_gx[:, 512 * sl:512 * (sl + 1)], fftb[64:66, 0:512])
            buf_guard[sl % 2] = cnt["mm"]
            dump_guard[sl % 2] = cnt["sd_o"]
        G.wait_ge(sd_o, cnt["sd_o"])
        G.wait_ge(s_cc, cnt["cc"])

        PREDMA = 7 * 16

        def issue_half_streams(eng, g_expr, slot, prior=None):
            if prior is not None:
                eng.wait_ge(sd_pre, prior)
            e0 = g_expr * HALF // 2 if False else None
            base_e = g_expr * 128
            eng.dma_start(out=pre_t[0][slot][0:RNN, 0:128], in_=s_ir[:, bass.ts(g_expr, 128)]).then_inc(sd_pre, 16)
            eng.dma_start(out=pre_t[0][slot][0:RNN, 128:256], in_=s_iz[:, bass.ts(g_expr, 128)]).then_inc(sd_pre, 16)
            eng.dma_start(out=pre_t[0][slot][0:SSM, 256:384], in_=s_bx[:, bass.ts(g_expr, 128)]).then_inc(sd_pre, 16)
            eng.dma_start(out=pre_t[1][slot][0:RNN, 0:128],
                          in_=s_ir[:, SHF:SEQ + 128][:, bass.ts(g_expr, 128)]).then_inc(sd_pre, 16)
            eng.dma_start(out=pre_t[1][slot][0:RNN, 128:256],
                          in_=s_iz[:, SHF:SEQ + 128][:, bass.ts(g_expr, 128)]).then_inc(sd_pre, 16)
            eng.dma_start(out=pre_t[1][slot][0:SSM, 256:384],
                          in_=s_bx[:, SHF:SEQ + 128][:, bass.ts(g_expr, 128)]).then_inc(sd_pre, 16)
            eng.dma_start(out=in_t[slot][:, :], in_=s_in[:, bass.ts(g_expr, HALF)]).then_inc(sd_pre, 16)

        V.wait_ge(s_act, cnt["act"])
        V.wait_ge(sd_o, cnt["sd_o"])
        gen(V.memset(pre_t[0][0][SSM:128, 256:384], 0.0)); vwait()
        gen(V.memset(pre_t[0][1][SSM:128, 256:384], 0.0)); vwait()
        gen(V.memset(pre_t[1][0][SSM:128, 256:384], 0.0)); vwait()
        gen(V.memset(pre_t[1][1][SSM:128, 256:384], 0.0)); vwait()
        gen(V.memset(prodo[:, :], 0.0)); vwait()
        gen(V.memset(prodf[:, :], 0.0)); vwait()
        G.wait_ge(s_gen, cnt["gen"])
        odma(s_ir[:, SEQ:SEQ + 128], prodo[0:RNN, 0:128])
        odma(s_iz[:, SEQ:SEQ + 128], prodo[0:RNN, 0:128])
        odma(s_bx[:, SEQ:SEQ + 128], prodo[0:SSM, 0:128])
        odma(s_in[:, SEQ:SEQ + 256], prodf[0:RNN, 0:256])
        for dk in range(2):
            G.wait_ge(sd_h, 16 * dk)
            G.dma_start(out=s_dum[:, 256 * dk:256 * dk + HALF], in_=hgw[:, 0:HALF]).then_inc(sd_h, 16)
        SY.wait_ge(sd_o, cnt["sd_o"])
        issue_half_streams(SY, 0, 0)
        G.wait_ge(sd_o, cnt["sd_o"])
        for e in ENGS:
            eng = getattr(nc, e)
            eng.drain().then_inc(bar[1], 1)
            eng.wait_ge(bar[1], 5)

        mm0 = cnt["mm"]
        act0 = cnt["act"]
        rT_dve = T.alloc_register("rT_dve")
        rT_pre = T.alloc_register("rT_pre")
        rT_h = T.alloc_register("rT_h")
        rV_act = V.alloc_register("rV_act")
        rV_act2 = V.alloc_register("rV_act2")
        rV_c1 = V.alloc_register("rV_c1")
        rV_h = V.alloc_register("rV_h")
        rS_mm = S.alloc_register("rS_mm")
        rS_sg = S.alloc_register("rS_sg")
        rS_sg2 = S.alloc_register("rS_sg2")
        rS_pre = S.alloc_register("rS_pre")
        rY_premm = SY.alloc_register("rY_premm")
        rY_act = SY.alloc_register("rY_act")
        rY_pre = SY.alloc_register("rY_pre")
        rG_dve = G.alloc_register("rG_dve")
        rG_h = G.alloc_register("rG_h")
        T.reg_mov(rT_dve, 0)
        T.reg_mov(rT_pre, PREDMA)
        T.reg_mov(rT_h, 32)
        V.reg_mov(rV_act, 1)
        V.reg_mov(rV_act2, 2)
        V.reg_mov(rV_c1, 1)
        V.reg_mov(rV_h, 32)
        S.reg_mov(rS_mm, 1)
        S.reg_mov(rS_sg, 2)
        S.reg_mov(rS_sg2, 1)
        S.reg_mov(rS_pre, PREDMA)
        SY.reg_mov(rY_premm, 0)
        SY.reg_mov(rY_act, 0)
        SY.reg_mov(rY_pre, PREDMA)
        G.reg_mov(rG_dve, HALF)
        G.reg_mov(rG_h, 32)
        with nc.Fori(0, NBODY, 1) as ib:
            for X in range(2):
                g_idx = ib * 2 + X
                SY.wait_ge(sd_pre, rY_pre)
                SY.reg_add(rY_pre, rY_pre, PREDMA)
                SY.wait_ge(s_premm, rY_premm)
                SY.reg_add(rY_premm, rY_premm, 2)
                SY.wait_ge(s_act2, rY_act)
                SY.reg_add(rY_act, rY_act, 2 * HALF)
                issue_half_streams(SY, g_idx + 1, 1 - X)
                T.wait_ge(sd_h, rT_h)
                T.reg_add(rT_h, rT_h, 32)
                V.wait_ge(sd_h, rV_h)
                V.reg_add(rV_h, rV_h, 32)
                T.wait_ge(sd_pre, rT_pre)
                T.reg_add(rT_pre, rT_pre, PREDMA)
                S.wait_ge(sd_pre, rS_pre)
                S.reg_add(rS_pre, rS_pre, PREDMA)
                T.matmul(pb[0][:, 0:384], i128[:, :], pre_t[0][X][:, 0:384], start=True, stop=False,
                         skip_group_check=True)
                T.matmul(pb[0][:, 384:512], i128[:, :], bhhn_blk[:, :], start=False, stop=True,
                         skip_group_check=True).then_inc(s_premm, 1)
                T.matmul(pb[1][:, 0:384], i128[:, :], pre_t[1][X][:, 0:384], start=True, stop=False,
                         skip_group_check=True)
                T.matmul(pb[1][:, 384:512], i128[:, :], bhhn_blk[:, :], start=False, stop=True,
                         skip_group_check=True).then_inc(s_premm, 1)
                for t in range(HALF):
                    par = t % 2
                    j = t // 2
                    sslot = t % 2
                    hprev = (X * HALF + t - 1) % (2 * HALF)
                    hcur = X * HALF + t
                    bank = pb[par]
                    T.wait_ge(s_dve, rT_dve)
                    T.matmul(bank[:, j:j + 1], wscan[:, 0:128], hgw[:, hprev:hprev + 1],
                             start=False, stop=False, skip_group_check=True)
                    T.matmul(bank[:, 128 + j:129 + j], wscan[:, 128:256], hgw[:, hprev:hprev + 1],
                             start=False, stop=False, skip_group_check=True).then_inc(s_mm2, 1)
                    T.matmul(bank[0:SSM, 256 + j:257 + j], wscan[0:SSM, 384:448], hsw[:, hprev:hprev + 1],
                             start=False, stop=False, skip_group_check=True)
                    T.matmul(bank[:, 384 + j:385 + j], wscan[:, 256:384], hgw[:, hprev:hprev + 1],
                             start=False, stop=False, skip_group_check=True).then_inc(s_mm2, 1)
                    T.reg_add(rT_dve, rT_dve, 1)
                    S.wait_ge(s_mm2, rS_mm)
                    S.activation(st3[:, 3 * sslot:3 * sslot + 2],
                                 bank[:, 0:512].rearrange("p (b j) -> p j b", b=4)[:, j, 0:2],
                                 AF.Sigmoid).then_inc(s_sg, 1)
                    S.wait_ge(s_mm2, rS_sg)
                    S.wait_ge(s_sg, rS_sg2)
                    S.activation(ncol[:, sslot:sslot + 1], bank[:, 384 + j:385 + j], AF.Tanh,
                                 scale=st3[:, 3 * sslot:3 * sslot + 1],
                                 bias=in_t[X][:, t:t + 1]).then_inc(s_act2, 1)
                    S.activation(st3[0:SSM, 3 * sslot + 2:3 * sslot + 3],
                                 bank[0:SSM, 256 + j:257 + j], AF.Sigmoid).then_inc(s_act2, 1)
                    S.reg_add(rS_mm, rS_mm, 2)
                    S.reg_add(rS_sg, rS_sg, 2)
                    S.reg_add(rS_sg2, rS_sg2, 1)
                    V.wait_ge(s_act2, rV_act)
                    V.scalar_tensor_tensor(csc[:, sslot:sslot + 1], hgw[:, hprev:hprev + 1],
                                           ncol[:, sslot:sslot + 1], st3[:, 3 * sslot + 1:3 * sslot + 2],
                                           ALU.subtract, ALU.mult).then_inc(s_c1, 1)
                    V.wait_ge(s_c1, rV_c1)
                    V.scalar_tensor_tensor(hgw[:, hcur:hcur + 1], csc[:, sslot:sslot + 1],
                                           ncol[:, sslot:sslot + 1], onesc[:, 0:1],
                                           ALU.add, ALU.mult)
                    V.wait_ge(s_act2, rV_act2)
                    V.tensor_mul(hsw[:, hcur:hcur + 1], bank[0:SSM, 256 + j:257 + j],
                                 st3[0:SSM, 3 * sslot + 2:3 * sslot + 3]).then_inc(s_dve, 1)
                    V.reg_add(rV_act, rV_act, 2)
                    V.reg_add(rV_act2, rV_act2, 2)
                    V.reg_add(rV_c1, rV_c1, 1)
                G.wait_ge(s_dve, rG_dve)
                G.reg_add(rG_dve, rG_dve, HALF)
                G.wait_ge(sd_h, rG_h)
                G.reg_add(rG_h, rG_h, 32)
                G.dma_start(out=s_hg[:, bass.ts(g_idx, HALF)],
                            in_=hgw[:, X * HALF:(X + 1) * HALF]).then_inc(sd_h, 16)
                G.dma_start(out=s_hs[:, bass.ts(g_idx, HALF)],
                            in_=hsw[:, X * HALF:(X + 1) * HALF]).then_inc(sd_h, 16)

        for e in ENGS:
            eng = getattr(nc, e)
            eng.drain().then_inc(bar[0], 1)
            eng.wait_ge(bar[0], 5)
        for e in ENGS:
            getattr(nc, e).wait_ge(sd_h, 32 + 32 * NHALF)

        NSLO = OWN // 512

        v = ldma(identf[:, :], ident[:, :])
        V.wait_ge(ld, v)
        gen(V.memset(onesf1[:, :], 1.0)); vwait()
        gen(V.memset(maxa[:, :], 0.0)); vwait()
        gen(V.memset(ccrow[:, :], 0.0)); vwait()

        def bcast_scal(col, src_ap):
            T.wait_ge(s_gen, cnt["gen"])
            m = mm(T.matmul(pq[:, 0:1], onesf1[0:1, 0:128], src_ap, start=True, stop=True))
            V.wait_ge(s_mm, m)
            gen(V.tensor_copy(scal[:, col:col + 1], pq[:, 0:1]))

        def maxcol_to_row(col, dst_col):
            vwait()
            V.wait_ge(sd_o, cnt["sd_o"])
            V.wait_ge(s_act, cnt["act"])
            gen(V.tensor_copy(fft[:, 0:1], maxa[:, col:col + 1]))
            T.wait_ge(s_gen, cnt["gen"])
            m = mm(T.transpose(ptr[0:1, 0:128], fft[:, 0:1], tp_id[:, :]))
            V.wait_ge(s_mm, m)
            gen(V.reduce_max(ccrow[0:1, dst_col:dst_col + 1], ptr[0:1, 0:128], AX.X)); vwait()

        qv = nc.sync.cc_rank(RG) % 4

        for ch in range(NCH):
            v = ldma(bigT[:, ch * OWN:(ch + 1) * OWN],
                     s_x1fm.rearrange("(c p) t -> c p t", p=128)[ch][:, :][:, bass.ts(qv, OWN)])
        v = ldma(hgO[:, :], s_hg[:, bass.ts(qv, OWN)])
        v = ldma(hsO[:, :], s_hs[:, bass.ts(qv, OWN)])
        v = ldma(gxO[:, :], s_gx[:, bass.ts(qv, OWN)])
        T.wait_ge(ld, v)

        for k in range(NTI):
            V.wait_ge(sd_o, cnt["sd_o"])
            v1 = ldma(xtile[:, :], xq[128 * k:128 * (k + 1), :])
            for half in range(2):
                T.wait_ge(ld, v1)
                for ch in range(NCH):
                    mi = T.matmul(pp[half][:, 0:512], bigT[:, ch * OWN + 128 * k:ch * OWN + 128 * (k + 1)],
                                  dwTb[:, DM * ch + 512 * half:DM * ch + 512 * (half + 1)],
                                  start=(ch == 0), stop=(ch == NCH - 1), skip_group_check=True)
                    if ch == NCH - 1:
                        m = mm(mi)
            for half in range(2):
                m = mm(T.matmul(pb[half][:, 0:512], hsO[0:SSM, 128 * k:128 * (k + 1)],
                                cwTb[0:SSM, 512 * half:512 * (half + 1)],
                                start=True, stop=True, skip_group_check=True))
            m = mm(T.matmul(pq[:, 0:512], hgO[:, 128 * k:128 * (k + 1)], projTb[:, 0:512], start=True, stop=True,
                            skip_group_check=True))
            m = mm(T.matmul(pp[2][:, 0:512], hgO[:, 128 * k:128 * (k + 1)], projTb[:, 512:1024], start=True, stop=True,
                            skip_group_check=True))
            V.wait_ge(ld, v1)
            gen(V.tensor_copy(fft[0:2, 0:128], gxO[0:2, 128 * k:128 * (k + 1)]))
            T.wait_ge(s_gen, cnt["gen"])
            m2 = mm(T.transpose(ptr[:, 0:2], fft[0:2, 0:128], tp_id[0:2, 0:2]))
            V.wait_ge(s_mm, m)
            V.wait_ge(s_mm, m2)
            gen(V.tensor_copy(g2t[:, 0:2], ptr[:, 0:2])); vwait()
            for half in range(2):
                H = slice(512 * half, 512 * (half + 1))
                gen(V.tensor_add(ytile[:, H], pp[half][:, 0:512], cbdb_b[:, H])); vwait()
                gen(V.tensor_add(ytile[:, H], ytile[:, H], pb[half][:, 0:512])); vwait()
                gen(V.tensor_scalar(ytile[:, H], ytile[:, H], g2t[:, 0:1], None, ALU.mult)); vwait()
                rb = pq if half == 0 else pp[2]
                gen(V.tensor_add(x2t[:, H], rb[:, 0:512], projb_b[:, H])); vwait()
                gen(V.tensor_scalar(x2t[:, H], x2t[:, H], g2t[:, 1:2], None, ALU.mult)); vwait()
                gen(V.tensor_add(ytile[:, H], ytile[:, H], x2t[:, H])); vwait()
            gen(V.bn_stats(stat6[:, 0:6], ytile[:, 0:512])); vwait()
            gen(V.bn_stats(stat6[:, 6:12], ytile[:, 512:1024])); vwait()
            gen(V.bn_aggr(stat2[:, 0:2], stat6[:, 0:12])); vwait()
            gen(V.tensor_scalar(tok4[:, 2:3], stat2[:, 1:2], EPS, None, ALU.add))
            S.wait_ge(s_gen, cnt["gen"])
            a = act(S.activation(tok4[:, 3:4], tok4[:, 2:3], AF.Sqrt))
            V.wait_ge(s_act, a)
            gen(V.reciprocal(tok4[:, 1:2], tok4[:, 3:4])); vwait()
            gen(V.tensor_scalar(ytile[:, :], ytile[:, :], stat2[:, 0:1], tok4[:, 1:2],
                                ALU.subtract, ALU.mult)); vwait()
            gen(V.tensor_add(x2t[:, :], ytile[:, :], xtile[:, :])); vwait()
            G.wait_ge(s_gen, cnt["gen"])
            odma(s_x2[128 * k:128 * (k + 1), :], x2t[:, :])
            gen(V.bn_stats(stat6[:, 0:6], x2t[:, 0:512])); vwait()
            gen(V.bn_stats(stat6[:, 6:12], x2t[:, 512:1024])); vwait()
            gen(V.bn_aggr(stat2[:, 0:2], stat6[:, 0:12])); vwait()
            gen(V.tensor_scalar(tok4[:, 2:3], stat2[:, 1:2], EPS, None, ALU.add))
            S.wait_ge(s_gen, cnt["gen"])
            a = act(S.activation(tok4[:, 3:4], tok4[:, 2:3], AF.Sqrt))
            V.wait_ge(s_act, a)
            gen(V.reciprocal(tok4[:, 1:2], tok4[:, 3:4])); vwait()
            gen(V.tensor_copy(tok4[:, 0:1], stat2[:, 0:1])); vwait()
            G.wait_ge(s_gen, cnt["gen"])
            odma(s_l3[128 * k:128 * (k + 1), 0:2], tok4[:, 0:2])
            gen(V.tensor_scalar(ytile[:, :], x2t[:, :], tok4[:, 0:1], None, ALU.subtract)); vwait()
            gen(V.reduce_max(tok4[:, 4:5], ytile[:, :], AX.X, apply_absolute_value=True)); vwait()
            gen(V.tensor_scalar(tok4[:, 5:6], tok4[:, 4:5], tok4[:, 1:2], None, ALU.mult)); vwait()
            gen(V.tensor_max(maxa[:, 0:1], maxa[:, 0:1], tok4[:, 5:6])); vwait()
        maxcol_to_row(0, 0)
        G.wait_ge(s_gen, cnt["gen"])
        odma(cc_i[0:1, 0:8], ccrow[0:1, 0:8])
        G.wait_ge(sd_o, cnt["sd_o"])
        cnt["cc"] += 1
        G.collective_compute("AllReduce", ALU.max, replica_groups=RG,
                             ins=[cc_i[:, :]], outs=[cc_o[:, :]]).then_inc(s_cc, 1)
        G.wait_ge(s_cc, cnt["cc"])
        SY.wait_ge(s_cc, cnt["cc"])
        v = ldma(ccrow2[0:1, 0:8], cc_o[0:1, 0:8])
        V.wait_ge(ld, v)
        gen(V.tensor_scalar(ccrow2[0:1, 1:2], ccrow2[0:1, 0:1], 1.0 / 32767.0, 1e-12, ALU.mult, ALU.max)); vwait()
        gen(V.reciprocal(ccrow2[0:1, 2:3], ccrow2[0:1, 1:2])); vwait()
        bcast_scal(0, ccrow2[0:1, 1:2])
        bcast_scal(1, ccrow2[0:1, 2:3])

        for k in range(NTI):
            V.wait_ge(sd_o, cnt["sd_o"])
            v1 = ldma(x2t[:, :], s_x2[128 * k:128 * (k + 1), :])
            v1 = ldma(tok4[:, 0:2], s_l3[128 * k:128 * (k + 1), 0:2])
            V.wait_ge(ld, v1)
            gen(V.tensor_scalar(tok4[:, 2:3], tok4[:, 1:2], scal[:, 1:2], None, ALU.mult)); vwait()
            gen(V.tensor_scalar(ytile[:, :], x2t[:, :], tok4[:, 0:1], tok4[:, 2:3],
                                ALU.subtract, ALU.mult)); vwait()
            gen(V.tensor_copy(i16t[:, :], ytile[:, :])); vwait()
            gen(V.tensor_copy(x1t[:, :], i16t[:, :])); vwait()
            for ch in range(NCH):
                T.wait_ge(s_gen, cnt["gen"])
                m = mm(T.transpose(ptr[:, :], x1t[:, 128 * ch:128 * (ch + 1)], tp_id[:, :]))
                V.wait_ge(s_mm, m)
                gen(V.tensor_copy(bigT[:, ch * OWN + 128 * k:ch * OWN + 128 * (k + 1)], ptr[:, :]))
        for (wsrc, dst, mcol) in ((ag_u_o, s_up, 1), (ag_g_o, s_gt, 2)):
            for g in range(NG):
                SY.wait_ge(s_mm, cnt["mm"])
                SY.wait_ge(ld, cnt["ld"])
                for ch in range(NCH):
                    v = ldma(wff[:, 512 * ch:512 * (ch + 1)],
                             wsrc[512 * g:512 * (g + 1), 128 * ch:128 * (ch + 1)], transpose=True, guard=False)
                T.wait_ge(ld, v)
                for k in range(NTI):
                    bank = pp[k % 2]
                    T.wait_ge(s_gen, cnt["gen"])
                    for ch in range(NCH):
                        mi = T.matmul(bank[:, 0:512], bigT[:, ch * OWN + 128 * k:ch * OWN + 128 * (k + 1)],
                                      wff[:, 512 * ch:512 * (ch + 1)],
                                      start=(ch == 0), stop=(ch == NCH - 1), skip_group_check=True)
                        if ch == NCH - 1:
                            m = mm(mi)
                    V.wait_ge(s_mm, m)
                    V.wait_ge(sd_o, cnt["sd_o"])
                    gen(V.tensor_scalar(ffta[:, 0:512], bank[:, 0:512], scal[:, 0:1], None, ALU.mult)); vwait()
                    gen(V.tensor_copy(fft[:, 0:512], ffta[:, 0:512])); vwait()
                    gen(V.reduce_max(tok4[:, 6:7], ffta[:, 0:512], AX.X, apply_absolute_value=True)); vwait()
                    gen(V.tensor_max(maxa[:, mcol:mcol + 1], maxa[:, mcol:mcol + 1], tok4[:, 6:7])); vwait()
                    G.wait_ge(s_gen, cnt["gen"])
                    odma(dst[128 * k:128 * (k + 1), 512 * g:512 * (g + 1)], fft[:, 0:512])
        maxcol_to_row(1, 1)
        maxcol_to_row(2, 2)
        G.wait_ge(s_gen, cnt["gen"])
        odma(cc_i[0:1, 0:8], ccrow[0:1, 0:8])
        G.wait_ge(sd_o, cnt["sd_o"])
        cnt["cc"] += 1
        G.collective_compute("AllReduce", ALU.max, replica_groups=RG,
                             ins=[cc_i[:, :]], outs=[cc_o[:, :]]).then_inc(s_cc, 1)
        G.wait_ge(s_cc, cnt["cc"])
        SY.wait_ge(s_cc, cnt["cc"])
        v = ldma(ccrow2[0:1, 0:8], cc_o[0:1, 0:8])
        V.wait_ge(ld, v)
        gen(V.tensor_scalar(ccrow2[0:1, 3:4], ccrow2[0:1, 1:2], 1.0 / 32767.0, 1e-12, ALU.mult, ALU.max)); vwait()
        gen(V.tensor_scalar(ccrow2[0:1, 4:5], ccrow2[0:1, 2:3], 1.0 / 32767.0, 1e-12, ALU.mult, ALU.max)); vwait()
        gen(V.reciprocal(ccrow2[0:1, 5:6], ccrow2[0:1, 3:4])); vwait()
        gen(V.reciprocal(ccrow2[0:1, 6:7], ccrow2[0:1, 4:5])); vwait()
        bcast_scal(2, ccrow2[0:1, 3:4])
        bcast_scal(3, ccrow2[0:1, 5:6])
        bcast_scal(4, ccrow2[0:1, 4:5])
        bcast_scal(5, ccrow2[0:1, 6:7])
        gen(V.memset(ccrow[:, :], 0.0)); vwait()

        for sl in range(NSLO):
            for ic in range(DIC):
                v = ldma(fft[:, 0:512], s_gt[512 * sl:512 * (sl + 1), 128 * ic:128 * (ic + 1)], transpose=True)
                V.wait_ge(ld, v)
                V.wait_ge(sd_o, cnt["sd_o"])
                gen(V.tensor_scalar(ffta[:, 0:512], fft[:, 0:512], scal[:, 5:6], None, ALU.mult)); vwait()
                gen(V.tensor_copy(i16t[:, 0:512], ffta[:, 0:512])); vwait()
                gen(V.tensor_copy(ffta[:, 0:512], i16t[:, 0:512]))
                S.wait_ge(s_gen, cnt["gen"])
                a = act(S.activation(fftb[:, 0:512], ffta[:, 0:512], AF.Sigmoid, scale=scal[:, 4:5]))
                V.wait_ge(s_act, a)
                gen(V.tensor_scalar(fftb[:, 0:512], fftb[:, 0:512], 65536.0, None, ALU.mult)); vwait()
                gen(V.tensor_copy(u16t[:, 0:512], fftb[:, 0:512])); vwait()
                gen(V.tensor_copy(fftb[:, 0:512], u16t[:, 0:512])); vwait()
                gen(V.tensor_mul(fftc[:, 0:512], ffta[:, 0:512], fftb[:, 0:512])); vwait()
                gen(V.tensor_copy(fft[:, 0:512], fftc[:, 0:512])); vwait()
                gen(V.reduce_max(tok4[:, 6:7], fftc[:, 0:512], AX.X, apply_absolute_value=True)); vwait()
                gen(V.tensor_max(maxa[:, 3:4], maxa[:, 3:4], tok4[:, 6:7])); vwait()
                G.wait_ge(s_gen, cnt["gen"])
                odma(s_gs[128 * ic:128 * (ic + 1), 512 * sl + qv:512 * sl + qv + 512] if False else
                     s_gs[128 * ic:128 * (ic + 1), 512 * sl:512 * (sl + 1)], fft[:, 0:512])
        vwait()
        gen(V.tensor_scalar(maxa[:, 3:4], maxa[:, 3:4], scal[:, 4:5], 1.0 / 65536.0, ALU.mult, ALU.mult)); vwait()
        maxcol_to_row(3, 3)
        G.wait_ge(s_gen, cnt["gen"])
        odma(cc_i[0:1, 0:8], ccrow[0:1, 0:8])
        G.wait_ge(sd_o, cnt["sd_o"])
        cnt["cc"] += 1
        G.collective_compute("AllReduce", ALU.max, replica_groups=RG,
                             ins=[cc_i[:, :]], outs=[cc_o[:, :]]).then_inc(s_cc, 1)
        G.wait_ge(s_cc, cnt["cc"])
        SY.wait_ge(s_cc, cnt["cc"])
        v = ldma(ccrow2[0:1, 0:8], cc_o[0:1, 0:8])
        V.wait_ge(ld, v)
        gen(V.tensor_scalar(ccrow2[0:1, 3:4], ccrow2[0:1, 3:4], 1.0 / 32767.0, 1e-12, ALU.mult, ALU.max)); vwait()
        gen(V.reciprocal(ccrow2[0:1, 5:6], ccrow2[0:1, 3:4])); vwait()
        gen(V.tensor_scalar(ccrow2[0:1, 6:7], ccrow2[0:1, 5:6], 1.0 / 65536.0, None, ALU.mult)); vwait()
        bcast_scal(6, ccrow2[0:1, 3:4])
        bcast_scal(7, ccrow2[0:1, 6:7])
        vwait()
        gen(V.tensor_mul(scal[:, 8:9], scal[:, 7:8], scal[:, 4:5])); vwait()
        gen(V.memset(ccrow[:, :], 0.0)); vwait()
        gen(V.memset(maxa[:, 0:1], 0.0)); vwait()

        for sl in range(NSLO):
            for ic in range(DIC):
                v = ldma(fft[:, 0:512], s_gs[128 * ic:128 * (ic + 1), 512 * sl:512 * (sl + 1)])
                v = ldma(fftc.bitcast(BF16)[:, 0:512], s_up[512 * sl:512 * (sl + 1), 128 * ic:128 * (ic + 1)],
                         transpose=True)
                V.wait_ge(ld, v)
                V.wait_ge(sd_o, cnt["sd_o"])
                gen(V.tensor_scalar(ffta[:, 0:512], fft[:, 0:512], scal[:, 8:9], None, ALU.mult)); vwait()
                gen(V.tensor_copy(i16t[:, 0:512], ffta[:, 0:512])); vwait()
                gen(V.tensor_copy(ffta[:, 0:512], i16t[:, 0:512])); vwait()
                gen(V.tensor_scalar(fftb[:, 0:512], fftc.bitcast(BF16)[:, 0:512], scal[:, 3:4], None, ALU.mult)); vwait()
                gen(V.tensor_copy(i16t[:, 0:512], fftb[:, 0:512])); vwait()
                gen(V.tensor_copy(fftb[:, 0:512], i16t[:, 0:512])); vwait()
                gen(V.tensor_mul(fftc[:, 0:512], ffta[:, 0:512], fftb[:, 0:512])); vwait()
                gen(V.tensor_copy(fft[:, 0:512], fftc[:, 0:512])); vwait()
                gen(V.reduce_max(tok4[:, 6:7], fftc[:, 0:512], AX.X, apply_absolute_value=True)); vwait()
                gen(V.tensor_max(maxa[:, 0:1], maxa[:, 0:1], tok4[:, 6:7])); vwait()
                G.wait_ge(s_gen, cnt["gen"])
                odma(s_op[128 * ic:128 * (ic + 1), 512 * sl:512 * (sl + 1)], fft[:, 0:512])
        vwait()
        gen(V.tensor_mul(scal[:, 9:10], scal[:, 6:7], scal[:, 2:3])); vwait()
        gen(V.tensor_scalar(maxa[:, 0:1], maxa[:, 0:1], scal[:, 9:10], None, ALU.mult)); vwait()
        maxcol_to_row(0, 4)
        G.wait_ge(s_gen, cnt["gen"])
        odma(cc_i[0:1, 0:8], ccrow[0:1, 0:8])
        G.wait_ge(sd_o, cnt["sd_o"])
        cnt["cc"] += 1
        G.collective_compute("AllReduce", ALU.max, replica_groups=RG,
                             ins=[cc_i[:, :]], outs=[cc_o[:, :]]).then_inc(s_cc, 1)
        G.wait_ge(s_cc, cnt["cc"])
        SY.wait_ge(s_cc, cnt["cc"])
        v = ldma(ccrow2[0:1, 0:8], cc_o[0:1, 0:8])
        V.wait_ge(ld, v)
        gen(V.tensor_scalar(ccrow2[0:1, 4:5], ccrow2[0:1, 4:5], 1.0 / 32767.0, 1e-12, ALU.mult, ALU.max)); vwait()
        gen(V.reciprocal(ccrow2[0:1, 5:6], ccrow2[0:1, 4:5])); vwait()
        bcast_scal(10, ccrow2[0:1, 4:5])
        bcast_scal(11, ccrow2[0:1, 5:6])
        vwait()
        gen(V.tensor_mul(scal[:, 12:13], scal[:, 9:10], scal[:, 11:12])); vwait()

        wdTb = bigT
        dwn = bigT.bitcast(F32)
        outt = ytile
        SY.wait_ge(s_mm, cnt["mm"])
        for ic in range(DIC):
            for dch in range(NCH):
                v = ldma(wdTb[:, (ic * NCH + dch) * 128:(ic * NCH + dch) * 128 + 128],
                         ag_d_o[128 * dch:128 * (dch + 1), 128 * ic:128 * (ic + 1)], transpose=True)
        T.wait_ge(ld, v)

        for sl in range(NSLO):
            for ic in range(DIC):
                v = ldma(fft[:, 0:512], s_op[128 * ic:128 * (ic + 1), 512 * sl:512 * (sl + 1)])
                V.wait_ge(ld, v)
                V.wait_ge(sd_o, cnt["sd_o"])
                gen(V.tensor_scalar(ffta[:, 0:512], fft[:, 0:512], scal[:, 12:13], None, ALU.mult)); vwait()
                gen(V.tensor_copy(i16t[:, 0:512], ffta[:, 0:512])); vwait()
                gen(V.tensor_copy(oslc[:, 512 * ic:512 * (ic + 1)], i16t[:, 0:512])); vwait()
            T.wait_ge(s_gen, cnt["gen"])
            for dch in range(NCH):
                bank = pp[dch % 2]
                T.wait_ge(s_gen, cnt["gen"])
                for ic in range(DIC):
                    mi = T.matmul(bank[:, 0:512], wdTb[:, (ic * NCH + dch) * 128:(ic * NCH + dch) * 128 + 128],
                                  oslc[:, 512 * ic:512 * (ic + 1)],
                                  start=(ic == 0), stop=(ic == DIC - 1), skip_group_check=True)
                    if ic == DIC - 1:
                        m = mm(mi)
                V.wait_ge(s_mm, m)
                gen(V.tensor_scalar(dwn[:, DIC * DM // 2 + 512 * dch:DIC * DM // 2 + 512 * (dch + 1)], bank[:, 0:512],
                                    scal[:, 10:11], None, ALU.mult)); vwait()
            for sub in range(4):
                V.wait_ge(sd_o, cnt["sd_o"])
                for dch in range(NCH):
                    T.wait_ge(s_gen, cnt["gen"])
                    m = mm(T.matmul(pq[:, 0:128], dwn[:, DIC * DM // 2 + 512 * dch + 128 * sub:DIC * DM // 2 + 512 * dch + 128 * (sub + 1)],
                                    identf[:, :], start=True, stop=True, is_transpose=True))
                    V.wait_ge(s_mm, m)
                    gen(V.tensor_copy(outt[:, 128 * dch:128 * (dch + 1)], pq[:, 0:128]))
                trow = 512 * sl + 128 * sub
                v1 = ldma(x2t[:, :], s_x2[trow:trow + 128, :])
                V.wait_ge(ld, v1); vwait()
                V.wait_ge(sd_o, cnt["sd_o"])
                gen(V.bn_stats(stat6[:, 0:6], outt[:, 0:512])); vwait()
                gen(V.bn_stats(stat6[:, 6:12], outt[:, 512:1024])); vwait()
                gen(V.bn_aggr(stat2[:, 0:2], stat6[:, 0:12])); vwait()
                gen(V.tensor_scalar(tok4[:, 2:3], stat2[:, 1:2], EPS, None, ALU.add))
                S.wait_ge(s_gen, cnt["gen"])
                a = act(S.activation(tok4[:, 3:4], tok4[:, 2:3], AF.Sqrt))
                V.wait_ge(s_act, a)
                gen(V.reciprocal(tok4[:, 1:2], tok4[:, 3:4])); vwait()
                gen(V.tensor_scalar(outt[:, :], outt[:, :], stat2[:, 0:1], tok4[:, 1:2],
                                    ALU.subtract, ALU.mult)); vwait()
                gen(V.tensor_add(outt[:, :], outt[:, :], x2t[:, :])); vwait()
                G.wait_ge(s_gen, cnt["gen"])
                odma(out[trow:trow + 128, :], outt[:, :])
        G.wait_ge(sd_o, cnt["sd_o"])
    return nc


def _stage(inputs, q, b, SEQ, DI):
    x = np.ascontiguousarray(inputs["L0_attn_x"][b]).astype(np.float32)
    OWN = SEQ // 4
    whh = inputs["L0_attn_W_hh"]
    wih = inputs["L0_attn_W_ih"]
    wall = np.zeros((DM, 512), np.float32)
    wall[:, 0:128] = wih[256:384].T * 0 if False else wih[0:128].T
    wall[:, 128:256] = wih[128:256].T
    wall[:, 256:384] = wih[256:384].T
    wall[:, 384:448] = inputs["L0_attn_Bw"].T
    wall[:, 448:450] = inputs["L0_attn_gate_w"].T
    bias_g = np.zeros((128, 4), np.float32)
    bih = inputs["L0_attn_b_ih"]
    bias_g[:, 0] = bih[0:128]
    bias_g[:, 1] = bih[128:256]
    bias_g[:, 2] = bih[256:384]
    bias_g[0:SSM, 3] = inputs["L0_attn_Bb"]
    bias_g[SSM:SSM + 2, 3] = inputs["L0_attn_gate_b"]
    bhh_g = np.ascontiguousarray(inputs["L0_attn_b_hh"].reshape(3, 128).T)
    d = {
        "xb": x,
        "xq": np.ascontiguousarray(x[q * OWN:(q + 1) * OWN]),
        "role": np.array([[q, b, 0, 0, 0, 0, 0, 0]], np.int32),
        "whhT": np.ascontiguousarray(inputs["L0_attn_W_hh"].T),
        "aT": np.ascontiguousarray(inputs["L0_attn_A"].T),
        "wall": wall,
        "bias_g": bias_g,
        "bhh_g": bhh_g,
        "ident": np.eye(128, dtype=np.float32),
        "cwT": np.ascontiguousarray(inputs["L0_attn_Cw"].T),
        "projT": np.ascontiguousarray(inputs["L0_attn_proj_w"].T),
        "cb_row": inputs["L0_attn_Cb"].reshape(1, DM).astype(np.float32),
        "db_row": inputs["L0_attn_Db"].reshape(1, DM).astype(np.float32),
        "pb_row": inputs["L0_attn_proj_b"].reshape(1, DM).astype(np.float32),
        "dwT": np.ascontiguousarray(inputs["L0_attn_Dw"].T),
        "wg_s": np.ascontiguousarray(inputs["L0_ffn_w_gate"][(q + 4 * b) * (DI // 8):(q + 4 * b + 1) * (DI // 8)]),
        "wu_s": np.ascontiguousarray(inputs["L0_ffn_w_up"][(q + 4 * b) * (DI // 8):(q + 4 * b + 1) * (DI // 8)]),
        "wd_s": np.ascontiguousarray(inputs["L0_ffn_w_down"][(q + 4 * b) * (DM // 8):(q + 4 * b + 1) * (DM // 8)]),
    }
    return {k: np.asarray(vv) for k, vv in d.items()}


def kernel(**inputs):
    SEQ = int(inputs["L0_attn_x"].shape[1])
    DI = int(inputs["L0_ffn_w_up"].shape[0])
    nc = build_program(SEQ, DI)
    in_maps = []
    for core in range(8):
        b, q = core // 4, core % 4
        in_maps.append(_stage(inputs, q, b, SEQ, DI))
    res = run_bass_kernel_spmd(nc, in_maps, core_ids=list(range(8)))
    OWN = SEQ // 4
    outp = np.zeros((2, SEQ, DM), np.float32)
    for core in range(8):
        b, q = core // 4, core % 4
        outp[b, q * OWN:(q + 1) * OWN] = res.results[core]["out"]
    return outp


# version 05
# speedup vs baseline: 1.2712x; 1.0453x over previous
import numpy as np
import contextlib
import concourse.bass as bass
import concourse.mybir as mybir
from concourse.bass_utils import run_bass_kernel_spmd

F32 = mybir.dt.float32
BF16 = mybir.dt.bfloat16
I16 = mybir.dt.int16
U16 = mybir.dt.uint16
I8 = mybir.dt.int8
AF = mybir.ActivationFunctionType
AX = mybir.AxisListType
ALU = mybir.AluOpType

DM = 1024
NCH = DM // 128
SSM = 64
RNN = 128
EPS = 1e-5
ENGS = ["sync", "tensor", "scalar", "vector", "gpsimd"]


def build_program(SEQ, DI, n_cores=8):
    HALF = 256
    NHALF = SEQ // HALF
    NBODY = NHALF // 2
    OWN = SEQ // 4
    NTI = OWN // 128
    NSL_ALL = SEQ // 512
    DIC = DI // 128
    DISH = DI // 8
    NG = DI // 512

    nc = bass.Bass("TRN2", target_bir_lowering=False)

    ap_in = lambda n, s, d=F32: nc.dram_tensor(n, s, d, kind="ExternalInput").ap()
    xb = ap_in("xb", [SEQ, DM])
    xq = ap_in("xq", [OWN, DM])
    role = ap_in("role", [1, 8], mybir.dt.int32)
    whhT = ap_in("whhT", [RNN, 384])
    aT = ap_in("aT", [SSM, SSM])
    wall = ap_in("wall", [DM, 512])
    bias_g = ap_in("bias_g", [128, 4])
    bhh_g = ap_in("bhh_g", [128, 3])
    ident = ap_in("ident", [128, 128])
    cwT = ap_in("cwT", [SSM, DM])
    projT = ap_in("projT", [RNN, DM])
    cb_row = ap_in("cb_row", [1, DM])
    db_row = ap_in("db_row", [1, DM])
    pb_row = ap_in("pb_row", [1, DM])
    dwT = ap_in("dwT", [DM, DM])
    wg_s = ap_in("wg_s", [DISH, DM])
    wu_s = ap_in("wu_s", [DISH, DM])
    wd_s = ap_in("wd_s", [DM // 8, DI])
    out = nc.dram_tensor("out", [OWN, DM], F32, kind="ExternalOutput").ap()

    dr = lambda n, s, d: nc.dram_tensor(n, s, d).ap()
    s_x1tm = dr("s_x1tm", [SEQ, DM], BF16)
    s_x1fm = dr("s_x1fm", [DM, SEQ], BF16)
    s_ir = dr("s_ir", [RNN, SEQ + 128], BF16)
    s_iz = dr("s_iz", [RNN, SEQ + 128], BF16)
    s_bx = dr("s_bx", [SSM, SEQ + 128], BF16)
    s_in = dr("s_in", [RNN, SEQ + 256], F32)
    s_gx = dr("s_gx", [2, SEQ], F32)
    s_hg = dr("s_hg", [RNN, SEQ], BF16)
    s_hs = dr("s_hs", [SSM, SEQ], BF16)
    s_x2 = dr("s_x2", [OWN, DM], F32)
    s_l3 = dr("s_l3", [OWN, 2], F32)
    s_up = dr("s_up", [OWN, DI], BF16)
    s_gt = dr("s_gt", [OWN, DI], BF16)
    s_gs = dr("s_gs", [DI, OWN], BF16)
    s_op = dr("s_op", [DI, OWN], BF16)
    s_dum = dr("s_dum", [RNN, 4 * 256], BF16)
    cc_i = dr("cc_i", [1, 8], F32)
    cc_o = dr("cc_o", [1, 8], F32)
    ag_g_i = dr("ag_g_i", [DISH, DM], BF16)
    ag_g_o = dr("ag_g_o", [DI, DM], BF16)
    ag_u_i = dr("ag_u_i", [DISH, DM], BF16)
    ag_u_o = dr("ag_u_o", [DI, DM], BF16)
    ag_d_i = dr("ag_d_i", [DM // 8, DI], BF16)
    ag_d_o = dr("ag_d_o", [DM, DI], BF16)
    RG = [list(range(n_cores))]

    es = contextlib.ExitStack()
    with es:
        _all_sb = []

        def sb(n, s, d):
            t = es.enter_context(nc.sbuf_tensor(n, s, d))
            _all_sb.append(t)
            return t
        sem = lambda n: es.enter_context(nc.semaphore(n))

        wscan = sb("wscan", [128, 448], BF16)
        i128 = sb("i128", [128, 128], BF16)
        tp_id = sb("tp_id", [128, 128], BF16)
        wallb = sb("wallb", [128, NCH * 512], BF16)
        cwTb = sb("cwTb", [SSM, DM], BF16)
        projTb = sb("projTb", [RNN, DM], BF16)
        biasg = sb("biasg", [128, 4], F32)
        bhhn_blk = sb("bhhn_blk", [128, 128], BF16)
        cbdb_b = sb("cbdb_b", [128, DM], F32)
        projb_b = sb("projb_b", [128, DM], F32)
        ones1 = sb("ones1", [1, DM], BF16)
        scal = sb("scal", [128, 16], F32)
        rolev = sb("rolev", [1, 8], mybir.dt.int32)
        pre_t = [[sb(f"pre{p}{s}", [128, 512], BF16) for s in range(2)] for p in range(2)]
        in_t = [sb(f"int{s}", [RNN, HALF], F32) for s in range(2)]
        hgw = sb("hgw", [RNN, 2 * HALF], BF16)
        hsw = sb("hsw", [SSM, 2 * HALF], BF16)
        st3 = sb("st3", [128, 6], F32)
        ncol = sb("ncol", [128, 2], F32)
        csc = sb("csc", [128, 2], F32)
        xtile = sb("xtile", [128, DM], F32)
        xtile2 = sb("xtile2", [128, DM], F32)
        stat6 = sb("stat6", [128, 64], F32)
        stat2 = sb("stat2", [128, 2], F32)
        tok4 = sb("tok4", [128, 8], F32)
        x1t = sb("x1t", [128, DM], BF16)
        xfm = [sb(f"xfm{s}", [128, NCH * 512], BF16) for s in range(2)]
        prodo = sb("prodo", [128, 512], BF16)
        prodf = sb("prodf", [128, 512], F32)
        hgt = sb("hgt", [RNN, 128], BF16)
        hst = sb("hst", [SSM, 128], BF16)
        hgO = sb("hgO", [RNN, SEQ // 4], BF16)
        hsO = sb("hsO", [SSM, SEQ // 4], BF16)
        gxO = sb("gxO", [2, SEQ // 4], F32)
        g2t = sb("g2t", [128, 2], F32)
        ytile = sb("ytile", [128, DM], F32)
        x2t = sb("x2t", [128, DM], F32)
        maxa = sb("maxa", [128, 4], F32)
        BIGW = max(NCH * (SEQ // 4), DIC * DM + NCH * DM)
        bigT = sb("bigT", [128, BIGW], BF16)
        wff = sb("wff", [128, NCH * 512], BF16)
        identf = sb("identf", [128, 128], F32)
        onesf1 = sb("onesf1", [1, 128], F32)
        onesc = sb("onesc", [128, 1], F32)
        ccrow = sb("ccrow", [1, 8], F32)
        ccrow2 = sb("ccrow2", [1, 8], F32)
        projrow = sb("projrow", [1, DM], BF16)
        oslc = sb("oslc", [128, max(DIC * 512, NCH * DM)], BF16)
        dwTb = oslc
        fft = sb("fft", [128, 512], BF16)
        ffta = sb("ffta", [128, 512], F32)
        fftb = sb("fftb", [128, 512], F32)
        fftc = sb("fftc", [128, 512], F32)
        i16t = prodf.bitcast(I16)
        u16t = prodo.bitcast(U16)
        pb = [es.enter_context(nc.psum_tensor(f"pb{p}", [128, 512], F32)) for p in range(2)]
        pp = [es.enter_context(nc.psum_tensor(f"pp{p}", [128, 512], F32)) for p in range(3)]
        pq = es.enter_context(nc.psum_tensor("pq", [128, 512], F32))
        ptr = es.enter_context(nc.psum_tensor("ptr", [128, 128], BF16))

        ld = sem("ld"); sd_pre = sem("sd_pre"); sd_h = sem("sd_h"); sd_o = sem("sd_o")
        s_mm = sem("s_mm"); s_act = sem("s_act"); s_dve = sem("s_dve")
        s_mm2 = sem("s_mm2"); s_act2 = sem("s_act2"); s_premm = sem("s_premm")
        s_sg = sem("s_sg"); s_c1 = sem("s_c1"); s_cc = sem("s_cc"); s_gen = sem("s_gen")
        bar = [sem(f"bar{p}") for p in range(2)]
        go = [sem(f"go{p}") for p in range(2)]
        block = es.enter_context(nc.Block())

        T, V, S, G, SY = nc.tensor, nc.vector, nc.scalar, nc.gpsimd, nc.sync
        cnt = {"ld": 0, "sd_o": 0, "gen": 0, "mm": 0, "act": 0, "cc": 0}

        def ldma(out_ap, in_ap, eng=SY, transpose=False, guard=True):
            if guard:
                eng.wait_ge(s_gen, cnt["gen"])
                eng.wait_ge(ld, cnt["ld"])
                eng.wait_ge(sd_o, cnt["sd_o"])
            cnt["ld"] += 16
            if transpose:
                eng.dma_start_transpose(out_ap, in_ap).then_inc(ld, 16)
            else:
                eng.dma_start(out=out_ap, in_=in_ap).then_inc(ld, 16)
            return cnt["ld"]

        def odma(out_ap, in_ap, eng=G):
            eng.wait_ge(sd_o, cnt["sd_o"])
            cnt["sd_o"] += 16
            eng.dma_start(out=out_ap, in_=in_ap).then_inc(sd_o, 16)
            return cnt["sd_o"]

        def gen(instr):
            cnt["gen"] += 1
            instr.then_inc(s_gen, 1)
            return cnt["gen"]

        def mm(instr):
            cnt["mm"] += 1
            instr.then_inc(s_mm, 1)
            return cnt["mm"]

        def act(instr):
            cnt["act"] += 1
            instr.then_inc(s_act, 1)
            return cnt["act"]

        def vwait(c=None):
            V.wait_ge(s_gen, cnt["gen"] if c is None else c)

        for _t in _all_sb:
            gen(V.memset(_t.ap(), 0)); vwait()
        v = ldma(rolev[:, :], role[:, :])
        v = ldma(xtile[0:RNN, 0:384], whhT[:, :])
        v = ldma(xtile[0:SSM, 384:448], aT[:, :])
        v = ldma(xtile2[:, 0:128], ident[:, :])
        v = ldma(biasg[:, 0:4], bias_g[:, :])
        v = ldma(tok4[:, 0:3], bhh_g[:, :])
        V.wait_ge(ld, v)
        gen(V.memset(wscan[:, :], 0.0)); vwait()
        gen(V.tensor_copy(wscan[0:RNN, 0:384], xtile[0:RNN, 0:384])); vwait()
        gen(V.tensor_copy(wscan[0:SSM, 384:448], xtile[0:SSM, 384:448])); vwait()
        gen(V.tensor_copy(i128[:, :], xtile2[:, 0:128])); vwait()
        gen(V.tensor_copy(tp_id[:, :], xtile2[:, 0:128])); vwait()
        gen(V.tensor_add(biasg[:, 0:2], biasg[:, 0:2], tok4[:, 0:2])); vwait()
        gen(V.memset(bhhn_blk[:, :], 0.0)); vwait()
        gen(V.tensor_scalar(bhhn_blk[:, :], bhhn_blk[:, :], tok4[:, 2:3], None, ALU.add)); vwait()
        gen(V.memset(ones1[:, :], 1.0)); vwait()
        gen(V.memset(hgw[:, :], 0.0)); vwait()
        gen(V.memset(hsw[:, :], 0.0)); vwait()
        gen(V.memset(onesc[:, :], 1.0)); vwait()
        gen(V.memset(maxa[:, :], 0.0)); vwait()
        gen(V.memset(scal[:, :], 0.0)); vwait()
        for ch in range(NCH):
            v = ldma(xtile[:, 0:512], wall.rearrange("(c p) s -> c p s", p=128)[ch])
            V.wait_ge(ld, v); vwait()
            gen(V.tensor_copy(wallb[:, 512 * ch:512 * (ch + 1)], xtile[:, 0:512]))
        for ch in range(NCH):
            v = ldma(xtile[:, 0:DM], dwT.rearrange("(c p) s -> c p s", p=128)[ch])
            V.wait_ge(ld, v); vwait()
            gen(V.tensor_copy(dwTb[:, DM * ch:DM * (ch + 1)], xtile[:, 0:DM]))
        v = ldma(xtile[0:SSM, 0:DM], cwT[:, :])
        V.wait_ge(ld, v); vwait()
        gen(V.tensor_copy(cwTb[:, :], xtile[0:SSM, 0:DM]))
        v = ldma(xtile[0:RNN, 0:DM], projT[:, :])
        V.wait_ge(ld, v); vwait()
        gen(V.tensor_copy(projTb[:, :], xtile[0:RNN, 0:DM]))
        v = ldma(xtile[0:1, 0:DM], cb_row[:, :])
        v = ldma(xtile2[0:1, 0:DM], db_row[:, :])
        v = ldma(ytile[0:1, 0:DM], pb_row[:, :])
        V.wait_ge(ld, v); vwait()
        gen(V.tensor_add(xtile[0:1, 0:DM], xtile[0:1, 0:DM], xtile2[0:1, 0:DM])); vwait()
        gen(V.tensor_copy(x1t[0:1, 0:DM], xtile[0:1, 0:DM])); vwait()
        gen(V.tensor_copy(projrow[0:1, 0:DM], ytile[0:1, 0:DM]))
        T.wait_ge(s_gen, cnt["gen"])
        for hh in range(DM // 512):
            m = mm(T.matmul(pp[2][:, 0:512], ones1[0:1, 0:128], x1t[0:1, 512 * hh:512 * (hh + 1)],
                            start=True, stop=True))
            V.wait_ge(s_mm, m)
            gen(V.tensor_copy(cbdb_b[:, 512 * hh:512 * (hh + 1)], pp[2][:, 0:512]))
            T.wait_ge(s_gen, cnt["gen"])
            m = mm(T.matmul(pp[2][:, 0:512], ones1[0:1, 0:128], projrow[0:1, 512 * hh:512 * (hh + 1)],
                            start=True, stop=True))
            V.wait_ge(s_mm, m)
            gen(V.tensor_copy(projb_b[:, 512 * hh:512 * (hh + 1)], pp[2][:, 0:512]))
            T.wait_ge(s_gen, cnt["gen"])

        def dq_panel(rr, cols):
            V.wait_ge(sd_o, cnt["sd_o"])
            nb = cols // 32
            gen(V.tensor_reduce(stat6[0:rr, 0:nb], xtile[0:rr, 0:cols].rearrange("p (b k) -> p b k", k=32),
                                AX.X, ALU.max, apply_absolute_value=True)); vwait()
            gen(V.tensor_scalar(stat6[0:rr, 0:nb], stat6[0:rr, 0:nb], 1.0 / 7.0, 1e-12, ALU.mult, ALU.max)); vwait()
            gen(V.reciprocal(stat6[0:rr, nb:2 * nb], stat6[0:rr, 0:nb]))
            for b in range(nb):
                vwait()
                gen(V.tensor_scalar(xtile2[0:rr, 32 * b:32 * (b + 1)], xtile[0:rr, 32 * b:32 * (b + 1)],
                                    stat6[0:rr, nb + b:nb + b + 1], None, ALU.mult))
            vwait()
            gen(V.tensor_copy(i16t[0:rr, 0:cols], xtile2[0:rr, 0:cols])); vwait()
            gen(V.tensor_copy(xtile2[0:rr, 0:cols], i16t[0:rr, 0:cols]))
            for b in range(nb):
                vwait()
                gen(V.tensor_scalar(x1t[0:rr, 32 * b:32 * (b + 1)], xtile2[0:rr, 32 * b:32 * (b + 1)],
                                    stat6[0:rr, b:b + 1], None, ALU.mult))

        def dequant(src_ap, rows, cols, dst_dram):
            ngr = (rows + 127) // 128
            for g in range(ngr):
                r0 = g * 128
                rr = min(128, rows - r0)
                for p0 in range(0, cols, DM):
                    pw = min(DM, cols - p0)
                    v = ldma(xtile[0:rr, 0:pw], src_ap[r0:r0 + rr, p0:p0 + pw])
                    V.wait_ge(ld, v)
                    dq_panel(rr, pw)
                    G.wait_ge(s_gen, cnt["gen"])
                    odma(dst_dram[r0:r0 + rr, p0:p0 + pw], x1t[0:rr, 0:pw])

        dequant(wg_s, DISH, DM, ag_g_i)
        dequant(wu_s, DISH, DM, ag_u_i)
        dequant(wd_s, DM // 8, DI, ag_d_i)
        G.wait_ge(sd_o, cnt["sd_o"])
        for (ai, ao) in ((ag_g_i, ag_g_o), (ag_u_i, ag_u_o), (ag_d_i, ag_d_o)):
            cnt["cc"] += 1
            G.collective_compute("AllGather", ALU.bypass, replica_groups=RG,
                                 ins=[ai[:, :]], outs=[ao[:, :]]).then_inc(s_cc, 1)

        NT_ALL = SEQ // 128
        for ti in range(NT_ALL):
            xt = xtile if ti % 2 == 0 else xtile2
            v = ldma(xt[:, :], xb[128 * ti:128 * (ti + 1), :])
            V.wait_ge(ld, v)
            vwait()
            gen(V.bn_stats(stat6[:, 0:6], xt[:, 0:512])); vwait()
            gen(V.bn_stats(stat6[:, 6:12], xt[:, 512:1024])); vwait()
            gen(V.bn_aggr(stat2[:, 0:2], stat6[:, 0:12])); vwait()
            gen(V.tensor_scalar(tok4[:, 2:3], stat2[:, 1:2], EPS, None, ALU.add))
            S.wait_ge(s_gen, cnt["gen"])
            a = act(S.activation(tok4[:, 3:4], tok4[:, 2:3], AF.Sqrt))
            V.wait_ge(s_act, a)
            gen(V.reciprocal(tok4[:, 1:2], tok4[:, 3:4])); vwait()
            V.wait_ge(sd_o, cnt["sd_o"])
            gen(V.tensor_scalar(x1t[:, :], xt[:, :], stat2[:, 0:1], tok4[:, 1:2], ALU.subtract, ALU.mult))
            G.wait_ge(s_gen, cnt["gen"])
            odma(s_x1tm[128 * ti:128 * (ti + 1), :], x1t[:, :])
        G.wait_ge(sd_o, cnt["sd_o"])
        SY.wait_ge(sd_o, cnt["sd_o"])

        evac_done = {0: 0, 1: 0}
        buf_guard = {0: 0, 1: 0}
        dump_guard = {0: 0, 1: 0}
        SHF = SEQ // 2
        for sl in range(NSL_ALL):
            buf = xfm[sl % 2]
            SY.wait_ge(s_mm, buf_guard[sl % 2])
            SY.wait_ge(sd_o, dump_guard[sl % 2])
            SY.wait_ge(ld, cnt["ld"])
            for ch in range(NCH):
                v = ldma(buf[:, 512 * ch:512 * (ch + 1)],
                         s_x1tm[512 * sl:512 * (sl + 1), 128 * ch:128 * (ch + 1)], transpose=True, guard=False)
            G.wait_ge(ld, v)
            for ch in range(NCH):
                odma(s_x1fm.rearrange("(c p) t -> c p t", p=128)[ch][:, 512 * sl:512 * (sl + 1)],
                     buf[:, 512 * ch:512 * (ch + 1)])
            T.wait_ge(ld, v)
            for g in range(4):
                bank = pp[g % 2]
                T.wait_ge(s_gen, evac_done[g % 2])
                for ch in range(NCH):
                    mi = T.matmul(bank[:, 0:512], wallb[:, 512 * ch + 128 * g:512 * ch + 128 * (g + 1)],
                                  buf[:, 512 * ch:512 * (ch + 1)],
                                  start=(ch == 0), stop=(ch == NCH - 1), skip_group_check=True)
                    if ch == NCH - 1:
                        m = mm(mi)
                V.wait_ge(s_mm, m)
                V.wait_ge(sd_o, cnt["sd_o"])
                if g < 2:
                    dst = s_ir if g == 0 else s_iz
                    gen(V.tensor_scalar(prodo[:, 0:256],
                                        bank[:, 0:512].rearrange("p (t two) -> p two t", two=2)[:, 0, :],
                                        biasg[:, g:g + 1], None, ALU.add)); vwait()
                    gen(V.tensor_scalar(prodo[:, 256:512],
                                        bank[:, 0:512].rearrange("p (t two) -> p two t", two=2)[:, 1, :],
                                        biasg[:, g:g + 1], None, ALU.add))
                    evac_done[g % 2] = cnt["gen"]
                    G.wait_ge(s_gen, cnt["gen"])
                    odma(dst[:, 256 * sl:256 * (sl + 1)], prodo[:, 0:256])
                    odma(dst[:, SHF + 256 * sl:SHF + 256 * (sl + 1)], prodo[:, 256:512])
                elif g == 2:
                    gen(V.tensor_scalar(prodf[:, 0:512], bank[:, 0:512], biasg[:, 2:3], None, ALU.add))
                    evac_done[g % 2] = cnt["gen"]
                    G.wait_ge(s_gen, cnt["gen"])
                    odma(s_in[:, 512 * sl:512 * (sl + 1)], prodf[:, 0:512])
                else:
                    gen(V.tensor_scalar(prodf[:, 0:512], bank[:, 0:512], biasg[:, 3:4], None, ALU.add))
                    ec = cnt["gen"]
                    vwait()
                    gen(V.tensor_copy(prodo[0:SSM, 0:256],
                                      prodf[0:SSM, 0:512].rearrange("p (t two) -> p two t", two=2)[:, 0, :])); vwait()
                    gen(V.tensor_copy(prodo[0:SSM, 256:512],
                                      prodf[0:SSM, 0:512].rearrange("p (t two) -> p two t", two=2)[:, 1, :]))
                    S.wait_ge(s_gen, ec)
                    S.wait_ge(sd_o, cnt["sd_o"])
                    a = act(S.activation(fftb[64:66, 0:512], prodf[64:66, 0:512], AF.Sigmoid))
                    evac_done[g % 2] = cnt["gen"]
                    G.wait_ge(s_gen, cnt["gen"])
                    G.wait_ge(s_act, a)
                    odma(s_bx[:, 256 * sl:256 * (sl + 1)], prodo[0:SSM, 0:256])
                    odma(s_bx[:, SHF + 256 * sl:SHF + 256 * (sl + 1)], prodo[0:SSM, 256:512])
                    odma(s_gx[:, 512 * sl:512 * (sl + 1)], fftb[64:66, 0:512])
            buf_guard[sl % 2] = cnt["mm"]
            dump_guard[sl % 2] = cnt["sd_o"]
        G.wait_ge(sd_o, cnt["sd_o"])
        G.wait_ge(s_cc, cnt["cc"])

        PREDMA = 7 * 16

        def issue_half_streams(eng, g_expr, slot, prior=None):
            if prior is not None:
                eng.wait_ge(sd_pre, prior)
            e0 = g_expr * HALF // 2 if False else None
            base_e = g_expr * 128
            eng.dma_start(out=pre_t[0][slot][0:RNN, 0:128], in_=s_ir[:, bass.ts(g_expr, 128)]).then_inc(sd_pre, 16)
            eng.dma_start(out=pre_t[0][slot][0:RNN, 128:256], in_=s_iz[:, bass.ts(g_expr, 128)]).then_inc(sd_pre, 16)
            eng.dma_start(out=pre_t[0][slot][0:SSM, 256:384], in_=s_bx[:, bass.ts(g_expr, 128)]).then_inc(sd_pre, 16)
            eng.dma_start(out=pre_t[1][slot][0:RNN, 0:128],
                          in_=s_ir[:, SHF:SEQ + 128][:, bass.ts(g_expr, 128)]).then_inc(sd_pre, 16)
            eng.dma_start(out=pre_t[1][slot][0:RNN, 128:256],
                          in_=s_iz[:, SHF:SEQ + 128][:, bass.ts(g_expr, 128)]).then_inc(sd_pre, 16)
            eng.dma_start(out=pre_t[1][slot][0:SSM, 256:384],
                          in_=s_bx[:, SHF:SEQ + 128][:, bass.ts(g_expr, 128)]).then_inc(sd_pre, 16)
            eng.dma_start(out=in_t[slot][:, :], in_=s_in[:, bass.ts(g_expr, HALF)]).then_inc(sd_pre, 16)

        V.wait_ge(s_act, cnt["act"])
        V.wait_ge(sd_o, cnt["sd_o"])
        gen(V.memset(pre_t[0][0][SSM:128, 256:384], 0.0)); vwait()
        gen(V.memset(pre_t[0][1][SSM:128, 256:384], 0.0)); vwait()
        gen(V.memset(pre_t[1][0][SSM:128, 256:384], 0.0)); vwait()
        gen(V.memset(pre_t[1][1][SSM:128, 256:384], 0.0)); vwait()
        gen(V.memset(prodo[:, :], 0.0)); vwait()
        gen(V.memset(prodf[:, :], 0.0)); vwait()
        G.wait_ge(s_gen, cnt["gen"])
        odma(s_ir[:, SEQ:SEQ + 128], prodo[0:RNN, 0:128])
        odma(s_iz[:, SEQ:SEQ + 128], prodo[0:RNN, 0:128])
        odma(s_bx[:, SEQ:SEQ + 128], prodo[0:SSM, 0:128])
        odma(s_in[:, SEQ:SEQ + 256], prodf[0:RNN, 0:256])
        for dk in range(2):
            G.wait_ge(sd_h, 16 * dk)
            G.dma_start(out=s_dum[:, 256 * dk:256 * dk + HALF], in_=hgw[:, 0:HALF]).then_inc(sd_h, 16)
        SY.wait_ge(sd_o, cnt["sd_o"])
        issue_half_streams(SY, 0, 0)
        G.wait_ge(sd_o, cnt["sd_o"])
        for e in ENGS:
            eng = getattr(nc, e)
            eng.drain().then_inc(bar[1], 1)
            eng.wait_ge(bar[1], 5)

        mm0 = cnt["mm"]
        act0 = cnt["act"]
        rT_dve = T.alloc_register("rT_dve")
        rT_pre = T.alloc_register("rT_pre")
        rT_h = T.alloc_register("rT_h")
        rV_act = V.alloc_register("rV_act")
        rV_act2 = V.alloc_register("rV_act2")
        rV_c1 = V.alloc_register("rV_c1")
        rV_h = V.alloc_register("rV_h")
        rS_mm = S.alloc_register("rS_mm")
        rS_sg = S.alloc_register("rS_sg")
        rS_sg2 = S.alloc_register("rS_sg2")
        rS_pre = S.alloc_register("rS_pre")
        rY_premm = SY.alloc_register("rY_premm")
        rY_act = SY.alloc_register("rY_act")
        rY_pre = SY.alloc_register("rY_pre")
        rG_dve = G.alloc_register("rG_dve")
        rG_h = G.alloc_register("rG_h")
        T.reg_mov(rT_dve, 0)
        T.reg_mov(rT_pre, PREDMA)
        T.reg_mov(rT_h, 32)
        V.reg_mov(rV_act, 1)
        V.reg_mov(rV_act2, 2)
        V.reg_mov(rV_c1, 1)
        V.reg_mov(rV_h, 32)
        S.reg_mov(rS_mm, 1)
        S.reg_mov(rS_sg, 2)
        S.reg_mov(rS_sg2, 1)
        S.reg_mov(rS_pre, PREDMA)
        SY.reg_mov(rY_premm, 0)
        SY.reg_mov(rY_act, 0)
        SY.reg_mov(rY_pre, PREDMA)
        G.reg_mov(rG_dve, HALF)
        G.reg_mov(rG_h, 32)
        with nc.Fori(0, NBODY, 1) as ib:
            for X in range(2):
                g_idx = ib * 2 + X
                SY.wait_ge(sd_pre, rY_pre)
                SY.reg_add(rY_pre, rY_pre, PREDMA)
                SY.wait_ge(s_premm, rY_premm)
                SY.reg_add(rY_premm, rY_premm, 2)
                SY.wait_ge(s_act2, rY_act)
                SY.reg_add(rY_act, rY_act, 2 * HALF)
                issue_half_streams(SY, g_idx + 1, 1 - X)
                T.wait_ge(sd_h, rT_h)
                T.reg_add(rT_h, rT_h, 32)
                V.wait_ge(sd_h, rV_h)
                V.reg_add(rV_h, rV_h, 32)
                T.wait_ge(sd_pre, rT_pre)
                T.reg_add(rT_pre, rT_pre, PREDMA)
                S.wait_ge(sd_pre, rS_pre)
                S.reg_add(rS_pre, rS_pre, PREDMA)
                T.matmul(pb[0][:, 0:384], i128[:, :], pre_t[0][X][:, 0:384], start=True, stop=False,
                         skip_group_check=True)
                T.matmul(pb[0][:, 384:512], i128[:, :], bhhn_blk[:, :], start=False, stop=True,
                         skip_group_check=True).then_inc(s_premm, 1)
                T.matmul(pb[1][:, 0:384], i128[:, :], pre_t[1][X][:, 0:384], start=True, stop=False,
                         skip_group_check=True)
                T.matmul(pb[1][:, 384:512], i128[:, :], bhhn_blk[:, :], start=False, stop=True,
                         skip_group_check=True).then_inc(s_premm, 1)
                for t in range(HALF):
                    par = t % 2
                    j = t // 2
                    sslot = t % 2
                    hprev = (X * HALF + t - 1) % (2 * HALF)
                    hcur = X * HALF + t
                    bank = pb[par]
                    T.wait_ge(s_dve, rT_dve)
                    T.matmul(bank[:, j:j + 1], wscan[:, 0:128], hgw[:, hprev:hprev + 1],
                             start=False, stop=False, skip_group_check=True)
                    T.matmul(bank[:, 128 + j:129 + j], wscan[:, 128:256], hgw[:, hprev:hprev + 1],
                             start=False, stop=False, skip_group_check=True).then_inc(s_mm2, 1)
                    T.matmul(bank[0:SSM, 256 + j:257 + j], wscan[0:SSM, 384:448], hsw[:, hprev:hprev + 1],
                             start=False, stop=False, skip_group_check=True)
                    T.matmul(bank[:, 384 + j:385 + j], wscan[:, 256:384], hgw[:, hprev:hprev + 1],
                             start=False, stop=False, skip_group_check=True).then_inc(s_mm2, 1)
                    T.reg_add(rT_dve, rT_dve, 1)
                    S.wait_ge(s_mm2, rS_mm)
                    S.activation(st3[:, 3 * sslot:3 * sslot + 2],
                                 bank[:, 0:512].rearrange("p (b j) -> p j b", b=4)[:, j, 0:2],
                                 AF.Sigmoid).then_inc(s_sg, 1)
                    S.wait_ge(s_mm2, rS_sg)
                    S.wait_ge(s_sg, rS_sg2)
                    S.activation(ncol[:, sslot:sslot + 1], bank[:, 384 + j:385 + j], AF.Tanh,
                                 scale=st3[:, 3 * sslot:3 * sslot + 1],
                                 bias=in_t[X][:, t:t + 1]).then_inc(s_act2, 1)
                    S.activation(st3[0:SSM, 3 * sslot + 2:3 * sslot + 3],
                                 bank[0:SSM, 256 + j:257 + j], AF.Sigmoid).then_inc(s_act2, 1)
                    S.reg_add(rS_mm, rS_mm, 2)
                    S.reg_add(rS_sg, rS_sg, 2)
                    S.reg_add(rS_sg2, rS_sg2, 1)
                    V.wait_ge(s_act2, rV_act)
                    V.scalar_tensor_tensor(csc[:, sslot:sslot + 1], hgw[:, hprev:hprev + 1],
                                           ncol[:, sslot:sslot + 1], st3[:, 3 * sslot + 1:3 * sslot + 2],
                                           ALU.subtract, ALU.mult).then_inc(s_c1, 1)
                    V.wait_ge(s_act2, rV_act2)
                    V.tensor_mul(hsw[:, hcur:hcur + 1], bank[0:SSM, 256 + j:257 + j],
                                 st3[0:SSM, 3 * sslot + 2:3 * sslot + 3])
                    V.wait_ge(s_c1, rV_c1)
                    V.tensor_add(hgw[:, hcur:hcur + 1], csc[:, sslot:sslot + 1],
                                 ncol[:, sslot:sslot + 1]).then_inc(s_dve, 1)
                    V.reg_add(rV_act, rV_act, 2)
                    V.reg_add(rV_act2, rV_act2, 2)
                    V.reg_add(rV_c1, rV_c1, 1)
                G.wait_ge(s_dve, rG_dve)
                G.reg_add(rG_dve, rG_dve, HALF)
                G.wait_ge(sd_h, rG_h)
                G.reg_add(rG_h, rG_h, 32)
                G.dma_start(out=s_hg[:, bass.ts(g_idx, HALF)],
                            in_=hgw[:, X * HALF:(X + 1) * HALF]).then_inc(sd_h, 16)
                G.dma_start(out=s_hs[:, bass.ts(g_idx, HALF)],
                            in_=hsw[:, X * HALF:(X + 1) * HALF]).then_inc(sd_h, 16)

        for e in ENGS:
            eng = getattr(nc, e)
            eng.drain().then_inc(bar[0], 1)
            eng.wait_ge(bar[0], 5)
        for e in ENGS:
            getattr(nc, e).wait_ge(sd_h, 32 + 32 * NHALF)

        NSLO = OWN // 512

        v = ldma(identf[:, :], ident[:, :])
        V.wait_ge(ld, v)
        gen(V.memset(onesf1[:, :], 1.0)); vwait()
        gen(V.memset(maxa[:, :], 0.0)); vwait()
        gen(V.memset(ccrow[:, :], 0.0)); vwait()

        def bcast_scal(col, src_ap):
            T.wait_ge(s_gen, cnt["gen"])
            m = mm(T.matmul(pq[:, 0:1], onesf1[0:1, 0:128], src_ap, start=True, stop=True))
            V.wait_ge(s_mm, m)
            gen(V.tensor_copy(scal[:, col:col + 1], pq[:, 0:1]))

        def maxcol_to_row(col, dst_col):
            vwait()
            V.wait_ge(sd_o, cnt["sd_o"])
            V.wait_ge(s_act, cnt["act"])
            gen(V.tensor_copy(fft[:, 0:1], maxa[:, col:col + 1]))
            T.wait_ge(s_gen, cnt["gen"])
            m = mm(T.transpose(ptr[0:1, 0:128], fft[:, 0:1], tp_id[:, :]))
            V.wait_ge(s_mm, m)
            gen(V.reduce_max(ccrow[0:1, dst_col:dst_col + 1], ptr[0:1, 0:128], AX.X)); vwait()

        qv = nc.sync.cc_rank(RG) % 4

        for ch in range(NCH):
            v = ldma(bigT[:, ch * OWN:(ch + 1) * OWN],
                     s_x1fm.rearrange("(c p) t -> c p t", p=128)[ch][:, :][:, bass.ts(qv, OWN)])
        v = ldma(hgO[:, :], s_hg[:, bass.ts(qv, OWN)])
        v = ldma(hsO[:, :], s_hs[:, bass.ts(qv, OWN)])
        v = ldma(gxO[:, :], s_gx[:, bass.ts(qv, OWN)])
        T.wait_ge(ld, v)

        for k in range(NTI):
            V.wait_ge(sd_o, cnt["sd_o"])
            v1 = ldma(xtile[:, :], xq[128 * k:128 * (k + 1), :])
            for half in range(2):
                T.wait_ge(ld, v1)
                for ch in range(NCH):
                    mi = T.matmul(pp[half][:, 0:512], bigT[:, ch * OWN + 128 * k:ch * OWN + 128 * (k + 1)],
                                  dwTb[:, DM * ch + 512 * half:DM * ch + 512 * (half + 1)],
                                  start=(ch == 0), stop=(ch == NCH - 1), skip_group_check=True)
                    if ch == NCH - 1:
                        m = mm(mi)
            for half in range(2):
                m = mm(T.matmul(pb[half][:, 0:512], hsO[0:SSM, 128 * k:128 * (k + 1)],
                                cwTb[0:SSM, 512 * half:512 * (half + 1)],
                                start=True, stop=True, skip_group_check=True))
            m = mm(T.matmul(pq[:, 0:512], hgO[:, 128 * k:128 * (k + 1)], projTb[:, 0:512], start=True, stop=True,
                            skip_group_check=True))
            m = mm(T.matmul(pp[2][:, 0:512], hgO[:, 128 * k:128 * (k + 1)], projTb[:, 512:1024], start=True, stop=True,
                            skip_group_check=True))
            V.wait_ge(ld, v1)
            gen(V.tensor_copy(fft[0:2, 0:128], gxO[0:2, 128 * k:128 * (k + 1)]))
            T.wait_ge(s_gen, cnt["gen"])
            m2 = mm(T.transpose(ptr[:, 0:2], fft[0:2, 0:128], tp_id[0:2, 0:2]))
            V.wait_ge(s_mm, m)
            V.wait_ge(s_mm, m2)
            gen(V.tensor_copy(g2t[:, 0:2], ptr[:, 0:2])); vwait()
            for half in range(2):
                H = slice(512 * half, 512 * (half + 1))
                gen(V.tensor_add(ytile[:, H], pp[half][:, 0:512], cbdb_b[:, H])); vwait()
                gen(V.tensor_add(ytile[:, H], ytile[:, H], pb[half][:, 0:512])); vwait()
                gen(V.tensor_scalar(ytile[:, H], ytile[:, H], g2t[:, 0:1], None, ALU.mult)); vwait()
                rb = pq if half == 0 else pp[2]
                gen(V.tensor_add(x2t[:, H], rb[:, 0:512], projb_b[:, H])); vwait()
                gen(V.tensor_scalar(x2t[:, H], x2t[:, H], g2t[:, 1:2], None, ALU.mult)); vwait()
                gen(V.tensor_add(ytile[:, H], ytile[:, H], x2t[:, H])); vwait()
            gen(V.bn_stats(stat6[:, 0:6], ytile[:, 0:512])); vwait()
            gen(V.bn_stats(stat6[:, 6:12], ytile[:, 512:1024])); vwait()
            gen(V.bn_aggr(stat2[:, 0:2], stat6[:, 0:12])); vwait()
            gen(V.tensor_scalar(tok4[:, 2:3], stat2[:, 1:2], EPS, None, ALU.add))
            S.wait_ge(s_gen, cnt["gen"])
            a = act(S.activation(tok4[:, 3:4], tok4[:, 2:3], AF.Sqrt))
            V.wait_ge(s_act, a)
            gen(V.reciprocal(tok4[:, 1:2], tok4[:, 3:4])); vwait()
            gen(V.tensor_scalar(ytile[:, :], ytile[:, :], stat2[:, 0:1], tok4[:, 1:2],
                                ALU.subtract, ALU.mult)); vwait()
            gen(V.tensor_add(x2t[:, :], ytile[:, :], xtile[:, :])); vwait()
            G.wait_ge(s_gen, cnt["gen"])
            odma(s_x2[128 * k:128 * (k + 1), :], x2t[:, :])
            gen(V.bn_stats(stat6[:, 0:6], x2t[:, 0:512])); vwait()
            gen(V.bn_stats(stat6[:, 6:12], x2t[:, 512:1024])); vwait()
            gen(V.bn_aggr(stat2[:, 0:2], stat6[:, 0:12])); vwait()
            gen(V.tensor_scalar(tok4[:, 2:3], stat2[:, 1:2], EPS, None, ALU.add))
            S.wait_ge(s_gen, cnt["gen"])
            a = act(S.activation(tok4[:, 3:4], tok4[:, 2:3], AF.Sqrt))
            V.wait_ge(s_act, a)
            gen(V.reciprocal(tok4[:, 1:2], tok4[:, 3:4])); vwait()
            gen(V.tensor_copy(tok4[:, 0:1], stat2[:, 0:1])); vwait()
            G.wait_ge(s_gen, cnt["gen"])
            odma(s_l3[128 * k:128 * (k + 1), 0:2], tok4[:, 0:2])
            gen(V.tensor_scalar(ytile[:, :], x2t[:, :], tok4[:, 0:1], None, ALU.subtract)); vwait()
            gen(V.reduce_max(tok4[:, 4:5], ytile[:, :], AX.X, apply_absolute_value=True)); vwait()
            gen(V.tensor_scalar(tok4[:, 5:6], tok4[:, 4:5], tok4[:, 1:2], None, ALU.mult)); vwait()
            gen(V.tensor_max(maxa[:, 0:1], maxa[:, 0:1], tok4[:, 5:6])); vwait()
        maxcol_to_row(0, 0)
        G.wait_ge(s_gen, cnt["gen"])
        odma(cc_i[0:1, 0:8], ccrow[0:1, 0:8])
        G.wait_ge(sd_o, cnt["sd_o"])
        cnt["cc"] += 1
        G.collective_compute("AllReduce", ALU.max, replica_groups=RG,
                             ins=[cc_i[:, :]], outs=[cc_o[:, :]]).then_inc(s_cc, 1)
        G.wait_ge(s_cc, cnt["cc"])
        SY.wait_ge(s_cc, cnt["cc"])
        v = ldma(ccrow2[0:1, 0:8], cc_o[0:1, 0:8])
        V.wait_ge(ld, v)
        gen(V.tensor_scalar(ccrow2[0:1, 1:2], ccrow2[0:1, 0:1], 1.0 / 32767.0, 1e-12, ALU.mult, ALU.max)); vwait()
        gen(V.reciprocal(ccrow2[0:1, 2:3], ccrow2[0:1, 1:2])); vwait()
        bcast_scal(0, ccrow2[0:1, 1:2])
        bcast_scal(1, ccrow2[0:1, 2:3])

        for k in range(NTI):
            V.wait_ge(sd_o, cnt["sd_o"])
            v1 = ldma(x2t[:, :], s_x2[128 * k:128 * (k + 1), :])
            v1 = ldma(tok4[:, 0:2], s_l3[128 * k:128 * (k + 1), 0:2])
            V.wait_ge(ld, v1)
            gen(V.tensor_scalar(tok4[:, 2:3], tok4[:, 1:2], scal[:, 1:2], None, ALU.mult)); vwait()
            gen(V.tensor_scalar(ytile[:, :], x2t[:, :], tok4[:, 0:1], tok4[:, 2:3],
                                ALU.subtract, ALU.mult)); vwait()
            gen(V.tensor_copy(i16t[:, :], ytile[:, :])); vwait()
            gen(V.tensor_copy(x1t[:, :], i16t[:, :])); vwait()
            for ch in range(NCH):
                T.wait_ge(s_gen, cnt["gen"])
                m = mm(T.transpose(ptr[:, :], x1t[:, 128 * ch:128 * (ch + 1)], tp_id[:, :]))
                V.wait_ge(s_mm, m)
                gen(V.tensor_copy(bigT[:, ch * OWN + 128 * k:ch * OWN + 128 * (k + 1)], ptr[:, :]))
        for (wsrc, dst, mcol) in ((ag_u_o, s_up, 1), (ag_g_o, s_gt, 2)):
            for g in range(NG):
                SY.wait_ge(s_mm, cnt["mm"])
                SY.wait_ge(ld, cnt["ld"])
                for ch in range(NCH):
                    v = ldma(wff[:, 512 * ch:512 * (ch + 1)],
                             wsrc[512 * g:512 * (g + 1), 128 * ch:128 * (ch + 1)], transpose=True, guard=False)
                T.wait_ge(ld, v)
                for k in range(NTI):
                    bank = pp[k % 2]
                    T.wait_ge(s_gen, cnt["gen"])
                    for ch in range(NCH):
                        mi = T.matmul(bank[:, 0:512], bigT[:, ch * OWN + 128 * k:ch * OWN + 128 * (k + 1)],
                                      wff[:, 512 * ch:512 * (ch + 1)],
                                      start=(ch == 0), stop=(ch == NCH - 1), skip_group_check=True)
                        if ch == NCH - 1:
                            m = mm(mi)
                    V.wait_ge(s_mm, m)
                    V.wait_ge(sd_o, cnt["sd_o"])
                    gen(V.tensor_scalar(ffta[:, 0:512], bank[:, 0:512], scal[:, 0:1], None, ALU.mult)); vwait()
                    gen(V.tensor_copy(fft[:, 0:512], ffta[:, 0:512])); vwait()
                    gen(V.reduce_max(tok4[:, 6:7], ffta[:, 0:512], AX.X, apply_absolute_value=True)); vwait()
                    gen(V.tensor_max(maxa[:, mcol:mcol + 1], maxa[:, mcol:mcol + 1], tok4[:, 6:7])); vwait()
                    G.wait_ge(s_gen, cnt["gen"])
                    odma(dst[128 * k:128 * (k + 1), 512 * g:512 * (g + 1)], fft[:, 0:512])
        maxcol_to_row(1, 1)
        maxcol_to_row(2, 2)
        G.wait_ge(s_gen, cnt["gen"])
        odma(cc_i[0:1, 0:8], ccrow[0:1, 0:8])
        G.wait_ge(sd_o, cnt["sd_o"])
        cnt["cc"] += 1
        G.collective_compute("AllReduce", ALU.max, replica_groups=RG,
                             ins=[cc_i[:, :]], outs=[cc_o[:, :]]).then_inc(s_cc, 1)
        G.wait_ge(s_cc, cnt["cc"])
        SY.wait_ge(s_cc, cnt["cc"])
        v = ldma(ccrow2[0:1, 0:8], cc_o[0:1, 0:8])
        V.wait_ge(ld, v)
        gen(V.tensor_scalar(ccrow2[0:1, 3:4], ccrow2[0:1, 1:2], 1.0 / 32767.0, 1e-12, ALU.mult, ALU.max)); vwait()
        gen(V.tensor_scalar(ccrow2[0:1, 4:5], ccrow2[0:1, 2:3], 1.0 / 32767.0, 1e-12, ALU.mult, ALU.max)); vwait()
        gen(V.reciprocal(ccrow2[0:1, 5:6], ccrow2[0:1, 3:4])); vwait()
        gen(V.reciprocal(ccrow2[0:1, 6:7], ccrow2[0:1, 4:5])); vwait()
        bcast_scal(2, ccrow2[0:1, 3:4])
        bcast_scal(3, ccrow2[0:1, 5:6])
        bcast_scal(4, ccrow2[0:1, 4:5])
        bcast_scal(5, ccrow2[0:1, 6:7])
        gen(V.memset(ccrow[:, :], 0.0)); vwait()

        for sl in range(NSLO):
            for ic in range(DIC):
                v = ldma(fft[:, 0:512], s_gt[512 * sl:512 * (sl + 1), 128 * ic:128 * (ic + 1)], transpose=True)
                V.wait_ge(ld, v)
                V.wait_ge(sd_o, cnt["sd_o"])
                gen(V.tensor_scalar(ffta[:, 0:512], fft[:, 0:512], scal[:, 5:6], None, ALU.mult)); vwait()
                gen(V.tensor_copy(i16t[:, 0:512], ffta[:, 0:512])); vwait()
                gen(V.tensor_copy(ffta[:, 0:512], i16t[:, 0:512]))
                S.wait_ge(s_gen, cnt["gen"])
                a = act(S.activation(fftb[:, 0:512], ffta[:, 0:512], AF.Sigmoid, scale=scal[:, 4:5]))
                V.wait_ge(s_act, a)
                gen(V.tensor_scalar(fftb[:, 0:512], fftb[:, 0:512], 65536.0, None, ALU.mult)); vwait()
                gen(V.tensor_copy(u16t[:, 0:512], fftb[:, 0:512])); vwait()
                gen(V.tensor_copy(fftb[:, 0:512], u16t[:, 0:512])); vwait()
                gen(V.tensor_mul(fftc[:, 0:512], ffta[:, 0:512], fftb[:, 0:512])); vwait()
                gen(V.tensor_copy(fft[:, 0:512], fftc[:, 0:512])); vwait()
                gen(V.reduce_max(tok4[:, 6:7], fftc[:, 0:512], AX.X, apply_absolute_value=True)); vwait()
                gen(V.tensor_max(maxa[:, 3:4], maxa[:, 3:4], tok4[:, 6:7])); vwait()
                G.wait_ge(s_gen, cnt["gen"])
                odma(s_gs[128 * ic:128 * (ic + 1), 512 * sl + qv:512 * sl + qv + 512] if False else
                     s_gs[128 * ic:128 * (ic + 1), 512 * sl:512 * (sl + 1)], fft[:, 0:512])
        vwait()
        gen(V.tensor_scalar(maxa[:, 3:4], maxa[:, 3:4], scal[:, 4:5], 1.0 / 65536.0, ALU.mult, ALU.mult)); vwait()
        maxcol_to_row(3, 3)
        G.wait_ge(s_gen, cnt["gen"])
        odma(cc_i[0:1, 0:8], ccrow[0:1, 0:8])
        G.wait_ge(sd_o, cnt["sd_o"])
        cnt["cc"] += 1
        G.collective_compute("AllReduce", ALU.max, replica_groups=RG,
                             ins=[cc_i[:, :]], outs=[cc_o[:, :]]).then_inc(s_cc, 1)
        G.wait_ge(s_cc, cnt["cc"])
        SY.wait_ge(s_cc, cnt["cc"])
        v = ldma(ccrow2[0:1, 0:8], cc_o[0:1, 0:8])
        V.wait_ge(ld, v)
        gen(V.tensor_scalar(ccrow2[0:1, 3:4], ccrow2[0:1, 3:4], 1.0 / 32767.0, 1e-12, ALU.mult, ALU.max)); vwait()
        gen(V.reciprocal(ccrow2[0:1, 5:6], ccrow2[0:1, 3:4])); vwait()
        gen(V.tensor_scalar(ccrow2[0:1, 6:7], ccrow2[0:1, 5:6], 1.0 / 65536.0, None, ALU.mult)); vwait()
        bcast_scal(6, ccrow2[0:1, 3:4])
        bcast_scal(7, ccrow2[0:1, 6:7])
        vwait()
        gen(V.tensor_mul(scal[:, 8:9], scal[:, 7:8], scal[:, 4:5])); vwait()
        gen(V.memset(ccrow[:, :], 0.0)); vwait()
        gen(V.memset(maxa[:, 0:1], 0.0)); vwait()

        for sl in range(NSLO):
            for ic in range(DIC):
                v = ldma(fft[:, 0:512], s_gs[128 * ic:128 * (ic + 1), 512 * sl:512 * (sl + 1)])
                v = ldma(fftc.bitcast(BF16)[:, 0:512], s_up[512 * sl:512 * (sl + 1), 128 * ic:128 * (ic + 1)],
                         transpose=True)
                V.wait_ge(ld, v)
                V.wait_ge(sd_o, cnt["sd_o"])
                gen(V.tensor_scalar(ffta[:, 0:512], fft[:, 0:512], scal[:, 8:9], None, ALU.mult)); vwait()
                gen(V.tensor_copy(i16t[:, 0:512], ffta[:, 0:512])); vwait()
                gen(V.tensor_copy(ffta[:, 0:512], i16t[:, 0:512])); vwait()
                gen(V.tensor_scalar(fftb[:, 0:512], fftc.bitcast(BF16)[:, 0:512], scal[:, 3:4], None, ALU.mult)); vwait()
                gen(V.tensor_copy(i16t[:, 0:512], fftb[:, 0:512])); vwait()
                gen(V.tensor_copy(fftb[:, 0:512], i16t[:, 0:512])); vwait()
                gen(V.tensor_mul(fftc[:, 0:512], ffta[:, 0:512], fftb[:, 0:512])); vwait()
                gen(V.tensor_copy(fft[:, 0:512], fftc[:, 0:512])); vwait()
                gen(V.reduce_max(tok4[:, 6:7], fftc[:, 0:512], AX.X, apply_absolute_value=True)); vwait()
                gen(V.tensor_max(maxa[:, 0:1], maxa[:, 0:1], tok4[:, 6:7])); vwait()
                G.wait_ge(s_gen, cnt["gen"])
                odma(s_op[128 * ic:128 * (ic + 1), 512 * sl:512 * (sl + 1)], fft[:, 0:512])
        vwait()
        gen(V.tensor_mul(scal[:, 9:10], scal[:, 6:7], scal[:, 2:3])); vwait()
        gen(V.tensor_scalar(maxa[:, 0:1], maxa[:, 0:1], scal[:, 9:10], None, ALU.mult)); vwait()
        maxcol_to_row(0, 4)
        G.wait_ge(s_gen, cnt["gen"])
        odma(cc_i[0:1, 0:8], ccrow[0:1, 0:8])
        G.wait_ge(sd_o, cnt["sd_o"])
        cnt["cc"] += 1
        G.collective_compute("AllReduce", ALU.max, replica_groups=RG,
                             ins=[cc_i[:, :]], outs=[cc_o[:, :]]).then_inc(s_cc, 1)
        G.wait_ge(s_cc, cnt["cc"])
        SY.wait_ge(s_cc, cnt["cc"])
        v = ldma(ccrow2[0:1, 0:8], cc_o[0:1, 0:8])
        V.wait_ge(ld, v)
        gen(V.tensor_scalar(ccrow2[0:1, 4:5], ccrow2[0:1, 4:5], 1.0 / 32767.0, 1e-12, ALU.mult, ALU.max)); vwait()
        gen(V.reciprocal(ccrow2[0:1, 5:6], ccrow2[0:1, 4:5])); vwait()
        bcast_scal(10, ccrow2[0:1, 4:5])
        bcast_scal(11, ccrow2[0:1, 5:6])
        vwait()
        gen(V.tensor_mul(scal[:, 12:13], scal[:, 9:10], scal[:, 11:12])); vwait()

        wdTb = bigT
        dwn = bigT.bitcast(F32)
        outt = ytile
        SY.wait_ge(s_mm, cnt["mm"])
        for ic in range(DIC):
            for dch in range(NCH):
                v = ldma(wdTb[:, (ic * NCH + dch) * 128:(ic * NCH + dch) * 128 + 128],
                         ag_d_o[128 * dch:128 * (dch + 1), 128 * ic:128 * (ic + 1)], transpose=True)
        T.wait_ge(ld, v)

        for sl in range(NSLO):
            for ic in range(DIC):
                v = ldma(fft[:, 0:512], s_op[128 * ic:128 * (ic + 1), 512 * sl:512 * (sl + 1)])
                V.wait_ge(ld, v)
                V.wait_ge(sd_o, cnt["sd_o"])
                gen(V.tensor_scalar(ffta[:, 0:512], fft[:, 0:512], scal[:, 12:13], None, ALU.mult)); vwait()
                gen(V.tensor_copy(i16t[:, 0:512], ffta[:, 0:512])); vwait()
                gen(V.tensor_copy(oslc[:, 512 * ic:512 * (ic + 1)], i16t[:, 0:512])); vwait()
            T.wait_ge(s_gen, cnt["gen"])
            for dch in range(NCH):
                bank = pp[dch % 2]
                T.wait_ge(s_gen, cnt["gen"])
                for ic in range(DIC):
                    mi = T.matmul(bank[:, 0:512], wdTb[:, (ic * NCH + dch) * 128:(ic * NCH + dch) * 128 + 128],
                                  oslc[:, 512 * ic:512 * (ic + 1)],
                                  start=(ic == 0), stop=(ic == DIC - 1), skip_group_check=True)
                    if ic == DIC - 1:
                        m = mm(mi)
                V.wait_ge(s_mm, m)
                gen(V.tensor_scalar(dwn[:, DIC * DM // 2 + 512 * dch:DIC * DM // 2 + 512 * (dch + 1)], bank[:, 0:512],
                                    scal[:, 10:11], None, ALU.mult)); vwait()
            for sub in range(4):
                V.wait_ge(sd_o, cnt["sd_o"])
                for dch in range(NCH):
                    T.wait_ge(s_gen, cnt["gen"])
                    m = mm(T.matmul(pq[:, 0:128], dwn[:, DIC * DM // 2 + 512 * dch + 128 * sub:DIC * DM // 2 + 512 * dch + 128 * (sub + 1)],
                                    identf[:, :], start=True, stop=True, is_transpose=True))
                    V.wait_ge(s_mm, m)
                    gen(V.tensor_copy(outt[:, 128 * dch:128 * (dch + 1)], pq[:, 0:128]))
                trow = 512 * sl + 128 * sub
                v1 = ldma(x2t[:, :], s_x2[trow:trow + 128, :])
                V.wait_ge(ld, v1); vwait()
                V.wait_ge(sd_o, cnt["sd_o"])
                gen(V.bn_stats(stat6[:, 0:6], outt[:, 0:512])); vwait()
                gen(V.bn_stats(stat6[:, 6:12], outt[:, 512:1024])); vwait()
                gen(V.bn_aggr(stat2[:, 0:2], stat6[:, 0:12])); vwait()
                gen(V.tensor_scalar(tok4[:, 2:3], stat2[:, 1:2], EPS, None, ALU.add))
                S.wait_ge(s_gen, cnt["gen"])
                a = act(S.activation(tok4[:, 3:4], tok4[:, 2:3], AF.Sqrt))
                V.wait_ge(s_act, a)
                gen(V.reciprocal(tok4[:, 1:2], tok4[:, 3:4])); vwait()
                gen(V.tensor_scalar(outt[:, :], outt[:, :], stat2[:, 0:1], tok4[:, 1:2],
                                    ALU.subtract, ALU.mult)); vwait()
                gen(V.tensor_add(outt[:, :], outt[:, :], x2t[:, :])); vwait()
                G.wait_ge(s_gen, cnt["gen"])
                odma(out[trow:trow + 128, :], outt[:, :])
        G.wait_ge(sd_o, cnt["sd_o"])
    return nc


def _stage(inputs, q, b, SEQ, DI):
    x = np.ascontiguousarray(inputs["L0_attn_x"][b]).astype(np.float32)
    OWN = SEQ // 4
    whh = inputs["L0_attn_W_hh"]
    wih = inputs["L0_attn_W_ih"]
    wall = np.zeros((DM, 512), np.float32)
    wall[:, 0:128] = wih[256:384].T * 0 if False else wih[0:128].T
    wall[:, 128:256] = wih[128:256].T
    wall[:, 256:384] = wih[256:384].T
    wall[:, 384:448] = inputs["L0_attn_Bw"].T
    wall[:, 448:450] = inputs["L0_attn_gate_w"].T
    bias_g = np.zeros((128, 4), np.float32)
    bih = inputs["L0_attn_b_ih"]
    bias_g[:, 0] = bih[0:128]
    bias_g[:, 1] = bih[128:256]
    bias_g[:, 2] = bih[256:384]
    bias_g[0:SSM, 3] = inputs["L0_attn_Bb"]
    bias_g[SSM:SSM + 2, 3] = inputs["L0_attn_gate_b"]
    bhh_g = np.ascontiguousarray(inputs["L0_attn_b_hh"].reshape(3, 128).T)
    d = {
        "xb": x,
        "xq": np.ascontiguousarray(x[q * OWN:(q + 1) * OWN]),
        "role": np.array([[q, b, 0, 0, 0, 0, 0, 0]], np.int32),
        "whhT": np.ascontiguousarray(inputs["L0_attn_W_hh"].T),
        "aT": np.ascontiguousarray(inputs["L0_attn_A"].T),
        "wall": wall,
        "bias_g": bias_g,
        "bhh_g": bhh_g,
        "ident": np.eye(128, dtype=np.float32),
        "cwT": np.ascontiguousarray(inputs["L0_attn_Cw"].T),
        "projT": np.ascontiguousarray(inputs["L0_attn_proj_w"].T),
        "cb_row": inputs["L0_attn_Cb"].reshape(1, DM).astype(np.float32),
        "db_row": inputs["L0_attn_Db"].reshape(1, DM).astype(np.float32),
        "pb_row": inputs["L0_attn_proj_b"].reshape(1, DM).astype(np.float32),
        "dwT": np.ascontiguousarray(inputs["L0_attn_Dw"].T),
        "wg_s": np.ascontiguousarray(inputs["L0_ffn_w_gate"][(q + 4 * b) * (DI // 8):(q + 4 * b + 1) * (DI // 8)]),
        "wu_s": np.ascontiguousarray(inputs["L0_ffn_w_up"][(q + 4 * b) * (DI // 8):(q + 4 * b + 1) * (DI // 8)]),
        "wd_s": np.ascontiguousarray(inputs["L0_ffn_w_down"][(q + 4 * b) * (DM // 8):(q + 4 * b + 1) * (DM // 8)]),
    }
    return {k: np.asarray(vv) for k, vv in d.items()}


def kernel(**inputs):
    SEQ = int(inputs["L0_attn_x"].shape[1])
    DI = int(inputs["L0_ffn_w_up"].shape[0])
    nc = build_program(SEQ, DI)
    in_maps = []
    for core in range(8):
        b, q = core // 4, core % 4
        in_maps.append(_stage(inputs, q, b, SEQ, DI))
    res = run_bass_kernel_spmd(nc, in_maps, core_ids=list(range(8)))
    OWN = SEQ // 4
    outp = np.zeros((2, SEQ, DM), np.float32)
    for core in range(8):
        b, q = core // 4, core % 4
        outp[b, q * OWN:(q + 1) * OWN] = res.results[core]["out"]
    return outp


# version 08
# speedup vs baseline: 1.4630x; 1.1509x over previous
import numpy as np
import contextlib
import concourse.bass as bass
import concourse.mybir as mybir
from concourse.bass_utils import run_bass_kernel_spmd

F32 = mybir.dt.float32
BF16 = mybir.dt.bfloat16
I16 = mybir.dt.int16
U16 = mybir.dt.uint16
I8 = mybir.dt.int8
AF = mybir.ActivationFunctionType
AX = mybir.AxisListType
ALU = mybir.AluOpType

DM = 1024
NCH = DM // 128
SSM = 64
RNN = 128
EPS = 1e-5
ENGS = ["sync", "tensor", "scalar", "vector", "gpsimd"]


def build_program(SEQ, DI, n_cores=8, debug=False):
    HALF = 256
    NHALF = SEQ // HALF
    NBODY = NHALF // 2
    OWN = SEQ // 4
    NTI = OWN // 128
    NSL_ALL = SEQ // 512
    DIC = DI // 128
    DISH = DI // 8
    NG = DI // 512

    nc = bass.Bass("TRN2", target_bir_lowering=False)

    ap_in = lambda n, s, d=F32: nc.dram_tensor(n, s, d, kind="ExternalInput").ap()
    xb = ap_in("xb", [SEQ, DM])
    xq = ap_in("xq", [OWN, DM])
    role = ap_in("role", [1, 8], mybir.dt.int32)
    whhT = ap_in("whhT", [RNN, 384])
    aT = ap_in("aT", [SSM, SSM])
    wall = ap_in("wall", [DM, 512])
    bias_g = ap_in("bias_g", [128, 4])
    bhh_g = ap_in("bhh_g", [128, 3])
    ident = ap_in("ident", [128, 128])
    cwT = ap_in("cwT", [SSM, DM])
    projT = ap_in("projT", [RNN, DM])
    cb_row = ap_in("cb_row", [1, DM])
    db_row = ap_in("db_row", [1, DM])
    pb_row = ap_in("pb_row", [1, DM])
    dwT = ap_in("dwT", [DM, DM])
    wg_s = ap_in("wg_s", [DISH, DM])
    wu_s = ap_in("wu_s", [DISH, DM])
    wd_s = ap_in("wd_s", [DM // 8, DI])
    out = nc.dram_tensor("out", [OWN, DM], F32, kind="ExternalOutput").ap()

    DBG = {"s_x2", "s_up", "s_gt", "s_gs", "s_op"}
    dr = lambda n, s, d: (nc.dram_tensor(n, s, d, kind="ExternalOutput") if (debug and n in DBG)
                          else nc.dram_tensor(n, s, d)).ap()
    s_x1tm = dr("s_x1tm", [SEQ, DM], BF16)
    s_x1fm = dr("s_x1fm", [DM, SEQ], BF16)
    s_ir = dr("s_ir", [RNN, SEQ + 128], BF16)
    s_iz = dr("s_iz", [RNN, SEQ + 128], BF16)
    s_bx = dr("s_bx", [SSM, SEQ + 128], BF16)
    s_in = dr("s_in", [RNN, SEQ + 256], F32)
    s_gx = dr("s_gx", [2, SEQ], F32)
    s_hg = dr("s_hg", [RNN, SEQ], BF16)
    s_hs = dr("s_hs", [SSM, SEQ], BF16)
    s_x2 = dr("s_x2", [OWN, DM], F32)
    s_l3 = dr("s_l3", [OWN, 2], F32)
    s_up = dr("s_up", [OWN, DI], BF16)
    s_gt = dr("s_gt", [OWN, DI], BF16)
    s_gs = dr("s_gs", [DI, OWN], BF16)
    s_op = dr("s_op", [DI, OWN], BF16)
    s_dum = dr("s_dum", [RNN, 4 * 256], BF16)
    cc_i = dr("cc_i", [1, 8], F32)
    cc_o = dr("cc_o", [1, 8], F32)
    ag_g_i = dr("ag_g_i", [DISH, DM], BF16)
    ag_g_o = dr("ag_g_o", [DI, DM], BF16)
    ag_u_i = dr("ag_u_i", [DISH, DM], BF16)
    ag_u_o = dr("ag_u_o", [DI, DM], BF16)
    ag_d_i = dr("ag_d_i", [DM // 8, DI], BF16)
    ag_d_o = dr("ag_d_o", [DM, DI], BF16)
    RG = [list(range(n_cores))]

    es = contextlib.ExitStack()
    with es:
        _all_sb = []

        def sb(n, s, d):
            t = es.enter_context(nc.sbuf_tensor(n, s, d))
            _all_sb.append(t)
            return t
        sem = lambda n: es.enter_context(nc.semaphore(n))

        wscan = sb("wscan", [128, 448], BF16)
        i128 = sb("i128", [128, 128], BF16)
        tp_id = sb("tp_id", [128, 128], BF16)
        wallb = sb("wallb", [128, NCH * 512], BF16)
        cwTb = sb("cwTb", [SSM, DM], BF16)
        projTb = sb("projTb", [RNN, DM], BF16)
        biasg = sb("biasg", [128, 4], F32)
        bhhn_blk = sb("bhhn_blk", [128, 128], BF16)
        cbdb_b = sb("cbdb_b", [128, DM], F32)
        projb_b = sb("projb_b", [128, DM], F32)
        ones1 = sb("ones1", [1, DM], BF16)
        scal = sb("scal", [128, 16], F32)
        rolev = sb("rolev", [1, 8], mybir.dt.int32)
        pre_t = [[sb(f"pre{p}{s}", [128, 512], BF16) for s in range(2)] for p in range(2)]
        in_t = [sb(f"int{s}", [RNN, HALF], F32) for s in range(2)]
        hgw = sb("hgw", [RNN, 2 * HALF], BF16)
        hsw = sb("hsw", [SSM, 2 * HALF], BF16)
        st3 = sb("st3", [128, 6], F32)
        ncol = sb("ncol", [128, 2], F32)
        csc = sb("csc", [128, 2], F32)
        xtile = sb("xtile", [128, DM], F32)
        xtile2 = sb("xtile2", [128, DM], F32)
        stat6 = sb("stat6", [128, 64], F32)
        stat2 = sb("stat2", [128, 2], F32)
        tok4 = sb("tok4", [128, 8], F32)
        x1t = sb("x1t", [128, DM], BF16)
        xfm = [sb(f"xfm{s}", [128, NCH * 512], BF16) for s in range(2)]
        prodo = sb("prodo", [128, 512], BF16)
        prodf = sb("prodf", [128, 512], F32)
        hgt = sb("hgt", [RNN, 128], BF16)
        hst = sb("hst", [SSM, 128], BF16)
        hgO = sb("hgO", [RNN, SEQ // 4], BF16)
        hsO = sb("hsO", [SSM, SEQ // 4], BF16)
        gxO = sb("gxO", [2, SEQ // 4], F32)
        g2t = sb("g2t", [128, 2], F32)
        ytile = sb("ytile", [128, DM], F32)
        x2t = sb("x2t", [128, DM], F32)
        maxa = sb("maxa", [128, 4], F32)
        BIGW = max(NCH * (SEQ // 4), DIC * DM + NCH * DM)
        bigT = sb("bigT", [128, BIGW], BF16)
        wff = sb("wff", [128, NCH * 512], BF16)
        identf = sb("identf", [128, 128], F32)
        onesf1 = sb("onesf1", [1, 128], F32)
        onesc = sb("onesc", [128, 1], F32)
        ccrow = sb("ccrow", [1, 8], F32)
        ccrow2 = sb("ccrow2", [1, 8], F32)
        projrow = sb("projrow", [1, DM], BF16)
        oslc = sb("oslc", [128, max(DIC * 512, NCH * DM)], BF16)
        dwTb = oslc
        fft = sb("fft", [128, 512], BF16)
        ffta = sb("ffta", [128, 512], F32)
        fftb = sb("fftb", [128, 512], F32)
        fftc = sb("fftc", [128, 512], F32)
        i16t = prodf.bitcast(I16)
        u16t = prodo.bitcast(U16)
        pb = [es.enter_context(nc.psum_tensor(f"pb{p}", [128, 512], F32)) for p in range(2)]
        pp = [es.enter_context(nc.psum_tensor(f"pp{p}", [128, 512], F32)) for p in range(3)]
        pq = es.enter_context(nc.psum_tensor("pq", [128, 512], F32))
        ptr = es.enter_context(nc.psum_tensor("ptr", [128, 128], BF16))

        ld = sem("ld"); sd_pre = sem("sd_pre"); sd_h = sem("sd_h"); sd_o = sem("sd_o")
        ld_a = sem("ld_a"); ld_b = sem("ld_b"); ld_c = sem("ld_c"); ld_d = sem("ld_d")
        s_mm = sem("s_mm"); s_act = sem("s_act"); s_dve = sem("s_dve")
        s_mm2 = sem("s_mm2"); s_act2 = sem("s_act2"); s_premm = sem("s_premm")
        s_sg = sem("s_sg"); s_c1 = sem("s_c1"); s_cc = sem("s_cc"); s_gen = sem("s_gen")
        bar = [sem(f"bar{p}") for p in range(2)]
        go = [sem(f"go{p}") for p in range(2)]
        block = es.enter_context(nc.Block())

        T, V, S, G, SY = nc.tensor, nc.vector, nc.scalar, nc.gpsimd, nc.sync
        cnt = {"ld": 0, "sd_o": 0, "gen": 0, "mm": 0, "act": 0, "cc": 0,
               "la": 0, "lb": 0, "lc": 0, "ld2": 0}
        LSEM = {"la": ld_a, "lb": ld_b, "lc": ld_c, "ld2": ld_d}

        def ldma2(key, out_ap, in_ap, transpose=False, eng=SY):
            eng.wait_ge(LSEM[key], cnt[key])
            cnt[key] += 16
            if transpose:
                eng.dma_start_transpose(out_ap, in_ap).then_inc(LSEM[key], 16)
            else:
                eng.dma_start(out=out_ap, in_=in_ap).then_inc(LSEM[key], 16)
            return cnt[key]

        def ldma(out_ap, in_ap, eng=SY, transpose=False, guard=True):
            if guard:
                eng.wait_ge(s_gen, cnt["gen"])
                eng.wait_ge(ld, cnt["ld"])
                eng.wait_ge(sd_o, cnt["sd_o"])
            cnt["ld"] += 16
            if transpose:
                eng.dma_start_transpose(out_ap, in_ap).then_inc(ld, 16)
            else:
                eng.dma_start(out=out_ap, in_=in_ap).then_inc(ld, 16)
            return cnt["ld"]

        def odma(out_ap, in_ap, eng=G):
            eng.wait_ge(sd_o, cnt["sd_o"])
            cnt["sd_o"] += 16
            eng.dma_start(out=out_ap, in_=in_ap).then_inc(sd_o, 16)
            return cnt["sd_o"]

        def gen(instr):
            cnt["gen"] += 1
            instr.then_inc(s_gen, 1)
            return cnt["gen"]

        def mm(instr):
            cnt["mm"] += 1
            instr.then_inc(s_mm, 1)
            return cnt["mm"]

        def act(instr):
            cnt["act"] += 1
            instr.then_inc(s_act, 1)
            return cnt["act"]

        def vwait(c=None):
            V.wait_ge(s_gen, cnt["gen"] if c is None else c)

        for _t in _all_sb:
            gen(V.memset(_t.ap(), 0)); vwait()
        v = ldma(rolev[:, :], role[:, :])
        v = ldma(xtile[0:RNN, 0:384], whhT[:, :])
        v = ldma(xtile[0:SSM, 384:448], aT[:, :])
        v = ldma(xtile2[:, 0:128], ident[:, :])
        v = ldma(biasg[:, 0:4], bias_g[:, :])
        v = ldma(tok4[:, 0:3], bhh_g[:, :])
        V.wait_ge(ld, v)
        gen(V.memset(wscan[:, :], 0.0)); vwait()
        gen(V.tensor_copy(wscan[0:RNN, 0:384], xtile[0:RNN, 0:384])); vwait()
        gen(V.tensor_copy(wscan[0:SSM, 384:448], xtile[0:SSM, 384:448])); vwait()
        gen(V.tensor_copy(i128[:, :], xtile2[:, 0:128])); vwait()
        gen(V.tensor_copy(tp_id[:, :], xtile2[:, 0:128])); vwait()
        gen(V.tensor_add(biasg[:, 0:2], biasg[:, 0:2], tok4[:, 0:2])); vwait()
        gen(V.memset(bhhn_blk[:, :], 0.0)); vwait()
        gen(V.tensor_scalar(bhhn_blk[:, :], bhhn_blk[:, :], tok4[:, 2:3], None, ALU.add)); vwait()
        gen(V.memset(ones1[:, :], 1.0)); vwait()
        gen(V.memset(hgw[:, :], 0.0)); vwait()
        gen(V.memset(hsw[:, :], 0.0)); vwait()
        gen(V.memset(onesc[:, :], 1.0)); vwait()
        gen(V.memset(maxa[:, :], 0.0)); vwait()
        gen(V.memset(scal[:, :], 0.0)); vwait()
        for ch in range(NCH):
            v = ldma(xtile[:, 0:512], wall.rearrange("(c p) s -> c p s", p=128)[ch])
            V.wait_ge(ld, v); vwait()
            gen(V.tensor_copy(wallb[:, 512 * ch:512 * (ch + 1)], xtile[:, 0:512]))
        for ch in range(NCH):
            v = ldma(xtile[:, 0:DM], dwT.rearrange("(c p) s -> c p s", p=128)[ch])
            V.wait_ge(ld, v); vwait()
            gen(V.tensor_copy(dwTb[:, DM * ch:DM * (ch + 1)], xtile[:, 0:DM]))
        v = ldma(xtile[0:SSM, 0:DM], cwT[:, :])
        V.wait_ge(ld, v); vwait()
        gen(V.tensor_copy(cwTb[:, :], xtile[0:SSM, 0:DM]))
        v = ldma(xtile[0:RNN, 0:DM], projT[:, :])
        V.wait_ge(ld, v); vwait()
        gen(V.tensor_copy(projTb[:, :], xtile[0:RNN, 0:DM]))
        v = ldma(xtile[0:1, 0:DM], cb_row[:, :])
        v = ldma(xtile2[0:1, 0:DM], db_row[:, :])
        v = ldma(ytile[0:1, 0:DM], pb_row[:, :])
        V.wait_ge(ld, v); vwait()
        gen(V.tensor_add(xtile[0:1, 0:DM], xtile[0:1, 0:DM], xtile2[0:1, 0:DM])); vwait()
        gen(V.tensor_copy(x1t[0:1, 0:DM], xtile[0:1, 0:DM])); vwait()
        gen(V.tensor_copy(projrow[0:1, 0:DM], ytile[0:1, 0:DM]))
        T.wait_ge(s_gen, cnt["gen"])
        for hh in range(DM // 512):
            m = mm(T.matmul(pp[2][:, 0:512], ones1[0:1, 0:128], x1t[0:1, 512 * hh:512 * (hh + 1)],
                            start=True, stop=True))
            V.wait_ge(s_mm, m)
            gen(V.tensor_copy(cbdb_b[:, 512 * hh:512 * (hh + 1)], pp[2][:, 0:512]))
            T.wait_ge(s_gen, cnt["gen"])
            m = mm(T.matmul(pp[2][:, 0:512], ones1[0:1, 0:128], projrow[0:1, 512 * hh:512 * (hh + 1)],
                            start=True, stop=True))
            V.wait_ge(s_mm, m)
            gen(V.tensor_copy(projb_b[:, 512 * hh:512 * (hh + 1)], pp[2][:, 0:512]))
            T.wait_ge(s_gen, cnt["gen"])

        def dq_panel(rr, cols):
            V.wait_ge(sd_o, cnt["sd_o"])
            nb = cols // 32
            gen(V.tensor_reduce(stat6[0:rr, 0:nb], xtile[0:rr, 0:cols].rearrange("p (b k) -> p b k", k=32),
                                AX.X, ALU.max, apply_absolute_value=True)); vwait()
            gen(V.tensor_scalar(stat6[0:rr, 0:nb], stat6[0:rr, 0:nb], 1.0 / 7.0, 1e-12, ALU.mult, ALU.max)); vwait()
            gen(V.reciprocal(stat6[0:rr, nb:2 * nb], stat6[0:rr, 0:nb]))
            for b in range(nb):
                vwait()
                gen(V.tensor_scalar(xtile2[0:rr, 32 * b:32 * (b + 1)], xtile[0:rr, 32 * b:32 * (b + 1)],
                                    stat6[0:rr, nb + b:nb + b + 1], None, ALU.mult))
            vwait()
            gen(V.tensor_copy(i16t[0:rr, 0:cols], xtile2[0:rr, 0:cols])); vwait()
            gen(V.tensor_copy(xtile2[0:rr, 0:cols], i16t[0:rr, 0:cols]))
            for b in range(nb):
                vwait()
                gen(V.tensor_scalar(x1t[0:rr, 32 * b:32 * (b + 1)], xtile2[0:rr, 32 * b:32 * (b + 1)],
                                    stat6[0:rr, b:b + 1], None, ALU.mult))

        def dequant(src_ap, rows, cols, dst_dram):
            ngr = (rows + 127) // 128
            for g in range(ngr):
                r0 = g * 128
                rr = min(128, rows - r0)
                for p0 in range(0, cols, DM):
                    pw = min(DM, cols - p0)
                    v = ldma(xtile[0:rr, 0:pw], src_ap[r0:r0 + rr, p0:p0 + pw])
                    V.wait_ge(ld, v)
                    dq_panel(rr, pw)
                    G.wait_ge(s_gen, cnt["gen"])
                    odma(dst_dram[r0:r0 + rr, p0:p0 + pw], x1t[0:rr, 0:pw])

        dequant(wg_s, DISH, DM, ag_g_i)
        dequant(wu_s, DISH, DM, ag_u_i)
        dequant(wd_s, DM // 8, DI, ag_d_i)
        G.wait_ge(sd_o, cnt["sd_o"])
        for (ai, ao) in ((ag_g_i, ag_g_o), (ag_u_i, ag_u_o), (ag_d_i, ag_d_o)):
            cnt["cc"] += 1
            G.collective_compute("AllGather", ALU.bypass, replica_groups=RG,
                                 ins=[ai[:, :]], outs=[ao[:, :]]).then_inc(s_cc, 1)

        NT_ALL = SEQ // 128
        for ti in range(NT_ALL):
            xt = xtile if ti % 2 == 0 else xtile2
            v = ldma(xt[:, :], xb[128 * ti:128 * (ti + 1), :])
            V.wait_ge(ld, v)
            vwait()
            gen(V.bn_stats(stat6[:, 0:6], xt[:, 0:512])); vwait()
            gen(V.bn_stats(stat6[:, 6:12], xt[:, 512:1024])); vwait()
            gen(V.bn_aggr(stat2[:, 0:2], stat6[:, 0:12])); vwait()
            gen(V.tensor_scalar(tok4[:, 2:3], stat2[:, 1:2], EPS, None, ALU.add))
            S.wait_ge(s_gen, cnt["gen"])
            a = act(S.activation(tok4[:, 3:4], tok4[:, 2:3], AF.Sqrt))
            V.wait_ge(s_act, a)
            gen(V.reciprocal(tok4[:, 1:2], tok4[:, 3:4])); vwait()
            V.wait_ge(sd_o, cnt["sd_o"])
            gen(V.tensor_scalar(x1t[:, :], xt[:, :], stat2[:, 0:1], tok4[:, 1:2], ALU.subtract, ALU.mult))
            G.wait_ge(s_gen, cnt["gen"])
            odma(s_x1tm[128 * ti:128 * (ti + 1), :], x1t[:, :])
        G.wait_ge(sd_o, cnt["sd_o"])
        SY.wait_ge(sd_o, cnt["sd_o"])

        evac_done = {0: 0, 1: 0}
        buf_guard = {0: 0, 1: 0}
        dump_guard = {0: 0, 1: 0}
        SHF = SEQ // 2
        for sl in range(NSL_ALL):
            buf = xfm[sl % 2]
            SY.wait_ge(s_mm, buf_guard[sl % 2])
            SY.wait_ge(sd_o, dump_guard[sl % 2])
            SY.wait_ge(ld, cnt["ld"])
            for ch in range(NCH):
                v = ldma(buf[:, 512 * ch:512 * (ch + 1)],
                         s_x1tm[512 * sl:512 * (sl + 1), 128 * ch:128 * (ch + 1)], transpose=True, guard=False)
            G.wait_ge(ld, v)
            for ch in range(NCH):
                odma(s_x1fm.rearrange("(c p) t -> c p t", p=128)[ch][:, 512 * sl:512 * (sl + 1)],
                     buf[:, 512 * ch:512 * (ch + 1)])
            T.wait_ge(ld, v)
            for g in range(4):
                bank = pp[g % 2]
                T.wait_ge(s_gen, evac_done[g % 2])
                for ch in range(NCH):
                    mi = T.matmul(bank[:, 0:512], wallb[:, 512 * ch + 128 * g:512 * ch + 128 * (g + 1)],
                                  buf[:, 512 * ch:512 * (ch + 1)],
                                  start=(ch == 0), stop=(ch == NCH - 1), skip_group_check=True)
                    if ch == NCH - 1:
                        m = mm(mi)
                V.wait_ge(s_mm, m)
                V.wait_ge(sd_o, cnt["sd_o"])
                if g < 2:
                    dst = s_ir if g == 0 else s_iz
                    gen(V.tensor_scalar(prodo[:, 0:256],
                                        bank[:, 0:512].rearrange("p (t two) -> p two t", two=2)[:, 0, :],
                                        biasg[:, g:g + 1], None, ALU.add)); vwait()
                    gen(V.tensor_scalar(prodo[:, 256:512],
                                        bank[:, 0:512].rearrange("p (t two) -> p two t", two=2)[:, 1, :],
                                        biasg[:, g:g + 1], None, ALU.add))
                    evac_done[g % 2] = cnt["gen"]
                    G.wait_ge(s_gen, cnt["gen"])
                    odma(dst[:, 256 * sl:256 * (sl + 1)], prodo[:, 0:256])
                    odma(dst[:, SHF + 256 * sl:SHF + 256 * (sl + 1)], prodo[:, 256:512])
                elif g == 2:
                    gen(V.tensor_scalar(prodf[:, 0:512], bank[:, 0:512], biasg[:, 2:3], None, ALU.add))
                    evac_done[g % 2] = cnt["gen"]
                    G.wait_ge(s_gen, cnt["gen"])
                    odma(s_in[:, 512 * sl:512 * (sl + 1)], prodf[:, 0:512])
                else:
                    gen(V.tensor_scalar(prodf[:, 0:512], bank[:, 0:512], biasg[:, 3:4], None, ALU.add))
                    ec = cnt["gen"]
                    vwait()
                    gen(V.tensor_copy(prodo[0:SSM, 0:256],
                                      prodf[0:SSM, 0:512].rearrange("p (t two) -> p two t", two=2)[:, 0, :])); vwait()
                    gen(V.tensor_copy(prodo[0:SSM, 256:512],
                                      prodf[0:SSM, 0:512].rearrange("p (t two) -> p two t", two=2)[:, 1, :]))
                    S.wait_ge(s_gen, ec)
                    S.wait_ge(sd_o, cnt["sd_o"])
                    a = act(S.activation(fftb[64:66, 0:512], prodf[64:66, 0:512], AF.Sigmoid))
                    evac_done[g % 2] = cnt["gen"]
                    G.wait_ge(s_gen, cnt["gen"])
                    G.wait_ge(s_act, a)
                    odma(s_bx[:, 256 * sl:256 * (sl + 1)], prodo[0:SSM, 0:256])
                    odma(s_bx[:, SHF + 256 * sl:SHF + 256 * (sl + 1)], prodo[0:SSM, 256:512])
                    odma(s_gx[:, 512 * sl:512 * (sl + 1)], fftb[64:66, 0:512])
            buf_guard[sl % 2] = cnt["mm"]
            dump_guard[sl % 2] = cnt["sd_o"]
        G.wait_ge(sd_o, cnt["sd_o"])
        G.wait_ge(s_cc, cnt["cc"])

        PREDMA = 7 * 16

        def issue_half_streams(eng, g_expr, slot, prior=None):
            if prior is not None:
                eng.wait_ge(sd_pre, prior)
            e0 = g_expr * HALF // 2 if False else None
            base_e = g_expr * 128
            eng.dma_start(out=pre_t[0][slot][0:RNN, 0:128], in_=s_ir[:, bass.ts(g_expr, 128)]).then_inc(sd_pre, 16)
            eng.dma_start(out=pre_t[0][slot][0:RNN, 128:256], in_=s_iz[:, bass.ts(g_expr, 128)]).then_inc(sd_pre, 16)
            eng.dma_start(out=pre_t[0][slot][0:SSM, 256:384], in_=s_bx[:, bass.ts(g_expr, 128)]).then_inc(sd_pre, 16)
            eng.dma_start(out=pre_t[1][slot][0:RNN, 0:128],
                          in_=s_ir[:, SHF:SEQ + 128][:, bass.ts(g_expr, 128)]).then_inc(sd_pre, 16)
            eng.dma_start(out=pre_t[1][slot][0:RNN, 128:256],
                          in_=s_iz[:, SHF:SEQ + 128][:, bass.ts(g_expr, 128)]).then_inc(sd_pre, 16)
            eng.dma_start(out=pre_t[1][slot][0:SSM, 256:384],
                          in_=s_bx[:, SHF:SEQ + 128][:, bass.ts(g_expr, 128)]).then_inc(sd_pre, 16)
            eng.dma_start(out=in_t[slot][:, :], in_=s_in[:, bass.ts(g_expr, HALF)]).then_inc(sd_pre, 16)

        V.wait_ge(s_act, cnt["act"])
        V.wait_ge(sd_o, cnt["sd_o"])
        gen(V.memset(pre_t[0][0][SSM:128, 256:384], 0.0)); vwait()
        gen(V.memset(pre_t[0][1][SSM:128, 256:384], 0.0)); vwait()
        gen(V.memset(pre_t[1][0][SSM:128, 256:384], 0.0)); vwait()
        gen(V.memset(pre_t[1][1][SSM:128, 256:384], 0.0)); vwait()
        gen(V.memset(prodo[:, :], 0.0)); vwait()
        gen(V.memset(prodf[:, :], 0.0)); vwait()
        G.wait_ge(s_gen, cnt["gen"])
        odma(s_ir[:, SEQ:SEQ + 128], prodo[0:RNN, 0:128])
        odma(s_iz[:, SEQ:SEQ + 128], prodo[0:RNN, 0:128])
        odma(s_bx[:, SEQ:SEQ + 128], prodo[0:SSM, 0:128])
        odma(s_in[:, SEQ:SEQ + 256], prodf[0:RNN, 0:256])
        for dk in range(2):
            G.wait_ge(sd_h, 16 * dk)
            G.dma_start(out=s_dum[:, 256 * dk:256 * dk + HALF], in_=hgw[:, 0:HALF]).then_inc(sd_h, 16)
        SY.wait_ge(sd_o, cnt["sd_o"])
        issue_half_streams(SY, 0, 0)
        G.wait_ge(sd_o, cnt["sd_o"])
        for e in ENGS:
            eng = getattr(nc, e)
            eng.drain().then_inc(bar[1], 1)
            eng.wait_ge(bar[1], 5)

        mm0 = cnt["mm"]
        act0 = cnt["act"]
        rT_dve = T.alloc_register("rT_dve")
        rT_pre = T.alloc_register("rT_pre")
        rT_h = T.alloc_register("rT_h")
        rV_act = V.alloc_register("rV_act")
        rV_act2 = V.alloc_register("rV_act2")
        rV_c1 = V.alloc_register("rV_c1")
        rV_h = V.alloc_register("rV_h")
        rS_mm = S.alloc_register("rS_mm")
        rS_sg = S.alloc_register("rS_sg")
        rS_sg2 = S.alloc_register("rS_sg2")
        rS_pre = S.alloc_register("rS_pre")
        rY_premm = SY.alloc_register("rY_premm")
        rY_act = SY.alloc_register("rY_act")
        rY_pre = SY.alloc_register("rY_pre")
        rG_dve = G.alloc_register("rG_dve")
        rG_h = G.alloc_register("rG_h")
        T.reg_mov(rT_dve, 0)
        T.reg_mov(rT_pre, PREDMA)
        T.reg_mov(rT_h, 32)
        V.reg_mov(rV_act, 1)
        V.reg_mov(rV_act2, 2)
        V.reg_mov(rV_c1, 1)
        V.reg_mov(rV_h, 32)
        S.reg_mov(rS_mm, 1)
        S.reg_mov(rS_sg, 2)
        S.reg_mov(rS_sg2, 1)
        S.reg_mov(rS_pre, PREDMA)
        SY.reg_mov(rY_premm, 0)
        SY.reg_mov(rY_act, 0)
        SY.reg_mov(rY_pre, PREDMA)
        G.reg_mov(rG_dve, HALF)
        G.reg_mov(rG_h, 32)
        with nc.Fori(0, NBODY, 1) as ib:
            for X in range(2):
                g_idx = ib * 2 + X
                SY.wait_ge(sd_pre, rY_pre)
                SY.reg_add(rY_pre, rY_pre, PREDMA)
                SY.wait_ge(s_premm, rY_premm)
                SY.reg_add(rY_premm, rY_premm, 2)
                SY.wait_ge(s_act2, rY_act)
                SY.reg_add(rY_act, rY_act, 2 * HALF)
                issue_half_streams(SY, g_idx + 1, 1 - X)
                T.wait_ge(sd_h, rT_h)
                T.reg_add(rT_h, rT_h, 32)
                V.wait_ge(sd_h, rV_h)
                V.reg_add(rV_h, rV_h, 32)
                T.wait_ge(sd_pre, rT_pre)
                T.reg_add(rT_pre, rT_pre, PREDMA)
                S.wait_ge(sd_pre, rS_pre)
                S.reg_add(rS_pre, rS_pre, PREDMA)
                T.matmul(pb[0][:, 0:384], i128[:, :], pre_t[0][X][:, 0:384], start=True, stop=False,
                         skip_group_check=True)
                T.matmul(pb[0][:, 384:512], i128[:, :], bhhn_blk[:, :], start=False, stop=True,
                         skip_group_check=True).then_inc(s_premm, 1)
                T.matmul(pb[1][:, 0:384], i128[:, :], pre_t[1][X][:, 0:384], start=True, stop=False,
                         skip_group_check=True)
                T.matmul(pb[1][:, 384:512], i128[:, :], bhhn_blk[:, :], start=False, stop=True,
                         skip_group_check=True).then_inc(s_premm, 1)
                for t in range(HALF):
                    par = t % 2
                    j = t // 2
                    sslot = t % 2
                    hprev = (X * HALF + t - 1) % (2 * HALF)
                    hcur = X * HALF + t
                    bank = pb[par]
                    T.wait_ge(s_dve, rT_dve)
                    T.matmul(bank[:, j:j + 1], wscan[:, 0:128], hgw[:, hprev:hprev + 1],
                             start=False, stop=False, skip_group_check=True)
                    T.matmul(bank[:, 128 + j:129 + j], wscan[:, 128:256], hgw[:, hprev:hprev + 1],
                             start=False, stop=False, skip_group_check=True).then_inc(s_mm2, 1)
                    T.matmul(bank[0:SSM, 256 + j:257 + j], wscan[0:SSM, 384:448], hsw[:, hprev:hprev + 1],
                             start=False, stop=False, skip_group_check=True)
                    T.matmul(bank[:, 384 + j:385 + j], wscan[:, 256:384], hgw[:, hprev:hprev + 1],
                             start=False, stop=False, skip_group_check=True).then_inc(s_mm2, 1)
                    T.reg_add(rT_dve, rT_dve, 1)
                    S.wait_ge(s_mm2, rS_mm)
                    S.activation(st3[:, 3 * sslot:3 * sslot + 2],
                                 bank[:, 0:512].rearrange("p (b j) -> p j b", b=4)[:, j, 0:2],
                                 AF.Sigmoid).then_inc(s_sg, 1)
                    S.wait_ge(s_mm2, rS_sg)
                    S.wait_ge(s_sg, rS_sg2)
                    S.activation(ncol[:, sslot:sslot + 1], bank[:, 384 + j:385 + j], AF.Tanh,
                                 scale=st3[:, 3 * sslot:3 * sslot + 1],
                                 bias=in_t[X][:, t:t + 1]).then_inc(s_act2, 1)
                    S.activation(st3[0:SSM, 3 * sslot + 2:3 * sslot + 3],
                                 bank[0:SSM, 256 + j:257 + j], AF.Sigmoid).then_inc(s_act2, 1)
                    S.reg_add(rS_mm, rS_mm, 2)
                    S.reg_add(rS_sg, rS_sg, 2)
                    S.reg_add(rS_sg2, rS_sg2, 1)
                    V.wait_ge(s_act2, rV_act)
                    V.scalar_tensor_tensor(csc[:, sslot:sslot + 1], hgw[:, hprev:hprev + 1],
                                           ncol[:, sslot:sslot + 1], st3[:, 3 * sslot + 1:3 * sslot + 2],
                                           ALU.subtract, ALU.mult).then_inc(s_c1, 1)
                    V.wait_ge(s_act2, rV_act2)
                    V.tensor_mul(hsw[:, hcur:hcur + 1], bank[0:SSM, 256 + j:257 + j],
                                 st3[0:SSM, 3 * sslot + 2:3 * sslot + 3])
                    V.wait_ge(s_c1, rV_c1)
                    V.tensor_add(hgw[:, hcur:hcur + 1], csc[:, sslot:sslot + 1],
                                 ncol[:, sslot:sslot + 1]).then_inc(s_dve, 1)
                    V.reg_add(rV_act, rV_act, 2)
                    V.reg_add(rV_act2, rV_act2, 2)
                    V.reg_add(rV_c1, rV_c1, 1)
                G.wait_ge(s_dve, rG_dve)
                G.reg_add(rG_dve, rG_dve, HALF)
                G.wait_ge(sd_h, rG_h)
                G.reg_add(rG_h, rG_h, 32)
                G.dma_start(out=s_hg[:, bass.ts(g_idx, HALF)],
                            in_=hgw[:, X * HALF:(X + 1) * HALF]).then_inc(sd_h, 16)
                G.dma_start(out=s_hs[:, bass.ts(g_idx, HALF)],
                            in_=hsw[:, X * HALF:(X + 1) * HALF]).then_inc(sd_h, 16)

        for e in ENGS:
            eng = getattr(nc, e)
            eng.drain().then_inc(bar[0], 1)
            eng.wait_ge(bar[0], 5)
        for e in ENGS:
            getattr(nc, e).wait_ge(sd_h, 32 + 32 * NHALF)

        NSLO = OWN // 512

        v = ldma(identf[:, :], ident[:, :])
        V.wait_ge(ld, v)
        gen(V.memset(onesf1[:, :], 1.0)); vwait()
        gen(V.memset(maxa[:, :], 0.0)); vwait()
        gen(V.memset(ccrow[:, :], 0.0)); vwait()

        def bcast_scal(col, src_ap):
            T.wait_ge(s_gen, cnt["gen"])
            m = mm(T.matmul(pq[:, 0:1], onesf1[0:1, 0:128], src_ap, start=True, stop=True))
            V.wait_ge(s_mm, m)
            gen(V.tensor_copy(scal[:, col:col + 1], pq[:, 0:1]))

        def maxcol_to_row(col, dst_col):
            vwait()
            V.wait_ge(sd_o, cnt["sd_o"])
            V.wait_ge(s_act, cnt["act"])
            gen(V.tensor_copy(fft[:, 0:1], maxa[:, col:col + 1]))
            T.wait_ge(s_gen, cnt["gen"])
            m = mm(T.transpose(ptr[0:1, 0:128], fft[:, 0:1], tp_id[:, :]))
            V.wait_ge(s_mm, m)
            gen(V.reduce_max(ccrow[0:1, dst_col:dst_col + 1], ptr[0:1, 0:128], AX.X)); vwait()

        qv = nc.sync.cc_rank(RG) % 4

        for ch in range(NCH):
            v = ldma(bigT[:, ch * OWN:(ch + 1) * OWN],
                     s_x1fm.rearrange("(c p) t -> c p t", p=128)[ch][:, :][:, bass.ts(qv, OWN)])
        v = ldma(hgO[:, :], s_hg[:, bass.ts(qv, OWN)])
        v = ldma(hsO[:, :], s_hs[:, bass.ts(qv, OWN)])
        v = ldma(gxO[:, :], s_gx[:, bass.ts(qv, OWN)])
        T.wait_ge(ld, v)

        for k in range(NTI):
            V.wait_ge(sd_o, cnt["sd_o"])
            v1 = ldma(xtile[:, :], xq[128 * k:128 * (k + 1), :])
            for half in range(2):
                T.wait_ge(ld, v1)
                for ch in range(NCH):
                    mi = T.matmul(pp[half][:, 0:512], bigT[:, ch * OWN + 128 * k:ch * OWN + 128 * (k + 1)],
                                  dwTb[:, DM * ch + 512 * half:DM * ch + 512 * (half + 1)],
                                  start=(ch == 0), stop=(ch == NCH - 1), skip_group_check=True)
                    if ch == NCH - 1:
                        m = mm(mi)
            for half in range(2):
                m = mm(T.matmul(pb[half][:, 0:512], hsO[0:SSM, 128 * k:128 * (k + 1)],
                                cwTb[0:SSM, 512 * half:512 * (half + 1)],
                                start=True, stop=True, skip_group_check=True))
            m = mm(T.matmul(pq[:, 0:512], hgO[:, 128 * k:128 * (k + 1)], projTb[:, 0:512], start=True, stop=True,
                            skip_group_check=True))
            m = mm(T.matmul(pp[2][:, 0:512], hgO[:, 128 * k:128 * (k + 1)], projTb[:, 512:1024], start=True, stop=True,
                            skip_group_check=True))
            V.wait_ge(ld, v1)
            gen(V.tensor_copy(fft[0:2, 0:128], gxO[0:2, 128 * k:128 * (k + 1)]))
            T.wait_ge(s_gen, cnt["gen"])
            m2 = mm(T.transpose(ptr[:, 0:2], fft[0:2, 0:128], tp_id[0:2, 0:2]))
            V.wait_ge(s_mm, m)
            V.wait_ge(s_mm, m2)
            gen(V.tensor_copy(g2t[:, 0:2], ptr[:, 0:2])); vwait()
            for half in range(2):
                H = slice(512 * half, 512 * (half + 1))
                gen(V.tensor_add(ytile[:, H], pp[half][:, 0:512], cbdb_b[:, H])); vwait()
                gen(V.tensor_add(ytile[:, H], ytile[:, H], pb[half][:, 0:512])); vwait()
                gen(V.tensor_scalar(ytile[:, H], ytile[:, H], g2t[:, 0:1], None, ALU.mult)); vwait()
                rb = pq if half == 0 else pp[2]
                gen(V.tensor_add(x2t[:, H], rb[:, 0:512], projb_b[:, H])); vwait()
                gen(V.tensor_scalar(x2t[:, H], x2t[:, H], g2t[:, 1:2], None, ALU.mult)); vwait()
                gen(V.tensor_add(ytile[:, H], ytile[:, H], x2t[:, H])); vwait()
            gen(V.bn_stats(stat6[:, 0:6], ytile[:, 0:512])); vwait()
            gen(V.bn_stats(stat6[:, 6:12], ytile[:, 512:1024])); vwait()
            gen(V.bn_aggr(stat2[:, 0:2], stat6[:, 0:12])); vwait()
            gen(V.tensor_scalar(tok4[:, 2:3], stat2[:, 1:2], EPS, None, ALU.add))
            S.wait_ge(s_gen, cnt["gen"])
            a = act(S.activation(tok4[:, 3:4], tok4[:, 2:3], AF.Sqrt))
            V.wait_ge(s_act, a)
            gen(V.reciprocal(tok4[:, 1:2], tok4[:, 3:4])); vwait()
            gen(V.tensor_scalar(ytile[:, :], ytile[:, :], stat2[:, 0:1], tok4[:, 1:2],
                                ALU.subtract, ALU.mult)); vwait()
            gen(V.tensor_add(x2t[:, :], ytile[:, :], xtile[:, :])); vwait()
            G.wait_ge(s_gen, cnt["gen"])
            odma(s_x2[128 * k:128 * (k + 1), :], x2t[:, :])
            gen(V.bn_stats(stat6[:, 0:6], x2t[:, 0:512])); vwait()
            gen(V.bn_stats(stat6[:, 6:12], x2t[:, 512:1024])); vwait()
            gen(V.bn_aggr(stat2[:, 0:2], stat6[:, 0:12])); vwait()
            gen(V.tensor_scalar(tok4[:, 2:3], stat2[:, 1:2], EPS, None, ALU.add))
            S.wait_ge(s_gen, cnt["gen"])
            a = act(S.activation(tok4[:, 3:4], tok4[:, 2:3], AF.Sqrt))
            V.wait_ge(s_act, a)
            gen(V.reciprocal(tok4[:, 1:2], tok4[:, 3:4])); vwait()
            gen(V.tensor_copy(tok4[:, 0:1], stat2[:, 0:1])); vwait()
            G.wait_ge(s_gen, cnt["gen"])
            odma(s_l3[128 * k:128 * (k + 1), 0:2], tok4[:, 0:2])
            gen(V.tensor_scalar(ytile[:, :], x2t[:, :], tok4[:, 0:1], None, ALU.subtract)); vwait()
            gen(V.reduce_max(tok4[:, 4:5], ytile[:, :], AX.X, apply_absolute_value=True)); vwait()
            gen(V.tensor_scalar(tok4[:, 5:6], tok4[:, 4:5], tok4[:, 1:2], None, ALU.mult)); vwait()
            gen(V.tensor_max(maxa[:, 0:1], maxa[:, 0:1], tok4[:, 5:6])); vwait()
        maxcol_to_row(0, 0)
        G.wait_ge(s_gen, cnt["gen"])
        odma(cc_i[0:1, 0:8], ccrow[0:1, 0:8])
        G.wait_ge(sd_o, cnt["sd_o"])
        cnt["cc"] += 1
        G.collective_compute("AllReduce", ALU.max, replica_groups=RG,
                             ins=[cc_i[:, :]], outs=[cc_o[:, :]]).then_inc(s_cc, 1)
        G.wait_ge(s_cc, cnt["cc"])
        SY.wait_ge(s_cc, cnt["cc"])
        v = ldma(ccrow2[0:1, 0:8], cc_o[0:1, 0:8])
        V.wait_ge(ld, v)
        gen(V.tensor_scalar(ccrow2[0:1, 1:2], ccrow2[0:1, 0:1], 1.0 / 32767.0, 1e-12, ALU.mult, ALU.max)); vwait()
        gen(V.reciprocal(ccrow2[0:1, 2:3], ccrow2[0:1, 1:2])); vwait()
        bcast_scal(0, ccrow2[0:1, 1:2])
        bcast_scal(1, ccrow2[0:1, 2:3])

        for k in range(NTI):
            V.wait_ge(sd_o, cnt["sd_o"])
            v1 = ldma(x2t[:, :], s_x2[128 * k:128 * (k + 1), :])
            v1 = ldma(tok4[:, 0:2], s_l3[128 * k:128 * (k + 1), 0:2])
            V.wait_ge(ld, v1)
            gen(V.tensor_scalar(tok4[:, 2:3], tok4[:, 1:2], scal[:, 1:2], None, ALU.mult)); vwait()
            gen(V.tensor_scalar(ytile[:, :], x2t[:, :], tok4[:, 0:1], tok4[:, 2:3],
                                ALU.subtract, ALU.mult)); vwait()
            gen(V.tensor_copy(i16t[:, :], ytile[:, :])); vwait()
            gen(V.tensor_copy(x1t[:, :], i16t[:, :])); vwait()
            for ch in range(NCH):
                T.wait_ge(s_gen, cnt["gen"])
                m = mm(T.transpose(ptr[:, :], x1t[:, 128 * ch:128 * (ch + 1)], tp_id[:, :]))
                V.wait_ge(s_mm, m)
                gen(V.tensor_copy(bigT[:, ch * OWN + 128 * k:ch * OWN + 128 * (k + 1)], ptr[:, :]))
        wffb = [wff, xfm[0]]
        groups = [(wsrc, dst, mcol, g)
                  for (wsrc, dst, mcol) in ((ag_u_o, s_up, 1), (ag_g_o, s_gt, 2))
                  for g in range(NG)]
        fstg = [fft, prodo]
        NGT = len(groups)

        def _issue_wffload(gi):
            wsrc, dst, mcol, g = groups[gi]
            buf = wffb[gi % 2]
            for ch in range(NCH):
                vv = ldma(buf[:, 512 * ch:512 * (ch + 1)],
                          wsrc[512 * g:512 * (g + 1), 128 * ch:128 * (ch + 1)], transpose=True, guard=False)
            return vv

        SY.wait_ge(s_mm, cnt["mm"])
        SY.wait_ge(ld, cnt["ld"])
        v_w = [None] * NGT
        v_w[0] = _issue_wffload(0)
        evac_gen = {}
        st_cnt = {}
        mm_end = {}
        ts0 = cnt["gen"]
        T.wait_ge(s_gen, ts0)
        tser = 0
        for gi in range(NGT):
            wsrc, dst, mcol, g = groups[gi]
            buf = wffb[gi % 2]
            T.wait_ge(ld, v_w[gi])
            for k in range(NTI):
                bank = pp[k % 2]
                if tser >= 2:
                    T.wait_ge(s_gen, evac_gen[tser - 2])
                for ch in range(NCH):
                    mi = T.matmul(bank[:, 0:512], bigT[:, ch * OWN + 128 * k:ch * OWN + 128 * (k + 1)],
                                  buf[:, 512 * ch:512 * (ch + 1)],
                                  start=(ch == 0), stop=(ch == NCH - 1), skip_group_check=True)
                    if ch == NCH - 1:
                        m = mm(mi)
                if k == 0 and gi + 1 < NGT:
                    if gi >= 1:
                        SY.wait_ge(s_mm, mm_end[gi - 1])
                    SY.wait_ge(ld, cnt["ld"])
                    v_w[gi + 1] = _issue_wffload(gi + 1)
                V.wait_ge(s_mm, m)
                gen(V.reduce_max(tok4[:, 6:7], bank[:, 0:512], AX.X, apply_absolute_value=True)); vwait()
                gen(V.tensor_max(maxa[:, mcol:mcol + 1], maxa[:, mcol:mcol + 1], tok4[:, 6:7])); vwait()
                if tser >= 2:
                    V.wait_ge(sd_o, st_cnt[tser - 2])
                gen(V.tensor_scalar(fstg[tser % 2][:, 0:512], bank[:, 0:512], scal[:, 0:1], None, ALU.mult))
                evac_gen[tser] = cnt["gen"]
                G.wait_ge(s_gen, cnt["gen"])
                st_cnt[tser] = odma(dst[128 * k:128 * (k + 1), 512 * g:512 * (g + 1)], fstg[tser % 2][:, 0:512])
                tser += 1
            mm_end[gi] = cnt["mm"]
        vwait()
        gen(V.tensor_scalar(maxa[:, 1:2], maxa[:, 1:2], scal[:, 0:1], None, ALU.mult)); vwait()
        gen(V.tensor_scalar(maxa[:, 2:3], maxa[:, 2:3], scal[:, 0:1], None, ALU.mult)); vwait()
        maxcol_to_row(1, 1)
        maxcol_to_row(2, 2)
        G.wait_ge(s_gen, cnt["gen"])
        odma(cc_i[0:1, 0:8], ccrow[0:1, 0:8])
        G.wait_ge(sd_o, cnt["sd_o"])
        cnt["cc"] += 1
        G.collective_compute("AllReduce", ALU.max, replica_groups=RG,
                             ins=[cc_i[:, :]], outs=[cc_o[:, :]]).then_inc(s_cc, 1)
        G.wait_ge(s_cc, cnt["cc"])
        SY.wait_ge(s_cc, cnt["cc"])
        v = ldma(ccrow2[0:1, 0:8], cc_o[0:1, 0:8])
        V.wait_ge(ld, v)
        gen(V.tensor_scalar(ccrow2[0:1, 3:4], ccrow2[0:1, 1:2], 1.0 / 32767.0, 1e-12, ALU.mult, ALU.max)); vwait()
        gen(V.tensor_scalar(ccrow2[0:1, 4:5], ccrow2[0:1, 2:3], 1.0 / 32767.0, 1e-12, ALU.mult, ALU.max)); vwait()
        gen(V.reciprocal(ccrow2[0:1, 5:6], ccrow2[0:1, 3:4])); vwait()
        gen(V.reciprocal(ccrow2[0:1, 6:7], ccrow2[0:1, 4:5])); vwait()
        bcast_scal(2, ccrow2[0:1, 3:4])
        bcast_scal(3, ccrow2[0:1, 5:6])
        bcast_scal(4, ccrow2[0:1, 4:5])
        bcast_scal(5, ccrow2[0:1, 6:7])
        gen(V.memset(ccrow[:, :], 0.0)); vwait()

        wdTb = bigT
        dwn = bigT.bitcast(F32)
        wd_jobs = [(ic, dch) for ic in range(DIC) for dch in range(NCH)]
        v_wd = 16 * len(wd_jobs)

        wd_cnt = [0]

        def _issue_wd(jobs):
            for (ic, dch) in jobs:
                SY.wait_ge(go[0], wd_cnt[0])
                wd_cnt[0] += 16
                SY.dma_start_transpose(
                    wdTb[:, (ic * NCH + dch) * 128:(ic * NCH + dch) * 128 + 128],
                    ag_d_o[128 * dch:128 * (dch + 1), 128 * ic:128 * (ic + 1)]).then_inc(go[0], 16)
        SY.wait_ge(s_mm, cnt["mm"])
        SY.wait_ge(s_gen, cnt["gen"])
        sgin = [x1t[:, 0:512], x1t[:, 512:1024]]
        ytb = ytile.bitcast(BF16)
        sgout = [ytb[:, 0:512], ytb[:, 512:1024]]
        sg_iters = [(sl, ic) for sl in range(NSLO) for ic in range(DIC)]
        sg_ldv = {}
        sg_head = {}
        sg_st = {}
        SY.wait_ge(s_gen, cnt["gen"])
        SY.wait_ge(ld, cnt["ld"])
        SY.wait_ge(sd_o, cnt["sd_o"])
        sl0, ic0 = sg_iters[0]
        sg_ldv[0] = ldma2("la", sgin[0], s_gt[512 * sl0:512 * (sl0 + 1), 128 * ic0:128 * (ic0 + 1)],
                          transpose=True)
        for t, (sl, ic) in enumerate(sg_iters):
            if t + 1 < len(sg_iters):
                if t >= 1:
                    SY.wait_ge(s_gen, sg_head[t - 1])
                sl2, ic2 = sg_iters[t + 1]
                sg_ldv[t + 1] = ldma2("la" if (t + 1) % 2 == 0 else "lb", sgin[(t + 1) % 2],
                                      s_gt[512 * sl2:512 * (sl2 + 1), 128 * ic2:128 * (ic2 + 1)],
                                      transpose=True)
            V.wait_ge(LSEM["la" if t % 2 == 0 else "lb"], sg_ldv[t])
            gen(V.tensor_scalar(ffta[:, 0:512], sgin[t % 2], scal[:, 5:6], None, ALU.mult))
            sg_head[t] = cnt["gen"]
            vwait()
            wd_per = -(-len(wd_jobs) // len(sg_iters))
            _issue_wd(wd_jobs[t * wd_per:(t + 1) * wd_per])
            gen(V.tensor_copy(i16t[:, 0:512], ffta[:, 0:512]))
            S.wait_ge(s_gen, cnt["gen"])
            a = act(S.activation(fftb[:, 0:512], i16t[:, 0:512], AF.Sigmoid, scale=scal[:, 4:5]))
            V.wait_ge(s_act, a)
            gen(V.tensor_scalar(fftb[:, 0:512], fftb[:, 0:512], 65536.0, None, ALU.mult)); vwait()
            gen(V.tensor_copy(u16t[:, 0:512], fftb[:, 0:512])); vwait()
            gen(V.tensor_mul(fftc[:, 0:512], i16t[:, 0:512], u16t[:, 0:512])); vwait()
            if t >= 2:
                V.wait_ge(sd_o, sg_st[t - 2])
            gen(V.tensor_copy(sgout[t % 2], fftc[:, 0:512])); vwait()
            gen(V.reduce_max(tok4[:, 6:7], fftc[:, 0:512], AX.X, apply_absolute_value=True)); vwait()
            gen(V.tensor_max(maxa[:, 3:4], maxa[:, 3:4], tok4[:, 6:7]))
            G.wait_ge(s_gen, cnt["gen"])
            sg_st[t] = odma(s_gs[128 * ic:128 * (ic + 1), 512 * sl:512 * (sl + 1)], sgout[t % 2])
        vwait()
        gen(V.tensor_scalar(maxa[:, 3:4], maxa[:, 3:4], scal[:, 4:5], 1.0 / 65536.0, ALU.mult, ALU.mult)); vwait()
        maxcol_to_row(3, 3)
        G.wait_ge(s_gen, cnt["gen"])
        odma(cc_i[0:1, 0:8], ccrow[0:1, 0:8])
        G.wait_ge(sd_o, cnt["sd_o"])
        cnt["cc"] += 1
        G.collective_compute("AllReduce", ALU.max, replica_groups=RG,
                             ins=[cc_i[:, :]], outs=[cc_o[:, :]]).then_inc(s_cc, 1)
        G.wait_ge(s_cc, cnt["cc"])
        SY.wait_ge(s_cc, cnt["cc"])
        v = ldma(ccrow2[0:1, 0:8], cc_o[0:1, 0:8])
        V.wait_ge(ld, v)
        gen(V.tensor_scalar(ccrow2[0:1, 3:4], ccrow2[0:1, 3:4], 1.0 / 32767.0, 1e-12, ALU.mult, ALU.max)); vwait()
        gen(V.reciprocal(ccrow2[0:1, 5:6], ccrow2[0:1, 3:4])); vwait()
        gen(V.tensor_scalar(ccrow2[0:1, 6:7], ccrow2[0:1, 5:6], 1.0 / 65536.0, None, ALU.mult)); vwait()
        bcast_scal(6, ccrow2[0:1, 3:4])
        bcast_scal(7, ccrow2[0:1, 6:7])
        vwait()
        gen(V.tensor_mul(scal[:, 8:9], scal[:, 7:8], scal[:, 4:5])); vwait()
        gen(V.memset(ccrow[:, :], 0.0)); vwait()
        gen(V.memset(maxa[:, 0:1], 0.0)); vwait()

        xtb = xtile.bitcast(BF16)
        gsin = [x1t[:, 0:512], x1t[:, 512:1024]]
        upin = [xtb[:, 0:512], xtb[:, 512:1024]]
        opout = [ytb[:, 0:512], ytb[:, 512:1024]]
        op_iters = [(sl, ic) for sl in range(NSLO) for ic in range(DIC)]
        op_ldv = {}
        op_head = {}
        op_st = {}
        SY.wait_ge(s_gen, cnt["gen"])
        SY.wait_ge(ld, cnt["ld"])
        SY.wait_ge(sd_o, cnt["sd_o"])
        sl0, ic0 = op_iters[0]
        op_gv = {}
        op_gv[0] = ldma2("la", gsin[0], s_gs[128 * ic0:128 * (ic0 + 1), 512 * sl0:512 * (sl0 + 1)])
        op_ldv[0] = ldma2("lc", upin[0], s_up[512 * sl0:512 * (sl0 + 1), 128 * ic0:128 * (ic0 + 1)],
                          transpose=True)
        for t, (sl, ic) in enumerate(op_iters):
            if t + 1 < len(op_iters):
                if t >= 1:
                    SY.wait_ge(s_gen, op_head[t - 1])
                sl2, ic2 = op_iters[t + 1]
                ka = "la" if (t + 1) % 2 == 0 else "lb"
                kc = "lc" if (t + 1) % 2 == 0 else "ld2"
                op_gv[t + 1] = ldma2(ka, gsin[(t + 1) % 2],
                                     s_gs[128 * ic2:128 * (ic2 + 1), 512 * sl2:512 * (sl2 + 1)])
                op_ldv[t + 1] = ldma2(kc, upin[(t + 1) % 2],
                                      s_up[512 * sl2:512 * (sl2 + 1), 128 * ic2:128 * (ic2 + 1)],
                                      transpose=True)
            V.wait_ge(LSEM["la" if t % 2 == 0 else "lb"], op_gv[t])
            V.wait_ge(LSEM["lc" if t % 2 == 0 else "ld2"], op_ldv[t])
            gen(V.tensor_scalar(ffta[:, 0:512], gsin[t % 2], scal[:, 8:9], None, ALU.mult)); vwait()
            gen(V.tensor_copy(i16t[:, 0:512], ffta[:, 0:512])); vwait()
            gen(V.tensor_scalar(fftb[:, 0:512], upin[t % 2], scal[:, 3:4], None, ALU.mult))
            op_head[t] = cnt["gen"]
            vwait()
            gen(V.tensor_copy(i16t[:, 512:1024], fftb[:, 0:512])); vwait()
            gen(V.tensor_mul(fftc[:, 0:512], i16t[:, 0:512], i16t[:, 512:1024])); vwait()
            if t >= 2:
                V.wait_ge(sd_o, op_st[t - 2])
            gen(V.tensor_copy(opout[t % 2], fftc[:, 0:512])); vwait()
            gen(V.reduce_max(tok4[:, 6:7], fftc[:, 0:512], AX.X, apply_absolute_value=True)); vwait()
            gen(V.tensor_max(maxa[:, 0:1], maxa[:, 0:1], tok4[:, 6:7]))
            G.wait_ge(s_gen, cnt["gen"])
            op_st[t] = odma(s_op[128 * ic:128 * (ic + 1), 512 * sl:512 * (sl + 1)], opout[t % 2])
        vwait()
        gen(V.tensor_mul(scal[:, 9:10], scal[:, 6:7], scal[:, 2:3])); vwait()
        gen(V.tensor_scalar(maxa[:, 0:1], maxa[:, 0:1], scal[:, 9:10], None, ALU.mult)); vwait()
        maxcol_to_row(0, 4)
        G.wait_ge(s_gen, cnt["gen"])
        odma(cc_i[0:1, 0:8], ccrow[0:1, 0:8])
        G.wait_ge(sd_o, cnt["sd_o"])
        cnt["cc"] += 1
        G.collective_compute("AllReduce", ALU.max, replica_groups=RG,
                             ins=[cc_i[:, :]], outs=[cc_o[:, :]]).then_inc(s_cc, 1)
        G.wait_ge(s_cc, cnt["cc"])
        SY.wait_ge(s_cc, cnt["cc"])
        v = ldma(ccrow2[0:1, 0:8], cc_o[0:1, 0:8])
        V.wait_ge(ld, v)
        gen(V.tensor_scalar(ccrow2[0:1, 4:5], ccrow2[0:1, 4:5], 1.0 / 32767.0, 1e-12, ALU.mult, ALU.max)); vwait()
        gen(V.reciprocal(ccrow2[0:1, 5:6], ccrow2[0:1, 4:5])); vwait()
        bcast_scal(10, ccrow2[0:1, 4:5])
        bcast_scal(11, ccrow2[0:1, 5:6])
        vwait()
        gen(V.tensor_mul(scal[:, 12:13], scal[:, 9:10], scal[:, 11:12])); vwait()

        outtb = [ytile, xtile2]
        x2b = [x2t, xtile]
        din = [fft, prodo]
        T.wait_ge(go[0], v_wd)
        m_prev_sl = None
        out_st = {}
        x2_ldv = {}
        last_cp = {0: None, 1: None}
        dn0_sdo = cnt["sd_o"]
        for sl in range(NSLO):
            dn_ldv = {}
            dn_head = {}
            SY.wait_ge(s_gen, cnt["gen"])
            SY.wait_ge(ld, cnt["ld"])
            SY.wait_ge(sd_o, cnt["sd_o"])
            dn_ldv[0] = ldma2("la", din[0][:, 0:512], s_op[0:128, 512 * sl:512 * (sl + 1)])
            for ic in range(DIC):
                if ic + 1 < DIC:
                    if ic >= 1:
                        SY.wait_ge(s_gen, dn_head[ic - 1])
                    dn_ldv[ic + 1] = ldma2("la" if (ic + 1) % 2 == 0 else "lb",
                                           din[(ic + 1) % 2][:, 0:512],
                                           s_op[128 * (ic + 1):128 * (ic + 2), 512 * sl:512 * (sl + 1)])
                V.wait_ge(LSEM["la" if ic % 2 == 0 else "lb"], dn_ldv[ic])
                if ic == 0 and m_prev_sl is not None:
                    V.wait_ge(s_mm, m_prev_sl)
                gen(V.tensor_scalar(ffta[:, 0:512], din[ic % 2][:, 0:512], scal[:, 12:13], None, ALU.mult))
                dn_head[ic] = cnt["gen"]
                vwait()
                gen(V.tensor_copy(i16t[:, 0:512], ffta[:, 0:512])); vwait()
                gen(V.tensor_copy(oslc[:, 512 * ic:512 * (ic + 1)], i16t[:, 0:512])); vwait()
            conv_end = cnt["gen"]
            T.wait_ge(s_gen, conv_end)
            ev_gen = {}
            for dch in range(NCH):
                bank = pp[dch % 2]
                if dch >= 2:
                    T.wait_ge(s_gen, ev_gen[dch - 2])
                for ic in range(DIC):
                    mi = T.matmul(bank[:, 0:512], wdTb[:, (ic * NCH + dch) * 128:(ic * NCH + dch) * 128 + 128],
                                  oslc[:, 512 * ic:512 * (ic + 1)],
                                  start=(ic == 0), stop=(ic == DIC - 1), skip_group_check=True)
                    if ic == DIC - 1:
                        m = mm(mi)
                V.wait_ge(s_mm, m)
                gen(V.tensor_scalar(dwn[:, DIC * DM // 2 + 512 * dch:DIC * DM // 2 + 512 * (dch + 1)], bank[:, 0:512],
                                    scal[:, 10:11], None, ALU.mult))
                ev_gen[dch] = cnt["gen"]
                vwait()
            m_prev_sl = m
            T.wait_ge(s_gen, ev_gen[NCH - 1])
            tp_dst = [pq, pp[2]]
            for sub in range(4):
                sser = sl * 4 + sub
                trow = 512 * sl + 128 * sub
                v1 = ldma(x2b[sser % 2][:, :], s_x2[trow:trow + 128, :])
                x2_ldv[sser] = v1
                ot = outtb[sser % 2]
                for dch in range(NCH):
                    p = dch % 2
                    dst = tp_dst[p]
                    if last_cp[p] is not None:
                        T.wait_ge(s_gen, last_cp[p])
                    m2 = mm(T.matmul(dst[:, 0:128], dwn[:, DIC * DM // 2 + 512 * dch + 128 * sub:DIC * DM // 2 + 512 * dch + 128 * (sub + 1)],
                                     identf[:, :], start=True, stop=True, is_transpose=True))
                    V.wait_ge(s_mm, m2)
                    if dch == 0:
                        V.wait_ge(sd_o, out_st[sser - 2] if sser >= 2 else dn0_sdo)
                    gen(V.tensor_copy(ot[:, 128 * dch:128 * (dch + 1)], dst[:, 0:128]))
                    last_cp[p] = cnt["gen"]
                    vwait()
                gen(V.bn_stats(stat6[:, 0:6], ot[:, 0:512])); vwait()
                gen(V.bn_stats(stat6[:, 6:12], ot[:, 512:1024])); vwait()
                gen(V.bn_aggr(stat2[:, 0:2], stat6[:, 0:12])); vwait()
                gen(V.tensor_scalar(tok4[:, 2:3], stat2[:, 1:2], EPS, None, ALU.add))
                S.wait_ge(s_gen, cnt["gen"])
                a = act(S.activation(tok4[:, 3:4], tok4[:, 2:3], AF.Sqrt))
                V.wait_ge(s_act, a)
                gen(V.reciprocal(tok4[:, 1:2], tok4[:, 3:4])); vwait()
                gen(V.tensor_scalar(ot[:, :], ot[:, :], stat2[:, 0:1], tok4[:, 1:2],
                                    ALU.subtract, ALU.mult)); vwait()
                V.wait_ge(ld, x2_ldv[sser])
                gen(V.tensor_add(ot[:, :], ot[:, :], x2b[sser % 2][:, :])); vwait()
                G.wait_ge(s_gen, cnt["gen"])
                out_st[sser] = odma(out[trow:trow + 128, :], ot[:, :])
        G.wait_ge(sd_o, cnt["sd_o"])
    return nc


def _stage(inputs, q, b, SEQ, DI):
    x = np.ascontiguousarray(inputs["L0_attn_x"][b]).astype(np.float32)
    OWN = SEQ // 4
    whh = inputs["L0_attn_W_hh"]
    wih = inputs["L0_attn_W_ih"]
    wall = np.zeros((DM, 512), np.float32)
    wall[:, 0:128] = wih[256:384].T * 0 if False else wih[0:128].T
    wall[:, 128:256] = wih[128:256].T
    wall[:, 256:384] = wih[256:384].T
    wall[:, 384:448] = inputs["L0_attn_Bw"].T
    wall[:, 448:450] = inputs["L0_attn_gate_w"].T
    bias_g = np.zeros((128, 4), np.float32)
    bih = inputs["L0_attn_b_ih"]
    bias_g[:, 0] = bih[0:128]
    bias_g[:, 1] = bih[128:256]
    bias_g[:, 2] = bih[256:384]
    bias_g[0:SSM, 3] = inputs["L0_attn_Bb"]
    bias_g[SSM:SSM + 2, 3] = inputs["L0_attn_gate_b"]
    bhh_g = np.ascontiguousarray(inputs["L0_attn_b_hh"].reshape(3, 128).T)
    d = {
        "xb": x,
        "xq": np.ascontiguousarray(x[q * OWN:(q + 1) * OWN]),
        "role": np.array([[q, b, 0, 0, 0, 0, 0, 0]], np.int32),
        "whhT": np.ascontiguousarray(inputs["L0_attn_W_hh"].T),
        "aT": np.ascontiguousarray(inputs["L0_attn_A"].T),
        "wall": wall,
        "bias_g": bias_g,
        "bhh_g": bhh_g,
        "ident": np.eye(128, dtype=np.float32),
        "cwT": np.ascontiguousarray(inputs["L0_attn_Cw"].T),
        "projT": np.ascontiguousarray(inputs["L0_attn_proj_w"].T),
        "cb_row": inputs["L0_attn_Cb"].reshape(1, DM).astype(np.float32),
        "db_row": inputs["L0_attn_Db"].reshape(1, DM).astype(np.float32),
        "pb_row": inputs["L0_attn_proj_b"].reshape(1, DM).astype(np.float32),
        "dwT": np.ascontiguousarray(inputs["L0_attn_Dw"].T),
        "wg_s": np.ascontiguousarray(inputs["L0_ffn_w_gate"][(q + 4 * b) * (DI // 8):(q + 4 * b + 1) * (DI // 8)]),
        "wu_s": np.ascontiguousarray(inputs["L0_ffn_w_up"][(q + 4 * b) * (DI // 8):(q + 4 * b + 1) * (DI // 8)]),
        "wd_s": np.ascontiguousarray(inputs["L0_ffn_w_down"][(q + 4 * b) * (DM // 8):(q + 4 * b + 1) * (DM // 8)]),
    }
    return {k: np.asarray(vv) for k, vv in d.items()}


def kernel(**inputs):
    SEQ = int(inputs["L0_attn_x"].shape[1])
    DI = int(inputs["L0_ffn_w_up"].shape[0])
    nc = build_program(SEQ, DI)
    in_maps = []
    for core in range(8):
        b, q = core // 4, core % 4
        in_maps.append(_stage(inputs, q, b, SEQ, DI))
    res = run_bass_kernel_spmd(nc, in_maps, core_ids=list(range(8)))
    OWN = SEQ // 4
    outp = np.zeros((2, SEQ, DM), np.float32)
    for core in range(8):
        b, q = core // 4, core % 4
        outp[b, q * OWN:(q + 1) * OWN] = res.results[core]["out"]
    return outp


# version 09
# speedup vs baseline: 1.5287x; 1.0449x over previous
import numpy as np
import contextlib
import concourse.bass as bass
import concourse.mybir as mybir
from concourse.bass_utils import run_bass_kernel_spmd

F32 = mybir.dt.float32
BF16 = mybir.dt.bfloat16
I16 = mybir.dt.int16
U16 = mybir.dt.uint16
I8 = mybir.dt.int8
AF = mybir.ActivationFunctionType
AX = mybir.AxisListType
ALU = mybir.AluOpType

DM = 1024
NCH = DM // 128
SSM = 64
RNN = 128
EPS = 1e-5
ENGS = ["sync", "tensor", "scalar", "vector", "gpsimd"]


def build_program(SEQ, DI, n_cores=8, debug=False):
    HALF = 256
    NHALF = SEQ // HALF
    NBODY = NHALF // 2
    OWN = SEQ // 4
    NTI = OWN // 128
    NSL_ALL = SEQ // 512
    DIC = DI // 128
    DISH = DI // 8
    NG = DI // 512

    nc = bass.Bass("TRN2", target_bir_lowering=False)

    ap_in = lambda n, s, d=F32: nc.dram_tensor(n, s, d, kind="ExternalInput").ap()
    xb = ap_in("xb", [SEQ, DM])
    xq = ap_in("xq", [OWN, DM])
    role = ap_in("role", [1, 8], mybir.dt.int32)
    whhT = ap_in("whhT", [RNN, 384])
    aT = ap_in("aT", [SSM, SSM])
    wall = ap_in("wall", [DM, 512])
    bias_g = ap_in("bias_g", [128, 4])
    bhh_g = ap_in("bhh_g", [128, 3])
    ident = ap_in("ident", [128, 128])
    cwT = ap_in("cwT", [SSM, DM])
    projT = ap_in("projT", [RNN, DM])
    cb_row = ap_in("cb_row", [1, DM])
    db_row = ap_in("db_row", [1, DM])
    pb_row = ap_in("pb_row", [1, DM])
    dwT = ap_in("dwT", [DM, DM])
    wg_s = ap_in("wg_s", [DISH, DM])
    wu_s = ap_in("wu_s", [DISH, DM])
    wd_s = ap_in("wd_s", [DM // 8, DI])
    out = nc.dram_tensor("out", [OWN, DM], F32, kind="ExternalOutput").ap()

    DBG = {"s_x2", "s_up", "s_gt", "s_gs", "s_op"}
    dr = lambda n, s, d: (nc.dram_tensor(n, s, d, kind="ExternalOutput") if (debug and n in DBG)
                          else nc.dram_tensor(n, s, d)).ap()
    s_x1tm = dr("s_x1tm", [SEQ, DM], BF16)
    s_x1fm = dr("s_x1fm", [DM, SEQ], BF16)
    s_ir = dr("s_ir", [RNN, SEQ + 128], BF16)
    s_iz = dr("s_iz", [RNN, SEQ + 128], BF16)
    s_bx = dr("s_bx", [SSM, SEQ + 128], BF16)
    s_in = dr("s_in", [RNN, SEQ + 256], F32)
    s_gx = dr("s_gx", [2, SEQ], F32)
    s_hg = dr("s_hg", [RNN, SEQ], BF16)
    s_hs = dr("s_hs", [SSM, SEQ], BF16)
    s_x2 = dr("s_x2", [OWN, DM], F32)
    s_l3 = dr("s_l3", [OWN, 2], F32)
    s_up = dr("s_up", [OWN, DI], BF16)
    s_gt = dr("s_gt", [OWN, DI], BF16)
    s_gs = dr("s_gs", [DI, OWN], BF16)
    s_op = dr("s_op", [DI, OWN], BF16)
    s_dum = dr("s_dum", [RNN, 4 * 256], BF16)
    cc_i = dr("cc_i", [1, 8], F32)
    cc_o = dr("cc_o", [1, 8], F32)
    ag_g_i = dr("ag_g_i", [DISH, DM], BF16)
    ag_g_o = dr("ag_g_o", [DI, DM], BF16)
    ag_u_i = dr("ag_u_i", [DISH, DM], BF16)
    ag_u_o = dr("ag_u_o", [DI, DM], BF16)
    ag_d_i = dr("ag_d_i", [DM // 8, DI], BF16)
    ag_d_o = dr("ag_d_o", [DM, DI], BF16)
    RG = [list(range(n_cores))]

    es = contextlib.ExitStack()
    with es:
        _all_sb = []

        def sb(n, s, d):
            t = es.enter_context(nc.sbuf_tensor(n, s, d))
            _all_sb.append(t)
            return t
        sem = lambda n: es.enter_context(nc.semaphore(n))

        wscan = sb("wscan", [128, 448], BF16)
        i128 = sb("i128", [128, 128], BF16)
        tp_id = sb("tp_id", [128, 128], BF16)
        wallb = sb("wallb", [128, NCH * 512], BF16)
        cwTb = sb("cwTb", [SSM, DM], BF16)
        projTb = sb("projTb", [RNN, DM], BF16)
        biasg = sb("biasg", [128, 4], F32)
        bhhn_blk = sb("bhhn_blk", [128, 128], BF16)
        cbdb_b = sb("cbdb_b", [128, DM], F32)
        projb_b = sb("projb_b", [128, DM], F32)
        ones1 = sb("ones1", [1, DM], BF16)
        scal = sb("scal", [128, 16], F32)
        rolev = sb("rolev", [1, 8], mybir.dt.int32)
        pre_t = [[sb(f"pre{p}{s}", [128, 512], BF16) for s in range(2)] for p in range(2)]
        in_t = [sb(f"int{s}", [RNN, HALF], F32) for s in range(2)]
        hgw = sb("hgw", [RNN, 2 * HALF], BF16)
        hsw = sb("hsw", [SSM, 2 * HALF], BF16)
        st3 = sb("st3", [128, 6], F32)
        ncol = sb("ncol", [128, 2], F32)
        csc = sb("csc", [128, 2], BF16)
        xtile = sb("xtile", [128, DM], F32)
        xtile2 = sb("xtile2", [128, DM], F32)
        stat6 = sb("stat6", [128, 64], F32)
        stat2 = sb("stat2", [128, 2], F32)
        tok4 = sb("tok4", [128, 8], F32)
        x1t = sb("x1t", [128, DM], BF16)
        xfm = [sb(f"xfm{s}", [128, NCH * 512], BF16) for s in range(2)]
        prodo = sb("prodo", [128, 512], BF16)
        prodf = sb("prodf", [128, 512], F32)
        hgt = sb("hgt", [RNN, 128], BF16)
        hst = sb("hst", [SSM, 128], BF16)
        hgO = sb("hgO", [RNN, SEQ // 4], BF16)
        hsO = sb("hsO", [SSM, SEQ // 4], BF16)
        gxO = sb("gxO", [2, SEQ // 4], F32)
        g2t = sb("g2t", [128, 2], F32)
        ytile = sb("ytile", [128, DM], F32)
        x2t = sb("x2t", [128, DM], F32)
        maxa = sb("maxa", [128, 4], F32)
        BIGW = max(NCH * (SEQ // 4), DIC * DM + NCH * DM)
        bigT = sb("bigT", [128, BIGW], BF16)
        wff = sb("wff", [128, NCH * 512], BF16)
        identf = sb("identf", [128, 128], F32)
        onesf1 = sb("onesf1", [1, 128], F32)
        onesc = sb("onesc", [128, 1], F32)
        ccrow = sb("ccrow", [1, 8], F32)
        ccrow2 = sb("ccrow2", [1, 8], F32)
        projrow = sb("projrow", [1, DM], BF16)
        oslc = sb("oslc", [128, max(DIC * 512, NCH * DM)], BF16)
        dwTb = oslc
        fft = sb("fft", [128, 512], BF16)
        ffta = sb("ffta", [128, 512], F32)
        fftb = sb("fftb", [128, 512], F32)
        fftc = sb("fftc", [128, 512], F32)
        i16t = prodf.bitcast(I16)
        u16t = prodo.bitcast(U16)
        pb = [es.enter_context(nc.psum_tensor(f"pb{p}", [128, 512], F32)) for p in range(2)]
        pp = [es.enter_context(nc.psum_tensor(f"pp{p}", [128, 512], F32)) for p in range(3)]
        pq = es.enter_context(nc.psum_tensor("pq", [128, 512], F32))
        ptr = es.enter_context(nc.psum_tensor("ptr", [128, 128], BF16))

        ld = sem("ld"); sd_pre = sem("sd_pre"); sd_h = sem("sd_h"); sd_o = sem("sd_o")
        ld_a = sem("ld_a"); ld_b = sem("ld_b"); ld_c = sem("ld_c"); ld_d = sem("ld_d")
        s_mm = sem("s_mm"); s_act = sem("s_act"); s_dve = sem("s_dve")
        s_mm2 = sem("s_mm2"); s_act2 = sem("s_act2"); s_premm = sem("s_premm")
        s_sg = sem("s_sg"); s_c1 = sem("s_c1"); s_cc = sem("s_cc"); s_gen = sem("s_gen")
        bar = [sem(f"bar{p}") for p in range(2)]
        go = [sem(f"go{p}") for p in range(2)]
        block = es.enter_context(nc.Block())

        T, V, S, G, SY = nc.tensor, nc.vector, nc.scalar, nc.gpsimd, nc.sync
        cnt = {"ld": 0, "sd_o": 0, "gen": 0, "mm": 0, "act": 0, "cc": 0,
               "la": 0, "lb": 0, "lc": 0, "ld2": 0}
        LSEM = {"la": ld_a, "lb": ld_b, "lc": ld_c, "ld2": ld_d}

        def ldma2(key, out_ap, in_ap, transpose=False, eng=SY):
            eng.wait_ge(LSEM[key], cnt[key])
            cnt[key] += 16
            if transpose:
                eng.dma_start_transpose(out_ap, in_ap).then_inc(LSEM[key], 16)
            else:
                eng.dma_start(out=out_ap, in_=in_ap).then_inc(LSEM[key], 16)
            return cnt[key]

        def ldma(out_ap, in_ap, eng=SY, transpose=False, guard=True):
            if guard:
                eng.wait_ge(s_gen, cnt["gen"])
                eng.wait_ge(ld, cnt["ld"])
                eng.wait_ge(sd_o, cnt["sd_o"])
            cnt["ld"] += 16
            if transpose:
                eng.dma_start_transpose(out_ap, in_ap).then_inc(ld, 16)
            else:
                eng.dma_start(out=out_ap, in_=in_ap).then_inc(ld, 16)
            return cnt["ld"]

        def odma(out_ap, in_ap, eng=G):
            eng.wait_ge(sd_o, cnt["sd_o"])
            cnt["sd_o"] += 16
            eng.dma_start(out=out_ap, in_=in_ap).then_inc(sd_o, 16)
            return cnt["sd_o"]

        def gen(instr):
            cnt["gen"] += 1
            instr.then_inc(s_gen, 1)
            return cnt["gen"]

        def mm(instr):
            cnt["mm"] += 1
            instr.then_inc(s_mm, 1)
            return cnt["mm"]

        def act(instr):
            cnt["act"] += 1
            instr.then_inc(s_act, 1)
            return cnt["act"]

        def vwait(c=None):
            V.wait_ge(s_gen, cnt["gen"] if c is None else c)

        for _t in _all_sb:
            gen(V.memset(_t.ap(), 0)); vwait()
        v = ldma(rolev[:, :], role[:, :])
        v = ldma(xtile[0:RNN, 0:384], whhT[:, :])
        v = ldma(xtile[0:SSM, 384:448], aT[:, :])
        v = ldma(xtile2[:, 0:128], ident[:, :])
        v = ldma(biasg[:, 0:4], bias_g[:, :])
        v = ldma(tok4[:, 0:3], bhh_g[:, :])
        V.wait_ge(ld, v)
        gen(V.memset(wscan[:, :], 0.0)); vwait()
        gen(V.tensor_copy(wscan[0:RNN, 0:384], xtile[0:RNN, 0:384])); vwait()
        gen(V.tensor_copy(wscan[0:SSM, 384:448], xtile[0:SSM, 384:448])); vwait()
        gen(V.tensor_copy(i128[:, :], xtile2[:, 0:128])); vwait()
        gen(V.tensor_copy(tp_id[:, :], xtile2[:, 0:128])); vwait()
        gen(V.tensor_add(biasg[:, 0:2], biasg[:, 0:2], tok4[:, 0:2])); vwait()
        gen(V.memset(bhhn_blk[:, :], 0.0)); vwait()
        gen(V.tensor_scalar(bhhn_blk[:, :], bhhn_blk[:, :], tok4[:, 2:3], None, ALU.add)); vwait()
        gen(V.memset(ones1[:, :], 1.0)); vwait()
        gen(V.memset(hgw[:, :], 0.0)); vwait()
        gen(V.memset(hsw[:, :], 0.0)); vwait()
        gen(V.memset(onesc[:, :], 1.0)); vwait()
        gen(V.memset(maxa[:, :], 0.0)); vwait()
        gen(V.memset(scal[:, :], 0.0)); vwait()
        for ch in range(NCH):
            v = ldma(xtile[:, 0:512], wall.rearrange("(c p) s -> c p s", p=128)[ch])
            V.wait_ge(ld, v); vwait()
            gen(V.tensor_copy(wallb[:, 512 * ch:512 * (ch + 1)], xtile[:, 0:512]))
        for ch in range(NCH):
            v = ldma(xtile[:, 0:DM], dwT.rearrange("(c p) s -> c p s", p=128)[ch])
            V.wait_ge(ld, v); vwait()
            gen(V.tensor_copy(dwTb[:, DM * ch:DM * (ch + 1)], xtile[:, 0:DM]))
        v = ldma(xtile[0:SSM, 0:DM], cwT[:, :])
        V.wait_ge(ld, v); vwait()
        gen(V.tensor_copy(cwTb[:, :], xtile[0:SSM, 0:DM]))
        v = ldma(xtile[0:RNN, 0:DM], projT[:, :])
        V.wait_ge(ld, v); vwait()
        gen(V.tensor_copy(projTb[:, :], xtile[0:RNN, 0:DM]))
        v = ldma(xtile[0:1, 0:DM], cb_row[:, :])
        v = ldma(xtile2[0:1, 0:DM], db_row[:, :])
        v = ldma(ytile[0:1, 0:DM], pb_row[:, :])
        V.wait_ge(ld, v); vwait()
        gen(V.tensor_add(xtile[0:1, 0:DM], xtile[0:1, 0:DM], xtile2[0:1, 0:DM])); vwait()
        gen(V.tensor_copy(x1t[0:1, 0:DM], xtile[0:1, 0:DM])); vwait()
        gen(V.tensor_copy(projrow[0:1, 0:DM], ytile[0:1, 0:DM]))
        T.wait_ge(s_gen, cnt["gen"])
        for hh in range(DM // 512):
            m = mm(T.matmul(pp[2][:, 0:512], ones1[0:1, 0:128], x1t[0:1, 512 * hh:512 * (hh + 1)],
                            start=True, stop=True))
            V.wait_ge(s_mm, m)
            gen(V.tensor_copy(cbdb_b[:, 512 * hh:512 * (hh + 1)], pp[2][:, 0:512]))
            T.wait_ge(s_gen, cnt["gen"])
            m = mm(T.matmul(pp[2][:, 0:512], ones1[0:1, 0:128], projrow[0:1, 512 * hh:512 * (hh + 1)],
                            start=True, stop=True))
            V.wait_ge(s_mm, m)
            gen(V.tensor_copy(projb_b[:, 512 * hh:512 * (hh + 1)], pp[2][:, 0:512]))
            T.wait_ge(s_gen, cnt["gen"])

        def dq_panel(rr, cols):
            V.wait_ge(sd_o, cnt["sd_o"])
            nb = cols // 32
            gen(V.tensor_reduce(stat6[0:rr, 0:nb], xtile[0:rr, 0:cols].rearrange("p (b k) -> p b k", k=32),
                                AX.X, ALU.max, apply_absolute_value=True)); vwait()
            gen(V.tensor_scalar(stat6[0:rr, 0:nb], stat6[0:rr, 0:nb], 1.0 / 7.0, 1e-12, ALU.mult, ALU.max)); vwait()
            gen(V.reciprocal(stat6[0:rr, nb:2 * nb], stat6[0:rr, 0:nb]))
            for b in range(nb):
                vwait()
                gen(V.tensor_scalar(xtile2[0:rr, 32 * b:32 * (b + 1)], xtile[0:rr, 32 * b:32 * (b + 1)],
                                    stat6[0:rr, nb + b:nb + b + 1], None, ALU.mult))
            vwait()
            gen(V.tensor_copy(i16t[0:rr, 0:cols], xtile2[0:rr, 0:cols])); vwait()
            gen(V.tensor_copy(xtile2[0:rr, 0:cols], i16t[0:rr, 0:cols]))
            for b in range(nb):
                vwait()
                gen(V.tensor_scalar(x1t[0:rr, 32 * b:32 * (b + 1)], xtile2[0:rr, 32 * b:32 * (b + 1)],
                                    stat6[0:rr, b:b + 1], None, ALU.mult))

        def dequant(src_ap, rows, cols, dst_dram):
            ngr = (rows + 127) // 128
            for g in range(ngr):
                r0 = g * 128
                rr = min(128, rows - r0)
                for p0 in range(0, cols, DM):
                    pw = min(DM, cols - p0)
                    v = ldma(xtile[0:rr, 0:pw], src_ap[r0:r0 + rr, p0:p0 + pw])
                    V.wait_ge(ld, v)
                    dq_panel(rr, pw)
                    G.wait_ge(s_gen, cnt["gen"])
                    odma(dst_dram[r0:r0 + rr, p0:p0 + pw], x1t[0:rr, 0:pw])

        dequant(wg_s, DISH, DM, ag_g_i)
        dequant(wu_s, DISH, DM, ag_u_i)
        dequant(wd_s, DM // 8, DI, ag_d_i)
        G.wait_ge(sd_o, cnt["sd_o"])
        for (ai, ao) in ((ag_g_i, ag_g_o), (ag_u_i, ag_u_o), (ag_d_i, ag_d_o)):
            cnt["cc"] += 1
            G.collective_compute("AllGather", ALU.bypass, replica_groups=RG,
                                 ins=[ai[:, :]], outs=[ao[:, :]]).then_inc(s_cc, 1)

        NT_ALL = SEQ // 128
        ytb2 = ytile.bitcast(BF16)
        lnout = [ytb2[:, 0:DM], ytb2[:, DM:2 * DM]]
        lnin = [xtile, xtile2]
        ln_ldv = {}
        ln_head = {}
        ln_st = {}
        SY.wait_ge(s_gen, cnt["gen"])
        SY.wait_ge(ld, cnt["ld"])
        SY.wait_ge(sd_o, cnt["sd_o"])
        ln_ldv[0] = ldma2("la", xtile[:, :], xb[0:128, :])
        for ti in range(NT_ALL):
            xt = lnin[ti % 2]
            if ti + 1 < NT_ALL:
                if ti >= 1:
                    SY.wait_ge(s_gen, ln_head[ti - 1])
                ln_ldv[ti + 1] = ldma2("la" if (ti + 1) % 2 == 0 else "lb",
                                       lnin[(ti + 1) % 2][:, :], xb[128 * (ti + 1):128 * (ti + 2), :])
            V.wait_ge(LSEM["la" if ti % 2 == 0 else "lb"], ln_ldv[ti])
            gen(V.bn_stats(stat6[:, 0:6], xt[:, 0:512])); vwait()
            gen(V.bn_stats(stat6[:, 6:12], xt[:, 512:1024])); vwait()
            gen(V.bn_aggr(stat2[:, 0:2], stat6[:, 0:12])); vwait()
            gen(V.tensor_scalar(tok4[:, 2:3], stat2[:, 1:2], EPS, None, ALU.add))
            S.wait_ge(s_gen, cnt["gen"])
            a = act(S.activation(tok4[:, 3:4], tok4[:, 2:3], AF.Sqrt))
            V.wait_ge(s_act, a)
            gen(V.reciprocal(tok4[:, 1:2], tok4[:, 3:4])); vwait()
            if ti >= 2:
                V.wait_ge(sd_o, ln_st[ti - 2])
            gen(V.tensor_scalar(lnout[ti % 2], xt[:, :], stat2[:, 0:1], tok4[:, 1:2], ALU.subtract, ALU.mult))
            ln_head[ti] = cnt["gen"]
            G.wait_ge(s_gen, cnt["gen"])
            ln_st[ti] = odma(s_x1tm[128 * ti:128 * (ti + 1), :], lnout[ti % 2])
        G.wait_ge(sd_o, cnt["sd_o"])
        SY.wait_ge(sd_o, cnt["sd_o"])

        evac_done = {0: 0, 1: 0}
        buf_guard = {0: 0, 1: 0}
        dump_guard = {0: 0, 1: 0}
        SHF = SEQ // 2
        b0r = [prodo, x1t[:, 0:512]]
        b1r = [fft, x1t[:, 512:1024]]
        f2r = [prodf, ffta]
        f3r = [fftb, fftc]
        bxr = [wff[:, 0:512], wff[:, 512:1024]]
        st_par = {0: 0, 1: 0}
        dump_cnt = {0: 0, 1: 0}
        go1 = [0]
        for sl in range(NSL_ALL):
            par = sl % 2
            buf = xfm[par]
            SY.wait_ge(s_mm, buf_guard[par])
            SY.wait_ge(go[1], go1[0])
            SY.wait_ge(ld, cnt["ld"])
            for ch in range(NCH):
                v = ldma(buf[:, 512 * ch:512 * (ch + 1)],
                         s_x1tm[512 * sl:512 * (sl + 1), 128 * ch:128 * (ch + 1)], transpose=True, guard=False)
            SY.wait_ge(ld, v)
            for ch in range(NCH):
                go1[0] += 16
                SY.dma_start(out=s_x1fm.rearrange("(c p) t -> c p t", p=128)[ch][:, 512 * sl:512 * (sl + 1)],
                             in_=buf[:, 512 * ch:512 * (ch + 1)]).then_inc(go[1], 16)
            dump_cnt[par] = go1[0]
            T.wait_ge(ld, v)
            b0, b1, f2, f3, bx = b0r[par], b1r[par], f2r[par], f3r[par], bxr[par]
            stores_before = st_par[par]
            for g in range(4):
                bank = pp[g % 2]
                T.wait_ge(s_gen, evac_done[g % 2])
                for ch in range(NCH):
                    mi = T.matmul(bank[:, 0:512], wallb[:, 512 * ch + 128 * g:512 * ch + 128 * (g + 1)],
                                  buf[:, 512 * ch:512 * (ch + 1)],
                                  start=(ch == 0), stop=(ch == NCH - 1), skip_group_check=True)
                    if ch == NCH - 1:
                        m = mm(mi)
                V.wait_ge(s_mm, m)
                if g == 0:
                    V.wait_ge(sd_o, stores_before)
                if g < 2:
                    dst = s_ir if g == 0 else s_iz
                    ob = b0 if g == 0 else b1
                    gen(V.tensor_scalar(ob[:, 0:256],
                                        bank[:, 0:512].rearrange("p (t two) -> p two t", two=2)[:, 0, :],
                                        biasg[:, g:g + 1], None, ALU.add)); vwait()
                    gen(V.tensor_scalar(ob[:, 256:512],
                                        bank[:, 0:512].rearrange("p (t two) -> p two t", two=2)[:, 1, :],
                                        biasg[:, g:g + 1], None, ALU.add))
                    evac_done[g % 2] = cnt["gen"]
                    G.wait_ge(s_gen, cnt["gen"])
                    odma(dst[:, 256 * sl:256 * (sl + 1)], ob[:, 0:256])
                    odma(dst[:, SHF + 256 * sl:SHF + 256 * (sl + 1)], ob[:, 256:512])
                elif g == 2:
                    gen(V.tensor_scalar(f2[:, 0:512], bank[:, 0:512], biasg[:, 2:3], None, ALU.add))
                    evac_done[g % 2] = cnt["gen"]
                    G.wait_ge(s_gen, cnt["gen"])
                    odma(s_in[:, 512 * sl:512 * (sl + 1)], f2[:, 0:512])
                else:
                    gen(V.tensor_scalar(f3[:, 0:512], bank[:, 0:512], biasg[:, 3:4], None, ALU.add))
                    ec = cnt["gen"]
                    vwait()
                    gen(V.tensor_copy(bx[0:SSM, 0:256],
                                      f3[0:SSM, 0:512].rearrange("p (t two) -> p two t", two=2)[:, 0, :])); vwait()
                    gen(V.tensor_copy(bx[0:SSM, 256:512],
                                      f3[0:SSM, 0:512].rearrange("p (t two) -> p two t", two=2)[:, 1, :]))
                    S.wait_ge(s_gen, ec)
                    a = act(S.activation(f3[64:66, 0:512], f3[64:66, 0:512], AF.Sigmoid))
                    evac_done[g % 2] = cnt["gen"]
                    G.wait_ge(s_gen, cnt["gen"])
                    G.wait_ge(s_act, a)
                    odma(s_bx[:, 256 * sl:256 * (sl + 1)], bx[0:SSM, 0:256])
                    odma(s_bx[:, SHF + 256 * sl:SHF + 256 * (sl + 1)], bx[0:SSM, 256:512])
                    odma(s_gx[:, 512 * sl:512 * (sl + 1)], f3[64:66, 0:512])
            buf_guard[par] = cnt["mm"]
            st_par[par] = cnt["sd_o"]
        G.wait_ge(sd_o, cnt["sd_o"])
        G.wait_ge(s_cc, cnt["cc"])

        PREDMA = 7 * 16

        def issue_half_streams(eng, g_expr, slot, prior=None):
            if prior is not None:
                eng.wait_ge(sd_pre, prior)
            e0 = g_expr * HALF // 2 if False else None
            base_e = g_expr * 128
            eng.dma_start(out=pre_t[0][slot][0:RNN, 0:128], in_=s_ir[:, bass.ts(g_expr, 128)]).then_inc(sd_pre, 16)
            eng.dma_start(out=pre_t[0][slot][0:RNN, 128:256], in_=s_iz[:, bass.ts(g_expr, 128)]).then_inc(sd_pre, 16)
            eng.dma_start(out=pre_t[0][slot][0:SSM, 256:384], in_=s_bx[:, bass.ts(g_expr, 128)]).then_inc(sd_pre, 16)
            eng.dma_start(out=pre_t[1][slot][0:RNN, 0:128],
                          in_=s_ir[:, SHF:SEQ + 128][:, bass.ts(g_expr, 128)]).then_inc(sd_pre, 16)
            eng.dma_start(out=pre_t[1][slot][0:RNN, 128:256],
                          in_=s_iz[:, SHF:SEQ + 128][:, bass.ts(g_expr, 128)]).then_inc(sd_pre, 16)
            eng.dma_start(out=pre_t[1][slot][0:SSM, 256:384],
                          in_=s_bx[:, SHF:SEQ + 128][:, bass.ts(g_expr, 128)]).then_inc(sd_pre, 16)
            eng.dma_start(out=in_t[slot][:, :], in_=s_in[:, bass.ts(g_expr, HALF)]).then_inc(sd_pre, 16)

        V.wait_ge(s_act, cnt["act"])
        V.wait_ge(sd_o, cnt["sd_o"])
        gen(V.memset(pre_t[0][0][SSM:128, 256:384], 0.0)); vwait()
        gen(V.memset(pre_t[0][1][SSM:128, 256:384], 0.0)); vwait()
        gen(V.memset(pre_t[1][0][SSM:128, 256:384], 0.0)); vwait()
        gen(V.memset(pre_t[1][1][SSM:128, 256:384], 0.0)); vwait()
        gen(V.memset(prodo[:, :], 0.0)); vwait()
        gen(V.memset(prodf[:, :], 0.0)); vwait()
        G.wait_ge(s_gen, cnt["gen"])
        odma(s_ir[:, SEQ:SEQ + 128], prodo[0:RNN, 0:128])
        odma(s_iz[:, SEQ:SEQ + 128], prodo[0:RNN, 0:128])
        odma(s_bx[:, SEQ:SEQ + 128], prodo[0:SSM, 0:128])
        odma(s_in[:, SEQ:SEQ + 256], prodf[0:RNN, 0:256])
        for dk in range(2):
            G.wait_ge(sd_h, 16 * dk)
            G.dma_start(out=s_dum[:, 256 * dk:256 * dk + HALF], in_=hgw[:, 0:HALF]).then_inc(sd_h, 16)
        SY.wait_ge(sd_o, cnt["sd_o"])
        issue_half_streams(SY, 0, 0)
        G.wait_ge(sd_o, cnt["sd_o"])
        for e in ENGS:
            eng = getattr(nc, e)
            eng.drain().then_inc(bar[1], 1)
            eng.wait_ge(bar[1], 5)

        mm0 = cnt["mm"]
        act0 = cnt["act"]
        rT_dve = T.alloc_register("rT_dve")
        rT_pre = T.alloc_register("rT_pre")
        rT_h = T.alloc_register("rT_h")
        rV_act = V.alloc_register("rV_act")
        rV_act2 = V.alloc_register("rV_act2")
        rV_c1 = V.alloc_register("rV_c1")
        rV_h = V.alloc_register("rV_h")
        rS_mm = S.alloc_register("rS_mm")
        rS_sg = S.alloc_register("rS_sg")
        rS_sg2 = S.alloc_register("rS_sg2")
        rS_pre = S.alloc_register("rS_pre")
        rY_premm = SY.alloc_register("rY_premm")
        rY_act = SY.alloc_register("rY_act")
        rY_pre = SY.alloc_register("rY_pre")
        rG_dve = G.alloc_register("rG_dve")
        rG_h = G.alloc_register("rG_h")
        T.reg_mov(rT_dve, 0)
        T.reg_mov(rT_pre, PREDMA)
        T.reg_mov(rT_h, 32)
        V.reg_mov(rV_act, 1)
        V.reg_mov(rV_act2, 2)
        V.reg_mov(rV_c1, 1)
        V.reg_mov(rV_h, 32)
        S.reg_mov(rS_mm, 1)
        S.reg_mov(rS_sg, 2)
        S.reg_mov(rS_sg2, 1)
        S.reg_mov(rS_pre, PREDMA)
        SY.reg_mov(rY_premm, 0)
        SY.reg_mov(rY_act, 0)
        SY.reg_mov(rY_pre, PREDMA)
        G.reg_mov(rG_dve, HALF)
        G.reg_mov(rG_h, 32)
        with nc.Fori(0, NBODY, 1) as ib:
            for X in range(2):
                g_idx = ib * 2 + X
                SY.wait_ge(sd_pre, rY_pre)
                SY.reg_add(rY_pre, rY_pre, PREDMA)
                SY.wait_ge(s_premm, rY_premm)
                SY.reg_add(rY_premm, rY_premm, 2)
                SY.wait_ge(s_act2, rY_act)
                SY.reg_add(rY_act, rY_act, 2 * HALF)
                issue_half_streams(SY, g_idx + 1, 1 - X)
                T.wait_ge(sd_h, rT_h)
                T.reg_add(rT_h, rT_h, 32)
                V.wait_ge(sd_h, rV_h)
                V.reg_add(rV_h, rV_h, 32)
                T.wait_ge(sd_pre, rT_pre)
                T.reg_add(rT_pre, rT_pre, PREDMA)
                S.wait_ge(sd_pre, rS_pre)
                S.reg_add(rS_pre, rS_pre, PREDMA)
                T.matmul(pb[0][:, 0:384], i128[:, :], pre_t[0][X][:, 0:384], start=True, stop=False,
                         skip_group_check=True)
                T.matmul(pb[0][:, 384:512], i128[:, :], bhhn_blk[:, :], start=False, stop=True,
                         skip_group_check=True).then_inc(s_premm, 1)
                T.matmul(pb[1][:, 0:384], i128[:, :], pre_t[1][X][:, 0:384], start=True, stop=False,
                         skip_group_check=True)
                T.matmul(pb[1][:, 384:512], i128[:, :], bhhn_blk[:, :], start=False, stop=True,
                         skip_group_check=True).then_inc(s_premm, 1)
                for t in range(HALF):
                    par = t % 2
                    j = t // 2
                    sslot = t % 2
                    hprev = (X * HALF + t - 1) % (2 * HALF)
                    hcur = X * HALF + t
                    bank = pb[par]
                    T.wait_ge(s_dve, rT_dve)
                    T.matmul(bank[:, j:j + 1], wscan[:, 0:128], hgw[:, hprev:hprev + 1],
                             start=False, stop=False, skip_group_check=True)
                    T.matmul(bank[:, 128 + j:129 + j], wscan[:, 128:256], hgw[:, hprev:hprev + 1],
                             start=False, stop=False, skip_group_check=True).then_inc(s_mm2, 1)
                    T.matmul(bank[0:SSM, 256 + j:257 + j], wscan[0:SSM, 384:448], hsw[:, hprev:hprev + 1],
                             start=False, stop=False, skip_group_check=True)
                    T.matmul(bank[:, 384 + j:385 + j], wscan[:, 256:384], hgw[:, hprev:hprev + 1],
                             start=False, stop=False, skip_group_check=True).then_inc(s_mm2, 1)
                    T.reg_add(rT_dve, rT_dve, 1)
                    S.wait_ge(s_mm2, rS_mm)
                    S.activation(st3[:, 3 * sslot:3 * sslot + 2],
                                 bank[:, 0:512].rearrange("p (b j) -> p j b", b=4)[:, j, 0:2],
                                 AF.Sigmoid).then_inc(s_sg, 1)
                    S.wait_ge(s_mm2, rS_sg)
                    S.wait_ge(s_sg, rS_sg2)
                    S.activation(ncol[:, sslot:sslot + 1], bank[:, 384 + j:385 + j], AF.Tanh,
                                 scale=st3[:, 3 * sslot:3 * sslot + 1],
                                 bias=in_t[X][:, t:t + 1]).then_inc(s_act2, 1)
                    S.activation(st3[0:SSM, 3 * sslot + 2:3 * sslot + 3],
                                 bank[0:SSM, 256 + j:257 + j], AF.Sigmoid).then_inc(s_act2, 1)
                    S.reg_add(rS_mm, rS_mm, 2)
                    S.reg_add(rS_sg, rS_sg, 2)
                    S.reg_add(rS_sg2, rS_sg2, 1)
                    V.wait_ge(s_act2, rV_act)
                    V.scalar_tensor_tensor(csc[:, sslot:sslot + 1], hgw[:, hprev:hprev + 1],
                                           ncol[:, sslot:sslot + 1], st3[:, 3 * sslot + 1:3 * sslot + 2],
                                           ALU.subtract, ALU.mult).then_inc(s_c1, 1)
                    V.wait_ge(s_act2, rV_act2)
                    V.tensor_mul(hsw[:, hcur:hcur + 1], bank[0:SSM, 256 + j:257 + j],
                                 st3[0:SSM, 3 * sslot + 2:3 * sslot + 3])
                    V.wait_ge(s_c1, rV_c1)
                    V.tensor_add(hgw[:, hcur:hcur + 1], csc[:, sslot:sslot + 1],
                                 ncol[:, sslot:sslot + 1]).then_inc(s_dve, 1)
                    V.reg_add(rV_act, rV_act, 2)
                    V.reg_add(rV_act2, rV_act2, 2)
                    V.reg_add(rV_c1, rV_c1, 1)
                G.wait_ge(s_dve, rG_dve)
                G.reg_add(rG_dve, rG_dve, HALF)
                G.wait_ge(sd_h, rG_h)
                G.reg_add(rG_h, rG_h, 32)
                G.dma_start(out=s_hg[:, bass.ts(g_idx, HALF)],
                            in_=hgw[:, X * HALF:(X + 1) * HALF]).then_inc(sd_h, 16)
                G.dma_start(out=s_hs[:, bass.ts(g_idx, HALF)],
                            in_=hsw[:, X * HALF:(X + 1) * HALF]).then_inc(sd_h, 16)

        for e in ENGS:
            eng = getattr(nc, e)
            eng.drain().then_inc(bar[0], 1)
            eng.wait_ge(bar[0], 5)
        for e in ENGS:
            getattr(nc, e).wait_ge(sd_h, 32 + 32 * NHALF)

        NSLO = OWN // 512

        v = ldma(identf[:, :], ident[:, :])
        V.wait_ge(ld, v)
        gen(V.memset(onesf1[:, :], 1.0)); vwait()
        gen(V.memset(maxa[:, :], 0.0)); vwait()
        gen(V.memset(ccrow[:, :], 0.0)); vwait()

        def bcast_scal(col, src_ap):
            T.wait_ge(s_gen, cnt["gen"])
            m = mm(T.matmul(pq[:, 0:1], onesf1[0:1, 0:128], src_ap, start=True, stop=True))
            V.wait_ge(s_mm, m)
            gen(V.tensor_copy(scal[:, col:col + 1], pq[:, 0:1]))

        def maxcol_to_row(col, dst_col):
            vwait()
            V.wait_ge(sd_o, cnt["sd_o"])
            V.wait_ge(s_act, cnt["act"])
            gen(V.tensor_copy(fft[:, 0:1], maxa[:, col:col + 1]))
            T.wait_ge(s_gen, cnt["gen"])
            m = mm(T.transpose(ptr[0:1, 0:128], fft[:, 0:1], tp_id[:, :]))
            V.wait_ge(s_mm, m)
            gen(V.reduce_max(ccrow[0:1, dst_col:dst_col + 1], ptr[0:1, 0:128], AX.X)); vwait()

        qv = nc.sync.cc_rank(RG) % 4

        SY.wait_ge(go[1], go1[0])
        for ch in range(NCH):
            v = ldma(bigT[:, ch * OWN:(ch + 1) * OWN],
                     s_x1fm.rearrange("(c p) t -> c p t", p=128)[ch][:, :][:, bass.ts(qv, OWN)])
        v = ldma(hgO[:, :], s_hg[:, bass.ts(qv, OWN)])
        v = ldma(hsO[:, :], s_hs[:, bass.ts(qv, OWN)])
        v = ldma(gxO[:, :], s_gx[:, bass.ts(qv, OWN)])
        T.wait_ge(ld, v)

        for k in range(NTI):
            V.wait_ge(sd_o, cnt["sd_o"])
            v1 = ldma(xtile[:, :], xq[128 * k:128 * (k + 1), :])
            for half in range(2):
                T.wait_ge(ld, v1)
                for ch in range(NCH):
                    mi = T.matmul(pp[half][:, 0:512], bigT[:, ch * OWN + 128 * k:ch * OWN + 128 * (k + 1)],
                                  dwTb[:, DM * ch + 512 * half:DM * ch + 512 * (half + 1)],
                                  start=(ch == 0), stop=(ch == NCH - 1), skip_group_check=True)
                    if ch == NCH - 1:
                        m = mm(mi)
            for half in range(2):
                m = mm(T.matmul(pb[half][:, 0:512], hsO[0:SSM, 128 * k:128 * (k + 1)],
                                cwTb[0:SSM, 512 * half:512 * (half + 1)],
                                start=True, stop=True, skip_group_check=True))
            m = mm(T.matmul(pq[:, 0:512], hgO[:, 128 * k:128 * (k + 1)], projTb[:, 0:512], start=True, stop=True,
                            skip_group_check=True))
            m = mm(T.matmul(pp[2][:, 0:512], hgO[:, 128 * k:128 * (k + 1)], projTb[:, 512:1024], start=True, stop=True,
                            skip_group_check=True))
            V.wait_ge(ld, v1)
            gen(V.tensor_copy(fft[0:2, 0:128], gxO[0:2, 128 * k:128 * (k + 1)]))
            T.wait_ge(s_gen, cnt["gen"])
            m2 = mm(T.transpose(ptr[:, 0:2], fft[0:2, 0:128], tp_id[0:2, 0:2]))
            V.wait_ge(s_mm, m)
            V.wait_ge(s_mm, m2)
            gen(V.tensor_copy(g2t[:, 0:2], ptr[:, 0:2])); vwait()
            for half in range(2):
                H = slice(512 * half, 512 * (half + 1))
                gen(V.tensor_add(ytile[:, H], pp[half][:, 0:512], cbdb_b[:, H])); vwait()
                gen(V.tensor_add(ytile[:, H], ytile[:, H], pb[half][:, 0:512])); vwait()
                gen(V.tensor_scalar(ytile[:, H], ytile[:, H], g2t[:, 0:1], None, ALU.mult)); vwait()
                rb = pq if half == 0 else pp[2]
                gen(V.tensor_add(x2t[:, H], rb[:, 0:512], projb_b[:, H])); vwait()
                gen(V.tensor_scalar(x2t[:, H], x2t[:, H], g2t[:, 1:2], None, ALU.mult)); vwait()
                gen(V.tensor_add(ytile[:, H], ytile[:, H], x2t[:, H])); vwait()
            gen(V.bn_stats(stat6[:, 0:6], ytile[:, 0:512])); vwait()
            gen(V.bn_stats(stat6[:, 6:12], ytile[:, 512:1024])); vwait()
            gen(V.bn_aggr(stat2[:, 0:2], stat6[:, 0:12])); vwait()
            gen(V.tensor_scalar(tok4[:, 2:3], stat2[:, 1:2], EPS, None, ALU.add))
            S.wait_ge(s_gen, cnt["gen"])
            a = act(S.activation(tok4[:, 3:4], tok4[:, 2:3], AF.Sqrt))
            V.wait_ge(s_act, a)
            gen(V.reciprocal(tok4[:, 1:2], tok4[:, 3:4])); vwait()
            gen(V.tensor_scalar(ytile[:, :], ytile[:, :], stat2[:, 0:1], tok4[:, 1:2],
                                ALU.subtract, ALU.mult)); vwait()
            gen(V.tensor_add(x2t[:, :], ytile[:, :], xtile[:, :])); vwait()
            G.wait_ge(s_gen, cnt["gen"])
            odma(s_x2[128 * k:128 * (k + 1), :], x2t[:, :])
            gen(V.bn_stats(stat6[:, 0:6], x2t[:, 0:512])); vwait()
            gen(V.bn_stats(stat6[:, 6:12], x2t[:, 512:1024])); vwait()
            gen(V.bn_aggr(stat2[:, 0:2], stat6[:, 0:12])); vwait()
            gen(V.tensor_scalar(tok4[:, 2:3], stat2[:, 1:2], EPS, None, ALU.add))
            S.wait_ge(s_gen, cnt["gen"])
            a = act(S.activation(tok4[:, 3:4], tok4[:, 2:3], AF.Sqrt))
            V.wait_ge(s_act, a)
            gen(V.reciprocal(tok4[:, 1:2], tok4[:, 3:4])); vwait()
            gen(V.tensor_copy(tok4[:, 0:1], stat2[:, 0:1])); vwait()
            G.wait_ge(s_gen, cnt["gen"])
            odma(s_l3[128 * k:128 * (k + 1), 0:2], tok4[:, 0:2])
            gen(V.tensor_scalar(ytile[:, :], x2t[:, :], tok4[:, 0:1], None, ALU.subtract)); vwait()
            gen(V.reduce_max(tok4[:, 4:5], ytile[:, :], AX.X, apply_absolute_value=True)); vwait()
            gen(V.tensor_scalar(tok4[:, 5:6], tok4[:, 4:5], tok4[:, 1:2], None, ALU.mult)); vwait()
            gen(V.tensor_max(maxa[:, 0:1], maxa[:, 0:1], tok4[:, 5:6])); vwait()
        maxcol_to_row(0, 0)
        G.wait_ge(s_gen, cnt["gen"])
        odma(cc_i[0:1, 0:8], ccrow[0:1, 0:8])
        G.wait_ge(sd_o, cnt["sd_o"])
        cnt["cc"] += 1
        G.collective_compute("AllReduce", ALU.max, replica_groups=RG,
                             ins=[cc_i[:, :]], outs=[cc_o[:, :]]).then_inc(s_cc, 1)
        G.wait_ge(s_cc, cnt["cc"])
        SY.wait_ge(s_cc, cnt["cc"])
        v = ldma(ccrow2[0:1, 0:8], cc_o[0:1, 0:8])
        V.wait_ge(ld, v)
        gen(V.tensor_scalar(ccrow2[0:1, 1:2], ccrow2[0:1, 0:1], 1.0 / 32767.0, 1e-12, ALU.mult, ALU.max)); vwait()
        gen(V.reciprocal(ccrow2[0:1, 2:3], ccrow2[0:1, 1:2])); vwait()
        bcast_scal(0, ccrow2[0:1, 1:2])
        bcast_scal(1, ccrow2[0:1, 2:3])

        for k in range(NTI):
            V.wait_ge(sd_o, cnt["sd_o"])
            v1 = ldma(x2t[:, :], s_x2[128 * k:128 * (k + 1), :])
            v1 = ldma(tok4[:, 0:2], s_l3[128 * k:128 * (k + 1), 0:2])
            V.wait_ge(ld, v1)
            gen(V.tensor_scalar(tok4[:, 2:3], tok4[:, 1:2], scal[:, 1:2], None, ALU.mult)); vwait()
            gen(V.tensor_scalar(ytile[:, :], x2t[:, :], tok4[:, 0:1], tok4[:, 2:3],
                                ALU.subtract, ALU.mult)); vwait()
            gen(V.tensor_copy(i16t[:, :], ytile[:, :])); vwait()
            gen(V.tensor_copy(x1t[:, :], i16t[:, :])); vwait()
            for ch in range(NCH):
                T.wait_ge(s_gen, cnt["gen"])
                m = mm(T.transpose(ptr[:, :], x1t[:, 128 * ch:128 * (ch + 1)], tp_id[:, :]))
                V.wait_ge(s_mm, m)
                gen(V.tensor_copy(bigT[:, ch * OWN + 128 * k:ch * OWN + 128 * (k + 1)], ptr[:, :]))
        wffb = [wff, xfm[0]]
        groups = [(wsrc, dst, mcol, g)
                  for (wsrc, dst, mcol) in ((ag_u_o, s_up, 1), (ag_g_o, s_gt, 2))
                  for g in range(NG)]
        fstg = [fft, prodo]
        NGT = len(groups)

        def _issue_wffload(gi):
            wsrc, dst, mcol, g = groups[gi]
            buf = wffb[gi % 2]
            for ch in range(NCH):
                vv = ldma(buf[:, 512 * ch:512 * (ch + 1)],
                          wsrc[512 * g:512 * (g + 1), 128 * ch:128 * (ch + 1)], transpose=True, guard=False)
            return vv

        SY.wait_ge(s_mm, cnt["mm"])
        SY.wait_ge(ld, cnt["ld"])
        v_w = [None] * NGT
        v_w[0] = _issue_wffload(0)
        evac_gen = {}
        st_cnt = {}
        mm_end = {}
        ts0 = cnt["gen"]
        T.wait_ge(s_gen, ts0)
        tser = 0
        for gi in range(NGT):
            wsrc, dst, mcol, g = groups[gi]
            buf = wffb[gi % 2]
            T.wait_ge(ld, v_w[gi])
            for k in range(NTI):
                bank = pp[k % 2]
                if tser >= 2:
                    T.wait_ge(s_gen, evac_gen[tser - 2])
                for ch in range(NCH):
                    mi = T.matmul(bank[:, 0:512], bigT[:, ch * OWN + 128 * k:ch * OWN + 128 * (k + 1)],
                                  buf[:, 512 * ch:512 * (ch + 1)],
                                  start=(ch == 0), stop=(ch == NCH - 1), skip_group_check=True)
                    if ch == NCH - 1:
                        m = mm(mi)
                if k == 0 and gi + 1 < NGT:
                    if gi >= 1:
                        SY.wait_ge(s_mm, mm_end[gi - 1])
                    SY.wait_ge(ld, cnt["ld"])
                    v_w[gi + 1] = _issue_wffload(gi + 1)
                V.wait_ge(s_mm, m)
                gen(V.reduce_max(tok4[:, 6:7], bank[:, 0:512], AX.X, apply_absolute_value=True)); vwait()
                gen(V.tensor_max(maxa[:, mcol:mcol + 1], maxa[:, mcol:mcol + 1], tok4[:, 6:7])); vwait()
                if tser >= 2:
                    V.wait_ge(sd_o, st_cnt[tser - 2])
                gen(V.tensor_scalar(fstg[tser % 2][:, 0:512], bank[:, 0:512], scal[:, 0:1], None, ALU.mult))
                evac_gen[tser] = cnt["gen"]
                G.wait_ge(s_gen, cnt["gen"])
                st_cnt[tser] = odma(dst[128 * k:128 * (k + 1), 512 * g:512 * (g + 1)], fstg[tser % 2][:, 0:512])
                tser += 1
            mm_end[gi] = cnt["mm"]
        vwait()
        gen(V.tensor_scalar(maxa[:, 1:2], maxa[:, 1:2], scal[:, 0:1], None, ALU.mult)); vwait()
        gen(V.tensor_scalar(maxa[:, 2:3], maxa[:, 2:3], scal[:, 0:1], None, ALU.mult)); vwait()
        maxcol_to_row(1, 1)
        maxcol_to_row(2, 2)
        G.wait_ge(s_gen, cnt["gen"])
        odma(cc_i[0:1, 0:8], ccrow[0:1, 0:8])
        G.wait_ge(sd_o, cnt["sd_o"])
        cnt["cc"] += 1
        G.collective_compute("AllReduce", ALU.max, replica_groups=RG,
                             ins=[cc_i[:, :]], outs=[cc_o[:, :]]).then_inc(s_cc, 1)
        G.wait_ge(s_cc, cnt["cc"])
        SY.wait_ge(s_cc, cnt["cc"])
        v = ldma(ccrow2[0:1, 0:8], cc_o[0:1, 0:8])
        V.wait_ge(ld, v)
        gen(V.tensor_scalar(ccrow2[0:1, 3:4], ccrow2[0:1, 1:2], 1.0 / 32767.0, 1e-12, ALU.mult, ALU.max)); vwait()
        gen(V.tensor_scalar(ccrow2[0:1, 4:5], ccrow2[0:1, 2:3], 1.0 / 32767.0, 1e-12, ALU.mult, ALU.max)); vwait()
        gen(V.reciprocal(ccrow2[0:1, 5:6], ccrow2[0:1, 3:4])); vwait()
        gen(V.reciprocal(ccrow2[0:1, 6:7], ccrow2[0:1, 4:5])); vwait()
        bcast_scal(2, ccrow2[0:1, 3:4])
        bcast_scal(3, ccrow2[0:1, 5:6])
        bcast_scal(4, ccrow2[0:1, 4:5])
        bcast_scal(5, ccrow2[0:1, 6:7])
        gen(V.memset(ccrow[:, :], 0.0)); vwait()

        wdTb = bigT
        dwn = bigT.bitcast(F32)
        wd_jobs = [(ic, dch) for ic in range(DIC) for dch in range(NCH)]
        v_wd = 16 * len(wd_jobs)

        wd_cnt = [0]

        def _issue_wd(jobs):
            for (ic, dch) in jobs:
                SY.wait_ge(go[0], wd_cnt[0])
                wd_cnt[0] += 16
                SY.dma_start_transpose(
                    wdTb[:, (ic * NCH + dch) * 128:(ic * NCH + dch) * 128 + 128],
                    ag_d_o[128 * dch:128 * (dch + 1), 128 * ic:128 * (ic + 1)]).then_inc(go[0], 16)
        SY.wait_ge(s_mm, cnt["mm"])
        SY.wait_ge(s_gen, cnt["gen"])
        sgin = [x1t[:, 0:512], x1t[:, 512:1024]]
        ytb = ytile.bitcast(BF16)
        sgout = [ytb[:, 0:512], ytb[:, 512:1024]]
        sg_iters = [(sl, ic) for sl in range(NSLO) for ic in range(DIC)]
        sg_ldv = {}
        sg_head = {}
        sg_st = {}
        SY.wait_ge(s_gen, cnt["gen"])
        SY.wait_ge(ld, cnt["ld"])
        SY.wait_ge(sd_o, cnt["sd_o"])
        sl0, ic0 = sg_iters[0]
        sg_ldv[0] = ldma2("la", sgin[0], s_gt[512 * sl0:512 * (sl0 + 1), 128 * ic0:128 * (ic0 + 1)],
                          transpose=True)
        for t, (sl, ic) in enumerate(sg_iters):
            if t + 1 < len(sg_iters):
                if t >= 1:
                    SY.wait_ge(s_gen, sg_head[t - 1])
                sl2, ic2 = sg_iters[t + 1]
                sg_ldv[t + 1] = ldma2("la" if (t + 1) % 2 == 0 else "lb", sgin[(t + 1) % 2],
                                      s_gt[512 * sl2:512 * (sl2 + 1), 128 * ic2:128 * (ic2 + 1)],
                                      transpose=True)
            V.wait_ge(LSEM["la" if t % 2 == 0 else "lb"], sg_ldv[t])
            gen(V.tensor_scalar(ffta[:, 0:512], sgin[t % 2], scal[:, 5:6], None, ALU.mult))
            sg_head[t] = cnt["gen"]
            vwait()
            wd_per = -(-len(wd_jobs) // len(sg_iters))
            _issue_wd(wd_jobs[t * wd_per:(t + 1) * wd_per])
            gen(V.tensor_copy(i16t[:, 0:512], ffta[:, 0:512]))
            S.wait_ge(s_gen, cnt["gen"])
            a = act(S.activation(fftb[:, 0:512], i16t[:, 0:512], AF.Sigmoid, scale=scal[:, 4:5]))
            V.wait_ge(s_act, a)
            gen(V.tensor_scalar(fftb[:, 0:512], fftb[:, 0:512], 65536.0, None, ALU.mult)); vwait()
            gen(V.tensor_copy(u16t[:, 0:512], fftb[:, 0:512])); vwait()
            gen(V.tensor_mul(fftc[:, 0:512], i16t[:, 0:512], u16t[:, 0:512])); vwait()
            if t >= 2:
                V.wait_ge(sd_o, sg_st[t - 2])
            gen(V.tensor_copy(sgout[t % 2], fftc[:, 0:512])); vwait()
            gen(V.reduce_max(tok4[:, 6:7], fftc[:, 0:512], AX.X, apply_absolute_value=True)); vwait()
            gen(V.tensor_max(maxa[:, 3:4], maxa[:, 3:4], tok4[:, 6:7]))
            G.wait_ge(s_gen, cnt["gen"])
            sg_st[t] = odma(s_gs[128 * ic:128 * (ic + 1), 512 * sl:512 * (sl + 1)], sgout[t % 2])
        vwait()
        gen(V.tensor_scalar(maxa[:, 3:4], maxa[:, 3:4], scal[:, 4:5], 1.0 / 65536.0, ALU.mult, ALU.mult)); vwait()
        maxcol_to_row(3, 3)
        G.wait_ge(s_gen, cnt["gen"])
        odma(cc_i[0:1, 0:8], ccrow[0:1, 0:8])
        G.wait_ge(sd_o, cnt["sd_o"])
        cnt["cc"] += 1
        G.collective_compute("AllReduce", ALU.max, replica_groups=RG,
                             ins=[cc_i[:, :]], outs=[cc_o[:, :]]).then_inc(s_cc, 1)
        G.wait_ge(s_cc, cnt["cc"])
        SY.wait_ge(s_cc, cnt["cc"])
        v = ldma(ccrow2[0:1, 0:8], cc_o[0:1, 0:8])
        V.wait_ge(ld, v)
        gen(V.tensor_scalar(ccrow2[0:1, 3:4], ccrow2[0:1, 3:4], 1.0 / 32767.0, 1e-12, ALU.mult, ALU.max)); vwait()
        gen(V.reciprocal(ccrow2[0:1, 5:6], ccrow2[0:1, 3:4])); vwait()
        gen(V.tensor_scalar(ccrow2[0:1, 6:7], ccrow2[0:1, 5:6], 1.0 / 65536.0, None, ALU.mult)); vwait()
        bcast_scal(6, ccrow2[0:1, 3:4])
        bcast_scal(7, ccrow2[0:1, 6:7])
        vwait()
        gen(V.tensor_mul(scal[:, 8:9], scal[:, 7:8], scal[:, 4:5])); vwait()
        gen(V.memset(ccrow[:, :], 0.0)); vwait()
        gen(V.memset(maxa[:, 0:1], 0.0)); vwait()

        xtb = xtile.bitcast(BF16)
        gsin = [x1t[:, 0:512], x1t[:, 512:1024]]
        upin = [xtb[:, 0:512], xtb[:, 512:1024]]
        opout = [ytb[:, 0:512], ytb[:, 512:1024]]
        op_iters = [(sl, ic) for sl in range(NSLO) for ic in range(DIC)]
        op_ldv = {}
        op_head = {}
        op_st = {}
        SY.wait_ge(s_gen, cnt["gen"])
        SY.wait_ge(ld, cnt["ld"])
        SY.wait_ge(sd_o, cnt["sd_o"])
        sl0, ic0 = op_iters[0]
        op_gv = {}
        op_gv[0] = ldma2("la", gsin[0], s_gs[128 * ic0:128 * (ic0 + 1), 512 * sl0:512 * (sl0 + 1)])
        op_ldv[0] = ldma2("lc", upin[0], s_up[512 * sl0:512 * (sl0 + 1), 128 * ic0:128 * (ic0 + 1)],
                          transpose=True)
        for t, (sl, ic) in enumerate(op_iters):
            if t + 1 < len(op_iters):
                if t >= 1:
                    SY.wait_ge(s_gen, op_head[t - 1])
                sl2, ic2 = op_iters[t + 1]
                ka = "la" if (t + 1) % 2 == 0 else "lb"
                kc = "lc" if (t + 1) % 2 == 0 else "ld2"
                op_gv[t + 1] = ldma2(ka, gsin[(t + 1) % 2],
                                     s_gs[128 * ic2:128 * (ic2 + 1), 512 * sl2:512 * (sl2 + 1)])
                op_ldv[t + 1] = ldma2(kc, upin[(t + 1) % 2],
                                      s_up[512 * sl2:512 * (sl2 + 1), 128 * ic2:128 * (ic2 + 1)],
                                      transpose=True)
            V.wait_ge(LSEM["la" if t % 2 == 0 else "lb"], op_gv[t])
            V.wait_ge(LSEM["lc" if t % 2 == 0 else "ld2"], op_ldv[t])
            gen(V.tensor_scalar(ffta[:, 0:512], gsin[t % 2], scal[:, 8:9], None, ALU.mult)); vwait()
            gen(V.tensor_copy(i16t[:, 0:512], ffta[:, 0:512])); vwait()
            gen(V.tensor_scalar(fftb[:, 0:512], upin[t % 2], scal[:, 3:4], None, ALU.mult))
            op_head[t] = cnt["gen"]
            vwait()
            gen(V.tensor_copy(i16t[:, 512:1024], fftb[:, 0:512])); vwait()
            gen(V.tensor_mul(fftc[:, 0:512], i16t[:, 0:512], i16t[:, 512:1024])); vwait()
            if t >= 2:
                V.wait_ge(sd_o, op_st[t - 2])
            gen(V.tensor_copy(opout[t % 2], fftc[:, 0:512])); vwait()
            gen(V.reduce_max(tok4[:, 6:7], fftc[:, 0:512], AX.X, apply_absolute_value=True)); vwait()
            gen(V.tensor_max(maxa[:, 0:1], maxa[:, 0:1], tok4[:, 6:7]))
            G.wait_ge(s_gen, cnt["gen"])
            op_st[t] = odma(s_op[128 * ic:128 * (ic + 1), 512 * sl:512 * (sl + 1)], opout[t % 2])
        vwait()
        gen(V.tensor_mul(scal[:, 9:10], scal[:, 6:7], scal[:, 2:3])); vwait()
        gen(V.tensor_scalar(maxa[:, 0:1], maxa[:, 0:1], scal[:, 9:10], None, ALU.mult)); vwait()
        maxcol_to_row(0, 4)
        G.wait_ge(s_gen, cnt["gen"])
        odma(cc_i[0:1, 0:8], ccrow[0:1, 0:8])
        G.wait_ge(sd_o, cnt["sd_o"])
        cnt["cc"] += 1
        G.collective_compute("AllReduce", ALU.max, replica_groups=RG,
                             ins=[cc_i[:, :]], outs=[cc_o[:, :]]).then_inc(s_cc, 1)
        G.wait_ge(s_cc, cnt["cc"])
        SY.wait_ge(s_cc, cnt["cc"])
        v = ldma(ccrow2[0:1, 0:8], cc_o[0:1, 0:8])
        V.wait_ge(ld, v)
        gen(V.tensor_scalar(ccrow2[0:1, 4:5], ccrow2[0:1, 4:5], 1.0 / 32767.0, 1e-12, ALU.mult, ALU.max)); vwait()
        gen(V.reciprocal(ccrow2[0:1, 5:6], ccrow2[0:1, 4:5])); vwait()
        bcast_scal(10, ccrow2[0:1, 4:5])
        bcast_scal(11, ccrow2[0:1, 5:6])
        vwait()
        gen(V.tensor_mul(scal[:, 12:13], scal[:, 9:10], scal[:, 11:12])); vwait()

        outtb = [ytile, xtile2]
        x2b = [x2t, xtile]
        din = [fft, prodo]
        T.wait_ge(go[0], v_wd)
        m_prev_sl = None
        out_st = {}
        x2_ldv = {}
        last_cp = {0: None, 1: None}
        dn0_sdo = cnt["sd_o"]
        for sl in range(NSLO):
            dn_ldv = {}
            dn_head = {}
            SY.wait_ge(s_gen, cnt["gen"])
            SY.wait_ge(ld, cnt["ld"])
            SY.wait_ge(sd_o, cnt["sd_o"])
            dn_ldv[0] = ldma2("la", din[0][:, 0:512], s_op[0:128, 512 * sl:512 * (sl + 1)])
            for ic in range(DIC):
                if ic + 1 < DIC:
                    if ic >= 1:
                        SY.wait_ge(s_gen, dn_head[ic - 1])
                    dn_ldv[ic + 1] = ldma2("la" if (ic + 1) % 2 == 0 else "lb",
                                           din[(ic + 1) % 2][:, 0:512],
                                           s_op[128 * (ic + 1):128 * (ic + 2), 512 * sl:512 * (sl + 1)])
                V.wait_ge(LSEM["la" if ic % 2 == 0 else "lb"], dn_ldv[ic])
                if ic == 0 and m_prev_sl is not None:
                    V.wait_ge(s_mm, m_prev_sl)
                gen(V.tensor_scalar(ffta[:, 0:512], din[ic % 2][:, 0:512], scal[:, 12:13], None, ALU.mult))
                dn_head[ic] = cnt["gen"]
                vwait()
                gen(V.tensor_copy(i16t[:, 0:512], ffta[:, 0:512])); vwait()
                gen(V.tensor_copy(oslc[:, 512 * ic:512 * (ic + 1)], i16t[:, 0:512])); vwait()
            conv_end = cnt["gen"]
            T.wait_ge(s_gen, conv_end)
            ev_gen = {}
            for dch in range(NCH):
                bank = pp[dch % 2]
                if dch >= 2:
                    T.wait_ge(s_gen, ev_gen[dch - 2])
                for ic in range(DIC):
                    mi = T.matmul(bank[:, 0:512], wdTb[:, (ic * NCH + dch) * 128:(ic * NCH + dch) * 128 + 128],
                                  oslc[:, 512 * ic:512 * (ic + 1)],
                                  start=(ic == 0), stop=(ic == DIC - 1), skip_group_check=True)
                    if ic == DIC - 1:
                        m = mm(mi)
                V.wait_ge(s_mm, m)
                gen(V.tensor_scalar(dwn[:, DIC * DM // 2 + 512 * dch:DIC * DM // 2 + 512 * (dch + 1)], bank[:, 0:512],
                                    scal[:, 10:11], None, ALU.mult))
                ev_gen[dch] = cnt["gen"]
                vwait()
            m_prev_sl = m
            T.wait_ge(s_gen, ev_gen[NCH - 1])
            tp_dst = [pq, pp[2]]
            for sub in range(4):
                sser = sl * 4 + sub
                trow = 512 * sl + 128 * sub
                v1 = ldma(x2b[sser % 2][:, :], s_x2[trow:trow + 128, :])
                x2_ldv[sser] = v1
                ot = outtb[sser % 2]
                for dch in range(NCH):
                    p = dch % 2
                    dst = tp_dst[p]
                    if last_cp[p] is not None:
                        T.wait_ge(s_gen, last_cp[p])
                    m2 = mm(T.matmul(dst[:, 0:128], dwn[:, DIC * DM // 2 + 512 * dch + 128 * sub:DIC * DM // 2 + 512 * dch + 128 * (sub + 1)],
                                     identf[:, :], start=True, stop=True, is_transpose=True))
                    V.wait_ge(s_mm, m2)
                    if dch == 0:
                        V.wait_ge(sd_o, out_st[sser - 2] if sser >= 2 else dn0_sdo)
                    gen(V.tensor_copy(ot[:, 128 * dch:128 * (dch + 1)], dst[:, 0:128]))
                    last_cp[p] = cnt["gen"]
                    vwait()
                gen(V.bn_stats(stat6[:, 0:6], ot[:, 0:512])); vwait()
                gen(V.bn_stats(stat6[:, 6:12], ot[:, 512:1024])); vwait()
                gen(V.bn_aggr(stat2[:, 0:2], stat6[:, 0:12])); vwait()
                gen(V.tensor_scalar(tok4[:, 2:3], stat2[:, 1:2], EPS, None, ALU.add))
                S.wait_ge(s_gen, cnt["gen"])
                a = act(S.activation(tok4[:, 3:4], tok4[:, 2:3], AF.Sqrt))
                V.wait_ge(s_act, a)
                gen(V.reciprocal(tok4[:, 1:2], tok4[:, 3:4])); vwait()
                gen(V.tensor_scalar(ot[:, :], ot[:, :], stat2[:, 0:1], tok4[:, 1:2],
                                    ALU.subtract, ALU.mult)); vwait()
                V.wait_ge(ld, x2_ldv[sser])
                gen(V.tensor_add(ot[:, :], ot[:, :], x2b[sser % 2][:, :])); vwait()
                G.wait_ge(s_gen, cnt["gen"])
                out_st[sser] = odma(out[trow:trow + 128, :], ot[:, :])
        G.wait_ge(sd_o, cnt["sd_o"])
    return nc


def _stage(inputs, q, b, SEQ, DI):
    x = np.ascontiguousarray(inputs["L0_attn_x"][b]).astype(np.float32)
    OWN = SEQ // 4
    whh = inputs["L0_attn_W_hh"]
    wih = inputs["L0_attn_W_ih"]
    wall = np.zeros((DM, 512), np.float32)
    wall[:, 0:128] = wih[256:384].T * 0 if False else wih[0:128].T
    wall[:, 128:256] = wih[128:256].T
    wall[:, 256:384] = wih[256:384].T
    wall[:, 384:448] = inputs["L0_attn_Bw"].T
    wall[:, 448:450] = inputs["L0_attn_gate_w"].T
    bias_g = np.zeros((128, 4), np.float32)
    bih = inputs["L0_attn_b_ih"]
    bias_g[:, 0] = bih[0:128]
    bias_g[:, 1] = bih[128:256]
    bias_g[:, 2] = bih[256:384]
    bias_g[0:SSM, 3] = inputs["L0_attn_Bb"]
    bias_g[SSM:SSM + 2, 3] = inputs["L0_attn_gate_b"]
    bhh_g = np.ascontiguousarray(inputs["L0_attn_b_hh"].reshape(3, 128).T)
    d = {
        "xb": x,
        "xq": np.ascontiguousarray(x[q * OWN:(q + 1) * OWN]),
        "role": np.array([[q, b, 0, 0, 0, 0, 0, 0]], np.int32),
        "whhT": np.ascontiguousarray(inputs["L0_attn_W_hh"].T),
        "aT": np.ascontiguousarray(inputs["L0_attn_A"].T),
        "wall": wall,
        "bias_g": bias_g,
        "bhh_g": bhh_g,
        "ident": np.eye(128, dtype=np.float32),
        "cwT": np.ascontiguousarray(inputs["L0_attn_Cw"].T),
        "projT": np.ascontiguousarray(inputs["L0_attn_proj_w"].T),
        "cb_row": inputs["L0_attn_Cb"].reshape(1, DM).astype(np.float32),
        "db_row": inputs["L0_attn_Db"].reshape(1, DM).astype(np.float32),
        "pb_row": inputs["L0_attn_proj_b"].reshape(1, DM).astype(np.float32),
        "dwT": np.ascontiguousarray(inputs["L0_attn_Dw"].T),
        "wg_s": np.ascontiguousarray(inputs["L0_ffn_w_gate"][(q + 4 * b) * (DI // 8):(q + 4 * b + 1) * (DI // 8)]),
        "wu_s": np.ascontiguousarray(inputs["L0_ffn_w_up"][(q + 4 * b) * (DI // 8):(q + 4 * b + 1) * (DI // 8)]),
        "wd_s": np.ascontiguousarray(inputs["L0_ffn_w_down"][(q + 4 * b) * (DM // 8):(q + 4 * b + 1) * (DM // 8)]),
    }
    return {k: np.asarray(vv) for k, vv in d.items()}


def kernel(**inputs):
    SEQ = int(inputs["L0_attn_x"].shape[1])
    DI = int(inputs["L0_ffn_w_up"].shape[0])
    nc = build_program(SEQ, DI)
    in_maps = []
    for core in range(8):
        b, q = core // 4, core % 4
        in_maps.append(_stage(inputs, q, b, SEQ, DI))
    res = run_bass_kernel_spmd(nc, in_maps, core_ids=list(range(8)))
    OWN = SEQ // 4
    outp = np.zeros((2, SEQ, DM), np.float32)
    for core in range(8):
        b, q = core // 4, core % 4
        outp[b, q * OWN:(q + 1) * OWN] = res.results[core]["out"]
    return outp


# version 10
# speedup vs baseline: 1.5301x; 1.0009x over previous
import numpy as np
import contextlib
import concourse.bass as bass
import concourse.mybir as mybir
from concourse.bass_utils import run_bass_kernel_spmd

F32 = mybir.dt.float32
BF16 = mybir.dt.bfloat16
I16 = mybir.dt.int16
U16 = mybir.dt.uint16
I8 = mybir.dt.int8
AF = mybir.ActivationFunctionType
AX = mybir.AxisListType
ALU = mybir.AluOpType

DM = 1024
NCH = DM // 128
SSM = 64
RNN = 128
EPS = 1e-5
ENGS = ["sync", "tensor", "scalar", "vector", "gpsimd"]


def build_program(SEQ, DI, n_cores=8, debug=False):
    HALF = 256
    NHALF = SEQ // HALF
    NBODY = NHALF // 2
    OWN = SEQ // 4
    NTI = OWN // 128
    NSL_ALL = SEQ // 512
    DIC = DI // 128
    DISH = DI // 8
    NG = DI // 512

    nc = bass.Bass("TRN2", target_bir_lowering=False)

    ap_in = lambda n, s, d=F32: nc.dram_tensor(n, s, d, kind="ExternalInput").ap()
    xb = ap_in("xb", [SEQ, DM])
    xq = ap_in("xq", [OWN, DM])
    role = ap_in("role", [1, 8], mybir.dt.int32)
    whhT = ap_in("whhT", [RNN, 384])
    aT = ap_in("aT", [SSM, SSM])
    wall = ap_in("wall", [DM, 512])
    bias_g = ap_in("bias_g", [128, 4])
    bhh_g = ap_in("bhh_g", [128, 3])
    ident = ap_in("ident", [128, 128])
    cwT = ap_in("cwT", [SSM, DM])
    projT = ap_in("projT", [RNN, DM])
    cb_row = ap_in("cb_row", [1, DM])
    db_row = ap_in("db_row", [1, DM])
    pb_row = ap_in("pb_row", [1, DM])
    dwT = ap_in("dwT", [DM, DM])
    wg_s = ap_in("wg_s", [DISH, DM])
    wu_s = ap_in("wu_s", [DISH, DM])
    wd_s = ap_in("wd_s", [DM // 8, DI])
    out = nc.dram_tensor("out", [OWN, DM], F32, kind="ExternalOutput").ap()

    DBG = {"s_x2", "s_up", "s_gt", "s_gs", "s_op"}
    dr = lambda n, s, d: (nc.dram_tensor(n, s, d, kind="ExternalOutput") if (debug and n in DBG)
                          else nc.dram_tensor(n, s, d)).ap()
    s_x1tm = dr("s_x1tm", [SEQ, DM], BF16)
    s_x1fm = dr("s_x1fm", [DM, SEQ], BF16)
    s_ir = dr("s_ir", [RNN, SEQ + 128], BF16)
    s_iz = dr("s_iz", [RNN, SEQ + 128], BF16)
    s_bx = dr("s_bx", [SSM, SEQ + 128], BF16)
    s_in = dr("s_in", [RNN, SEQ + 256], F32)
    s_gx = dr("s_gx", [2, SEQ], F32)
    s_hg = dr("s_hg", [RNN, SEQ], BF16)
    s_hs = dr("s_hs", [SSM, SEQ], BF16)
    s_x2 = dr("s_x2", [OWN, DM], F32)
    s_l3 = dr("s_l3", [OWN, 2], F32)
    s_up = dr("s_up", [OWN, DI], BF16)
    s_gt = dr("s_gt", [OWN, DI], BF16)
    s_gs = dr("s_gs", [DI, OWN], BF16)
    s_op = dr("s_op", [DI, OWN], BF16)
    s_dum = dr("s_dum", [RNN, 4 * 256], BF16)
    cc_i = dr("cc_i", [1, 8], F32)
    cc_o = dr("cc_o", [1, 8], F32)
    ag_g_i = dr("ag_g_i", [DISH, DM], BF16)
    ag_g_o = dr("ag_g_o", [DI, DM], BF16)
    ag_u_i = dr("ag_u_i", [DISH, DM], BF16)
    ag_u_o = dr("ag_u_o", [DI, DM], BF16)
    ag_d_i = dr("ag_d_i", [DM // 8, DI], BF16)
    ag_d_o = dr("ag_d_o", [DM, DI], BF16)
    RG = [list(range(n_cores))]

    es = contextlib.ExitStack()
    with es:
        _all_sb = []

        def sb(n, s, d):
            t = es.enter_context(nc.sbuf_tensor(n, s, d))
            _all_sb.append(t)
            return t
        sem = lambda n: es.enter_context(nc.semaphore(n))

        wscan = sb("wscan", [128, 448], BF16)
        i128 = sb("i128", [128, 128], BF16)
        tp_id = sb("tp_id", [128, 128], BF16)
        wallb = sb("wallb", [128, NCH * 512], BF16)
        cwTb = sb("cwTb", [SSM, DM], BF16)
        projTb = sb("projTb", [RNN, DM], BF16)
        biasg = sb("biasg", [128, 4], F32)
        bhhn_blk = sb("bhhn_blk", [128, 128], BF16)
        cbdb_b = sb("cbdb_b", [128, DM], F32)
        projb_b = sb("projb_b", [128, DM], F32)
        ones1 = sb("ones1", [1, DM], BF16)
        scal = sb("scal", [128, 16], F32)
        rolev = sb("rolev", [1, 8], mybir.dt.int32)
        pre_t = [[sb(f"pre{p}{s}", [128, 512], BF16) for s in range(2)] for p in range(2)]
        in_t = [sb(f"int{s}", [RNN, HALF], F32) for s in range(2)]
        hgw = sb("hgw", [RNN, 2 * HALF], BF16)
        hsw = sb("hsw", [SSM, 2 * HALF], BF16)
        st3 = sb("st3", [128, 6], F32)
        ncol = sb("ncol", [128, 2], BF16)
        csc = sb("csc", [128, 2], BF16)
        xtile = sb("xtile", [128, DM], F32)
        xtile2 = sb("xtile2", [128, DM], F32)
        stat6 = sb("stat6", [128, 64], F32)
        stat2 = sb("stat2", [128, 2], F32)
        tok4 = sb("tok4", [128, 8], F32)
        x1t = sb("x1t", [128, DM], BF16)
        xfm = [sb(f"xfm{s}", [128, NCH * 512], BF16) for s in range(2)]
        prodo = sb("prodo", [128, 512], BF16)
        prodf = sb("prodf", [128, 512], F32)
        hgt = sb("hgt", [RNN, 128], BF16)
        hst = sb("hst", [SSM, 128], BF16)
        hgO = sb("hgO", [RNN, SEQ // 4], BF16)
        hsO = sb("hsO", [SSM, SEQ // 4], BF16)
        gxO = sb("gxO", [2, SEQ // 4], F32)
        g2t = sb("g2t", [128, 2], F32)
        ytile = sb("ytile", [128, DM], F32)
        x2t = sb("x2t", [128, DM], F32)
        maxa = sb("maxa", [128, 4], F32)
        BIGW = max(NCH * (SEQ // 4), DIC * DM + NCH * DM)
        bigT = sb("bigT", [128, BIGW], BF16)
        wff = sb("wff", [128, NCH * 512], BF16)
        identf = sb("identf", [128, 128], F32)
        onesf1 = sb("onesf1", [1, 128], F32)
        onesc = sb("onesc", [128, 1], F32)
        ccrow = sb("ccrow", [1, 8], F32)
        ccrow2 = sb("ccrow2", [1, 8], F32)
        projrow = sb("projrow", [1, DM], BF16)
        oslc = sb("oslc", [128, max(DIC * 512, NCH * DM)], BF16)
        dwTb = oslc
        fft = sb("fft", [128, 512], BF16)
        ffta = sb("ffta", [128, 512], F32)
        fftb = sb("fftb", [128, 512], F32)
        fftc = sb("fftc", [128, 512], F32)
        i16t = prodf.bitcast(I16)
        u16t = prodo.bitcast(U16)
        pb = [es.enter_context(nc.psum_tensor(f"pb{p}", [128, 512], F32)) for p in range(2)]
        pp = [es.enter_context(nc.psum_tensor(f"pp{p}", [128, 512], F32)) for p in range(3)]
        pq = es.enter_context(nc.psum_tensor("pq", [128, 512], F32))
        ptr = es.enter_context(nc.psum_tensor("ptr", [128, 128], BF16))

        ld = sem("ld"); sd_pre = sem("sd_pre"); sd_h = sem("sd_h"); sd_o = sem("sd_o")
        ld_a = sem("ld_a"); ld_b = sem("ld_b"); ld_c = sem("ld_c"); ld_d = sem("ld_d")
        s_mm = sem("s_mm"); s_act = sem("s_act"); s_dve = sem("s_dve")
        s_mm2 = sem("s_mm2"); s_act2 = sem("s_act2"); s_premm = sem("s_premm")
        s_sg = sem("s_sg"); s_c1 = sem("s_c1"); s_cc = sem("s_cc"); s_gen = sem("s_gen")
        bar = [sem(f"bar{p}") for p in range(2)]
        go = [sem(f"go{p}") for p in range(2)]
        block = es.enter_context(nc.Block())

        T, V, S, G, SY = nc.tensor, nc.vector, nc.scalar, nc.gpsimd, nc.sync
        cnt = {"ld": 0, "sd_o": 0, "gen": 0, "mm": 0, "act": 0, "cc": 0,
               "la": 0, "lb": 0, "lc": 0, "ld2": 0}
        LSEM = {"la": ld_a, "lb": ld_b, "lc": ld_c, "ld2": ld_d}

        def ldma2(key, out_ap, in_ap, transpose=False, eng=SY):
            eng.wait_ge(LSEM[key], cnt[key])
            cnt[key] += 16
            if transpose:
                eng.dma_start_transpose(out_ap, in_ap).then_inc(LSEM[key], 16)
            else:
                eng.dma_start(out=out_ap, in_=in_ap).then_inc(LSEM[key], 16)
            return cnt[key]

        def ldma(out_ap, in_ap, eng=SY, transpose=False, guard=True):
            if guard:
                eng.wait_ge(s_gen, cnt["gen"])
                eng.wait_ge(ld, cnt["ld"])
                eng.wait_ge(sd_o, cnt["sd_o"])
            cnt["ld"] += 16
            if transpose:
                eng.dma_start_transpose(out_ap, in_ap).then_inc(ld, 16)
            else:
                eng.dma_start(out=out_ap, in_=in_ap).then_inc(ld, 16)
            return cnt["ld"]

        def odma(out_ap, in_ap, eng=G):
            eng.wait_ge(sd_o, cnt["sd_o"])
            cnt["sd_o"] += 16
            eng.dma_start(out=out_ap, in_=in_ap).then_inc(sd_o, 16)
            return cnt["sd_o"]

        def gen(instr):
            cnt["gen"] += 1
            instr.then_inc(s_gen, 1)
            return cnt["gen"]

        def mm(instr):
            cnt["mm"] += 1
            instr.then_inc(s_mm, 1)
            return cnt["mm"]

        def act(instr):
            cnt["act"] += 1
            instr.then_inc(s_act, 1)
            return cnt["act"]

        def vwait(c=None):
            V.wait_ge(s_gen, cnt["gen"] if c is None else c)

        for _t in _all_sb:
            gen(V.memset(_t.ap(), 0)); vwait()
        v = ldma(rolev[:, :], role[:, :])
        v = ldma(xtile[0:RNN, 0:384], whhT[:, :])
        v = ldma(xtile[0:SSM, 384:448], aT[:, :])
        v = ldma(xtile2[:, 0:128], ident[:, :])
        v = ldma(biasg[:, 0:4], bias_g[:, :])
        v = ldma(tok4[:, 0:3], bhh_g[:, :])
        V.wait_ge(ld, v)
        gen(V.memset(wscan[:, :], 0.0)); vwait()
        gen(V.tensor_copy(wscan[0:RNN, 0:384], xtile[0:RNN, 0:384])); vwait()
        gen(V.tensor_copy(wscan[0:SSM, 384:448], xtile[0:SSM, 384:448])); vwait()
        gen(V.tensor_copy(i128[:, :], xtile2[:, 0:128])); vwait()
        gen(V.tensor_copy(tp_id[:, :], xtile2[:, 0:128])); vwait()
        gen(V.tensor_add(biasg[:, 0:2], biasg[:, 0:2], tok4[:, 0:2])); vwait()
        gen(V.memset(bhhn_blk[:, :], 0.0)); vwait()
        gen(V.tensor_scalar(bhhn_blk[:, :], bhhn_blk[:, :], tok4[:, 2:3], None, ALU.add)); vwait()
        gen(V.memset(ones1[:, :], 1.0)); vwait()
        gen(V.memset(hgw[:, :], 0.0)); vwait()
        gen(V.memset(hsw[:, :], 0.0)); vwait()
        gen(V.memset(onesc[:, :], 1.0)); vwait()
        gen(V.memset(maxa[:, :], 0.0)); vwait()
        gen(V.memset(scal[:, :], 0.0)); vwait()
        for ch in range(NCH):
            v = ldma(xtile[:, 0:512], wall.rearrange("(c p) s -> c p s", p=128)[ch])
            V.wait_ge(ld, v); vwait()
            gen(V.tensor_copy(wallb[:, 512 * ch:512 * (ch + 1)], xtile[:, 0:512]))
        for ch in range(NCH):
            v = ldma(xtile[:, 0:DM], dwT.rearrange("(c p) s -> c p s", p=128)[ch])
            V.wait_ge(ld, v); vwait()
            gen(V.tensor_copy(dwTb[:, DM * ch:DM * (ch + 1)], xtile[:, 0:DM]))
        v = ldma(xtile[0:SSM, 0:DM], cwT[:, :])
        V.wait_ge(ld, v); vwait()
        gen(V.tensor_copy(cwTb[:, :], xtile[0:SSM, 0:DM]))
        v = ldma(xtile[0:RNN, 0:DM], projT[:, :])
        V.wait_ge(ld, v); vwait()
        gen(V.tensor_copy(projTb[:, :], xtile[0:RNN, 0:DM]))
        v = ldma(xtile[0:1, 0:DM], cb_row[:, :])
        v = ldma(xtile2[0:1, 0:DM], db_row[:, :])
        v = ldma(ytile[0:1, 0:DM], pb_row[:, :])
        V.wait_ge(ld, v); vwait()
        gen(V.tensor_add(xtile[0:1, 0:DM], xtile[0:1, 0:DM], xtile2[0:1, 0:DM])); vwait()
        gen(V.tensor_copy(x1t[0:1, 0:DM], xtile[0:1, 0:DM])); vwait()
        gen(V.tensor_copy(projrow[0:1, 0:DM], ytile[0:1, 0:DM]))
        T.wait_ge(s_gen, cnt["gen"])
        for hh in range(DM // 512):
            m = mm(T.matmul(pp[2][:, 0:512], ones1[0:1, 0:128], x1t[0:1, 512 * hh:512 * (hh + 1)],
                            start=True, stop=True))
            V.wait_ge(s_mm, m)
            gen(V.tensor_copy(cbdb_b[:, 512 * hh:512 * (hh + 1)], pp[2][:, 0:512]))
            T.wait_ge(s_gen, cnt["gen"])
            m = mm(T.matmul(pp[2][:, 0:512], ones1[0:1, 0:128], projrow[0:1, 512 * hh:512 * (hh + 1)],
                            start=True, stop=True))
            V.wait_ge(s_mm, m)
            gen(V.tensor_copy(projb_b[:, 512 * hh:512 * (hh + 1)], pp[2][:, 0:512]))
            T.wait_ge(s_gen, cnt["gen"])

        def dq_panel(rr, cols):
            V.wait_ge(sd_o, cnt["sd_o"])
            nb = cols // 32
            gen(V.tensor_reduce(stat6[0:rr, 0:nb], xtile[0:rr, 0:cols].rearrange("p (b k) -> p b k", k=32),
                                AX.X, ALU.max, apply_absolute_value=True)); vwait()
            gen(V.tensor_scalar(stat6[0:rr, 0:nb], stat6[0:rr, 0:nb], 1.0 / 7.0, 1e-12, ALU.mult, ALU.max)); vwait()
            gen(V.reciprocal(stat6[0:rr, nb:2 * nb], stat6[0:rr, 0:nb]))
            for b in range(nb):
                vwait()
                gen(V.tensor_scalar(xtile2[0:rr, 32 * b:32 * (b + 1)], xtile[0:rr, 32 * b:32 * (b + 1)],
                                    stat6[0:rr, nb + b:nb + b + 1], None, ALU.mult))
            vwait()
            gen(V.tensor_copy(i16t[0:rr, 0:cols], xtile2[0:rr, 0:cols])); vwait()
            gen(V.tensor_copy(xtile2[0:rr, 0:cols], i16t[0:rr, 0:cols]))
            for b in range(nb):
                vwait()
                gen(V.tensor_scalar(x1t[0:rr, 32 * b:32 * (b + 1)], xtile2[0:rr, 32 * b:32 * (b + 1)],
                                    stat6[0:rr, b:b + 1], None, ALU.mult))

        def dequant(src_ap, rows, cols, dst_dram):
            ngr = (rows + 127) // 128
            for g in range(ngr):
                r0 = g * 128
                rr = min(128, rows - r0)
                for p0 in range(0, cols, DM):
                    pw = min(DM, cols - p0)
                    v = ldma(xtile[0:rr, 0:pw], src_ap[r0:r0 + rr, p0:p0 + pw])
                    V.wait_ge(ld, v)
                    dq_panel(rr, pw)
                    G.wait_ge(s_gen, cnt["gen"])
                    odma(dst_dram[r0:r0 + rr, p0:p0 + pw], x1t[0:rr, 0:pw])

        dequant(wg_s, DISH, DM, ag_g_i)
        dequant(wu_s, DISH, DM, ag_u_i)
        dequant(wd_s, DM // 8, DI, ag_d_i)
        G.wait_ge(sd_o, cnt["sd_o"])
        for (ai, ao) in ((ag_g_i, ag_g_o), (ag_u_i, ag_u_o), (ag_d_i, ag_d_o)):
            cnt["cc"] += 1
            G.collective_compute("AllGather", ALU.bypass, replica_groups=RG,
                                 ins=[ai[:, :]], outs=[ao[:, :]]).then_inc(s_cc, 1)

        NT_ALL = SEQ // 128
        ytb2 = ytile.bitcast(BF16)
        lnout = [ytb2[:, 0:DM], ytb2[:, DM:2 * DM]]
        lnin = [xtile, xtile2]
        ln_ldv = {}
        ln_head = {}
        ln_st = {}
        SY.wait_ge(s_gen, cnt["gen"])
        SY.wait_ge(ld, cnt["ld"])
        SY.wait_ge(sd_o, cnt["sd_o"])
        ln_ldv[0] = ldma2("la", xtile[:, :], xb[0:128, :])
        for ti in range(NT_ALL):
            xt = lnin[ti % 2]
            if ti + 1 < NT_ALL:
                if ti >= 1:
                    SY.wait_ge(s_gen, ln_head[ti - 1])
                ln_ldv[ti + 1] = ldma2("la" if (ti + 1) % 2 == 0 else "lb",
                                       lnin[(ti + 1) % 2][:, :], xb[128 * (ti + 1):128 * (ti + 2), :])
            V.wait_ge(LSEM["la" if ti % 2 == 0 else "lb"], ln_ldv[ti])
            gen(V.bn_stats(stat6[:, 0:6], xt[:, 0:512])); vwait()
            gen(V.bn_stats(stat6[:, 6:12], xt[:, 512:1024])); vwait()
            gen(V.bn_aggr(stat2[:, 0:2], stat6[:, 0:12])); vwait()
            gen(V.tensor_scalar(tok4[:, 2:3], stat2[:, 1:2], EPS, None, ALU.add))
            S.wait_ge(s_gen, cnt["gen"])
            a = act(S.activation(tok4[:, 3:4], tok4[:, 2:3], AF.Sqrt))
            V.wait_ge(s_act, a)
            gen(V.reciprocal(tok4[:, 1:2], tok4[:, 3:4])); vwait()
            if ti >= 2:
                V.wait_ge(sd_o, ln_st[ti - 2])
            gen(V.tensor_scalar(lnout[ti % 2], xt[:, :], stat2[:, 0:1], tok4[:, 1:2], ALU.subtract, ALU.mult))
            ln_head[ti] = cnt["gen"]
            G.wait_ge(s_gen, cnt["gen"])
            ln_st[ti] = odma(s_x1tm[128 * ti:128 * (ti + 1), :], lnout[ti % 2])
        G.wait_ge(sd_o, cnt["sd_o"])
        SY.wait_ge(sd_o, cnt["sd_o"])

        evac_done = {0: 0, 1: 0}
        buf_guard = {0: 0, 1: 0}
        dump_guard = {0: 0, 1: 0}
        SHF = SEQ // 2
        b0r = [prodo, x1t[:, 0:512]]
        b1r = [fft, x1t[:, 512:1024]]
        f2r = [prodf, ffta]
        f3r = [fftb, fftc]
        bxr = [wff[:, 0:512], wff[:, 512:1024]]
        st_par = {0: 0, 1: 0}
        dump_cnt = {0: 0, 1: 0}
        go1 = [0]
        for sl in range(NSL_ALL):
            par = sl % 2
            buf = xfm[par]
            SY.wait_ge(s_mm, buf_guard[par])
            SY.wait_ge(go[1], go1[0])
            SY.wait_ge(ld, cnt["ld"])
            for ch in range(NCH):
                v = ldma(buf[:, 512 * ch:512 * (ch + 1)],
                         s_x1tm[512 * sl:512 * (sl + 1), 128 * ch:128 * (ch + 1)], transpose=True, guard=False)
            SY.wait_ge(ld, v)
            for ch in range(NCH):
                go1[0] += 16
                SY.dma_start(out=s_x1fm.rearrange("(c p) t -> c p t", p=128)[ch][:, 512 * sl:512 * (sl + 1)],
                             in_=buf[:, 512 * ch:512 * (ch + 1)]).then_inc(go[1], 16)
            dump_cnt[par] = go1[0]
            T.wait_ge(ld, v)
            b0, b1, f2, f3, bx = b0r[par], b1r[par], f2r[par], f3r[par], bxr[par]
            stores_before = st_par[par]
            for g in range(4):
                bank = pp[g % 2]
                T.wait_ge(s_gen, evac_done[g % 2])
                for ch in range(NCH):
                    mi = T.matmul(bank[:, 0:512], wallb[:, 512 * ch + 128 * g:512 * ch + 128 * (g + 1)],
                                  buf[:, 512 * ch:512 * (ch + 1)],
                                  start=(ch == 0), stop=(ch == NCH - 1), skip_group_check=True)
                    if ch == NCH - 1:
                        m = mm(mi)
                V.wait_ge(s_mm, m)
                if g == 0:
                    V.wait_ge(sd_o, stores_before)
                if g < 2:
                    dst = s_ir if g == 0 else s_iz
                    ob = b0 if g == 0 else b1
                    gen(V.tensor_scalar(ob[:, 0:256],
                                        bank[:, 0:512].rearrange("p (t two) -> p two t", two=2)[:, 0, :],
                                        biasg[:, g:g + 1], None, ALU.add)); vwait()
                    gen(V.tensor_scalar(ob[:, 256:512],
                                        bank[:, 0:512].rearrange("p (t two) -> p two t", two=2)[:, 1, :],
                                        biasg[:, g:g + 1], None, ALU.add))
                    evac_done[g % 2] = cnt["gen"]
                    G.wait_ge(s_gen, cnt["gen"])
                    odma(dst[:, 256 * sl:256 * (sl + 1)], ob[:, 0:256])
                    odma(dst[:, SHF + 256 * sl:SHF + 256 * (sl + 1)], ob[:, 256:512])
                elif g == 2:
                    gen(V.tensor_scalar(f2[:, 0:512], bank[:, 0:512], biasg[:, 2:3], None, ALU.add))
                    evac_done[g % 2] = cnt["gen"]
                    G.wait_ge(s_gen, cnt["gen"])
                    odma(s_in[:, 512 * sl:512 * (sl + 1)], f2[:, 0:512])
                else:
                    gen(V.tensor_scalar(f3[:, 0:512], bank[:, 0:512], biasg[:, 3:4], None, ALU.add))
                    ec = cnt["gen"]
                    vwait()
                    gen(V.tensor_copy(bx[0:SSM, 0:256],
                                      f3[0:SSM, 0:512].rearrange("p (t two) -> p two t", two=2)[:, 0, :])); vwait()
                    gen(V.tensor_copy(bx[0:SSM, 256:512],
                                      f3[0:SSM, 0:512].rearrange("p (t two) -> p two t", two=2)[:, 1, :]))
                    S.wait_ge(s_gen, ec)
                    a = act(S.activation(f3[64:66, 0:512], f3[64:66, 0:512], AF.Sigmoid))
                    evac_done[g % 2] = cnt["gen"]
                    G.wait_ge(s_gen, cnt["gen"])
                    G.wait_ge(s_act, a)
                    odma(s_bx[:, 256 * sl:256 * (sl + 1)], bx[0:SSM, 0:256])
                    odma(s_bx[:, SHF + 256 * sl:SHF + 256 * (sl + 1)], bx[0:SSM, 256:512])
                    odma(s_gx[:, 512 * sl:512 * (sl + 1)], f3[64:66, 0:512])
            buf_guard[par] = cnt["mm"]
            st_par[par] = cnt["sd_o"]
        G.wait_ge(sd_o, cnt["sd_o"])
        G.wait_ge(s_cc, cnt["cc"])

        PREDMA = 7 * 16

        def issue_half_streams(eng, g_expr, slot, prior=None):
            if prior is not None:
                eng.wait_ge(sd_pre, prior)
            e0 = g_expr * HALF // 2 if False else None
            base_e = g_expr * 128
            eng.dma_start(out=pre_t[0][slot][0:RNN, 0:128], in_=s_ir[:, bass.ts(g_expr, 128)]).then_inc(sd_pre, 16)
            eng.dma_start(out=pre_t[0][slot][0:RNN, 128:256], in_=s_iz[:, bass.ts(g_expr, 128)]).then_inc(sd_pre, 16)
            eng.dma_start(out=pre_t[0][slot][0:SSM, 256:384], in_=s_bx[:, bass.ts(g_expr, 128)]).then_inc(sd_pre, 16)
            eng.dma_start(out=pre_t[1][slot][0:RNN, 0:128],
                          in_=s_ir[:, SHF:SEQ + 128][:, bass.ts(g_expr, 128)]).then_inc(sd_pre, 16)
            eng.dma_start(out=pre_t[1][slot][0:RNN, 128:256],
                          in_=s_iz[:, SHF:SEQ + 128][:, bass.ts(g_expr, 128)]).then_inc(sd_pre, 16)
            eng.dma_start(out=pre_t[1][slot][0:SSM, 256:384],
                          in_=s_bx[:, SHF:SEQ + 128][:, bass.ts(g_expr, 128)]).then_inc(sd_pre, 16)
            eng.dma_start(out=in_t[slot][:, :], in_=s_in[:, bass.ts(g_expr, HALF)]).then_inc(sd_pre, 16)

        V.wait_ge(s_act, cnt["act"])
        V.wait_ge(sd_o, cnt["sd_o"])
        gen(V.memset(pre_t[0][0][SSM:128, 256:384], 0.0)); vwait()
        gen(V.memset(pre_t[0][1][SSM:128, 256:384], 0.0)); vwait()
        gen(V.memset(pre_t[1][0][SSM:128, 256:384], 0.0)); vwait()
        gen(V.memset(pre_t[1][1][SSM:128, 256:384], 0.0)); vwait()
        gen(V.memset(prodo[:, :], 0.0)); vwait()
        gen(V.memset(prodf[:, :], 0.0)); vwait()
        G.wait_ge(s_gen, cnt["gen"])
        odma(s_ir[:, SEQ:SEQ + 128], prodo[0:RNN, 0:128])
        odma(s_iz[:, SEQ:SEQ + 128], prodo[0:RNN, 0:128])
        odma(s_bx[:, SEQ:SEQ + 128], prodo[0:SSM, 0:128])
        odma(s_in[:, SEQ:SEQ + 256], prodf[0:RNN, 0:256])
        for dk in range(2):
            G.wait_ge(sd_h, 16 * dk)
            G.dma_start(out=s_dum[:, 256 * dk:256 * dk + HALF], in_=hgw[:, 0:HALF]).then_inc(sd_h, 16)
        SY.wait_ge(sd_o, cnt["sd_o"])
        issue_half_streams(SY, 0, 0)
        G.wait_ge(sd_o, cnt["sd_o"])
        for e in ENGS:
            eng = getattr(nc, e)
            eng.drain().then_inc(bar[1], 1)
            eng.wait_ge(bar[1], 5)

        mm0 = cnt["mm"]
        act0 = cnt["act"]
        rT_dve = T.alloc_register("rT_dve")
        rT_pre = T.alloc_register("rT_pre")
        rT_h = T.alloc_register("rT_h")
        rV_act = V.alloc_register("rV_act")
        rV_act2 = V.alloc_register("rV_act2")
        rV_c1 = V.alloc_register("rV_c1")
        rV_h = V.alloc_register("rV_h")
        rS_mm = S.alloc_register("rS_mm")
        rS_sg = S.alloc_register("rS_sg")
        rS_sg2 = S.alloc_register("rS_sg2")
        rS_pre = S.alloc_register("rS_pre")
        rY_premm = SY.alloc_register("rY_premm")
        rY_act = SY.alloc_register("rY_act")
        rY_pre = SY.alloc_register("rY_pre")
        rG_dve = G.alloc_register("rG_dve")
        rG_h = G.alloc_register("rG_h")
        T.reg_mov(rT_dve, 0)
        T.reg_mov(rT_pre, PREDMA)
        T.reg_mov(rT_h, 32)
        V.reg_mov(rV_act, 1)
        V.reg_mov(rV_act2, 2)
        V.reg_mov(rV_c1, 1)
        V.reg_mov(rV_h, 32)
        S.reg_mov(rS_mm, 1)
        S.reg_mov(rS_sg, 2)
        S.reg_mov(rS_sg2, 1)
        S.reg_mov(rS_pre, PREDMA)
        SY.reg_mov(rY_premm, 0)
        SY.reg_mov(rY_act, 0)
        SY.reg_mov(rY_pre, PREDMA)
        G.reg_mov(rG_dve, HALF)
        G.reg_mov(rG_h, 32)
        with nc.Fori(0, NBODY, 1) as ib:
            for X in range(2):
                g_idx = ib * 2 + X
                SY.wait_ge(sd_pre, rY_pre)
                SY.reg_add(rY_pre, rY_pre, PREDMA)
                SY.wait_ge(s_premm, rY_premm)
                SY.reg_add(rY_premm, rY_premm, 2)
                SY.wait_ge(s_act2, rY_act)
                SY.reg_add(rY_act, rY_act, 2 * HALF)
                issue_half_streams(SY, g_idx + 1, 1 - X)
                T.wait_ge(sd_h, rT_h)
                T.reg_add(rT_h, rT_h, 32)
                V.wait_ge(sd_h, rV_h)
                V.reg_add(rV_h, rV_h, 32)
                T.wait_ge(sd_pre, rT_pre)
                T.reg_add(rT_pre, rT_pre, PREDMA)
                S.wait_ge(sd_pre, rS_pre)
                S.reg_add(rS_pre, rS_pre, PREDMA)
                T.matmul(pb[0][:, 0:384], i128[:, :], pre_t[0][X][:, 0:384], start=True, stop=False,
                         skip_group_check=True)
                T.matmul(pb[0][:, 384:512], i128[:, :], bhhn_blk[:, :], start=False, stop=True,
                         skip_group_check=True).then_inc(s_premm, 1)
                T.matmul(pb[1][:, 0:384], i128[:, :], pre_t[1][X][:, 0:384], start=True, stop=False,
                         skip_group_check=True)
                T.matmul(pb[1][:, 384:512], i128[:, :], bhhn_blk[:, :], start=False, stop=True,
                         skip_group_check=True).then_inc(s_premm, 1)
                for t in range(HALF):
                    par = t % 2
                    j = t // 2
                    sslot = t % 2
                    hprev = (X * HALF + t - 1) % (2 * HALF)
                    hcur = X * HALF + t
                    bank = pb[par]
                    T.wait_ge(s_dve, rT_dve)
                    T.matmul(bank[:, j:j + 1], wscan[:, 0:128], hgw[:, hprev:hprev + 1],
                             start=False, stop=False, skip_group_check=True)
                    T.matmul(bank[:, 128 + j:129 + j], wscan[:, 128:256], hgw[:, hprev:hprev + 1],
                             start=False, stop=False, skip_group_check=True).then_inc(s_mm2, 1)
                    T.matmul(bank[0:SSM, 256 + j:257 + j], wscan[0:SSM, 384:448], hsw[:, hprev:hprev + 1],
                             start=False, stop=False, skip_group_check=True)
                    T.matmul(bank[:, 384 + j:385 + j], wscan[:, 256:384], hgw[:, hprev:hprev + 1],
                             start=False, stop=False, skip_group_check=True).then_inc(s_mm2, 1)
                    T.reg_add(rT_dve, rT_dve, 1)
                    S.wait_ge(s_mm2, rS_mm)
                    S.activation(st3[:, 3 * sslot:3 * sslot + 2],
                                 bank[:, 0:512].rearrange("p (b j) -> p j b", b=4)[:, j, 0:2],
                                 AF.Sigmoid).then_inc(s_sg, 1)
                    S.wait_ge(s_mm2, rS_sg)
                    S.wait_ge(s_sg, rS_sg2)
                    S.activation(ncol[:, sslot:sslot + 1], bank[:, 384 + j:385 + j], AF.Tanh,
                                 scale=st3[:, 3 * sslot:3 * sslot + 1],
                                 bias=in_t[X][:, t:t + 1]).then_inc(s_act2, 1)
                    S.activation(st3[0:SSM, 3 * sslot + 2:3 * sslot + 3],
                                 bank[0:SSM, 256 + j:257 + j], AF.Sigmoid).then_inc(s_act2, 1)
                    S.reg_add(rS_mm, rS_mm, 2)
                    S.reg_add(rS_sg, rS_sg, 2)
                    S.reg_add(rS_sg2, rS_sg2, 1)
                    V.wait_ge(s_act2, rV_act)
                    V.scalar_tensor_tensor(csc[:, sslot:sslot + 1], hgw[:, hprev:hprev + 1],
                                           ncol[:, sslot:sslot + 1], st3[:, 3 * sslot + 1:3 * sslot + 2],
                                           ALU.subtract, ALU.mult).then_inc(s_c1, 1)
                    V.wait_ge(s_act2, rV_act2)
                    V.tensor_mul(hsw[:, hcur:hcur + 1], bank[0:SSM, 256 + j:257 + j],
                                 st3[0:SSM, 3 * sslot + 2:3 * sslot + 3])
                    V.wait_ge(s_c1, rV_c1)
                    V.tensor_add(hgw[:, hcur:hcur + 1], csc[:, sslot:sslot + 1],
                                 ncol[:, sslot:sslot + 1]).then_inc(s_dve, 1)
                    V.reg_add(rV_act, rV_act, 2)
                    V.reg_add(rV_act2, rV_act2, 2)
                    V.reg_add(rV_c1, rV_c1, 1)
                G.wait_ge(s_dve, rG_dve)
                G.reg_add(rG_dve, rG_dve, HALF)
                G.wait_ge(sd_h, rG_h)
                G.reg_add(rG_h, rG_h, 32)
                G.dma_start(out=s_hg[:, bass.ts(g_idx, HALF)],
                            in_=hgw[:, X * HALF:(X + 1) * HALF]).then_inc(sd_h, 16)
                G.dma_start(out=s_hs[:, bass.ts(g_idx, HALF)],
                            in_=hsw[:, X * HALF:(X + 1) * HALF]).then_inc(sd_h, 16)

        for e in ENGS:
            eng = getattr(nc, e)
            eng.drain().then_inc(bar[0], 1)
            eng.wait_ge(bar[0], 5)
        for e in ENGS:
            getattr(nc, e).wait_ge(sd_h, 32 + 32 * NHALF)

        NSLO = OWN // 512

        v = ldma(identf[:, :], ident[:, :])
        V.wait_ge(ld, v)
        gen(V.memset(onesf1[:, :], 1.0)); vwait()
        gen(V.memset(maxa[:, :], 0.0)); vwait()
        gen(V.memset(ccrow[:, :], 0.0)); vwait()

        def bcast_scal(col, src_ap):
            T.wait_ge(s_gen, cnt["gen"])
            m = mm(T.matmul(pq[:, 0:1], onesf1[0:1, 0:128], src_ap, start=True, stop=True))
            V.wait_ge(s_mm, m)
            gen(V.tensor_copy(scal[:, col:col + 1], pq[:, 0:1]))

        def maxcol_to_row(col, dst_col):
            vwait()
            V.wait_ge(sd_o, cnt["sd_o"])
            V.wait_ge(s_act, cnt["act"])
            gen(V.tensor_copy(fft[:, 0:1], maxa[:, col:col + 1]))
            T.wait_ge(s_gen, cnt["gen"])
            m = mm(T.transpose(ptr[0:1, 0:128], fft[:, 0:1], tp_id[:, :]))
            V.wait_ge(s_mm, m)
            gen(V.reduce_max(ccrow[0:1, dst_col:dst_col + 1], ptr[0:1, 0:128], AX.X)); vwait()

        qv = nc.sync.cc_rank(RG) % 4

        SY.wait_ge(go[1], go1[0])
        for ch in range(NCH):
            v = ldma(bigT[:, ch * OWN:(ch + 1) * OWN],
                     s_x1fm.rearrange("(c p) t -> c p t", p=128)[ch][:, :][:, bass.ts(qv, OWN)])
        v = ldma(hgO[:, :], s_hg[:, bass.ts(qv, OWN)])
        v = ldma(hsO[:, :], s_hs[:, bass.ts(qv, OWN)])
        v = ldma(gxO[:, :], s_gx[:, bass.ts(qv, OWN)])
        T.wait_ge(ld, v)

        for k in range(NTI):
            V.wait_ge(sd_o, cnt["sd_o"])
            v1 = ldma(xtile[:, :], xq[128 * k:128 * (k + 1), :])
            for half in range(2):
                T.wait_ge(ld, v1)
                for ch in range(NCH):
                    mi = T.matmul(pp[half][:, 0:512], bigT[:, ch * OWN + 128 * k:ch * OWN + 128 * (k + 1)],
                                  dwTb[:, DM * ch + 512 * half:DM * ch + 512 * (half + 1)],
                                  start=(ch == 0), stop=(ch == NCH - 1), skip_group_check=True)
                    if ch == NCH - 1:
                        m = mm(mi)
            for half in range(2):
                m = mm(T.matmul(pb[half][:, 0:512], hsO[0:SSM, 128 * k:128 * (k + 1)],
                                cwTb[0:SSM, 512 * half:512 * (half + 1)],
                                start=True, stop=True, skip_group_check=True))
            m = mm(T.matmul(pq[:, 0:512], hgO[:, 128 * k:128 * (k + 1)], projTb[:, 0:512], start=True, stop=True,
                            skip_group_check=True))
            m = mm(T.matmul(pp[2][:, 0:512], hgO[:, 128 * k:128 * (k + 1)], projTb[:, 512:1024], start=True, stop=True,
                            skip_group_check=True))
            V.wait_ge(ld, v1)
            gen(V.tensor_copy(fft[0:2, 0:128], gxO[0:2, 128 * k:128 * (k + 1)]))
            T.wait_ge(s_gen, cnt["gen"])
            m2 = mm(T.transpose(ptr[:, 0:2], fft[0:2, 0:128], tp_id[0:2, 0:2]))
            V.wait_ge(s_mm, m)
            V.wait_ge(s_mm, m2)
            gen(V.tensor_copy(g2t[:, 0:2], ptr[:, 0:2])); vwait()
            for half in range(2):
                H = slice(512 * half, 512 * (half + 1))
                gen(V.tensor_add(ytile[:, H], pp[half][:, 0:512], cbdb_b[:, H])); vwait()
                gen(V.tensor_add(ytile[:, H], ytile[:, H], pb[half][:, 0:512])); vwait()
                gen(V.tensor_scalar(ytile[:, H], ytile[:, H], g2t[:, 0:1], None, ALU.mult)); vwait()
                rb = pq if half == 0 else pp[2]
                gen(V.tensor_add(x2t[:, H], rb[:, 0:512], projb_b[:, H])); vwait()
                gen(V.tensor_scalar(x2t[:, H], x2t[:, H], g2t[:, 1:2], None, ALU.mult)); vwait()
                gen(V.tensor_add(ytile[:, H], ytile[:, H], x2t[:, H])); vwait()
            gen(V.bn_stats(stat6[:, 0:6], ytile[:, 0:512])); vwait()
            gen(V.bn_stats(stat6[:, 6:12], ytile[:, 512:1024])); vwait()
            gen(V.bn_aggr(stat2[:, 0:2], stat6[:, 0:12])); vwait()
            gen(V.tensor_scalar(tok4[:, 2:3], stat2[:, 1:2], EPS, None, ALU.add))
            S.wait_ge(s_gen, cnt["gen"])
            a = act(S.activation(tok4[:, 3:4], tok4[:, 2:3], AF.Sqrt))
            V.wait_ge(s_act, a)
            gen(V.reciprocal(tok4[:, 1:2], tok4[:, 3:4])); vwait()
            gen(V.tensor_scalar(ytile[:, :], ytile[:, :], stat2[:, 0:1], tok4[:, 1:2],
                                ALU.subtract, ALU.mult)); vwait()
            gen(V.tensor_add(x2t[:, :], ytile[:, :], xtile[:, :])); vwait()
            G.wait_ge(s_gen, cnt["gen"])
            odma(s_x2[128 * k:128 * (k + 1), :], x2t[:, :])
            gen(V.bn_stats(stat6[:, 0:6], x2t[:, 0:512])); vwait()
            gen(V.bn_stats(stat6[:, 6:12], x2t[:, 512:1024])); vwait()
            gen(V.bn_aggr(stat2[:, 0:2], stat6[:, 0:12])); vwait()
            gen(V.tensor_scalar(tok4[:, 2:3], stat2[:, 1:2], EPS, None, ALU.add))
            S.wait_ge(s_gen, cnt["gen"])
            a = act(S.activation(tok4[:, 3:4], tok4[:, 2:3], AF.Sqrt))
            V.wait_ge(s_act, a)
            gen(V.reciprocal(tok4[:, 1:2], tok4[:, 3:4])); vwait()
            gen(V.tensor_copy(tok4[:, 0:1], stat2[:, 0:1])); vwait()
            G.wait_ge(s_gen, cnt["gen"])
            odma(s_l3[128 * k:128 * (k + 1), 0:2], tok4[:, 0:2])
            gen(V.tensor_scalar(ytile[:, :], x2t[:, :], tok4[:, 0:1], None, ALU.subtract)); vwait()
            gen(V.reduce_max(tok4[:, 4:5], ytile[:, :], AX.X, apply_absolute_value=True)); vwait()
            gen(V.tensor_scalar(tok4[:, 5:6], tok4[:, 4:5], tok4[:, 1:2], None, ALU.mult)); vwait()
            gen(V.tensor_max(maxa[:, 0:1], maxa[:, 0:1], tok4[:, 5:6])); vwait()
        maxcol_to_row(0, 0)
        G.wait_ge(s_gen, cnt["gen"])
        odma(cc_i[0:1, 0:8], ccrow[0:1, 0:8])
        G.wait_ge(sd_o, cnt["sd_o"])
        cnt["cc"] += 1
        G.collective_compute("AllReduce", ALU.max, replica_groups=RG,
                             ins=[cc_i[:, :]], outs=[cc_o[:, :]]).then_inc(s_cc, 1)
        G.wait_ge(s_cc, cnt["cc"])
        SY.wait_ge(s_cc, cnt["cc"])
        v = ldma(ccrow2[0:1, 0:8], cc_o[0:1, 0:8])
        V.wait_ge(ld, v)
        gen(V.tensor_scalar(ccrow2[0:1, 1:2], ccrow2[0:1, 0:1], 1.0 / 32767.0, 1e-12, ALU.mult, ALU.max)); vwait()
        gen(V.reciprocal(ccrow2[0:1, 2:3], ccrow2[0:1, 1:2])); vwait()
        bcast_scal(0, ccrow2[0:1, 1:2])
        bcast_scal(1, ccrow2[0:1, 2:3])

        for k in range(NTI):
            V.wait_ge(sd_o, cnt["sd_o"])
            v1 = ldma(x2t[:, :], s_x2[128 * k:128 * (k + 1), :])
            v1 = ldma(tok4[:, 0:2], s_l3[128 * k:128 * (k + 1), 0:2])
            V.wait_ge(ld, v1)
            gen(V.tensor_scalar(tok4[:, 2:3], tok4[:, 1:2], scal[:, 1:2], None, ALU.mult)); vwait()
            gen(V.tensor_scalar(ytile[:, :], x2t[:, :], tok4[:, 0:1], tok4[:, 2:3],
                                ALU.subtract, ALU.mult)); vwait()
            gen(V.tensor_copy(i16t[:, :], ytile[:, :])); vwait()
            gen(V.tensor_copy(x1t[:, :], i16t[:, :])); vwait()
            for ch in range(NCH):
                T.wait_ge(s_gen, cnt["gen"])
                m = mm(T.transpose(ptr[:, :], x1t[:, 128 * ch:128 * (ch + 1)], tp_id[:, :]))
                V.wait_ge(s_mm, m)
                gen(V.tensor_copy(bigT[:, ch * OWN + 128 * k:ch * OWN + 128 * (k + 1)], ptr[:, :]))
        wffb = [wff, xfm[0]]
        groups = [(wsrc, dst, mcol, g)
                  for (wsrc, dst, mcol) in ((ag_u_o, s_up, 1), (ag_g_o, s_gt, 2))
                  for g in range(NG)]
        fstg = [fft, prodo]
        NGT = len(groups)

        def _issue_wffload(gi):
            wsrc, dst, mcol, g = groups[gi]
            buf = wffb[gi % 2]
            for ch in range(NCH):
                vv = ldma(buf[:, 512 * ch:512 * (ch + 1)],
                          wsrc[512 * g:512 * (g + 1), 128 * ch:128 * (ch + 1)], transpose=True, guard=False)
            return vv

        SY.wait_ge(s_mm, cnt["mm"])
        SY.wait_ge(ld, cnt["ld"])
        v_w = [None] * NGT
        v_w[0] = _issue_wffload(0)
        evac_gen = {}
        st_cnt = {}
        mm_end = {}
        ts0 = cnt["gen"]
        T.wait_ge(s_gen, ts0)
        tser = 0
        for gi in range(NGT):
            wsrc, dst, mcol, g = groups[gi]
            buf = wffb[gi % 2]
            T.wait_ge(ld, v_w[gi])
            for k in range(NTI):
                bank = pp[k % 2]
                if tser >= 2:
                    T.wait_ge(s_gen, evac_gen[tser - 2])
                for ch in range(NCH):
                    mi = T.matmul(bank[:, 0:512], bigT[:, ch * OWN + 128 * k:ch * OWN + 128 * (k + 1)],
                                  buf[:, 512 * ch:512 * (ch + 1)],
                                  start=(ch == 0), stop=(ch == NCH - 1), skip_group_check=True)
                    if ch == NCH - 1:
                        m = mm(mi)
                if k == 0 and gi + 1 < NGT:
                    if gi >= 1:
                        SY.wait_ge(s_mm, mm_end[gi - 1])
                    SY.wait_ge(ld, cnt["ld"])
                    v_w[gi + 1] = _issue_wffload(gi + 1)
                V.wait_ge(s_mm, m)
                gen(V.reduce_max(tok4[:, 6:7], bank[:, 0:512], AX.X, apply_absolute_value=True)); vwait()
                gen(V.tensor_max(maxa[:, mcol:mcol + 1], maxa[:, mcol:mcol + 1], tok4[:, 6:7])); vwait()
                if tser >= 2:
                    V.wait_ge(sd_o, st_cnt[tser - 2])
                gen(V.tensor_scalar(fstg[tser % 2][:, 0:512], bank[:, 0:512], scal[:, 0:1], None, ALU.mult))
                evac_gen[tser] = cnt["gen"]
                G.wait_ge(s_gen, cnt["gen"])
                st_cnt[tser] = odma(dst[128 * k:128 * (k + 1), 512 * g:512 * (g + 1)], fstg[tser % 2][:, 0:512])
                tser += 1
            mm_end[gi] = cnt["mm"]
        vwait()
        gen(V.tensor_scalar(maxa[:, 1:2], maxa[:, 1:2], scal[:, 0:1], None, ALU.mult)); vwait()
        gen(V.tensor_scalar(maxa[:, 2:3], maxa[:, 2:3], scal[:, 0:1], None, ALU.mult)); vwait()
        maxcol_to_row(1, 1)
        maxcol_to_row(2, 2)
        G.wait_ge(s_gen, cnt["gen"])
        odma(cc_i[0:1, 0:8], ccrow[0:1, 0:8])
        G.wait_ge(sd_o, cnt["sd_o"])
        cnt["cc"] += 1
        G.collective_compute("AllReduce", ALU.max, replica_groups=RG,
                             ins=[cc_i[:, :]], outs=[cc_o[:, :]]).then_inc(s_cc, 1)
        G.wait_ge(s_cc, cnt["cc"])
        SY.wait_ge(s_cc, cnt["cc"])
        v = ldma(ccrow2[0:1, 0:8], cc_o[0:1, 0:8])
        V.wait_ge(ld, v)
        gen(V.tensor_scalar(ccrow2[0:1, 3:4], ccrow2[0:1, 1:2], 1.0 / 32767.0, 1e-12, ALU.mult, ALU.max)); vwait()
        gen(V.tensor_scalar(ccrow2[0:1, 4:5], ccrow2[0:1, 2:3], 1.0 / 32767.0, 1e-12, ALU.mult, ALU.max)); vwait()
        gen(V.reciprocal(ccrow2[0:1, 5:6], ccrow2[0:1, 3:4])); vwait()
        gen(V.reciprocal(ccrow2[0:1, 6:7], ccrow2[0:1, 4:5])); vwait()
        bcast_scal(2, ccrow2[0:1, 3:4])
        bcast_scal(3, ccrow2[0:1, 5:6])
        bcast_scal(4, ccrow2[0:1, 4:5])
        bcast_scal(5, ccrow2[0:1, 6:7])
        gen(V.memset(ccrow[:, :], 0.0)); vwait()

        wdTb = bigT
        dwn = bigT.bitcast(F32)
        wd_jobs = [(ic, dch) for ic in range(DIC) for dch in range(NCH)]
        v_wd = 16 * len(wd_jobs)

        wd_cnt = [0]

        def _issue_wd(jobs):
            for (ic, dch) in jobs:
                SY.wait_ge(go[0], wd_cnt[0])
                wd_cnt[0] += 16
                SY.dma_start_transpose(
                    wdTb[:, (ic * NCH + dch) * 128:(ic * NCH + dch) * 128 + 128],
                    ag_d_o[128 * dch:128 * (dch + 1), 128 * ic:128 * (ic + 1)]).then_inc(go[0], 16)
        SY.wait_ge(s_mm, cnt["mm"])
        SY.wait_ge(s_gen, cnt["gen"])
        sgin = [x1t[:, 0:512], x1t[:, 512:1024]]
        ytb = ytile.bitcast(BF16)
        sgout = [ytb[:, 0:512], ytb[:, 512:1024]]
        sg_iters = [(sl, ic) for sl in range(NSLO) for ic in range(DIC)]
        sg_ldv = {}
        sg_head = {}
        sg_st = {}
        SY.wait_ge(s_gen, cnt["gen"])
        SY.wait_ge(ld, cnt["ld"])
        SY.wait_ge(sd_o, cnt["sd_o"])
        sl0, ic0 = sg_iters[0]
        sg_ldv[0] = ldma2("la", sgin[0], s_gt[512 * sl0:512 * (sl0 + 1), 128 * ic0:128 * (ic0 + 1)],
                          transpose=True)
        for t, (sl, ic) in enumerate(sg_iters):
            if t + 1 < len(sg_iters):
                if t >= 1:
                    SY.wait_ge(s_gen, sg_head[t - 1])
                sl2, ic2 = sg_iters[t + 1]
                sg_ldv[t + 1] = ldma2("la" if (t + 1) % 2 == 0 else "lb", sgin[(t + 1) % 2],
                                      s_gt[512 * sl2:512 * (sl2 + 1), 128 * ic2:128 * (ic2 + 1)],
                                      transpose=True)
            V.wait_ge(LSEM["la" if t % 2 == 0 else "lb"], sg_ldv[t])
            gen(V.tensor_scalar(ffta[:, 0:512], sgin[t % 2], scal[:, 5:6], None, ALU.mult))
            sg_head[t] = cnt["gen"]
            vwait()
            wd_per = -(-len(wd_jobs) // len(sg_iters))
            _issue_wd(wd_jobs[t * wd_per:(t + 1) * wd_per])
            gen(V.tensor_copy(i16t[:, 0:512], ffta[:, 0:512]))
            S.wait_ge(s_gen, cnt["gen"])
            a = act(S.activation(fftb[:, 0:512], i16t[:, 0:512], AF.Sigmoid, scale=scal[:, 4:5]))
            V.wait_ge(s_act, a)
            gen(V.tensor_scalar(fftb[:, 0:512], fftb[:, 0:512], 65536.0, None, ALU.mult)); vwait()
            gen(V.tensor_copy(u16t[:, 0:512], fftb[:, 0:512])); vwait()
            gen(V.tensor_mul(fftc[:, 0:512], i16t[:, 0:512], u16t[:, 0:512])); vwait()
            if t >= 2:
                V.wait_ge(sd_o, sg_st[t - 2])
            gen(V.tensor_copy(sgout[t % 2], fftc[:, 0:512])); vwait()
            gen(V.reduce_max(tok4[:, 6:7], fftc[:, 0:512], AX.X, apply_absolute_value=True)); vwait()
            gen(V.tensor_max(maxa[:, 3:4], maxa[:, 3:4], tok4[:, 6:7]))
            G.wait_ge(s_gen, cnt["gen"])
            sg_st[t] = odma(s_gs[128 * ic:128 * (ic + 1), 512 * sl:512 * (sl + 1)], sgout[t % 2])
        vwait()
        gen(V.tensor_scalar(maxa[:, 3:4], maxa[:, 3:4], scal[:, 4:5], 1.0 / 65536.0, ALU.mult, ALU.mult)); vwait()
        maxcol_to_row(3, 3)
        G.wait_ge(s_gen, cnt["gen"])
        odma(cc_i[0:1, 0:8], ccrow[0:1, 0:8])
        G.wait_ge(sd_o, cnt["sd_o"])
        cnt["cc"] += 1
        G.collective_compute("AllReduce", ALU.max, replica_groups=RG,
                             ins=[cc_i[:, :]], outs=[cc_o[:, :]]).then_inc(s_cc, 1)
        G.wait_ge(s_cc, cnt["cc"])
        SY.wait_ge(s_cc, cnt["cc"])
        v = ldma(ccrow2[0:1, 0:8], cc_o[0:1, 0:8])
        V.wait_ge(ld, v)
        gen(V.tensor_scalar(ccrow2[0:1, 3:4], ccrow2[0:1, 3:4], 1.0 / 32767.0, 1e-12, ALU.mult, ALU.max)); vwait()
        gen(V.reciprocal(ccrow2[0:1, 5:6], ccrow2[0:1, 3:4])); vwait()
        gen(V.tensor_scalar(ccrow2[0:1, 6:7], ccrow2[0:1, 5:6], 1.0 / 65536.0, None, ALU.mult)); vwait()
        bcast_scal(6, ccrow2[0:1, 3:4])
        bcast_scal(7, ccrow2[0:1, 6:7])
        vwait()
        gen(V.tensor_mul(scal[:, 8:9], scal[:, 7:8], scal[:, 4:5])); vwait()
        gen(V.memset(ccrow[:, :], 0.0)); vwait()
        gen(V.memset(maxa[:, 0:1], 0.0)); vwait()

        xtb = xtile.bitcast(BF16)
        gsin = [x1t[:, 0:512], x1t[:, 512:1024]]
        upin = [xtb[:, 0:512], xtb[:, 512:1024]]
        opout = [ytb[:, 0:512], ytb[:, 512:1024]]
        op_iters = [(sl, ic) for sl in range(NSLO) for ic in range(DIC)]
        op_ldv = {}
        op_head = {}
        op_st = {}
        SY.wait_ge(s_gen, cnt["gen"])
        SY.wait_ge(ld, cnt["ld"])
        SY.wait_ge(sd_o, cnt["sd_o"])
        sl0, ic0 = op_iters[0]
        op_gv = {}
        op_gv[0] = ldma2("la", gsin[0], s_gs[128 * ic0:128 * (ic0 + 1), 512 * sl0:512 * (sl0 + 1)])
        op_ldv[0] = ldma2("lc", upin[0], s_up[512 * sl0:512 * (sl0 + 1), 128 * ic0:128 * (ic0 + 1)],
                          transpose=True)
        for t, (sl, ic) in enumerate(op_iters):
            if t + 1 < len(op_iters):
                if t >= 1:
                    SY.wait_ge(s_gen, op_head[t - 1])
                sl2, ic2 = op_iters[t + 1]
                ka = "la" if (t + 1) % 2 == 0 else "lb"
                kc = "lc" if (t + 1) % 2 == 0 else "ld2"
                op_gv[t + 1] = ldma2(ka, gsin[(t + 1) % 2],
                                     s_gs[128 * ic2:128 * (ic2 + 1), 512 * sl2:512 * (sl2 + 1)])
                op_ldv[t + 1] = ldma2(kc, upin[(t + 1) % 2],
                                      s_up[512 * sl2:512 * (sl2 + 1), 128 * ic2:128 * (ic2 + 1)],
                                      transpose=True)
            V.wait_ge(LSEM["la" if t % 2 == 0 else "lb"], op_gv[t])
            V.wait_ge(LSEM["lc" if t % 2 == 0 else "ld2"], op_ldv[t])
            gen(V.tensor_scalar(ffta[:, 0:512], gsin[t % 2], scal[:, 8:9], None, ALU.mult)); vwait()
            gen(V.tensor_copy(i16t[:, 0:512], ffta[:, 0:512])); vwait()
            gen(V.tensor_scalar(fftb[:, 0:512], upin[t % 2], scal[:, 3:4], None, ALU.mult))
            op_head[t] = cnt["gen"]
            vwait()
            gen(V.tensor_copy(i16t[:, 512:1024], fftb[:, 0:512])); vwait()
            gen(V.tensor_mul(fftc[:, 0:512], i16t[:, 0:512], i16t[:, 512:1024])); vwait()
            if t >= 2:
                V.wait_ge(sd_o, op_st[t - 2])
            gen(V.tensor_copy(opout[t % 2], fftc[:, 0:512])); vwait()
            gen(V.reduce_max(tok4[:, 6:7], fftc[:, 0:512], AX.X, apply_absolute_value=True)); vwait()
            gen(V.tensor_max(maxa[:, 0:1], maxa[:, 0:1], tok4[:, 6:7]))
            G.wait_ge(s_gen, cnt["gen"])
            op_st[t] = odma(s_op[128 * ic:128 * (ic + 1), 512 * sl:512 * (sl + 1)], opout[t % 2])
        vwait()
        gen(V.tensor_mul(scal[:, 9:10], scal[:, 6:7], scal[:, 2:3])); vwait()
        gen(V.tensor_scalar(maxa[:, 0:1], maxa[:, 0:1], scal[:, 9:10], None, ALU.mult)); vwait()
        maxcol_to_row(0, 4)
        G.wait_ge(s_gen, cnt["gen"])
        odma(cc_i[0:1, 0:8], ccrow[0:1, 0:8])
        G.wait_ge(sd_o, cnt["sd_o"])
        cnt["cc"] += 1
        G.collective_compute("AllReduce", ALU.max, replica_groups=RG,
                             ins=[cc_i[:, :]], outs=[cc_o[:, :]]).then_inc(s_cc, 1)
        G.wait_ge(s_cc, cnt["cc"])
        SY.wait_ge(s_cc, cnt["cc"])
        v = ldma(ccrow2[0:1, 0:8], cc_o[0:1, 0:8])
        V.wait_ge(ld, v)
        gen(V.tensor_scalar(ccrow2[0:1, 4:5], ccrow2[0:1, 4:5], 1.0 / 32767.0, 1e-12, ALU.mult, ALU.max)); vwait()
        gen(V.reciprocal(ccrow2[0:1, 5:6], ccrow2[0:1, 4:5])); vwait()
        bcast_scal(10, ccrow2[0:1, 4:5])
        bcast_scal(11, ccrow2[0:1, 5:6])
        vwait()
        gen(V.tensor_mul(scal[:, 12:13], scal[:, 9:10], scal[:, 11:12])); vwait()

        outtb = [ytile, xtile2]
        x2b = [x2t, xtile]
        din = [fft, prodo]
        T.wait_ge(go[0], v_wd)
        m_prev_sl = None
        out_st = {}
        x2_ldv = {}
        last_cp = {0: None, 1: None}
        dn0_sdo = cnt["sd_o"]
        for sl in range(NSLO):
            dn_ldv = {}
            dn_head = {}
            SY.wait_ge(s_gen, cnt["gen"])
            SY.wait_ge(ld, cnt["ld"])
            SY.wait_ge(sd_o, cnt["sd_o"])
            dn_ldv[0] = ldma2("la", din[0][:, 0:512], s_op[0:128, 512 * sl:512 * (sl + 1)])
            for ic in range(DIC):
                if ic + 1 < DIC:
                    if ic >= 1:
                        SY.wait_ge(s_gen, dn_head[ic - 1])
                    dn_ldv[ic + 1] = ldma2("la" if (ic + 1) % 2 == 0 else "lb",
                                           din[(ic + 1) % 2][:, 0:512],
                                           s_op[128 * (ic + 1):128 * (ic + 2), 512 * sl:512 * (sl + 1)])
                V.wait_ge(LSEM["la" if ic % 2 == 0 else "lb"], dn_ldv[ic])
                if ic == 0 and m_prev_sl is not None:
                    V.wait_ge(s_mm, m_prev_sl)
                gen(V.tensor_scalar(ffta[:, 0:512], din[ic % 2][:, 0:512], scal[:, 12:13], None, ALU.mult))
                dn_head[ic] = cnt["gen"]
                vwait()
                gen(V.tensor_copy(i16t[:, 0:512], ffta[:, 0:512])); vwait()
                gen(V.tensor_copy(oslc[:, 512 * ic:512 * (ic + 1)], i16t[:, 0:512])); vwait()
            conv_end = cnt["gen"]
            T.wait_ge(s_gen, conv_end)
            ev_gen = {}
            for dch in range(NCH):
                bank = pp[dch % 2]
                if dch >= 2:
                    T.wait_ge(s_gen, ev_gen[dch - 2])
                for ic in range(DIC):
                    mi = T.matmul(bank[:, 0:512], wdTb[:, (ic * NCH + dch) * 128:(ic * NCH + dch) * 128 + 128],
                                  oslc[:, 512 * ic:512 * (ic + 1)],
                                  start=(ic == 0), stop=(ic == DIC - 1), skip_group_check=True)
                    if ic == DIC - 1:
                        m = mm(mi)
                V.wait_ge(s_mm, m)
                gen(V.tensor_scalar(dwn[:, DIC * DM // 2 + 512 * dch:DIC * DM // 2 + 512 * (dch + 1)], bank[:, 0:512],
                                    scal[:, 10:11], None, ALU.mult))
                ev_gen[dch] = cnt["gen"]
                vwait()
            m_prev_sl = m
            T.wait_ge(s_gen, ev_gen[NCH - 1])
            tp_dst = [pq, pp[2]]
            for sub in range(4):
                sser = sl * 4 + sub
                trow = 512 * sl + 128 * sub
                v1 = ldma(x2b[sser % 2][:, :], s_x2[trow:trow + 128, :])
                x2_ldv[sser] = v1
                ot = outtb[sser % 2]
                for dch in range(NCH):
                    p = dch % 2
                    dst = tp_dst[p]
                    if last_cp[p] is not None:
                        T.wait_ge(s_gen, last_cp[p])
                    m2 = mm(T.matmul(dst[:, 0:128], dwn[:, DIC * DM // 2 + 512 * dch + 128 * sub:DIC * DM // 2 + 512 * dch + 128 * (sub + 1)],
                                     identf[:, :], start=True, stop=True, is_transpose=True))
                    V.wait_ge(s_mm, m2)
                    if dch == 0:
                        V.wait_ge(sd_o, out_st[sser - 2] if sser >= 2 else dn0_sdo)
                    gen(V.tensor_copy(ot[:, 128 * dch:128 * (dch + 1)], dst[:, 0:128]))
                    last_cp[p] = cnt["gen"]
                    vwait()
                gen(V.bn_stats(stat6[:, 0:6], ot[:, 0:512])); vwait()
                gen(V.bn_stats(stat6[:, 6:12], ot[:, 512:1024])); vwait()
                gen(V.bn_aggr(stat2[:, 0:2], stat6[:, 0:12])); vwait()
                gen(V.tensor_scalar(tok4[:, 2:3], stat2[:, 1:2], EPS, None, ALU.add))
                S.wait_ge(s_gen, cnt["gen"])
                a = act(S.activation(tok4[:, 3:4], tok4[:, 2:3], AF.Sqrt))
                V.wait_ge(s_act, a)
                gen(V.reciprocal(tok4[:, 1:2], tok4[:, 3:4])); vwait()
                gen(V.tensor_scalar(ot[:, :], ot[:, :], stat2[:, 0:1], tok4[:, 1:2],
                                    ALU.subtract, ALU.mult)); vwait()
                V.wait_ge(ld, x2_ldv[sser])
                gen(V.tensor_add(ot[:, :], ot[:, :], x2b[sser % 2][:, :])); vwait()
                G.wait_ge(s_gen, cnt["gen"])
                out_st[sser] = odma(out[trow:trow + 128, :], ot[:, :])
        G.wait_ge(sd_o, cnt["sd_o"])
    return nc


def _stage(inputs, q, b, SEQ, DI):
    x = np.ascontiguousarray(inputs["L0_attn_x"][b]).astype(np.float32)
    OWN = SEQ // 4
    whh = inputs["L0_attn_W_hh"]
    wih = inputs["L0_attn_W_ih"]
    wall = np.zeros((DM, 512), np.float32)
    wall[:, 0:128] = wih[256:384].T * 0 if False else wih[0:128].T
    wall[:, 128:256] = wih[128:256].T
    wall[:, 256:384] = wih[256:384].T
    wall[:, 384:448] = inputs["L0_attn_Bw"].T
    wall[:, 448:450] = inputs["L0_attn_gate_w"].T
    bias_g = np.zeros((128, 4), np.float32)
    bih = inputs["L0_attn_b_ih"]
    bias_g[:, 0] = bih[0:128]
    bias_g[:, 1] = bih[128:256]
    bias_g[:, 2] = bih[256:384]
    bias_g[0:SSM, 3] = inputs["L0_attn_Bb"]
    bias_g[SSM:SSM + 2, 3] = inputs["L0_attn_gate_b"]
    bhh_g = np.ascontiguousarray(inputs["L0_attn_b_hh"].reshape(3, 128).T)
    d = {
        "xb": x,
        "xq": np.ascontiguousarray(x[q * OWN:(q + 1) * OWN]),
        "role": np.array([[q, b, 0, 0, 0, 0, 0, 0]], np.int32),
        "whhT": np.ascontiguousarray(inputs["L0_attn_W_hh"].T),
        "aT": np.ascontiguousarray(inputs["L0_attn_A"].T),
        "wall": wall,
        "bias_g": bias_g,
        "bhh_g": bhh_g,
        "ident": np.eye(128, dtype=np.float32),
        "cwT": np.ascontiguousarray(inputs["L0_attn_Cw"].T),
        "projT": np.ascontiguousarray(inputs["L0_attn_proj_w"].T),
        "cb_row": inputs["L0_attn_Cb"].reshape(1, DM).astype(np.float32),
        "db_row": inputs["L0_attn_Db"].reshape(1, DM).astype(np.float32),
        "pb_row": inputs["L0_attn_proj_b"].reshape(1, DM).astype(np.float32),
        "dwT": np.ascontiguousarray(inputs["L0_attn_Dw"].T),
        "wg_s": np.ascontiguousarray(inputs["L0_ffn_w_gate"][(q + 4 * b) * (DI // 8):(q + 4 * b + 1) * (DI // 8)]),
        "wu_s": np.ascontiguousarray(inputs["L0_ffn_w_up"][(q + 4 * b) * (DI // 8):(q + 4 * b + 1) * (DI // 8)]),
        "wd_s": np.ascontiguousarray(inputs["L0_ffn_w_down"][(q + 4 * b) * (DM // 8):(q + 4 * b + 1) * (DM // 8)]),
    }
    return {k: np.asarray(vv) for k, vv in d.items()}


def kernel(**inputs):
    SEQ = int(inputs["L0_attn_x"].shape[1])
    DI = int(inputs["L0_ffn_w_up"].shape[0])
    nc = build_program(SEQ, DI)
    in_maps = []
    for core in range(8):
        b, q = core // 4, core % 4
        in_maps.append(_stage(inputs, q, b, SEQ, DI))
    res = run_bass_kernel_spmd(nc, in_maps, core_ids=list(range(8)))
    OWN = SEQ // 4
    outp = np.zeros((2, SEQ, DM), np.float32)
    for core in range(8):
        b, q = core // 4, core % 4
        outp[b, q * OWN:(q + 1) * OWN] = res.results[core]["out"]
    return outp


# version 12
# speedup vs baseline: 1.5484x; 1.0120x over previous
import numpy as np
import contextlib
import concourse.bass as bass
import concourse.mybir as mybir
from concourse.bass_utils import run_bass_kernel_spmd

F32 = mybir.dt.float32
BF16 = mybir.dt.bfloat16
I16 = mybir.dt.int16
U16 = mybir.dt.uint16
I8 = mybir.dt.int8
AF = mybir.ActivationFunctionType
AX = mybir.AxisListType
ALU = mybir.AluOpType

DM = 1024
NCH = DM // 128
SSM = 64
RNN = 128
EPS = 1e-5
ENGS = ["sync", "tensor", "scalar", "vector", "gpsimd"]


def build_program(SEQ, DI, n_cores=8, debug=False):
    HALF = 256
    NHALF = SEQ // HALF
    NBODY = NHALF // 2
    OWN = SEQ // 4
    NTI = OWN // 128
    NSL_ALL = SEQ // 512
    DIC = DI // 128
    DISH = DI // 8
    NG = DI // 512

    nc = bass.Bass("TRN2", target_bir_lowering=False)

    ap_in = lambda n, s, d=F32: nc.dram_tensor(n, s, d, kind="ExternalInput").ap()
    xb = ap_in("xb", [SEQ, DM])
    xq = ap_in("xq", [OWN, DM])
    role = ap_in("role", [1, 8], mybir.dt.int32)
    whhT = ap_in("whhT", [RNN, 384])
    aT = ap_in("aT", [SSM, SSM])
    wall = ap_in("wall", [DM, 512])
    bias_g = ap_in("bias_g", [128, 4])
    bhh_g = ap_in("bhh_g", [128, 3])
    ident = ap_in("ident", [128, 128])
    cwT = ap_in("cwT", [SSM, DM])
    projT = ap_in("projT", [RNN, DM])
    cb_row = ap_in("cb_row", [1, DM])
    db_row = ap_in("db_row", [1, DM])
    pb_row = ap_in("pb_row", [1, DM])
    dwT = ap_in("dwT", [DM, DM])
    wg_s = ap_in("wg_s", [DISH, DM])
    wu_s = ap_in("wu_s", [DISH, DM])
    wd_s = ap_in("wd_s", [DM // 8, DI])
    out = nc.dram_tensor("out", [OWN, DM], F32, kind="ExternalOutput").ap()

    DBG = {"s_x2", "s_up", "s_gt", "s_gs", "s_op"}
    dr = lambda n, s, d: (nc.dram_tensor(n, s, d, kind="ExternalOutput") if (debug and n in DBG)
                          else nc.dram_tensor(n, s, d)).ap()
    s_x1tm = dr("s_x1tm", [SEQ, DM], BF16)
    s_x1fm = dr("s_x1fm", [DM, SEQ], BF16)
    s_ir = dr("s_ir", [RNN, SEQ + 128], BF16)
    s_iz = dr("s_iz", [RNN, SEQ + 128], BF16)
    s_bx = dr("s_bx", [SSM, SEQ + 128], BF16)
    s_in = dr("s_in", [RNN, SEQ + 256], F32)
    s_gx = dr("s_gx", [2, SEQ], F32)
    s_hg = dr("s_hg", [RNN, SEQ], BF16)
    s_hs = dr("s_hs", [SSM, SEQ], BF16)
    s_x2 = dr("s_x2", [OWN, DM], F32)
    s_l3 = dr("s_l3", [OWN, 2], F32)
    s_up = dr("s_up", [OWN, DI], BF16)
    s_gt = dr("s_gt", [OWN, DI], BF16)
    s_gs = dr("s_gs", [DI, OWN], BF16)
    s_op = dr("s_op", [DI, OWN], BF16)
    s_dum = dr("s_dum", [RNN, 4 * 256], BF16)
    cc_i = dr("cc_i", [1, 8], F32)
    cc_o = dr("cc_o", [1, 8], F32)
    ag_g_i = dr("ag_g_i", [DISH, DM], BF16)
    ag_g_o = dr("ag_g_o", [DI, DM], BF16)
    ag_u_i = dr("ag_u_i", [DISH, DM], BF16)
    ag_u_o = dr("ag_u_o", [DI, DM], BF16)
    ag_d_i = dr("ag_d_i", [DM // 8, DI], BF16)
    ag_d_o = dr("ag_d_o", [DM, DI], BF16)
    RG = [list(range(n_cores))]

    es = contextlib.ExitStack()
    with es:
        _all_sb = []

        def sb(n, s, d):
            t = es.enter_context(nc.sbuf_tensor(n, s, d))
            _all_sb.append(t)
            return t
        sem = lambda n: es.enter_context(nc.semaphore(n))

        wscan = sb("wscan", [128, 448], BF16)
        i128 = sb("i128", [128, 128], BF16)
        tp_id = sb("tp_id", [128, 128], BF16)
        wallb = sb("wallb", [128, NCH * 512], BF16)
        cwTb = sb("cwTb", [SSM, DM], BF16)
        projTb = sb("projTb", [RNN, DM], BF16)
        biasg = sb("biasg", [128, 4], F32)
        bhhn_blk = sb("bhhn_blk", [128, 128], BF16)
        ones1 = sb("ones1", [1, DM], BF16)
        scal = sb("scal", [128, 16], F32)
        rolev = sb("rolev", [1, 8], mybir.dt.int32)
        pre_t = [[sb(f"pre{p}{s}", [128, 512], BF16) for s in range(2)] for p in range(2)]
        in_t = [sb(f"int{s}", [RNN, HALF], F32) for s in range(2)]
        hgw = sb("hgw", [RNN, 2 * HALF], BF16)
        hsw = sb("hsw", [SSM, 2 * HALF], BF16)
        st3 = sb("st3", [128, 6], F32)
        ncol = sb("ncol", [128, 2], BF16)
        csc = sb("csc", [128, 2], BF16)
        xtile = sb("xtile", [128, DM], F32)
        xtile2 = sb("xtile2", [128, DM], F32)
        stat6 = sb("stat6", [128, 64], F32)
        stat2 = sb("stat2", [128, 2], F32)
        tok4 = sb("tok4", [128, 8], F32)
        x1t = sb("x1t", [128, DM], BF16)
        xfm = [sb(f"xfm{s}", [128, NCH * 512], BF16) for s in range(2)]
        prodo = sb("prodo", [128, 512], BF16)
        prodf = sb("prodf", [128, 512], F32)
        hgt = sb("hgt", [RNN, 128], BF16)
        hst = sb("hst", [SSM, 128], BF16)
        hgO = sb("hgO", [RNN, SEQ // 4], BF16)
        hsO = sb("hsO", [SSM, SEQ // 4], BF16)
        gxO = sb("gxO", [2, SEQ // 4], F32)
        g2t = sb("g2t", [128, 2], F32)
        ytile = sb("ytile", [128, DM], F32)
        x2t = sb("x2t", [128, DM], F32)
        maxa = sb("maxa", [128, 4], F32)
        BIGW = max(NCH * (SEQ // 4), DIC * DM + NCH * DM)
        bigT = sb("bigT", [128, BIGW], BF16)
        wff = sb("wff", [128, NCH * 512], BF16)
        identf = sb("identf", [128, 128], F32)
        onesf1 = sb("onesf1", [1, 128], F32)
        onesc = sb("onesc", [128, 1], F32)
        ccrow = sb("ccrow", [1, 8], F32)
        ccrow2 = sb("ccrow2", [1, 8], F32)
        projrow = sb("projrow", [1, DM], BF16)
        cdrow = sb("cdrow", [1, DM], BF16)
        oslc = sb("oslc", [128, max(DIC * 512, NCH * DM)], BF16)
        dwTb = oslc
        fft = sb("fft", [128, 512], BF16)
        ffta = sb("ffta", [128, 512], F32)
        fftb = sb("fftb", [128, 512], F32)
        fftc = sb("fftc", [128, 512], F32)
        i16t = prodf.bitcast(I16)
        u16t = prodo.bitcast(U16)
        pb = [es.enter_context(nc.psum_tensor(f"pb{p}", [128, 512], F32)) for p in range(2)]
        pp = [es.enter_context(nc.psum_tensor(f"pp{p}", [128, 512], F32)) for p in range(3)]
        pq = es.enter_context(nc.psum_tensor("pq", [128, 512], F32))
        ptr = es.enter_context(nc.psum_tensor("ptr", [128, 128], BF16))

        ld = sem("ld"); sd_pre = sem("sd_pre"); sd_h = sem("sd_h"); sd_o = sem("sd_o")
        ld_a = sem("ld_a"); ld_b = sem("ld_b"); ld_c = sem("ld_c"); ld_d = sem("ld_d")
        s_mm = sem("s_mm"); s_act = sem("s_act"); s_dve = sem("s_dve")
        s_mm2 = sem("s_mm2"); s_act2 = sem("s_act2"); s_premm = sem("s_premm")
        s_sg = sem("s_sg"); s_c1 = sem("s_c1"); s_cc = sem("s_cc"); s_gen = sem("s_gen")
        bar = [sem(f"bar{p}") for p in range(2)]
        go = [sem(f"go{p}") for p in range(2)]
        block = es.enter_context(nc.Block())

        T, V, S, G, SY = nc.tensor, nc.vector, nc.scalar, nc.gpsimd, nc.sync
        cnt = {"ld": 0, "sd_o": 0, "gen": 0, "mm": 0, "act": 0, "cc": 0,
               "la": 0, "lb": 0, "lc": 0, "ld2": 0}
        LSEM = {"la": ld_a, "lb": ld_b, "lc": ld_c, "ld2": ld_d}

        def ldma2(key, out_ap, in_ap, transpose=False, eng=SY):
            eng.wait_ge(LSEM[key], cnt[key])
            cnt[key] += 16
            if transpose:
                eng.dma_start_transpose(out_ap, in_ap).then_inc(LSEM[key], 16)
            else:
                eng.dma_start(out=out_ap, in_=in_ap).then_inc(LSEM[key], 16)
            return cnt[key]

        def ldma(out_ap, in_ap, eng=SY, transpose=False, guard=True):
            if guard:
                eng.wait_ge(s_gen, cnt["gen"])
                eng.wait_ge(ld, cnt["ld"])
                eng.wait_ge(sd_o, cnt["sd_o"])
            cnt["ld"] += 16
            if transpose:
                eng.dma_start_transpose(out_ap, in_ap).then_inc(ld, 16)
            else:
                eng.dma_start(out=out_ap, in_=in_ap).then_inc(ld, 16)
            return cnt["ld"]

        def odma(out_ap, in_ap, eng=G):
            eng.wait_ge(sd_o, cnt["sd_o"])
            cnt["sd_o"] += 16
            eng.dma_start(out=out_ap, in_=in_ap).then_inc(sd_o, 16)
            return cnt["sd_o"]

        def gen(instr):
            cnt["gen"] += 1
            instr.then_inc(s_gen, 1)
            return cnt["gen"]

        def mm(instr):
            cnt["mm"] += 1
            instr.then_inc(s_mm, 1)
            return cnt["mm"]

        def act(instr):
            cnt["act"] += 1
            instr.then_inc(s_act, 1)
            return cnt["act"]

        def vwait(c=None):
            V.wait_ge(s_gen, cnt["gen"] if c is None else c)

        for _t in _all_sb:
            gen(V.memset(_t.ap(), 0)); vwait()
        v = ldma(rolev[:, :], role[:, :])
        v = ldma(xtile[0:RNN, 0:384], whhT[:, :])
        v = ldma(xtile[0:SSM, 384:448], aT[:, :])
        v = ldma(xtile2[:, 0:128], ident[:, :])
        v = ldma(biasg[:, 0:4], bias_g[:, :])
        v = ldma(tok4[:, 0:3], bhh_g[:, :])
        V.wait_ge(ld, v)
        gen(V.memset(wscan[:, :], 0.0)); vwait()
        gen(V.tensor_copy(wscan[0:RNN, 0:384], xtile[0:RNN, 0:384])); vwait()
        gen(V.tensor_copy(wscan[0:SSM, 384:448], xtile[0:SSM, 384:448])); vwait()
        gen(V.tensor_copy(i128[:, :], xtile2[:, 0:128])); vwait()
        gen(V.tensor_copy(tp_id[:, :], xtile2[:, 0:128])); vwait()
        gen(V.tensor_add(biasg[:, 0:2], biasg[:, 0:2], tok4[:, 0:2])); vwait()
        gen(V.memset(bhhn_blk[:, :], 0.0)); vwait()
        gen(V.tensor_scalar(bhhn_blk[:, :], bhhn_blk[:, :], tok4[:, 2:3], None, ALU.add)); vwait()
        gen(V.memset(ones1[:, :], 1.0)); vwait()
        gen(V.memset(hgw[:, :], 0.0)); vwait()
        gen(V.memset(hsw[:, :], 0.0)); vwait()
        gen(V.memset(onesc[:, :], 1.0)); vwait()
        gen(V.memset(maxa[:, :], 0.0)); vwait()
        gen(V.memset(scal[:, :], 0.0)); vwait()
        for ch in range(NCH):
            v = ldma(xtile[:, 0:512], wall.rearrange("(c p) s -> c p s", p=128)[ch])
            V.wait_ge(ld, v); vwait()
            gen(V.tensor_copy(wallb[:, 512 * ch:512 * (ch + 1)], xtile[:, 0:512]))
        for ch in range(NCH):
            v = ldma(xtile[:, 0:DM], dwT.rearrange("(c p) s -> c p s", p=128)[ch])
            V.wait_ge(ld, v); vwait()
            gen(V.tensor_copy(dwTb[:, DM * ch:DM * (ch + 1)], xtile[:, 0:DM]))
        v = ldma(xtile[0:SSM, 0:DM], cwT[:, :])
        V.wait_ge(ld, v); vwait()
        gen(V.tensor_copy(cwTb[:, :], xtile[0:SSM, 0:DM]))
        v = ldma(xtile[0:RNN, 0:DM], projT[:, :])
        V.wait_ge(ld, v); vwait()
        gen(V.tensor_copy(projTb[:, :], xtile[0:RNN, 0:DM]))
        v = ldma(xtile[0:1, 0:DM], cb_row[:, :])
        v = ldma(xtile2[0:1, 0:DM], db_row[:, :])
        v = ldma(ytile[0:1, 0:DM], pb_row[:, :])
        V.wait_ge(ld, v); vwait()
        gen(V.tensor_add(xtile[0:1, 0:DM], xtile[0:1, 0:DM], xtile2[0:1, 0:DM])); vwait()
        gen(V.tensor_copy(x1t[0:1, 0:DM], xtile[0:1, 0:DM])); vwait()
        gen(V.tensor_copy(cdrow[0:1, 0:DM], xtile[0:1, 0:DM])); vwait()
        gen(V.tensor_copy(projrow[0:1, 0:DM], ytile[0:1, 0:DM]))

        def dq_panel(rr, cols):
            V.wait_ge(sd_o, cnt["sd_o"])
            nb = cols // 32
            gen(V.tensor_reduce(stat6[0:rr, 0:nb], xtile[0:rr, 0:cols].rearrange("p (b k) -> p b k", k=32),
                                AX.X, ALU.max, apply_absolute_value=True)); vwait()
            gen(V.tensor_scalar(stat6[0:rr, 0:nb], stat6[0:rr, 0:nb], 1.0 / 7.0, 1e-12, ALU.mult, ALU.max)); vwait()
            gen(V.reciprocal(stat6[0:rr, nb:2 * nb], stat6[0:rr, 0:nb]))
            for b in range(nb):
                vwait()
                gen(V.tensor_scalar(xtile2[0:rr, 32 * b:32 * (b + 1)], xtile[0:rr, 32 * b:32 * (b + 1)],
                                    stat6[0:rr, nb + b:nb + b + 1], None, ALU.mult))
            vwait()
            gen(V.tensor_copy(i16t[0:rr, 0:cols], xtile2[0:rr, 0:cols])); vwait()
            gen(V.tensor_copy(xtile2[0:rr, 0:cols], i16t[0:rr, 0:cols]))
            for b in range(nb):
                vwait()
                gen(V.tensor_scalar(x1t[0:rr, 32 * b:32 * (b + 1)], xtile2[0:rr, 32 * b:32 * (b + 1)],
                                    stat6[0:rr, b:b + 1], None, ALU.mult))

        def dequant(src_ap, rows, cols, dst_dram):
            ngr = (rows + 127) // 128
            for g in range(ngr):
                r0 = g * 128
                rr = min(128, rows - r0)
                for p0 in range(0, cols, DM):
                    pw = min(DM, cols - p0)
                    v = ldma(xtile[0:rr, 0:pw], src_ap[r0:r0 + rr, p0:p0 + pw])
                    V.wait_ge(ld, v)
                    dq_panel(rr, pw)
                    G.wait_ge(s_gen, cnt["gen"])
                    odma(dst_dram[r0:r0 + rr, p0:p0 + pw], x1t[0:rr, 0:pw])

        dequant(wg_s, DISH, DM, ag_g_i)
        dequant(wu_s, DISH, DM, ag_u_i)
        dequant(wd_s, DM // 8, DI, ag_d_i)
        G.wait_ge(sd_o, cnt["sd_o"])
        for (ai, ao) in ((ag_g_i, ag_g_o), (ag_u_i, ag_u_o), (ag_d_i, ag_d_o)):
            cnt["cc"] += 1
            G.collective_compute("AllGather", ALU.bypass, replica_groups=RG,
                                 ins=[ai[:, :]], outs=[ao[:, :]]).then_inc(s_cc, 1)

        NT_ALL = SEQ // 128
        ytb2 = ytile.bitcast(BF16)
        lnout = [ytb2[:, 0:DM], ytb2[:, DM:2 * DM]]
        lnin = [xtile, xtile2]
        ln_ldv = {}
        ln_head = {}
        ln_st = {}
        SY.wait_ge(s_gen, cnt["gen"])
        SY.wait_ge(ld, cnt["ld"])
        SY.wait_ge(sd_o, cnt["sd_o"])
        ln_ldv[0] = ldma2("la", xtile[:, :], xb[0:128, :])
        for ti in range(NT_ALL):
            xt = lnin[ti % 2]
            if ti + 1 < NT_ALL:
                if ti >= 1:
                    SY.wait_ge(s_gen, ln_head[ti - 1])
                ln_ldv[ti + 1] = ldma2("la" if (ti + 1) % 2 == 0 else "lb",
                                       lnin[(ti + 1) % 2][:, :], xb[128 * (ti + 1):128 * (ti + 2), :])
            V.wait_ge(LSEM["la" if ti % 2 == 0 else "lb"], ln_ldv[ti])
            gen(V.bn_stats(stat6[:, 0:6], xt[:, 0:512])); vwait()
            gen(V.bn_stats(stat6[:, 6:12], xt[:, 512:1024])); vwait()
            gen(V.bn_aggr(stat2[:, 0:2], stat6[:, 0:12])); vwait()
            gen(V.tensor_scalar(tok4[:, 2:3], stat2[:, 1:2], EPS, None, ALU.add))
            S.wait_ge(s_gen, cnt["gen"])
            a = act(S.activation(tok4[:, 3:4], tok4[:, 2:3], AF.Sqrt))
            V.wait_ge(s_act, a)
            gen(V.reciprocal(tok4[:, 1:2], tok4[:, 3:4])); vwait()
            if ti >= 2:
                V.wait_ge(sd_o, ln_st[ti - 2])
            gen(V.tensor_scalar(lnout[ti % 2], xt[:, :], stat2[:, 0:1], tok4[:, 1:2], ALU.subtract, ALU.mult))
            ln_head[ti] = cnt["gen"]
            G.wait_ge(s_gen, cnt["gen"])
            ln_st[ti] = odma(s_x1tm[128 * ti:128 * (ti + 1), :], lnout[ti % 2])
        G.wait_ge(sd_o, cnt["sd_o"])
        SY.wait_ge(sd_o, cnt["sd_o"])

        evac_done = {0: 0, 1: 0}
        buf_guard = {0: 0, 1: 0}
        dump_guard = {0: 0, 1: 0}
        SHF = SEQ // 2
        b0r = [prodo, x1t[:, 0:512]]
        b1r = [fft, x1t[:, 512:1024]]
        f2r = [prodf, ffta]
        f3r = [fftb, fftc]
        bxr = [wff[:, 0:512], wff[:, 512:1024]]
        st_par = {0: 0, 1: 0}
        dump_cnt = {0: 0, 1: 0}
        go1 = [0]
        for sl in range(NSL_ALL):
            par = sl % 2
            buf = xfm[par]
            SY.wait_ge(s_mm, buf_guard[par])
            SY.wait_ge(go[1], go1[0])
            SY.wait_ge(ld, cnt["ld"])
            for ch in range(NCH):
                v = ldma(buf[:, 512 * ch:512 * (ch + 1)],
                         s_x1tm[512 * sl:512 * (sl + 1), 128 * ch:128 * (ch + 1)], transpose=True, guard=False)
            SY.wait_ge(ld, v)
            for ch in range(NCH):
                go1[0] += 16
                SY.dma_start(out=s_x1fm.rearrange("(c p) t -> c p t", p=128)[ch][:, 512 * sl:512 * (sl + 1)],
                             in_=buf[:, 512 * ch:512 * (ch + 1)]).then_inc(go[1], 16)
            dump_cnt[par] = go1[0]
            T.wait_ge(ld, v)
            b0, b1, f2, f3, bx = b0r[par], b1r[par], f2r[par], f3r[par], bxr[par]
            stores_before = st_par[par]
            for g in range(4):
                bank = pp[g % 2]
                T.wait_ge(s_gen, evac_done[g % 2])
                for ch in range(NCH):
                    mi = T.matmul(bank[:, 0:512], wallb[:, 512 * ch + 128 * g:512 * ch + 128 * (g + 1)],
                                  buf[:, 512 * ch:512 * (ch + 1)],
                                  start=(ch == 0), stop=(ch == NCH - 1), skip_group_check=True)
                    if ch == NCH - 1:
                        m = mm(mi)
                V.wait_ge(s_mm, m)
                if g == 0:
                    V.wait_ge(sd_o, stores_before)
                if g < 2:
                    dst = s_ir if g == 0 else s_iz
                    ob = b0 if g == 0 else b1
                    gen(V.tensor_scalar(ob[:, 0:256],
                                        bank[:, 0:512].rearrange("p (t two) -> p two t", two=2)[:, 0, :],
                                        biasg[:, g:g + 1], None, ALU.add)); vwait()
                    gen(V.tensor_scalar(ob[:, 256:512],
                                        bank[:, 0:512].rearrange("p (t two) -> p two t", two=2)[:, 1, :],
                                        biasg[:, g:g + 1], None, ALU.add))
                    evac_done[g % 2] = cnt["gen"]
                    G.wait_ge(s_gen, cnt["gen"])
                    odma(dst[:, 256 * sl:256 * (sl + 1)], ob[:, 0:256])
                    odma(dst[:, SHF + 256 * sl:SHF + 256 * (sl + 1)], ob[:, 256:512])
                elif g == 2:
                    gen(V.tensor_scalar(f2[:, 0:512], bank[:, 0:512], biasg[:, 2:3], None, ALU.add))
                    evac_done[g % 2] = cnt["gen"]
                    G.wait_ge(s_gen, cnt["gen"])
                    odma(s_in[:, 512 * sl:512 * (sl + 1)], f2[:, 0:512])
                else:
                    gen(V.tensor_scalar(f3[:, 0:512], bank[:, 0:512], biasg[:, 3:4], None, ALU.add))
                    ec = cnt["gen"]
                    vwait()
                    gen(V.tensor_copy(bx[0:SSM, 0:256],
                                      f3[0:SSM, 0:512].rearrange("p (t two) -> p two t", two=2)[:, 0, :])); vwait()
                    gen(V.tensor_copy(bx[0:SSM, 256:512],
                                      f3[0:SSM, 0:512].rearrange("p (t two) -> p two t", two=2)[:, 1, :]))
                    S.wait_ge(s_gen, ec)
                    a = act(S.activation(f3[64:66, 0:512], f3[64:66, 0:512], AF.Sigmoid))
                    evac_done[g % 2] = cnt["gen"]
                    G.wait_ge(s_gen, cnt["gen"])
                    G.wait_ge(s_act, a)
                    odma(s_bx[:, 256 * sl:256 * (sl + 1)], bx[0:SSM, 0:256])
                    odma(s_bx[:, SHF + 256 * sl:SHF + 256 * (sl + 1)], bx[0:SSM, 256:512])
                    odma(s_gx[:, 512 * sl:512 * (sl + 1)], f3[64:66, 0:512])
            buf_guard[par] = cnt["mm"]
            st_par[par] = cnt["sd_o"]
        G.wait_ge(sd_o, cnt["sd_o"])
        G.wait_ge(s_cc, cnt["cc"])

        PREDMA = 7 * 16

        def issue_half_streams(eng, g_expr, slot, prior=None):
            if prior is not None:
                eng.wait_ge(sd_pre, prior)
            e0 = g_expr * HALF // 2 if False else None
            base_e = g_expr * 128
            eng.dma_start(out=pre_t[0][slot][0:RNN, 0:128], in_=s_ir[:, bass.ts(g_expr, 128)]).then_inc(sd_pre, 16)
            eng.dma_start(out=pre_t[0][slot][0:RNN, 128:256], in_=s_iz[:, bass.ts(g_expr, 128)]).then_inc(sd_pre, 16)
            eng.dma_start(out=pre_t[0][slot][0:SSM, 256:384], in_=s_bx[:, bass.ts(g_expr, 128)]).then_inc(sd_pre, 16)
            eng.dma_start(out=pre_t[1][slot][0:RNN, 0:128],
                          in_=s_ir[:, SHF:SEQ + 128][:, bass.ts(g_expr, 128)]).then_inc(sd_pre, 16)
            eng.dma_start(out=pre_t[1][slot][0:RNN, 128:256],
                          in_=s_iz[:, SHF:SEQ + 128][:, bass.ts(g_expr, 128)]).then_inc(sd_pre, 16)
            eng.dma_start(out=pre_t[1][slot][0:SSM, 256:384],
                          in_=s_bx[:, SHF:SEQ + 128][:, bass.ts(g_expr, 128)]).then_inc(sd_pre, 16)
            eng.dma_start(out=in_t[slot][:, :], in_=s_in[:, bass.ts(g_expr, HALF)]).then_inc(sd_pre, 16)

        V.wait_ge(s_act, cnt["act"])
        V.wait_ge(sd_o, cnt["sd_o"])
        gen(V.memset(pre_t[0][0][SSM:128, 256:384], 0.0)); vwait()
        gen(V.memset(pre_t[0][1][SSM:128, 256:384], 0.0)); vwait()
        gen(V.memset(pre_t[1][0][SSM:128, 256:384], 0.0)); vwait()
        gen(V.memset(pre_t[1][1][SSM:128, 256:384], 0.0)); vwait()
        gen(V.memset(prodo[:, :], 0.0)); vwait()
        gen(V.memset(prodf[:, :], 0.0)); vwait()
        G.wait_ge(s_gen, cnt["gen"])
        odma(s_ir[:, SEQ:SEQ + 128], prodo[0:RNN, 0:128])
        odma(s_iz[:, SEQ:SEQ + 128], prodo[0:RNN, 0:128])
        odma(s_bx[:, SEQ:SEQ + 128], prodo[0:SSM, 0:128])
        odma(s_in[:, SEQ:SEQ + 256], prodf[0:RNN, 0:256])
        for dk in range(2):
            G.wait_ge(sd_h, 16 * dk)
            G.dma_start(out=s_dum[:, 256 * dk:256 * dk + HALF], in_=hgw[:, 0:HALF]).then_inc(sd_h, 16)
        SY.wait_ge(sd_o, cnt["sd_o"])
        issue_half_streams(SY, 0, 0)
        G.wait_ge(sd_o, cnt["sd_o"])
        for e in ENGS:
            eng = getattr(nc, e)
            eng.drain().then_inc(bar[1], 1)
            eng.wait_ge(bar[1], 5)

        mm0 = cnt["mm"]
        act0 = cnt["act"]
        rT_dve = T.alloc_register("rT_dve")
        rT_pre = T.alloc_register("rT_pre")
        rT_h = T.alloc_register("rT_h")
        rV_act = V.alloc_register("rV_act")
        rV_act2 = V.alloc_register("rV_act2")
        rV_c1 = V.alloc_register("rV_c1")
        rV_h = V.alloc_register("rV_h")
        rS_mm = S.alloc_register("rS_mm")
        rS_sg = S.alloc_register("rS_sg")
        rS_sg2 = S.alloc_register("rS_sg2")
        rS_pre = S.alloc_register("rS_pre")
        rY_premm = SY.alloc_register("rY_premm")
        rY_act = SY.alloc_register("rY_act")
        rY_pre = SY.alloc_register("rY_pre")
        rG_dve = G.alloc_register("rG_dve")
        rG_h = G.alloc_register("rG_h")
        T.reg_mov(rT_dve, 0)
        T.reg_mov(rT_pre, PREDMA)
        T.reg_mov(rT_h, 32)
        V.reg_mov(rV_act, 1)
        V.reg_mov(rV_act2, 2)
        V.reg_mov(rV_c1, 1)
        V.reg_mov(rV_h, 32)
        S.reg_mov(rS_mm, 1)
        S.reg_mov(rS_sg, 2)
        S.reg_mov(rS_sg2, 1)
        S.reg_mov(rS_pre, PREDMA)
        SY.reg_mov(rY_premm, 0)
        SY.reg_mov(rY_act, 0)
        SY.reg_mov(rY_pre, PREDMA)
        G.reg_mov(rG_dve, HALF)
        G.reg_mov(rG_h, 32)
        with nc.Fori(0, NBODY, 1) as ib:
            for X in range(2):
                g_idx = ib * 2 + X
                SY.wait_ge(sd_pre, rY_pre)
                SY.reg_add(rY_pre, rY_pre, PREDMA)
                SY.wait_ge(s_premm, rY_premm)
                SY.reg_add(rY_premm, rY_premm, 2)
                SY.wait_ge(s_act2, rY_act)
                SY.reg_add(rY_act, rY_act, 2 * HALF)
                issue_half_streams(SY, g_idx + 1, 1 - X)
                T.wait_ge(sd_h, rT_h)
                T.reg_add(rT_h, rT_h, 32)
                V.wait_ge(sd_h, rV_h)
                V.reg_add(rV_h, rV_h, 32)
                T.wait_ge(sd_pre, rT_pre)
                T.reg_add(rT_pre, rT_pre, PREDMA)
                S.wait_ge(sd_pre, rS_pre)
                S.reg_add(rS_pre, rS_pre, PREDMA)
                T.matmul(pb[0][:, 0:384], i128[:, :], pre_t[0][X][:, 0:384], start=True, stop=False,
                         skip_group_check=True)
                T.matmul(pb[0][:, 384:512], i128[:, :], bhhn_blk[:, :], start=False, stop=True,
                         skip_group_check=True).then_inc(s_premm, 1)
                T.matmul(pb[1][:, 0:384], i128[:, :], pre_t[1][X][:, 0:384], start=True, stop=False,
                         skip_group_check=True)
                T.matmul(pb[1][:, 384:512], i128[:, :], bhhn_blk[:, :], start=False, stop=True,
                         skip_group_check=True).then_inc(s_premm, 1)
                for t in range(HALF):
                    par = t % 2
                    j = t // 2
                    sslot = t % 2
                    hprev = (X * HALF + t - 1) % (2 * HALF)
                    hcur = X * HALF + t
                    bank = pb[par]
                    T.wait_ge(s_dve, rT_dve)
                    T.matmul(bank[:, j:j + 1], wscan[:, 0:128], hgw[:, hprev:hprev + 1],
                             start=False, stop=False, skip_group_check=True)
                    T.matmul(bank[:, 128 + j:129 + j], wscan[:, 128:256], hgw[:, hprev:hprev + 1],
                             start=False, stop=False, skip_group_check=True).then_inc(s_mm2, 1)
                    T.matmul(bank[0:SSM, 256 + j:257 + j], wscan[0:SSM, 384:448], hsw[:, hprev:hprev + 1],
                             start=False, stop=False, skip_group_check=True)
                    T.matmul(bank[:, 384 + j:385 + j], wscan[:, 256:384], hgw[:, hprev:hprev + 1],
                             start=False, stop=False, skip_group_check=True).then_inc(s_mm2, 1)
                    T.reg_add(rT_dve, rT_dve, 1)
                    S.wait_ge(s_mm2, rS_mm)
                    S.activation(st3[:, 3 * sslot:3 * sslot + 2],
                                 bank[:, 0:512].rearrange("p (b j) -> p j b", b=4)[:, j, 0:2],
                                 AF.Sigmoid).then_inc(s_sg, 1)
                    S.wait_ge(s_mm2, rS_sg)
                    S.wait_ge(s_sg, rS_sg2)
                    S.activation(ncol[:, sslot:sslot + 1], bank[:, 384 + j:385 + j], AF.Tanh,
                                 scale=st3[:, 3 * sslot:3 * sslot + 1],
                                 bias=in_t[X][:, t:t + 1]).then_inc(s_act2, 1)
                    S.activation(st3[0:SSM, 3 * sslot + 2:3 * sslot + 3],
                                 bank[0:SSM, 256 + j:257 + j], AF.Sigmoid).then_inc(s_act2, 1)
                    S.reg_add(rS_mm, rS_mm, 2)
                    S.reg_add(rS_sg, rS_sg, 2)
                    S.reg_add(rS_sg2, rS_sg2, 1)
                    V.wait_ge(s_act2, rV_act)
                    V.scalar_tensor_tensor(csc[:, sslot:sslot + 1], hgw[:, hprev:hprev + 1],
                                           ncol[:, sslot:sslot + 1], st3[:, 3 * sslot + 1:3 * sslot + 2],
                                           ALU.subtract, ALU.mult).then_inc(s_c1, 1)
                    V.wait_ge(s_act2, rV_act2)
                    V.tensor_mul(hsw[:, hcur:hcur + 1], bank[0:SSM, 256 + j:257 + j],
                                 st3[0:SSM, 3 * sslot + 2:3 * sslot + 3])
                    V.wait_ge(s_c1, rV_c1)
                    V.tensor_add(hgw[:, hcur:hcur + 1], csc[:, sslot:sslot + 1],
                                 ncol[:, sslot:sslot + 1]).then_inc(s_dve, 1)
                    V.reg_add(rV_act, rV_act, 2)
                    V.reg_add(rV_act2, rV_act2, 2)
                    V.reg_add(rV_c1, rV_c1, 1)
                G.wait_ge(s_dve, rG_dve)
                G.reg_add(rG_dve, rG_dve, HALF)
                G.wait_ge(sd_h, rG_h)
                G.reg_add(rG_h, rG_h, 32)
                G.dma_start(out=s_hg[:, bass.ts(g_idx, HALF)],
                            in_=hgw[:, X * HALF:(X + 1) * HALF]).then_inc(sd_h, 16)
                G.dma_start(out=s_hs[:, bass.ts(g_idx, HALF)],
                            in_=hsw[:, X * HALF:(X + 1) * HALF]).then_inc(sd_h, 16)

        for e in ENGS:
            eng = getattr(nc, e)
            eng.drain().then_inc(bar[0], 1)
            eng.wait_ge(bar[0], 5)
        for e in ENGS:
            getattr(nc, e).wait_ge(sd_h, 32 + 32 * NHALF)

        NSLO = OWN // 512

        v = ldma(identf[:, :], ident[:, :])
        V.wait_ge(ld, v)
        gen(V.memset(onesf1[:, :], 1.0)); vwait()
        gen(V.memset(maxa[:, :], 0.0)); vwait()
        gen(V.memset(ccrow[:, :], 0.0)); vwait()

        def bcast_scal(col, src_ap):
            T.wait_ge(s_gen, cnt["gen"])
            m = mm(T.matmul(pq[:, 0:1], onesf1[0:1, 0:128], src_ap, start=True, stop=True))
            V.wait_ge(s_mm, m)
            gen(V.tensor_copy(scal[:, col:col + 1], pq[:, 0:1]))

        def maxcol_to_row(col, dst_col):
            vwait()
            V.wait_ge(sd_o, cnt["sd_o"])
            V.wait_ge(s_act, cnt["act"])
            gen(V.tensor_copy(fft[:, 0:1], maxa[:, col:col + 1]))
            T.wait_ge(s_gen, cnt["gen"])
            m = mm(T.transpose(ptr[0:1, 0:128], fft[:, 0:1], tp_id[:, :]))
            V.wait_ge(s_mm, m)
            gen(V.reduce_max(ccrow[0:1, dst_col:dst_col + 1], ptr[0:1, 0:128], AX.X)); vwait()

        qv = nc.sync.cc_rank(RG) % 4

        SY.wait_ge(go[1], go1[0])
        for ch in range(NCH):
            v = ldma(bigT[:, ch * OWN:(ch + 1) * OWN],
                     s_x1fm.rearrange("(c p) t -> c p t", p=128)[ch][:, :][:, bass.ts(qv, OWN)])
        v = ldma(hgO[:, :], s_hg[:, bass.ts(qv, OWN)])
        v = ldma(hsO[:, :], s_hs[:, bass.ts(qv, OWN)])
        v = ldma(gxO[:, :], s_gx[:, bass.ts(qv, OWN)])
        T.wait_ge(ld, v)
        V.wait_ge(ld, v)

        dx_banks = [[pp[0], pp[1]], [pb[0], pb[1]]]
        xinb = [xtile, xtile2]
        ev_dx = {}
        ev_rnn = {}
        res_gen = {}
        xv = {}
        st_top = {}
        SY.wait_ge(s_gen, cnt["gen"])
        xv[0] = ldma2("lc", xtile[:, :], xq[0:128, :])
        for k in range(NTI):
            dxb = dx_banks[k % 2]
            if k + 1 < NTI:
                if k >= 1:
                    SY.wait_ge(s_gen, res_gen[k - 1])
                xv[k + 1] = ldma2("lc" if (k + 1) % 2 == 0 else "ld2",
                                  xinb[(k + 1) % 2][:, :], xq[128 * (k + 1):128 * (k + 2), :])
            V.wait_ge(sd_o, cnt["sd_o"] if k == 0 else st_top[k - 1])
            if k >= 2:
                T.wait_ge(s_gen, ev_dx[k - 2])
            for half in range(2):
                H = slice(512 * half, 512 * (half + 1))
                T.matmul(dxb[half][:, 0:512], ones1[0:1, 0:128], cdrow[0:1, H],
                         start=True, stop=False, skip_group_check=True)
                for ch in range(NCH):
                    T.matmul(dxb[half][:, 0:512], bigT[:, ch * OWN + 128 * k:ch * OWN + 128 * (k + 1)],
                             dwTb[:, DM * ch + 512 * half:DM * ch + 512 * (half + 1)],
                             start=False, stop=False, skip_group_check=True)
                mm(T.matmul(dxb[half][:, 0:512], hsO[0:SSM, 128 * k:128 * (k + 1)],
                            cwTb[0:SSM, 512 * half:512 * (half + 1)],
                            start=False, stop=True, skip_group_check=True))
            gen(V.tensor_copy(fft[0:2, 0:128], gxO[0:2, 128 * k:128 * (k + 1)]))
            T.wait_ge(s_gen, cnt["gen"])
            mm(T.transpose(ptr[:, 0:2], fft[0:2, 0:128], tp_id[0:2, 0:2]))
            if k >= 1:
                T.wait_ge(s_gen, ev_rnn[k - 1])
            for half in range(2):
                H = slice(512 * half, 512 * (half + 1))
                rb = pq if half == 0 else pp[2]
                T.matmul(rb[:, 0:512], ones1[0:1, 0:128], projrow[0:1, H],
                         start=True, stop=False, skip_group_check=True)
                m3 = mm(T.matmul(rb[:, 0:512], hgO[:, 128 * k:128 * (k + 1)], projTb[:, H],
                                 start=False, stop=True, skip_group_check=True))
            V.wait_ge(s_mm, m3)
            gen(V.tensor_copy(g2t[:, 0:2], ptr[:, 0:2])); vwait()
            for half in range(2):
                H = slice(512 * half, 512 * (half + 1))
                rb = pq if half == 0 else pp[2]
                gen(V.tensor_scalar(ytile[:, H], dxb[half][:, 0:512], g2t[:, 0:1], None, ALU.mult)); vwait()
                gen(V.scalar_tensor_tensor(ytile[:, H], rb[:, 0:512], g2t[:, 1:2], ytile[:, H],
                                           ALU.mult, ALU.add)); vwait()
            ev_dx[k] = cnt["gen"]
            ev_rnn[k] = cnt["gen"]
            gen(V.bn_stats(stat6[:, 0:6], ytile[:, 0:512])); vwait()
            gen(V.bn_stats(stat6[:, 6:12], ytile[:, 512:1024])); vwait()
            gen(V.bn_aggr(stat2[:, 0:2], stat6[:, 0:12])); vwait()
            gen(V.tensor_scalar(tok4[:, 2:3], stat2[:, 1:2], EPS, None, ALU.add))
            S.wait_ge(s_gen, cnt["gen"])
            a = act(S.activation(tok4[:, 3:4], tok4[:, 2:3], AF.Sqrt))
            V.wait_ge(s_act, a)
            gen(V.reciprocal(tok4[:, 1:2], tok4[:, 3:4])); vwait()
            gen(V.tensor_scalar(ytile[:, :], ytile[:, :], stat2[:, 0:1], tok4[:, 1:2],
                                ALU.subtract, ALU.mult)); vwait()
            V.wait_ge(LSEM["lc" if k % 2 == 0 else "ld2"], xv[k])
            gen(V.tensor_add(x2t[:, :], ytile[:, :], xinb[k % 2][:, :]))
            res_gen[k] = cnt["gen"]
            vwait()
            G.wait_ge(s_gen, cnt["gen"])
            odma(s_x2[128 * k:128 * (k + 1), :], x2t[:, :])
            gen(V.bn_stats(stat6[:, 0:6], x2t[:, 0:512])); vwait()
            gen(V.bn_stats(stat6[:, 6:12], x2t[:, 512:1024])); vwait()
            gen(V.bn_aggr(stat2[:, 0:2], stat6[:, 0:12])); vwait()
            gen(V.tensor_scalar(tok4[:, 2:3], stat2[:, 1:2], EPS, None, ALU.add))
            S.wait_ge(s_gen, cnt["gen"])
            a = act(S.activation(tok4[:, 3:4], tok4[:, 2:3], AF.Sqrt))
            V.wait_ge(s_act, a)
            gen(V.reciprocal(tok4[:, 1:2], tok4[:, 3:4])); vwait()
            gen(V.tensor_copy(tok4[:, 0:1], stat2[:, 0:1])); vwait()
            G.wait_ge(s_gen, cnt["gen"])
            st_top[k] = odma(s_l3[128 * k:128 * (k + 1), 0:2], tok4[:, 0:2])
            gen(V.tensor_scalar(ytile[:, :], x2t[:, :], tok4[:, 0:1], None, ALU.subtract)); vwait()
            gen(V.reduce_max(tok4[:, 4:5], ytile[:, :], AX.X, apply_absolute_value=True)); vwait()
            gen(V.tensor_scalar(tok4[:, 5:6], tok4[:, 4:5], tok4[:, 1:2], None, ALU.mult)); vwait()
            gen(V.tensor_max(maxa[:, 0:1], maxa[:, 0:1], tok4[:, 5:6])); vwait()
        maxcol_to_row(0, 0)
        G.wait_ge(s_gen, cnt["gen"])
        odma(cc_i[0:1, 0:8], ccrow[0:1, 0:8])
        G.wait_ge(sd_o, cnt["sd_o"])
        cnt["cc"] += 1
        G.collective_compute("AllReduce", ALU.max, replica_groups=RG,
                             ins=[cc_i[:, :]], outs=[cc_o[:, :]]).then_inc(s_cc, 1)
        G.wait_ge(s_cc, cnt["cc"])
        SY.wait_ge(s_cc, cnt["cc"])
        v = ldma(ccrow2[0:1, 0:8], cc_o[0:1, 0:8])
        V.wait_ge(ld, v)
        gen(V.tensor_scalar(ccrow2[0:1, 1:2], ccrow2[0:1, 0:1], 1.0 / 32767.0, 1e-12, ALU.mult, ALU.max)); vwait()
        gen(V.reciprocal(ccrow2[0:1, 2:3], ccrow2[0:1, 1:2])); vwait()
        bcast_scal(0, ccrow2[0:1, 1:2])
        bcast_scal(1, ccrow2[0:1, 2:3])

        for k in range(NTI):
            V.wait_ge(sd_o, cnt["sd_o"])
            v1 = ldma(x2t[:, :], s_x2[128 * k:128 * (k + 1), :])
            v1 = ldma(tok4[:, 0:2], s_l3[128 * k:128 * (k + 1), 0:2])
            V.wait_ge(ld, v1)
            gen(V.tensor_scalar(tok4[:, 2:3], tok4[:, 1:2], scal[:, 1:2], None, ALU.mult)); vwait()
            gen(V.tensor_scalar(ytile[:, :], x2t[:, :], tok4[:, 0:1], tok4[:, 2:3],
                                ALU.subtract, ALU.mult)); vwait()
            gen(V.tensor_copy(i16t[:, :], ytile[:, :])); vwait()
            gen(V.tensor_copy(x1t[:, :], i16t[:, :])); vwait()
            for ch in range(NCH):
                T.wait_ge(s_gen, cnt["gen"])
                m = mm(T.transpose(ptr[:, :], x1t[:, 128 * ch:128 * (ch + 1)], tp_id[:, :]))
                V.wait_ge(s_mm, m)
                gen(V.tensor_copy(bigT[:, ch * OWN + 128 * k:ch * OWN + 128 * (k + 1)], ptr[:, :]))
        wffb = [wff, xfm[0]]
        groups = [(wsrc, dst, mcol, g)
                  for (wsrc, dst, mcol) in ((ag_u_o, s_up, 1), (ag_g_o, s_gt, 2))
                  for g in range(NG)]
        fstg = [fft, prodo]
        NGT = len(groups)

        def _issue_wffload(gi):
            wsrc, dst, mcol, g = groups[gi]
            buf = wffb[gi % 2]
            for ch in range(NCH):
                vv = ldma(buf[:, 512 * ch:512 * (ch + 1)],
                          wsrc[512 * g:512 * (g + 1), 128 * ch:128 * (ch + 1)], transpose=True, guard=False)
            return vv

        SY.wait_ge(s_mm, cnt["mm"])
        SY.wait_ge(ld, cnt["ld"])
        v_w = [None] * NGT
        v_w[0] = _issue_wffload(0)
        evac_gen = {}
        st_cnt = {}
        mm_end = {}
        ts0 = cnt["gen"]
        T.wait_ge(s_gen, ts0)
        tser = 0
        for gi in range(NGT):
            wsrc, dst, mcol, g = groups[gi]
            buf = wffb[gi % 2]
            T.wait_ge(ld, v_w[gi])
            for k in range(NTI):
                bank = pp[k % 2]
                if tser >= 2:
                    T.wait_ge(s_gen, evac_gen[tser - 2])
                for ch in range(NCH):
                    mi = T.matmul(bank[:, 0:512], bigT[:, ch * OWN + 128 * k:ch * OWN + 128 * (k + 1)],
                                  buf[:, 512 * ch:512 * (ch + 1)],
                                  start=(ch == 0), stop=(ch == NCH - 1), skip_group_check=True)
                    if ch == NCH - 1:
                        m = mm(mi)
                if k == 0 and gi + 1 < NGT:
                    if gi >= 1:
                        SY.wait_ge(s_mm, mm_end[gi - 1])
                    SY.wait_ge(ld, cnt["ld"])
                    v_w[gi + 1] = _issue_wffload(gi + 1)
                V.wait_ge(s_mm, m)
                gen(V.reduce_max(tok4[:, 6:7], bank[:, 0:512], AX.X, apply_absolute_value=True)); vwait()
                gen(V.tensor_max(maxa[:, mcol:mcol + 1], maxa[:, mcol:mcol + 1], tok4[:, 6:7])); vwait()
                if tser >= 2:
                    V.wait_ge(sd_o, st_cnt[tser - 2])
                gen(V.tensor_scalar(fstg[tser % 2][:, 0:512], bank[:, 0:512], scal[:, 0:1], None, ALU.mult))
                evac_gen[tser] = cnt["gen"]
                G.wait_ge(s_gen, cnt["gen"])
                st_cnt[tser] = odma(dst[128 * k:128 * (k + 1), 512 * g:512 * (g + 1)], fstg[tser % 2][:, 0:512])
                tser += 1
            mm_end[gi] = cnt["mm"]
        vwait()
        gen(V.tensor_scalar(maxa[:, 1:2], maxa[:, 1:2], scal[:, 0:1], None, ALU.mult)); vwait()
        gen(V.tensor_scalar(maxa[:, 2:3], maxa[:, 2:3], scal[:, 0:1], None, ALU.mult)); vwait()
        maxcol_to_row(1, 1)
        maxcol_to_row(2, 2)
        G.wait_ge(s_gen, cnt["gen"])
        odma(cc_i[0:1, 0:8], ccrow[0:1, 0:8])
        G.wait_ge(sd_o, cnt["sd_o"])
        cnt["cc"] += 1
        G.collective_compute("AllReduce", ALU.max, replica_groups=RG,
                             ins=[cc_i[:, :]], outs=[cc_o[:, :]]).then_inc(s_cc, 1)
        G.wait_ge(s_cc, cnt["cc"])
        SY.wait_ge(s_cc, cnt["cc"])
        v = ldma(ccrow2[0:1, 0:8], cc_o[0:1, 0:8])
        V.wait_ge(ld, v)
        gen(V.tensor_scalar(ccrow2[0:1, 3:4], ccrow2[0:1, 1:2], 1.0 / 32767.0, 1e-12, ALU.mult, ALU.max)); vwait()
        gen(V.tensor_scalar(ccrow2[0:1, 4:5], ccrow2[0:1, 2:3], 1.0 / 32767.0, 1e-12, ALU.mult, ALU.max)); vwait()
        gen(V.reciprocal(ccrow2[0:1, 5:6], ccrow2[0:1, 3:4])); vwait()
        gen(V.reciprocal(ccrow2[0:1, 6:7], ccrow2[0:1, 4:5])); vwait()
        bcast_scal(2, ccrow2[0:1, 3:4])
        bcast_scal(3, ccrow2[0:1, 5:6])
        bcast_scal(4, ccrow2[0:1, 4:5])
        bcast_scal(5, ccrow2[0:1, 6:7])
        gen(V.memset(ccrow[:, :], 0.0)); vwait()

        wdTb = bigT
        dwn = bigT.bitcast(F32)
        wd_jobs = [(ic, dch) for ic in range(DIC) for dch in range(NCH)]
        v_wd = 16 * len(wd_jobs)

        wd_cnt = [0]

        def _issue_wd(jobs):
            for (ic, dch) in jobs:
                SY.wait_ge(go[0], wd_cnt[0])
                wd_cnt[0] += 16
                SY.dma_start_transpose(
                    wdTb[:, (ic * NCH + dch) * 128:(ic * NCH + dch) * 128 + 128],
                    ag_d_o[128 * dch:128 * (dch + 1), 128 * ic:128 * (ic + 1)]).then_inc(go[0], 16)
        SY.wait_ge(s_mm, cnt["mm"])
        SY.wait_ge(s_gen, cnt["gen"])
        sgin = [x1t[:, 0:512], x1t[:, 512:1024]]
        ytb = ytile.bitcast(BF16)
        sgout = [ytb[:, 0:512], ytb[:, 512:1024]]
        sg_iters = [(sl, ic) for sl in range(NSLO) for ic in range(DIC)]
        sg_ldv = {}
        sg_head = {}
        sg_st = {}
        SY.wait_ge(s_gen, cnt["gen"])
        SY.wait_ge(ld, cnt["ld"])
        SY.wait_ge(sd_o, cnt["sd_o"])
        sl0, ic0 = sg_iters[0]
        sg_ldv[0] = ldma2("la", sgin[0], s_gt[512 * sl0:512 * (sl0 + 1), 128 * ic0:128 * (ic0 + 1)],
                          transpose=True)
        for t, (sl, ic) in enumerate(sg_iters):
            if t + 1 < len(sg_iters):
                if t >= 1:
                    SY.wait_ge(s_gen, sg_head[t - 1])
                sl2, ic2 = sg_iters[t + 1]
                sg_ldv[t + 1] = ldma2("la" if (t + 1) % 2 == 0 else "lb", sgin[(t + 1) % 2],
                                      s_gt[512 * sl2:512 * (sl2 + 1), 128 * ic2:128 * (ic2 + 1)],
                                      transpose=True)
            V.wait_ge(LSEM["la" if t % 2 == 0 else "lb"], sg_ldv[t])
            gen(V.tensor_scalar(ffta[:, 0:512], sgin[t % 2], scal[:, 5:6], None, ALU.mult))
            sg_head[t] = cnt["gen"]
            vwait()
            wd_per = -(-len(wd_jobs) // len(sg_iters))
            _issue_wd(wd_jobs[t * wd_per:(t + 1) * wd_per])
            gen(V.tensor_copy(i16t[:, 0:512], ffta[:, 0:512]))
            S.wait_ge(s_gen, cnt["gen"])
            a = act(S.activation(fftb[:, 0:512], i16t[:, 0:512], AF.Sigmoid, scale=scal[:, 4:5]))
            V.wait_ge(s_act, a)
            gen(V.tensor_scalar(fftb[:, 0:512], fftb[:, 0:512], 65536.0, None, ALU.mult)); vwait()
            gen(V.tensor_copy(u16t[:, 0:512], fftb[:, 0:512])); vwait()
            gen(V.tensor_mul(fftc[:, 0:512], i16t[:, 0:512], u16t[:, 0:512])); vwait()
            if t >= 2:
                V.wait_ge(sd_o, sg_st[t - 2])
            gen(V.tensor_copy(sgout[t % 2], fftc[:, 0:512])); vwait()
            gen(V.reduce_max(tok4[:, 6:7], fftc[:, 0:512], AX.X, apply_absolute_value=True)); vwait()
            gen(V.tensor_max(maxa[:, 3:4], maxa[:, 3:4], tok4[:, 6:7]))
            G.wait_ge(s_gen, cnt["gen"])
            sg_st[t] = odma(s_gs[128 * ic:128 * (ic + 1), 512 * sl:512 * (sl + 1)], sgout[t % 2])
        vwait()
        gen(V.tensor_scalar(maxa[:, 3:4], maxa[:, 3:4], scal[:, 4:5], 1.0 / 65536.0, ALU.mult, ALU.mult)); vwait()
        maxcol_to_row(3, 3)
        G.wait_ge(s_gen, cnt["gen"])
        odma(cc_i[0:1, 0:8], ccrow[0:1, 0:8])
        G.wait_ge(sd_o, cnt["sd_o"])
        cnt["cc"] += 1
        G.collective_compute("AllReduce", ALU.max, replica_groups=RG,
                             ins=[cc_i[:, :]], outs=[cc_o[:, :]]).then_inc(s_cc, 1)
        G.wait_ge(s_cc, cnt["cc"])
        SY.wait_ge(s_cc, cnt["cc"])
        v = ldma(ccrow2[0:1, 0:8], cc_o[0:1, 0:8])
        V.wait_ge(ld, v)
        gen(V.tensor_scalar(ccrow2[0:1, 3:4], ccrow2[0:1, 3:4], 1.0 / 32767.0, 1e-12, ALU.mult, ALU.max)); vwait()
        gen(V.reciprocal(ccrow2[0:1, 5:6], ccrow2[0:1, 3:4])); vwait()
        gen(V.tensor_scalar(ccrow2[0:1, 6:7], ccrow2[0:1, 5:6], 1.0 / 65536.0, None, ALU.mult)); vwait()
        bcast_scal(6, ccrow2[0:1, 3:4])
        bcast_scal(7, ccrow2[0:1, 6:7])
        vwait()
        gen(V.tensor_mul(scal[:, 8:9], scal[:, 7:8], scal[:, 4:5])); vwait()
        gen(V.memset(ccrow[:, :], 0.0)); vwait()
        gen(V.memset(maxa[:, 0:1], 0.0)); vwait()

        xtb = xtile.bitcast(BF16)
        gsin = [x1t[:, 0:512], x1t[:, 512:1024]]
        upin = [xtb[:, 0:512], xtb[:, 512:1024]]
        opout = [ytb[:, 0:512], ytb[:, 512:1024]]
        op_iters = [(sl, ic) for sl in range(NSLO) for ic in range(DIC)]
        op_ldv = {}
        op_head = {}
        op_st = {}
        SY.wait_ge(s_gen, cnt["gen"])
        SY.wait_ge(ld, cnt["ld"])
        SY.wait_ge(sd_o, cnt["sd_o"])
        sl0, ic0 = op_iters[0]
        op_gv = {}
        op_gv[0] = ldma2("la", gsin[0], s_gs[128 * ic0:128 * (ic0 + 1), 512 * sl0:512 * (sl0 + 1)])
        op_ldv[0] = ldma2("lc", upin[0], s_up[512 * sl0:512 * (sl0 + 1), 128 * ic0:128 * (ic0 + 1)],
                          transpose=True)
        for t, (sl, ic) in enumerate(op_iters):
            if t + 1 < len(op_iters):
                if t >= 1:
                    SY.wait_ge(s_gen, op_head[t - 1])
                sl2, ic2 = op_iters[t + 1]
                ka = "la" if (t + 1) % 2 == 0 else "lb"
                kc = "lc" if (t + 1) % 2 == 0 else "ld2"
                op_gv[t + 1] = ldma2(ka, gsin[(t + 1) % 2],
                                     s_gs[128 * ic2:128 * (ic2 + 1), 512 * sl2:512 * (sl2 + 1)])
                op_ldv[t + 1] = ldma2(kc, upin[(t + 1) % 2],
                                      s_up[512 * sl2:512 * (sl2 + 1), 128 * ic2:128 * (ic2 + 1)],
                                      transpose=True)
            V.wait_ge(LSEM["la" if t % 2 == 0 else "lb"], op_gv[t])
            V.wait_ge(LSEM["lc" if t % 2 == 0 else "ld2"], op_ldv[t])
            gen(V.tensor_scalar(ffta[:, 0:512], gsin[t % 2], scal[:, 8:9], None, ALU.mult)); vwait()
            gen(V.tensor_copy(i16t[:, 0:512], ffta[:, 0:512])); vwait()
            gen(V.tensor_scalar(fftb[:, 0:512], upin[t % 2], scal[:, 3:4], None, ALU.mult))
            op_head[t] = cnt["gen"]
            vwait()
            gen(V.tensor_copy(i16t[:, 512:1024], fftb[:, 0:512])); vwait()
            gen(V.tensor_mul(fftc[:, 0:512], i16t[:, 0:512], i16t[:, 512:1024])); vwait()
            if t >= 2:
                V.wait_ge(sd_o, op_st[t - 2])
            gen(V.tensor_copy(opout[t % 2], fftc[:, 0:512])); vwait()
            gen(V.reduce_max(tok4[:, 6:7], fftc[:, 0:512], AX.X, apply_absolute_value=True)); vwait()
            gen(V.tensor_max(maxa[:, 0:1], maxa[:, 0:1], tok4[:, 6:7]))
            G.wait_ge(s_gen, cnt["gen"])
            op_st[t] = odma(s_op[128 * ic:128 * (ic + 1), 512 * sl:512 * (sl + 1)], opout[t % 2])
        vwait()
        gen(V.tensor_mul(scal[:, 9:10], scal[:, 6:7], scal[:, 2:3])); vwait()
        gen(V.tensor_scalar(maxa[:, 0:1], maxa[:, 0:1], scal[:, 9:10], None, ALU.mult)); vwait()
        maxcol_to_row(0, 4)
        G.wait_ge(s_gen, cnt["gen"])
        odma(cc_i[0:1, 0:8], ccrow[0:1, 0:8])
        G.wait_ge(sd_o, cnt["sd_o"])
        cnt["cc"] += 1
        G.collective_compute("AllReduce", ALU.max, replica_groups=RG,
                             ins=[cc_i[:, :]], outs=[cc_o[:, :]]).then_inc(s_cc, 1)
        G.wait_ge(s_cc, cnt["cc"])
        SY.wait_ge(s_cc, cnt["cc"])
        v = ldma(ccrow2[0:1, 0:8], cc_o[0:1, 0:8])
        V.wait_ge(ld, v)
        gen(V.tensor_scalar(ccrow2[0:1, 4:5], ccrow2[0:1, 4:5], 1.0 / 32767.0, 1e-12, ALU.mult, ALU.max)); vwait()
        gen(V.reciprocal(ccrow2[0:1, 5:6], ccrow2[0:1, 4:5])); vwait()
        bcast_scal(10, ccrow2[0:1, 4:5])
        bcast_scal(11, ccrow2[0:1, 5:6])
        vwait()
        gen(V.tensor_mul(scal[:, 12:13], scal[:, 9:10], scal[:, 11:12])); vwait()

        outtb = [ytile, xtile2]
        x2b = [x2t, xtile]
        din = [fft, prodo]
        T.wait_ge(go[0], v_wd)
        m_prev_sl = None
        out_st = {}
        x2_ldv = {}
        last_cp = {0: None, 1: None}
        dn0_sdo = cnt["sd_o"]
        for sl in range(NSLO):
            dn_ldv = {}
            dn_head = {}
            SY.wait_ge(s_gen, cnt["gen"])
            SY.wait_ge(ld, cnt["ld"])
            SY.wait_ge(sd_o, cnt["sd_o"])
            dn_ldv[0] = ldma2("la", din[0][:, 0:512], s_op[0:128, 512 * sl:512 * (sl + 1)])
            for ic in range(DIC):
                if ic + 1 < DIC:
                    if ic >= 1:
                        SY.wait_ge(s_gen, dn_head[ic - 1])
                    dn_ldv[ic + 1] = ldma2("la" if (ic + 1) % 2 == 0 else "lb",
                                           din[(ic + 1) % 2][:, 0:512],
                                           s_op[128 * (ic + 1):128 * (ic + 2), 512 * sl:512 * (sl + 1)])
                V.wait_ge(LSEM["la" if ic % 2 == 0 else "lb"], dn_ldv[ic])
                if ic == 0 and m_prev_sl is not None:
                    V.wait_ge(s_mm, m_prev_sl)
                gen(V.tensor_scalar(ffta[:, 0:512], din[ic % 2][:, 0:512], scal[:, 12:13], None, ALU.mult))
                dn_head[ic] = cnt["gen"]
                vwait()
                gen(V.tensor_copy(i16t[:, 0:512], ffta[:, 0:512])); vwait()
                gen(V.tensor_copy(oslc[:, 512 * ic:512 * (ic + 1)], i16t[:, 0:512])); vwait()
            conv_end = cnt["gen"]
            T.wait_ge(s_gen, conv_end)
            ev_gen = {}
            for dch in range(NCH):
                bank = pp[dch % 2]
                if dch >= 2:
                    T.wait_ge(s_gen, ev_gen[dch - 2])
                for ic in range(DIC):
                    mi = T.matmul(bank[:, 0:512], wdTb[:, (ic * NCH + dch) * 128:(ic * NCH + dch) * 128 + 128],
                                  oslc[:, 512 * ic:512 * (ic + 1)],
                                  start=(ic == 0), stop=(ic == DIC - 1), skip_group_check=True)
                    if ic == DIC - 1:
                        m = mm(mi)
                V.wait_ge(s_mm, m)
                gen(V.tensor_scalar(dwn[:, DIC * DM // 2 + 512 * dch:DIC * DM // 2 + 512 * (dch + 1)], bank[:, 0:512],
                                    scal[:, 10:11], None, ALU.mult))
                ev_gen[dch] = cnt["gen"]
                vwait()
            m_prev_sl = m
            T.wait_ge(s_gen, ev_gen[NCH - 1])
            tp_dst = [pq, pp[2]]
            for sub in range(4):
                sser = sl * 4 + sub
                trow = 512 * sl + 128 * sub
                v1 = ldma(x2b[sser % 2][:, :], s_x2[trow:trow + 128, :])
                x2_ldv[sser] = v1
                ot = outtb[sser % 2]
                for dch in range(NCH):
                    p = dch % 2
                    dst = tp_dst[p]
                    if last_cp[p] is not None:
                        T.wait_ge(s_gen, last_cp[p])
                    m2 = mm(T.matmul(dst[:, 0:128], dwn[:, DIC * DM // 2 + 512 * dch + 128 * sub:DIC * DM // 2 + 512 * dch + 128 * (sub + 1)],
                                     identf[:, :], start=True, stop=True, is_transpose=True))
                    V.wait_ge(s_mm, m2)
                    if dch == 0:
                        V.wait_ge(sd_o, out_st[sser - 2] if sser >= 2 else dn0_sdo)
                    gen(V.tensor_copy(ot[:, 128 * dch:128 * (dch + 1)], dst[:, 0:128]))
                    last_cp[p] = cnt["gen"]
                    vwait()
                gen(V.bn_stats(stat6[:, 0:6], ot[:, 0:512])); vwait()
                gen(V.bn_stats(stat6[:, 6:12], ot[:, 512:1024])); vwait()
                gen(V.bn_aggr(stat2[:, 0:2], stat6[:, 0:12])); vwait()
                gen(V.tensor_scalar(tok4[:, 2:3], stat2[:, 1:2], EPS, None, ALU.add))
                S.wait_ge(s_gen, cnt["gen"])
                a = act(S.activation(tok4[:, 3:4], tok4[:, 2:3], AF.Sqrt))
                V.wait_ge(s_act, a)
                gen(V.reciprocal(tok4[:, 1:2], tok4[:, 3:4])); vwait()
                gen(V.tensor_scalar(ot[:, :], ot[:, :], stat2[:, 0:1], tok4[:, 1:2],
                                    ALU.subtract, ALU.mult)); vwait()
                V.wait_ge(ld, x2_ldv[sser])
                gen(V.tensor_add(ot[:, :], ot[:, :], x2b[sser % 2][:, :])); vwait()
                G.wait_ge(s_gen, cnt["gen"])
                out_st[sser] = odma(out[trow:trow + 128, :], ot[:, :])
        G.wait_ge(sd_o, cnt["sd_o"])
    return nc


def _stage(inputs, q, b, SEQ, DI):
    x = np.ascontiguousarray(inputs["L0_attn_x"][b]).astype(np.float32)
    OWN = SEQ // 4
    whh = inputs["L0_attn_W_hh"]
    wih = inputs["L0_attn_W_ih"]
    wall = np.zeros((DM, 512), np.float32)
    wall[:, 0:128] = wih[256:384].T * 0 if False else wih[0:128].T
    wall[:, 128:256] = wih[128:256].T
    wall[:, 256:384] = wih[256:384].T
    wall[:, 384:448] = inputs["L0_attn_Bw"].T
    wall[:, 448:450] = inputs["L0_attn_gate_w"].T
    bias_g = np.zeros((128, 4), np.float32)
    bih = inputs["L0_attn_b_ih"]
    bias_g[:, 0] = bih[0:128]
    bias_g[:, 1] = bih[128:256]
    bias_g[:, 2] = bih[256:384]
    bias_g[0:SSM, 3] = inputs["L0_attn_Bb"]
    bias_g[SSM:SSM + 2, 3] = inputs["L0_attn_gate_b"]
    bhh_g = np.ascontiguousarray(inputs["L0_attn_b_hh"].reshape(3, 128).T)
    d = {
        "xb": x,
        "xq": np.ascontiguousarray(x[q * OWN:(q + 1) * OWN]),
        "role": np.array([[q, b, 0, 0, 0, 0, 0, 0]], np.int32),
        "whhT": np.ascontiguousarray(inputs["L0_attn_W_hh"].T),
        "aT": np.ascontiguousarray(inputs["L0_attn_A"].T),
        "wall": wall,
        "bias_g": bias_g,
        "bhh_g": bhh_g,
        "ident": np.eye(128, dtype=np.float32),
        "cwT": np.ascontiguousarray(inputs["L0_attn_Cw"].T),
        "projT": np.ascontiguousarray(inputs["L0_attn_proj_w"].T),
        "cb_row": inputs["L0_attn_Cb"].reshape(1, DM).astype(np.float32),
        "db_row": inputs["L0_attn_Db"].reshape(1, DM).astype(np.float32),
        "pb_row": inputs["L0_attn_proj_b"].reshape(1, DM).astype(np.float32),
        "dwT": np.ascontiguousarray(inputs["L0_attn_Dw"].T),
        "wg_s": np.ascontiguousarray(inputs["L0_ffn_w_gate"][(q + 4 * b) * (DI // 8):(q + 4 * b + 1) * (DI // 8)]),
        "wu_s": np.ascontiguousarray(inputs["L0_ffn_w_up"][(q + 4 * b) * (DI // 8):(q + 4 * b + 1) * (DI // 8)]),
        "wd_s": np.ascontiguousarray(inputs["L0_ffn_w_down"][(q + 4 * b) * (DM // 8):(q + 4 * b + 1) * (DM // 8)]),
    }
    return {k: np.asarray(vv) for k, vv in d.items()}


def kernel(**inputs):
    SEQ = int(inputs["L0_attn_x"].shape[1])
    DI = int(inputs["L0_ffn_w_up"].shape[0])
    nc = build_program(SEQ, DI)
    in_maps = []
    for core in range(8):
        b, q = core // 4, core % 4
        in_maps.append(_stage(inputs, q, b, SEQ, DI))
    res = run_bass_kernel_spmd(nc, in_maps, core_ids=list(range(8)))
    OWN = SEQ // 4
    outp = np.zeros((2, SEQ, DM), np.float32)
    for core in range(8):
        b, q = core // 4, core % 4
        outp[b, q * OWN:(q + 1) * OWN] = res.results[core]["out"]
    return outp
